# version 04
# baseline (speedup 1.0000x reference)
import numpy as np
import ml_dtypes
from contextlib import ExitStack

import concourse.bass as bass
import concourse.mybir as mybir
from concourse.bass_utils import run_bass_kernel_spmd
from concourse.alu_op_type import AluOpType as OP

F32 = mybir.dt.float32
BF16 = mybir.dt.bfloat16
I32 = mybir.dt.int32
AF = mybir.ActivationFunctionType
AX = mybir.AxisListType.X
bfdt = ml_dtypes.bfloat16

S, E, H, DH, HC = 2048, 1024, 16, 64, 4096
NT = S // 128
NP = H // 2
NDT = E // 128
NHC = HC // 128
CH = 256
NCH = S // CH
NW = 18
DEBUG = 0

ENGINES = ("sync", "gpsimd", "vector", "scalar", "tensor")


class Sched:
    def __init__(self):
        self.ops = {e: [] for e in ENGINES}
        self.cnt = {}
        self.last_wait = {e: {} for e in ENGINES}

    def v(self, sem):
        return self.cnt.get(sem, 0)

    def op(self, eng, fn, waits=(), inc=None, inc_n=1, bare_inc=False):
        w = []
        for s, val in waits:
            if val > 0 and self.last_wait[eng].get(s, -1) < val:
                w.append((s, val))
                self.last_wait[eng][s] = val
        self.ops[eng].append((fn, w, inc, inc_n, bare_inc))
        if inc is not None:
            self.cnt[inc] = self.cnt.get(inc, 0) + inc_n
        return self.cnt.get(inc, 0) if inc else None

    def drain(self, eng):
        self.op(eng, lambda h: h.drain())

    def barrier(self):
        for st in getattr(self, "streams", []):
            for w in st.wait_all():
                self.op("sync", None, waits=(w,))
        for e in ENGINES:
            self.op(e, None, inc="bar", inc_n=1, bare_inc=True)
        tot = self.v("bar")
        for e in ENGINES:
            self.op(e, None, waits=(("bar", tot),))

    def emit(self, block, sems):
        def make(ename):
            def body(h):
                for fn, waits, inc, inc_n, bare in self.ops[ename]:
                    for s, val in waits:
                        h.wait_ge(sems[s], val)
                    if bare:
                        h.sem_inc(sems[inc], inc_n)
                        continue
                    if fn is None:
                        continue
                    instr = fn(h)
                    if inc is not None:
                        instr.then_inc(sems[inc], inc_n)
            return body
        block.sync(make("sync"))
        block.gpsimd(make("gpsimd"))
        block.vector(make("vector"))
        block.scalar(make("scalar"))
        block.tensor(make("tensor"))


class Stream:

    def __init__(self, sc, eng, sems, names):
        self.sc = sc
        self.eng = eng
        self.sems = names
        self.counts = [0] * len(names)
        self.n = 0

    def dma(self, fn, waits=()):
        i = self.n
        k = i % len(self.sems)
        self.n += 1
        self.counts[k] += 1
        self.sc.op(self.eng, fn, waits=waits, inc=self.sems[k], inc_n=16)
        return i

    def wait_for(self, i):
        if i < 0:
            return []
        k = i % len(self.sems)
        return [(self.sems[k], 16 * (i // len(self.sems) + 1))]

    def wait_last(self):
        return self.wait_for(self.n - 1)

    def wait_all(self):
        return [(s, 16 * c) for s, c in zip(self.sems, self.counts) if c > 0]


class Banks:

    def __init__(self, banks, free0=0):
        self.banks = banks
        self.free_at = [free0] * len(banks)
        self.i = 0

    def get(self):
        k = self.i % len(self.banks)
        self.i += 1
        return self.banks[k], self.free_at[k], k

    def set_free(self, k, cv):
        self.free_at[k] = cv


def build_program():
    nc = bass.Bass("TRN2", target_bir_lowering=False)

    x_d = nc.dram_tensor("x", [S, E], F32, kind="ExternalInput").ap()
    wq_d = nc.dram_tensor("wq", [E, E], F32, kind="ExternalInput").ap()
    wk_d = nc.dram_tensor("wk", [E, E], F32, kind="ExternalInput").ap()
    wv_d = nc.dram_tensor("wv", [E, E], F32, kind="ExternalInput").ap()
    toe_d = nc.dram_tensor("toe", [16, 31, 128, 128], F32, kind="ExternalInput").ap()
    gam_d = nc.dram_tensor("gam", [1, E], F32, kind="ExternalInput").ap()
    bet_d = nc.dram_tensor("bet", [1, E], F32, kind="ExternalInput").ap()
    w1_d = nc.dram_tensor("w1", [HC, E], F32, kind="ExternalInput").ap()
    cw_d = nc.dram_tensor("cw", [128, 3 * NHC], F32, kind="ExternalInput").ap()
    cb_d = nc.dram_tensor("cb", [128, NHC], F32, kind="ExternalInput").ap()
    w2_d = nc.dram_tensor("w2", [E, HC], F32, kind="ExternalInput").ap()
    out_d = nc.dram_tensor("out", [S, E], F32, kind="ExternalOutput").ap()

    qT_d = nc.dram_tensor("qT_s", [E, S], BF16, kind="Internal").ap()
    kT_d = nc.dram_tensor("kT_s", [E, S], BF16, kind="Internal").ap()
    o_d = nc.dram_tensor("o_s", [S, E], F32,
                         kind="ExternalOutput" if DEBUG == 1 else "Internal").ap()
    x2_d = nc.dram_tensor("x2_s", [S, E], F32,
                          kind="ExternalOutput" if DEBUG == 2 else "Internal").ap()
    y_d = nc.dram_tensor("y_s", [S, E], F32,
                         kind="ExternalOutput" if DEBUG == 3 else "Internal").ap()
    w2qT_d = nc.dram_tensor("w2qT_s", [HC, E], BF16, kind="Internal").ap()
    v_sd = nc.dram_tensor("v_sd", [S, E], BF16, kind="Internal").ap()
    rx_d = nc.dram_tensor("rx_d", [NT, 128], F32, kind="Internal").ap()
    mx_d = nc.dram_tensor("mx_d", [2, 128], F32, kind="Internal").ap()
    xqT_d = nc.dram_tensor("xqT_s", [E, S], BF16, kind="Internal").ap()

    sc = Sched()
    root = ExitStack()
    sems = {}
    stream_sems = ["gax", "toe", "cwl", "qk", "vl", "vs",
                   "sq0", "sq1", "so0", "so1", "wl0", "wl1", "sw",
                   "lc0", "lc1", "lc2", "lc3", "sx2", "sxq0", "sxq1", "srx",
                   "lq0", "lq1", "lw20", "lw21", "lr", "sm", "lm",
                   "sy0", "sy1", "le0", "le1", "le2", "le3", "se0", "se1"]
    for s in ["cT", "cV", "cS", "cG", "bar"] + stream_sems:
        sems[s] = root.enter_context(nc.semaphore(s))
    sc.streams = []

    def mkstream(eng, names):
        st = Stream(sc, eng, sems, names)
        sc.streams.append(st)
        return st

    st_gax = mkstream("gpsimd", ["gax"])
    st_toe = mkstream("gpsimd", ["toe"])
    st_cwl = mkstream("sync", ["cwl"])
    st_qk = mkstream("sync", ["qk"])
    st_vl = mkstream("sync", ["vl"])
    st_vs = mkstream("scalar", ["vs"])
    st_sq = mkstream("scalar", ["sq0", "sq1"])
    st_so = mkstream("scalar", ["so0", "so1"])
    st_wl = mkstream("sync", ["wl0", "wl1"])
    st_sw = mkstream("scalar", ["sw"])
    st_lc = mkstream("sync", ["lc0", "lc1", "lc2", "lc3"])
    st_sx2 = mkstream("scalar", ["sx2"])
    st_sxq = mkstream("scalar", ["sxq0", "sxq1"])
    st_srx = mkstream("scalar", ["srx"])
    st_lq = mkstream("sync", ["lq0", "lq1"])
    st_lw2 = mkstream("sync", ["lw20", "lw21"])
    st_lr = mkstream("sync", ["lr"])
    st_sm = mkstream("scalar", ["sm"])
    st_lm = mkstream("sync", ["lm"])
    st_sy = mkstream("scalar", ["sy0", "sy1"])
    st_le = mkstream("sync", ["le0", "le1", "le2", "le3"])
    st_se = mkstream("sync", ["se0", "se1"])

    def sb(stack, name, shape, dt=F32):
        return stack.enter_context(nc.sbuf_tensor(name, shape, dt))

    ident_b = sb(root, "ident_b", [128, 128], BF16)
    ident_f = sb(root, "ident_f", [128, 128], F32)
    onesc_f = sb(root, "onesc_f", [1, 128], F32)
    ones65 = sb(root, "ones65", [65, 128], F32)
    cst = sb(root, "cst", [128, 4], F32)
    rx1 = sb(root, "rx1", [128, NT], F32)
    cw_sb = sb(root, "cw_sb", [128, 3 * NHC], F32)
    cb_sb = sb(root, "cb_sb", [128, NHC], F32)
    w1sc = sb(root, "w1sc", [128, 2], F32)
    rxrow_s = sb(root, "rxrow_s", [16, 128], F32)
    w2sc = sb(root, "w2sc", [128, 2], F32)
    esAB = ExitStack()
    v_pad = sb(esAB, "v_pad", [128, NT * H * 65], BF16)
    v_km = sb(esAB, "v_km", [128, 16 * E], BF16)

    esA = ExitStack()
    x_bf = sb(esA, "x_bf", [128, NT * E], BF16)
    xT = sb(esA, "xT", [128, NDT * S], BF16)
    wq_sb = sb(esA, "wq_sb", [128, NDT * E], BF16)
    wk_sb = sb(esA, "wk_sb", [128, NDT * E], BF16)
    wv_sb = sb(esA, "wv_sb", [128, NDT * E], BF16)
    stgA = sb(esA, "stgA", [128, 2 * 512], BF16)
    pA = [esA.enter_context(nc.psum_tensor(f"pA{i}", [128, 512], F32)) for i in range(6)]
    pAb = [esA.enter_context(nc.psum_tensor(f"pAb{i}", [128, 1024], BF16)) for i in range(2)]

    sc.op("gpsimd", lambda h: h.memset(cst[:, 0:1], 0.03125))
    sc.op("gpsimd", lambda h: h.memset(cst[:, 1:2], -0.5))
    sc.op("gpsimd", lambda h: h.memset(onesc_f[:], 1.0))
    sc.op("gpsimd", lambda h: h.memset(ones65[:], 1.0))
    sc.op("gpsimd", lambda h: h.memset(ident_b[:], 0.0))
    sc.op("gpsimd", lambda h: h.memset(ident_f[:], 0.0))
    sc.drain("gpsimd")
    sc.op("gpsimd", lambda h: h.affine_select(out=ident_b[:], in_=ident_b[:],
          compare_op=OP.not_equal, fill=1.0, base=0, pattern=[[-1, 128]],
          channel_multiplier=1))
    sc.op("gpsimd", lambda h: h.affine_select(out=ident_f[:], in_=ident_f[:],
          compare_op=OP.not_equal, fill=1.0, base=0, pattern=[[-1, 128]],
          channel_multiplier=1), inc="cG")
    vp4 = v_pad[:].rearrange("p (t h e) -> p t h e", t=NT, h=H, e=65)
    sc.op("gpsimd", lambda h: h.memset(vp4[:, :, :, 64:65], 1.0), inc="cG")

    st_gax.dma(lambda h: h.dma_start(
        out=x_bf[:].rearrange("p (t d) -> p t d", d=E),
        in_=x_d.rearrange("(t p) d -> p t d", p=128)))
    for wd, wsb in ((wq_d, wq_sb), (wk_d, wk_sb), (wv_d, wv_sb)):
        st_gax.dma(lambda h, wd=wd, wsb=wsb: h.dma_start(
            out=wsb[:].rearrange("p (dt e) -> p dt e", e=E),
            in_=wd.rearrange("(dt p) e -> p dt e", p=128)))
    dG_A = st_gax.wait_all()

    st_cwl.dma(lambda h: h.dma_start(out=cw_sb[:], in_=cw_d))
    st_cwl.dma(lambda h: h.dma_start(out=cb_sb[:], in_=cb_d))

    bksA = Banks(pAb)
    for t in range(NT):
        for dt in range(NDT):
            bk, freev, ki = bksA.get()
            sc.op("tensor", lambda h, bk=bk, t=t, dt=dt: h.transpose(
                bk[:, 0:128], x_bf[:, t * E + 128 * dt: t * E + 128 * (dt + 1)],
                ident_b[:]), waits=dG_A + [("cG", 1), ("cV", freev)], inc="cT")
            ev = sc.op("vector", lambda h, bk=bk, t=t, dt=dt: h.tensor_copy(
                xT[:, dt * S + 128 * t: dt * S + 128 * (t + 1)], bk[:, 0:128]),
                waits=[("cT", sc.v("cT"))], inc="cV")
            bksA.set_free(ki, ev)
    cV_xT = sc.v("cV")

    bksQ = Banks(pA[0:3])
    nstg = 0
    stg_dmas = []
    for wsb, dst in ((wq_sb, qT_d), (wk_sb, kT_d)):
        for j in range(NDT):
            for nch in range(4):
                bk, freev, ki = bksQ.get()
                for dt in range(NDT):
                    waits = [("cV", cV_xT)] + ([("cV", freev)] if dt == 0 else [])
                    sc.op("tensor", lambda h, bk=bk, wsb=wsb, j=j, nch=nch, dt=dt:
                          h.matmul(bk[:, 0:512],
                                   wsb[:, dt * E + 128 * j: dt * E + 128 * (j + 1)],
                                   xT[:, dt * S + 512 * nch: dt * S + 512 * (nch + 1)],
                                   start=(dt == 0), stop=(dt == NDT - 1)),
                          waits=waits, inc="cT" if dt == NDT - 1 else None)
                stx = stgA[:, (nstg % 2) * 512:(nstg % 2) * 512 + 512]
                wv2 = [("cT", sc.v("cT"))] + st_sq.wait_for(nstg - 2)
                ev = sc.op("vector", lambda h, stx=stx, bk=bk: h.tensor_copy(
                    stx, bk[:, 0:512]), waits=wv2, inc="cV")
                bksQ.set_free(ki, ev)
                st_sq.dma(lambda h, stx=stx, dst=dst, j=j, nch=nch:
                          h.dma_start(out=dst[128 * j:128 * (j + 1),
                                              512 * nch:512 * (nch + 1)], in_=stx),
                          waits=[("cV", ev)])
                nstg += 1
    v_nat = x_bf
    bksV = Banks(pA[3:6], free0=sc.v("cV"))
    for t in range(NT):
        for nch in range(2):
            bk, freev, ki = bksV.get()
            for dt in range(NDT):
                waits = [("cV", cV_xT)] + ([("cV", freev)] if dt == 0 else [])
                sc.op("tensor", lambda h, bk=bk, t=t, nch=nch, dt=dt:
                      h.matmul(bk[:, 0:512],
                               xT[:, dt * S + 128 * t: dt * S + 128 * (t + 1)],
                               wv_sb[:, dt * E + 512 * nch: dt * E + 512 * (nch + 1)],
                               start=(dt == 0), stop=(dt == NDT - 1)),
                      waits=waits, inc="cT" if dt == NDT - 1 else None)
            ev = sc.op("vector", lambda h, t=t, nch=nch, bk=bk: h.tensor_copy(
                v_nat[:, t * E + 512 * nch: t * E + 512 * (nch + 1)], bk[:, 0:512]),
                waits=[("cT", sc.v("cT"))], inc="cV")
            bksV.set_free(ki, ev)
    cV_vp = sc.v("cV")
    st_vs.dma(lambda h: h.dma_start(
        out=v_sd.rearrange("(t p) e -> p t e", p=128),
        in_=v_nat[:].rearrange("p (t e) -> p t e", e=E)),
        waits=[("cV", cV_vp)])
    dvs = st_vs.wait_all()
    for t in range(NT):
        st_vl.dma(lambda h, t=t: h.dma_start(
            out=vp4[:, t, :, 0:64],
            in_=v_sd[128 * t:128 * (t + 1), :].rearrange(
                "p (hh e) -> p hh e", e=64)),
            waits=dvs)
    st_vl.dma(lambda h: h.dma_start(
        out=v_km[:].rearrange("p (k c) -> p k c", k=16),
        in_=v_sd.rearrange("(p k) c -> p k c", k=16)))
    sc.barrier()
    esA.close()

    esB = ExitStack()
    toe = sb(esB, "toe_sb", [128, 16 * NW * 128], BF16)
    qkp = sb(esB, "qkp", [128, 2 * S], BF16)
    p_bf = sb(esB, "p_bf", [128, 2 * 2048], BF16)
    pvA_sb = sb(esB, "pvA_sb", [65, 512], F32)
    pvB_sb = sb(esB, "pvB_sb", [65, 512], F32)
    bias_sb = sb(esB, "bias_sb", [128, 512], F32)
    rcp = sb(esB, "rcp", [128, 2], F32)
    tb_sb = sb(esB, "tb_sb", [128, 128], F32)
    o_stage = sb(esB, "o_stage", [128, 1024], F32)
    sAB = esB.enter_context(nc.psum_tensor("sAB", [128, 2048], F32))
    pvA = esB.enter_context(nc.psum_tensor("pvA", [128, 512], F32))
    pvB = esB.enter_context(nc.psum_tensor("pvB", [128, 512], F32))
    pb = [esB.enter_context(nc.psum_tensor(f"pb{i}", [128, 512], F32)) for i in range(2)]

    toev = toe[:].rearrange("p (k w a) -> p k w a", k=16, w=NW, a=128)

    def toe_load(dlt, wait_ct):
        slot = (dlt + 15) % NW
        w = [("cT", wait_ct)] if wait_ct > 0 else []
        st_toe.dma(lambda h, dlt=dlt, slot=slot: h.dma_start(
            out=toev[:, :, slot, :],
            in_=toe_d[:, dlt + 15].rearrange("k p a -> p k a")),
            waits=w)

    for dlt in range(-1, 16):
        toe_load(dlt, 0)
    dG_need = st_toe.wait_all()

    st_qk.dma(lambda h: h.dma_start(out=qkp[:, 0:S], in_=qT_d[0:128, :]))
    st_qk.dma(lambda h: h.dma_start(out=qkp[:, S:2 * S], in_=kT_d[0:128, :]))

    exp_done = 0
    o_dmas = []
    pv_free = 0
    tr_free = 0
    bias_bank_free = [0, 0]
    for j in range(NP):
        hA, hB = 2 * j, 2 * j + 1
        dH_pair = st_qk.wait_all()
        qperm = qkp[:, 0:S].rearrange("p (a c) -> p c a", a=128, c=16)
        for qc in range(4):
            c0 = 4 * qc
            slA0 = ((c0 - hA) + 15) % NW
            slB0 = ((c0 - hB) + 15) % NW
            fill = []
            for kap in range(16):
                w0 = []
                if kap == 0:
                    w0 = list(dG_need) + [("cV", bias_bank_free[0]),
                                          ("cV", bias_bank_free[1])]
                fill.append((lambda h, kap=kap, slA0=slA0, j=j: h.matmul(
                    pb[0][0:64, 0:128],
                    v_km[:, kap * E + 128 * j: kap * E + 128 * j + 64],
                    toe[:, kap * (NW * 128) + slA0 * 128:
                        kap * (NW * 128) + (slA0 + 1) * 128],
                    start=(kap == 0), stop=(kap == 15), tile_position=(0, 0)), w0))
                fill.append((lambda h, kap=kap, slB0=slB0, j=j: h.matmul(
                    pb[1][64:128, 0:128],
                    v_km[:, kap * E + 128 * j + 64: kap * E + 128 * (j + 1)],
                    toe[:, kap * (NW * 128) + slB0 * 128:
                        kap * (NW * 128) + (slB0 + 1) * 128],
                    start=(kap == 0), stop=(kap == 15), tile_position=(0, 64)), ()))
            fi = 0
            for g in range(8):
                u0, u1 = 2 * g, 2 * g + 1
                wqk = list(dH_pair)
                if exp_done:
                    wqk.append(("cS", exp_done))
                for ui, u in enumerate((u0, u1)):
                    off = ui * 1024
                    sc.op("tensor", lambda h, u=u, qc=qc, off=off, qperm=qperm:
                          h.matmul(sAB[:, off:off + 512],
                                   qkp[0:64, S + 128 * u: S + 128 * (u + 1)],
                                   qperm[0:64, 4 * qc:4 * qc + 4, :],
                                   start=True, stop=True),
                          waits=wqk if ui == 0 else ())
                    sc.op("tensor", lambda h, u=u, qc=qc, off=off, qperm=qperm:
                          h.matmul(sAB[:, off + 512:off + 1024],
                                   qkp[64:128, S + 128 * u: S + 128 * (u + 1)],
                                   qperm[64:128, 4 * qc:4 * qc + 4, :],
                                   start=True, stop=True),
                          inc="cT" if ui == 1 else None)
                qk_ct = sc.v("cT")
                if g > 0:
                    gp = (g - 1) % 2
                    for ui, u in enumerate((2 * g - 2, 2 * g - 1)):
                        wpv = []
                        if g == 1 and ui == 0 and pv_free:
                            wpv = [("cV", pv_free)]
                        sc.op("tensor", lambda h, u=u, hA=hA, gp=gp, ui=ui: h.matmul(
                            pvA[0:65, 0:512],
                            v_pad[:, u * (H * 65) + hA * 65: u * (H * 65) + hA * 65 + 65],
                            p_bf[:, gp * 2048 + ui * 1024: gp * 2048 + ui * 1024 + 512],
                            start=(u == 0), stop=False), waits=wpv)
                        sc.op("tensor", lambda h, u=u, hB=hB, gp=gp, ui=ui: h.matmul(
                            pvB[0:65, 0:512],
                            v_pad[:, u * (H * 65) + hB * 65: u * (H * 65) + hB * 65 + 65],
                            p_bf[:, gp * 2048 + ui * 1024 + 512:
                                 gp * 2048 + ui * 1024 + 1024],
                            start=(u == 0), stop=False))
                while fi < len(fill) and fi < 4 * (g + 1):
                    fn, w0 = fill[fi]
                    sc.op("tensor", fn, waits=w0,
                          inc="cT" if fi == len(fill) - 1 else None)
                    fi += 1
                exp_done = sc.op("scalar", lambda h, g=g: h.activation(
                    out=p_bf[:, (g % 2) * 2048:(g % 2) * 2048 + 2048],
                    in_=sAB[:], func=AF.Exp, scale=cst[:, 0:1]),
                    waits=[("cT", qk_ct), ("cG", 1)], inc="cS")
            while fi < len(fill):
                fn, w0 = fill[fi]
                sc.op("tensor", fn, waits=w0,
                      inc="cT" if fi == len(fill) - 1 else None)
                fi += 1
            bias_c0_ct = sc.v("cT")
            gp = 1
            for ui, u in enumerate((14, 15)):
                sc.op("tensor", lambda h, u=u, hA=hA, ui=ui: h.matmul(
                    pvA[0:65, 0:512],
                    v_pad[:, u * (H * 65) + hA * 65: u * (H * 65) + hA * 65 + 65],
                    p_bf[:, gp * 2048 + ui * 1024: gp * 2048 + ui * 1024 + 512],
                    start=False, stop=(u == 15)),
                    waits=[("cS", exp_done)])
                sc.op("tensor", lambda h, u=u, hB=hB, ui=ui: h.matmul(
                    pvB[0:65, 0:512],
                    v_pad[:, u * (H * 65) + hB * 65: u * (H * 65) + hB * 65 + 65],
                    p_bf[:, gp * 2048 + ui * 1024 + 512: gp * 2048 + ui * 1024 + 1024],
                    start=False, stop=(u == 15)),
                    inc="cT" if u == 15 else None)
            pv_done = sc.v("cT")
            sc.op("vector", lambda h: h.tensor_copy(pvA_sb[:], pvA[0:65, 0:512]),
                  waits=[("cT", pv_done)])
            pv_ev = sc.op("vector", lambda h: h.tensor_copy(
                pvB_sb[:], pvB[0:65, 0:512]), inc="cV")
            bias_ev = [0, 0, 0, 0]
            sc.op("vector", lambda h: h.tensor_copy(
                bias_sb[0:64, 0:128], pb[0][0:64, 0:128]),
                waits=[("cT", bias_c0_ct)])
            bias_ev[0] = sc.op("vector", lambda h: h.tensor_copy(
                bias_sb[64:128, 0:128], pb[1][64:128, 0:128]), inc="cV")
            for ci in range(1, 4):
                c = 4 * qc + ci
                slA = ((c - hA) + 15) % NW
                slB = ((c - hB) + 15) % NW
                for kap in range(16):
                    w0 = [("cV", bias_ev[ci - 1])] if kap == 0 else []
                    sc.op("tensor", lambda h, kap=kap, slA=slA, j=j: h.matmul(
                        pb[0][0:64, 0:128],
                        v_km[:, kap * E + 128 * j: kap * E + 128 * j + 64],
                        toe[:, kap * (NW * 128) + slA * 128:
                            kap * (NW * 128) + (slA + 1) * 128],
                        start=(kap == 0), stop=(kap == 15), tile_position=(0, 0)),
                        waits=w0)
                    sc.op("tensor", lambda h, kap=kap, slB=slB, j=j: h.matmul(
                        pb[1][64:128, 0:128],
                        v_km[:, kap * E + 128 * j + 64: kap * E + 128 * (j + 1)],
                        toe[:, kap * (NW * 128) + slB * 128:
                            kap * (NW * 128) + (slB + 1) * 128],
                        start=(kap == 0), stop=(kap == 15), tile_position=(0, 64)),
                        inc="cT" if kap == 15 else None)
                bct = sc.v("cT")
                sc.op("vector", lambda h, ci=ci: h.tensor_copy(
                    bias_sb[0:64, 128 * ci:128 * (ci + 1)], pb[0][0:64, 0:128]),
                    waits=[("cT", bct)])
                bias_ev[ci] = sc.op("vector", lambda h, ci=ci: h.tensor_copy(
                    bias_sb[64:128, 128 * ci:128 * (ci + 1)], pb[1][64:128, 0:128]),
                    inc="cV")
            bias_bank_free = [bias_ev[3], bias_ev[3]]
            half = (qc % 2) * 512
            last_stt = 0
            for ci in range(4):
                trbank = pb[ci % 2]
                wtr = [("cV", pv_ev)]
                if ci == 0:
                    wtr.append(("cV", bias_ev[3]))
                if ci >= 1:
                    wtr.append(("cV", last_stt))
                trA_ct = sc.op("tensor", lambda h, ci=ci: h.transpose(
                    pvA[:, 0:65], pvA_sb[0:65, 128 * ci:128 * (ci + 1)],
                    ident_f[0:65, 0:65]), waits=wtr, inc="cT")
                trB_ct = sc.op("tensor", lambda h, ci=ci: h.transpose(
                    pvB[:, 0:65], pvB_sb[0:65, 128 * ci:128 * (ci + 1)],
                    ident_f[0:65, 0:65]), inc="cT")
                trX_ct = sc.op("tensor", lambda h, ci=ci, trbank=trbank: h.transpose(
                    trbank[:, 0:128], bias_sb[:, 128 * ci:128 * (ci + 1)],
                    ident_f[:]), inc="cT")
                sc.op("vector", lambda h: h.reciprocal(rcp[:, 0:1], pvA[:, 64:65]),
                      waits=[("cT", trX_ct)])
                sc.drain("vector")
                sc.op("vector", lambda h: h.reciprocal(rcp[:, 1:2], pvB[:, 64:65]))
                sc.drain("vector")
                sc.op("vector", lambda h, trbank=trbank: h.tensor_copy(
                    tb_sb[:], trbank[:, 0:128]), inc="cV")
                sc.drain("vector")
                wst = st_so.wait_for(len(o_dmas) - 2)
                sc.op("vector", lambda h, ci=ci, half=half: h.scalar_tensor_tensor(
                    out=o_stage[:, half + 128 * ci: half + 128 * ci + 64],
                    in0=pvA[:, 0:64], scalar=rcp[:, 0:1], in1=tb_sb[:, 0:64],
                    op0=OP.mult, op1=OP.add), waits=wst)
                last_stt = sc.op("vector", lambda h, ci=ci, half=half:
                                 h.scalar_tensor_tensor(
                    out=o_stage[:, half + 128 * ci + 64: half + 128 * (ci + 1)],
                    in0=pvB[:, 0:64], scalar=rcp[:, 1:2], in1=tb_sb[:, 64:128],
                    op0=OP.mult, op1=OP.add), inc="cV")
            pv_free = last_stt
            tr_free = last_stt
            bias_bank_free = [last_stt, last_stt]
            dsto = o_d[:, 128 * j:128 * (j + 1)] \
                .rearrange("(a c) e -> a c e", a=128, c=16)[:, 4 * qc:4 * qc + 4, :]
            srco = o_stage[:, half:half + 512].rearrange("a (c e) -> a c e", c=4, e=128)
            st_so.dma(lambda h, dsto=dsto, srco=srco: h.dma_start(
                out=dsto, in_=srco), waits=[("cV", last_stt)])
            o_dmas.append(st_so.n)
        if j + 1 < NP:
            st_qk.dma(lambda h, j=j: h.dma_start(
                out=qkp[:, 0:S], in_=qT_d[128 * (j + 1):128 * (j + 2), :]),
                waits=[("cT", sc.v("cT"))])
            st_qk.dma(lambda h, j=j: h.dma_start(
                out=qkp[:, S:2 * S], in_=kT_d[128 * (j + 1):128 * (j + 2), :]))
            toe_load(-2 * (j + 1), sc.v("cT"))
            toe_load(-2 * (j + 1) - 1, sc.v("cT"))
            dG_need = st_toe.wait_all()
    sc.barrier()
    esB.close()
    esAB.close()

    esCD = ExitStack()
    w1qT = sb(esCD, "w1qT", [128, NDT * HC], BF16)
    esC = ExitStack()
    gb_sb = sb(esC, "gb_sb", [128, E], F32)
    bb_sb = sb(esC, "bb_sb", [128, E], F32)
    wt = sb(esC, "wt", [128, 2 * 2048], F32)
    mp_t = sb(esC, "mp_t", [128, 2048], F32)
    mm_t = sb(esC, "mm_t", [128, 2048], F32)
    wqn = sb(esC, "wqn", [128, 2048], BF16)
    stgW = sb(esC, "stgW", [128, HC], BF16)
    xa = sb(esC, "xa", [128, 2 * E], F32)
    ob = sb(esC, "ob", [128, 2 * E], F32)
    r1 = sb(esC, "r1", [128, E], F32)
    t1s = sb(esC, "t1s", [128, E], F32)
    frs = mm_t[:, 0:E]
    uq = esC.enter_context(nc.sbuf_tensor("uq", [128, E], I32))
    bns = mp_t[:, 0:E]
    xq_t = wqn[:, 0:E]
    stC = stgW
    stat = sb(esC, "stat", [128, 12], F32)
    wsum = sb(esC, "wsum", [128, NHC], F32)
    pC = [esC.enter_context(nc.psum_tensor(f"pC{i}", [128, 512], F32)) for i in range(4)]
    pCb = [esC.enter_context(nc.psum_tensor(f"pCb{i}", [128, 1024], BF16)) for i in range(4)]

    st_cwl.dma(lambda h: h.dma_start(out=gb_sb[0:1, :], in_=gam_d))
    st_cwl.dma(lambda h: h.dma_start(out=bb_sb[0:1, :], in_=bet_d))
    dH_gb = st_cwl.wait_all()
    for i, bc in enumerate((gb_sb, bb_sb)):
        for nch in range(2):
            ct = sc.op("tensor", lambda h, bc=bc, i=i, nch=nch: h.matmul(
                pC[2 * i + nch][:, 0:512], onesc_f[0:1, :],
                bc[0:1, 512 * nch:512 * (nch + 1)], start=True, stop=True),
                waits=dH_gb, inc="cT")
            sc.op("vector", lambda h, bc=bc, i=i, nch=nch: h.tensor_copy(
                bc[:, 512 * nch:512 * (nch + 1)], pC[2 * i + nch][:, 0:512]),
                waits=[("cT", ct)], inc="cV")
    gb_ready = sc.v("cV")

    def wquant(wi, wd, wscb):
        ntile = NHC if wi == 0 else NDT
        width = E if wi == 0 else HC
        npc = width // 2048 if width > 2048 else 1
        pw = width // npc
        cons = [sc.v("cV")] * 2
        ip = 0
        for tt in range(ntile):
            for pc in range(npc):
                par = ip % 2
                ip += 1
                buf = wt[:, par * 2048:par * 2048 + pw]
                i = st_wl.dma(lambda h, buf=buf, wd=wd, tt=tt, pc=pc, pw=pw:
                              h.dma_start(out=buf,
                                          in_=wd[128 * tt:128 * (tt + 1),
                                                 pw * pc:pw * (pc + 1)]),
                              waits=[("cV", cons[par])])
                cons[par] = sc.op("vector", lambda h, buf=buf, tt=tt, pc=pc, npc=npc:
                             h.reduce_sum(wsum[:, tt * npc + pc:tt * npc + pc + 1]
                                          if npc > 1 else wsum[:, tt:tt + 1],
                                          buf, axis=AX, apply_absolute_value=True),
                             waits=st_wl.wait_for(i), inc="cV")
        sc.drain("vector")
        sc.op("vector", lambda h, n=ntile * npc: h.reduce_sum(
            stat[:, 0:1], wsum[:, 0:n], axis=AX), inc="cV")
        ct = sc.op("tensor", lambda h: h.transpose(
            pC[0][0:1, 0:128], stat[:, 0:1], ident_f[0:128, 0:128]),
            waits=[("cV", sc.v("cV")), ("cV", gb_ready)], inc="cT")
        sc.op("vector", lambda h: h.reduce_sum(
            stat[0:1, 1:2], pC[0][0:1, 0:128], axis=AX), waits=[("cT", ct)])
        sc.drain("vector")
        sc.op("vector", lambda h: h.tensor_scalar(
            out=stat[0:1, 2:3], in0=stat[0:1, 1:2], scalar1=1.0 / (HC * E),
            scalar2=1e-5, op0=OP.mult, op1=OP.max))
        sc.drain("vector")
        cv = sc.op("vector", lambda h: h.reciprocal(stat[0:1, 3:4], stat[0:1, 2:3]),
                   inc="cV")
        ct = sc.op("tensor", lambda h: h.matmul(
            pC[1][:, 0:2], onesc_f[0:1, :], stat[0:1, 2:4], start=True, stop=True),
            waits=[("cV", cv)], inc="cT")
        scd = sc.op("vector", lambda h, wscb=wscb: h.tensor_copy(
            wscb[:], pC[1][:, 0:2]), waits=[("cT", ct)], inc="cV")
        bksW = Banks(pCb, free0=sc.v("cV"))
        stg_done = 0
        cons = [scd] * 2
        ip = 0
        wqn_free = 0
        for tt in range(ntile):
            for pc in range(npc):
                par = ip % 2
                ip += 1
                buf = wt[:, par * 2048:par * 2048 + pw]
                i = st_wl.dma(lambda h, buf=buf, wd=wd, tt=tt, pc=pc, pw=pw:
                              h.dma_start(out=buf,
                                          in_=wd[128 * tt:128 * (tt + 1),
                                                 pw * pc:pw * (pc + 1)]),
                              waits=[("cV", cons[par])])
                sc.op("vector", lambda h, buf=buf, pw=pw, wscb=wscb: h.tensor_scalar(
                    out=mp_t[:, 0:pw], in0=buf, scalar1=wscb[:, 1:2], scalar2=0.5,
                    op0=OP.mult, op1=OP.is_ge),
                    waits=st_wl.wait_for(i) + [("cT", wqn_free)])
                cons[par] = sc.op("vector", lambda h, buf=buf, pw=pw, wscb=wscb:
                             h.tensor_scalar(
                    out=mm_t[:, 0:pw], in0=buf, scalar1=wscb[:, 1:2], scalar2=-0.5,
                    op0=OP.mult, op1=OP.is_le), inc="cV")
                sc.drain("vector")
                qdone = sc.op("vector", lambda h, pw=pw: h.tensor_sub(
                    wqn[:, 0:pw], mp_t[:, 0:pw], mm_t[:, 0:pw]), inc="cV")
                nb = pw // 128
                for b in range(nb):
                    bg = pc * nb + b
                    bk, freev, ki = bksW.get()
                    ct = sc.op("tensor", lambda h, bk=bk, b=b: h.transpose(
                        bk[:, 0:128], wqn[:, 128 * b:128 * (b + 1)], ident_b[:]),
                        waits=[("cV", qdone), ("cV", freev)], inc="cT")
                    if wi == 0:
                        ev = sc.op("vector", lambda h, bk=bk, tt=tt, bg=bg:
                                   h.tensor_copy(
                            w1qT[:, bg * HC + 128 * tt: bg * HC + 128 * (tt + 1)],
                            bk[:, 0:128]), waits=[("cT", ct)], inc="cV")
                    else:
                        wv = [("cT", ct)] + st_sw.wait_for(st_sw.n - 1)
                        ev = sc.op("vector", lambda h, bk=bk, bg=bg: h.tensor_copy(
                            stgW[:, 128 * bg:128 * (bg + 1)], bk[:, 0:128]),
                            waits=wv, inc="cV")
                    bksW.set_free(ki, ev)
                wqn_free = sc.v("cT")
            if wi == 1:
                dstw = w2qT_d[:, 128 * tt:128 * (tt + 1)] \
                    .rearrange("(b p) e -> p b e", p=128)
                srcw = stgW[:].rearrange("p (b e) -> p b e", e=128)
                st_sw.dma(lambda h, dstw=dstw, srcw=srcw:
                          h.dma_start(out=dstw, in_=srcw),
                          waits=[("cV", sc.v("cV"))])

    wquant(0, w1_d, w1sc)
    wquant(1, w2_d, w2sc)
    wq_end_ct = sc.v("cT")
    wq_end_ds = st_sw.wait_all()

    bksX = Banks(pCb, free0=sc.v("cV"))
    xq_dmas = []
    for t in range(NT):
        par = t % 2
        xat = xa[:, par * E:(par + 1) * E]
        obt = ob[:, par * E:(par + 1) * E]
        w = [("cV", sc.v("cV") if t >= 2 else 0)]
        i1 = st_lc.dma(lambda h, xat=xat, t=t: h.dma_start(
            out=xat, in_=x_d[128 * t:128 * (t + 1), :]), waits=w)
        i2 = st_lc.dma(lambda h, obt=obt, t=t: h.dma_start(
            out=obt, in_=o_d[128 * t:128 * (t + 1), :]))
        sc.op("vector", lambda h, obt=obt: h.reduce_sum(
            stat[:, 0:1], obt, axis=AX),
            waits=st_lc.wait_for(i2) + [("cV", gb_ready)])
        sc.drain("vector")
        sc.op("vector", lambda h: h.tensor_scalar(
            out=stat[:, 1:2], in0=stat[:, 0:1], scalar1=-1.0 / E, scalar2=None,
            op0=OP.mult), inc="cV")
        mu_done = sc.v("cV")
        sq_done = sc.op("scalar", lambda h, obt=obt: h.activation(
            out=t1s[:], in_=obt, func=AF.Square, bias=stat[:, 1:2],
            accum_out=stat[:, 2:3]), waits=[("cV", mu_done)], inc="cS")
        sc.op("vector", lambda h: h.tensor_scalar(
            out=stat[:, 3:4], in0=stat[:, 2:3], scalar1=1.0 / E, scalar2=1e-3,
            op0=OP.mult, op1=OP.add), waits=[("cS", sq_done)], inc="cV")
        ln_done = sc.op("scalar", lambda h: h.activation(
            out=stat[:, 4:5], in_=stat[:, 3:4], func=AF.Ln),
            waits=[("cV", sc.v("cV"))], inc="cS")
        rstd_done = sc.op("scalar", lambda h: h.activation(
            out=stat[:, 5:6], in_=stat[:, 4:5], func=AF.Exp, scale=cst[:, 1:2]),
            waits=[("cS", ln_done)], inc="cS")
        sc.op("vector", lambda h, obt=obt: h.tensor_scalar(
            out=t1s[:], in0=obt, scalar1=stat[:, 1:2], scalar2=stat[:, 5:6],
            op0=OP.add, op1=OP.mult), waits=[("cS", rstd_done)])
        sc.drain("vector")
        sc.op("vector", lambda h: h.tensor_mul(t1s[:], t1s[:], gb_sb[:]))
        sc.drain("vector")
        sc.op("vector", lambda h: h.tensor_add(t1s[:], t1s[:], bb_sb[:]))
        sc.drain("vector")
        sc.op("vector", lambda h, xat=xat: h.tensor_add(r1[:], t1s[:], xat),
              waits=st_lc.wait_for(i1), inc="cV")
        sc.drain("vector")
        sc.op("vector", lambda h: h.reduce_sum(stat[:, 6:7], r1[:], axis=AX))
        sc.drain("vector")
        sc.op("vector", lambda h: h.tensor_scalar(
            out=stat[:, 7:8], in0=stat[:, 6:7], scalar1=-1.0 / E, scalar2=None,
            op0=OP.mult), inc="cV")
        sq2 = sc.op("scalar", lambda h: h.activation(
            out=t1s[:], in_=r1[:], func=AF.Square, bias=stat[:, 7:8],
            accum_out=stat[:, 8:9]), waits=[("cV", sc.v("cV"))], inc="cS")
        sc.op("vector", lambda h: h.tensor_scalar(
            out=stat[:, 9:10], in0=stat[:, 8:9], scalar1=1.0 / E, scalar2=1e-5,
            op0=OP.mult, op1=OP.add), waits=[("cS", sq2)], inc="cV")
        ln2 = sc.op("scalar", lambda h: h.activation(
            out=stat[:, 10:11], in_=stat[:, 9:10], func=AF.Ln),
            waits=[("cV", sc.v("cV"))], inc="cS")
        rstd2 = sc.op("scalar", lambda h: h.activation(
            out=stat[:, 11:12], in_=stat[:, 10:11], func=AF.Exp, scale=cst[:, 1:2]),
            waits=[("cS", ln2)], inc="cS")
        x2done = sc.op("vector", lambda h: h.tensor_scalar(
            out=t1s[:], in0=r1[:], scalar1=stat[:, 7:8], scalar2=stat[:, 11:12],
            op0=OP.add, op1=OP.mult), waits=[("cS", rstd2)], inc="cV")
        st_sx2.dma(lambda h, t=t: h.dma_start(
            out=x2_d[128 * t:128 * (t + 1), :], in_=t1s[:]),
            waits=[("cV", x2done)])
        sc.drain("vector")
        sc.op("vector", lambda h: h.tensor_reduce(
            out=stat[:, 0:1], in_=t1s[:], axis=AX, op=OP.max,
            apply_absolute_value=True))
        sc.drain("vector")
        sc.op("vector", lambda h: h.tensor_scalar(
            out=stat[:, 1:2], in0=stat[:, 0:1], scalar1=1e-5, scalar2=None,
            op0=OP.max))
        sc.drain("vector")
        sc.op("vector", lambda h, t=t: h.tensor_scalar(
            out=rx1[:, t:t + 1], in0=stat[:, 1:2], scalar1=1.0 / 127.0, scalar2=None,
            op0=OP.mult))
        sc.op("vector", lambda h: h.reciprocal(stat[:, 2:3], stat[:, 1:2]))
        sc.drain("vector")
        sc.op("vector", lambda h: h.tensor_scalar(
            out=stat[:, 3:4], in0=stat[:, 2:3], scalar1=127.0, scalar2=None,
            op0=OP.mult), inc="cV")
        sc.op("vector", lambda h: h.tensor_scalar(
            out=t1s[:], in0=t1s[:], scalar1=stat[:, 3:4], scalar2=128.5,
            op0=OP.mult, op1=OP.add),
            waits=[("cV", sc.v("cV"))] + st_sx2.wait_for(t))
        sc.drain("vector")
        sc.op("vector", lambda h: h.tensor_copy(uq[:], t1s[:]))
        sc.drain("vector")
        sc.op("vector", lambda h: h.tensor_sub(frs, t1s[:], uq[:]))
        sc.drain("vector")
        sc.op("vector", lambda h: h.scalar_tensor_tensor(
            out=t1s[:], in0=frs, scalar=1.0, in1=uq[:],
            op0=OP.is_ge, op1=OP.add))
        sc.op("vector", lambda h: h.tensor_scalar(
            out=bns, in0=frs, scalar1=0.0, scalar2=-1.0,
            op0=OP.is_lt, op1=OP.mult))
        sc.drain("vector")
        xqdone = sc.op("vector", lambda h: h.scalar_tensor_tensor(
            out=xq_t, in0=bns, scalar=-128.0, in1=t1s[:],
            op0=OP.add, op1=OP.add), waits=[("cT", wq_end_ct)], inc="cV")
        for b in range(NDT):
            bk, freev, ki = bksX.get()
            ct = sc.op("tensor", lambda h, bk=bk, b=b: h.transpose(
                bk[:, 0:128], xq_t[:, 128 * b:128 * (b + 1)], ident_b[:]),
                waits=[("cV", xqdone), ("cV", freev)], inc="cT")
            wv = [("cT", ct)] + st_sxq.wait_for(len(xq_dmas) - 2)
            if len(xq_dmas) == 0 and b == 0:
                wv += wq_end_ds
            ev = sc.op("vector", lambda h, bk=bk, b=b, par=par: h.tensor_copy(
                stC[:, par * 1024 + 128 * b: par * 1024 + 128 * (b + 1)],
                bk[:, 0:128]), waits=wv, inc="cV")
            bksX.set_free(ki, ev)
        dstx = xqT_d[:, 128 * t:128 * (t + 1)].rearrange("(b p) e -> p b e", p=128)
        srcx = stC[:, par * 1024:(par + 1) * 1024].rearrange("p (b e) -> p b e", e=128)
        st_sxq.dma(lambda h, dstx=dstx, srcx=srcx: h.dma_start(
            out=dstx, in_=srcx), waits=[("cV", sc.v("cV"))])
        xq_dmas.append(st_sxq.n)
    ct = sc.op("tensor", lambda h: h.transpose(
        pC[2][0:16, 0:128], rx1[:], ident_f[:]),
        waits=[("cV", sc.v("cV"))], inc="cT")
    rxcv = sc.op("vector", lambda h: h.tensor_copy(rxrow_s[:], pC[2][0:16, 0:128]),
                 waits=[("cT", ct)], inc="cV")
    st_srx.dma(lambda h: h.dma_start(out=rx_d, in_=rxrow_s[:]),
               waits=[("cV", rxcv)])
    sc.barrier()
    esC.close()

    esD = ExitStack()
    hsl = [sb(esD, f"hsl{i}", [128, NHC * CH], BF16) for i in range(2)]
    gsl = sb(esD, "gsl", [128, NHC * CH], BF16)
    xqc = sb(esD, "xqc", [128, 2 * NDT * CH], BF16)
    w2c = sb(esD, "w2c", [128, 2 * 4 * E], BF16)
    rxw = sb(esD, "rxw", [128, CH], F32)
    s2b = sb(esD, "s2b", [128, CH], F32)
    s2rep = sb(esD, "s2rep", [128, 2 * CH], F32)
    s2r2 = sb(esD, "s2r2", [2, 128], F32)
    rxr = sb(esD, "rxr", [1, CH], F32)
    rxr2 = sb(esD, "rxr2", [1, CH], F32)
    mxt = sb(esD, "mxt", [128, 8], F32)
    rx2d = sb(esD, "rx2d", [128, 2], F32)
    macc = sb(esD, "macc", [128, CH], F32)
    cacc = sb(esD, "cacc", [128, CH], F32)
    cacc2 = sb(esD, "cacc2", [128, CH], F32)
    halo = [sb(esD, f"halo{i}", [128, NHC], BF16) for i in range(3)]
    zcol = sb(esD, "zcol", [128, NHC], BF16)
    tq2p = [sb(esD, f"tq2{i}", [128, 2 * CH], F32) for i in range(2)]
    uq2p = [esD.enter_context(nc.sbuf_tensor(f"uq2{i}", [128, 2 * CH], I32))
            for i in range(2)]
    fr2p = [sb(esD, f"fr2{i}", [128, 2 * CH], F32) for i in range(2)]
    bn2p = [sb(esD, f"bn2{i}", [128, 2 * CH], F32) for i in range(2)]
    gtmp = sb(esD, "gtmp", [128, CH], F32)
    y_sb = sb(esD, "y_sb", [128, E], F32)
    pD = [esD.enter_context(nc.psum_tensor(f"pD{i}", [128, 512], F32)) for i in range(8)]

    sc.op("gpsimd", lambda h: h.memset(zcol[:], 0.0), inc="cG")
    cG_z = sc.v("cG")

    xq_dh = {}
    h_done = {}
    rxr_free = [0]
    rxr2_free = [0]
    w2_prev = [0]
    q2_prev = [0, 0]
    ffn2_done = {}
    pd6_free = 0
    pd45_free = 0
    y_dmas = []

    def load_xq_chunk(k):
        dst = xqc[:, (k % 2) * NDT * CH:(k % 2 + 1) * NDT * CH] \
            .rearrange("p (b s) -> p b s", s=CH)
        src = xqT_d[:, CH * k:CH * (k + 1)].rearrange("(b p) s -> p b s", p=128)
        w = [("cT", h_done.get(k - 2, (0, 0))[1])] if k >= 2 else []
        return st_lq.dma(lambda h, dst=dst, src=src: h.dma_start(
            out=dst, in_=src), waits=w)

    def ffn1(k):
        nonlocal pd6_free
        dh = xq_dh[k]
        kk = k % 2
        i = st_lr.dma(lambda h, k=k: h.dma_start(
            out=rxr[0:1, :].rearrange("o (a b) -> o a b", a=2),
            in_=rx_d[2 * k:2 * k + 2, :]),
            waits=[("cT", rxr_free[0])] + st_srx.wait_all())
        ct = sc.op("tensor", lambda h: h.matmul(
            pD[6][:, 0:CH], onesc_f[0:1, :], rxr[0:1, :], start=True, stop=True),
            waits=st_lr.wait_for(i) + [("cV", pd6_free)], inc="cT")
        rxr_free[0] = ct
        rxwd = sc.op("vector", lambda h: h.tensor_scalar(
            out=rxw[:], in0=pD[6][:, 0:CH], scalar1=w1sc[:, 0:1], scalar2=None,
            op0=OP.mult), waits=[("cT", ct)], inc="cV")
        pd6_free = rxwd
        bks = Banks(pD[0:4], free0=sc.v("cV"))
        hguard = ffn2_done.get(k - 2, 0)
        for hc in range(NHC):
            bk, freev, ki = bks.get()
            for dt in range(NDT):
                w = st_lq.wait_for(dh) + ([("cV", freev)] if dt == 0 else [])
                sc.op("tensor", lambda h, bk=bk, hc=hc, dt=dt, kk=kk: h.matmul(
                    bk[:, 0:CH],
                    w1qT[:, dt * HC + 128 * hc: dt * HC + 128 * (hc + 1)],
                    xqc[:, kk * NDT * CH + dt * CH: kk * NDT * CH + (dt + 1) * CH],
                    start=(dt == 0), stop=(dt == NDT - 1)),
                    waits=w, inc="cT" if dt == NDT - 1 else None)
            ev = sc.op("vector", lambda h, bk=bk, hc=hc, kk=kk: h.tensor_mul(
                hsl[kk][:, hc * CH:(hc + 1) * CH], bk[:, 0:CH], rxw[:]),
                waits=[("cT", sc.v("cT")), ("cV", rxwd), ("cT", hguard)], inc="cV")
            bks.set_free(ki, ev)
        mm_ct = sc.v("cT")
        sc.drain("vector")
        snap = sc.op("vector", lambda h, kk=kk, k=k: h.tensor_copy(
            halo[k % 3][:], hsl[kk][:].rearrange("p (c s) -> p c s", s=CH)[:, :, CH - 1]),
            inc="cV")
        return (snap, mm_ct)

    def conv_quant_ffn2(m, last):
        nonlocal pd6_free, pd45_free
        km = m % 2
        hm = hsl[km]
        hl = halo[(m - 1) % 3] if m >= 1 else zcol
        h_cv = h_done[m][0]
        r_cv = h_done[m + 1][0] if not last else h_done[m][0]
        sil_done = 0
        cvd = 0
        for hc in range(NHC):
            w = [("cV", h_cv), ("cG", cG_z), ("cG", sc.v("cG"))]
            sc.drain("scalar")
            sc.op("scalar", lambda h, hc=hc, hm=hm: h.mul(
                cacc[:, 1:CH], hm[:, hc * CH: hc * CH + CH - 1],
                cw_sb[:, 3 * hc:3 * hc + 1]), waits=w)
            cs1 = sc.op("scalar", lambda h, hc=hc, hl=hl: h.mul(
                cacc[:, 0:1], hl[:, hc:hc + 1], cw_sb[:, 3 * hc:3 * hc + 1]),
                inc="cS")
            sc.drain("gpsimd")
            sc.op("gpsimd", lambda h, hc=hc, hm=hm: h.tensor_scalar(
                out=gtmp[:], in0=hm[:, hc * CH:(hc + 1) * CH],
                scalar1=cw_sb[:, 3 * hc + 1:3 * hc + 2], scalar2=None,
                op0=OP.mult), waits=[("cV", h_cv)])
            sc.drain("gpsimd")
            cg2 = sc.op("gpsimd", lambda h: h.tensor_tensor(
                out=cacc2[:], in0=gtmp[:], in1=cacc[:], op=OP.add),
                waits=[("cS", cs1), ("cV", cvd)], inc="cG")
            sc.op("vector", lambda h, hc=hc, hm=hm: h.scalar_tensor_tensor(
                out=cacc[:, 0:CH - 1], in0=hm[:, hc * CH + 1:(hc + 1) * CH],
                scalar=cw_sb[:, 3 * hc + 2:3 * hc + 3], in1=cacc2[:, 0:CH - 1],
                op0=OP.mult, op1=OP.add), waits=[("cG", cg2)])
            if last:
                rsl = zcol[:, hc:hc + 1]
            else:
                rsl = hsl[(m + 1) % 2][:, hc * CH: hc * CH + 1]
            cvd = sc.op("vector", lambda h, hc=hc, rsl=rsl: h.scalar_tensor_tensor(
                out=cacc[:, CH - 1:CH], in0=rsl,
                scalar=cw_sb[:, 3 * hc + 2:3 * hc + 3], in1=cacc2[:, CH - 1:CH],
                op0=OP.mult, op1=OP.add), waits=[("cV", r_cv)], inc="cV")
            sil_done = sc.op("scalar", lambda h, hc=hc: h.activation(
                out=gsl[:, hc * CH:(hc + 1) * CH], in_=cacc[:], func=AF.Silu,
                bias=cb_sb[:, hc:hc + 1]), waits=[("cV", cvd)], inc="cS")
        mdone = sc.op("vector", lambda h: h.tensor_reduce(
            out=macc[:], in_=gsl[:].rearrange("p (c s) -> p s c", c=NHC),
            axis=AX, op=OP.max, apply_absolute_value=True),
            waits=[("cS", sil_done)], inc="cV")
        for blk in range(CH // 128):
            ct = sc.op("tensor", lambda h, blk=blk: h.transpose(
                pD[4 + blk][:, 0:128], macc[:, 128 * blk:128 * (blk + 1)],
                ident_f[:]), waits=[("cV", mdone), ("cV", pd45_free)], inc="cT")
            sc.op("vector", lambda h, blk=blk: h.tensor_reduce(
                out=mxt[:, blk:blk + 1], in_=pD[4 + blk][:, 0:128], axis=AX,
                op=OP.max), waits=[("cT", ct)], inc="cV")
        pd45_free = sc.v("cV")
        sc.drain("vector")
        sc.op("vector", lambda h: h.tensor_scalar(
            out=mxt[:, 2:4], in0=mxt[:, 0:2], scalar1=1e-5, scalar2=None,
            op0=OP.max))
        sc.drain("vector")
        sc.op("vector", lambda h: h.reciprocal(mxt[:, 4:6], mxt[:, 2:4]))
        sc.drain("vector")
        sc.op("vector", lambda h: h.tensor_scalar(
            out=mxt[:, 6:8], in0=mxt[:, 4:6], scalar1=127.0, scalar2=None,
            op0=OP.mult))
        scq = sc.op("vector", lambda h: h.tensor_scalar(
            out=rx2d[:], in0=mxt[:, 2:4], scalar1=w2sc[:, 0:1], scalar2=1.0 / 127.0,
            op0=OP.mult, op1=OP.mult), inc="cV")
        ct = sc.op("tensor", lambda h: h.transpose(
            pD[7][0:2, 0:128], mxt[:, 6:8], ident_f[:]),
            waits=[("cV", scq), ("cV", pd6_free)], inc="cT")
        s2rd = sc.op("vector", lambda h: h.tensor_copy(s2r2[:], pD[7][0:2, 0:128]),
                     waits=[("cT", ct)], inc="cV")
        i = st_sm.dma(lambda h: h.dma_start(out=mx_d, in_=s2r2[:]),
                      waits=[("cV", s2rd)] + st_lm.wait_all())
        i2 = st_lm.dma(lambda h: h.dma_start(
            out=rxr2[0:1, :].rearrange("o (a b) -> o a b", a=2), in_=mx_d),
            waits=st_sm.wait_all() + [("cT", rxr2_free[0])])
        ct = sc.op("tensor", lambda h: h.matmul(
            pD[6][:, 0:CH], onesc_f[0:1, :], rxr2[0:1, :], start=True, stop=True),
            waits=st_lm.wait_for(i2) + [("cV", pd6_free)], inc="cT")
        rxr2_free[0] = ct
        s2bd0 = sc.op("vector", lambda h: h.tensor_copy(s2b[:], pD[6][:, 0:CH]),
                      waits=[("cT", ct)], inc="cV")
        pd6_free = s2bd0
        sc.drain("vector")
        for r in range(2):
            sc.op("vector", lambda h, r=r: h.tensor_copy(
                s2rep[:, r * CH:(r + 1) * CH], s2b[:]),
                waits=[("cG", sc.v("cG"))] if r == 0 else (),
                inc="cV" if r == 1 else None)
        s2bd = sc.v("cV")
        q2done = 0
        v6_done = {-2: q2_prev[0], -1: q2_prev[1]}
        for qg in range(NHC // 2):
            par = qg % 2
            tq2, uq2, fr2, bn2 = tq2p[par], uq2p[par], fr2p[par], bn2p[par]
            sl = slice(qg * 2 * CH, (qg + 1) * 2 * CH)
            sc.drain("gpsimd")
            gm = sc.op("gpsimd", lambda h, sl=sl, tq2=tq2: h.tensor_tensor(
                out=tq2[:], in0=gsl[:, sl], in1=s2rep[:], op=OP.mult),
                waits=[("cV", s2bd), ("cV", v6_done[qg - 2]), ("cS", sil_done)],
                inc="cG")
            sc.drain("vector")
            sc.op("vector", lambda h, tq2=tq2, uq2=uq2: h.tensor_scalar(
                out=uq2[:], in0=tq2[:], scalar1=1.0, scalar2=128.5,
                op0=OP.mult, op1=OP.add), waits=[("cG", gm)])
            sc.drain("vector")
            vf = sc.op("vector", lambda h, tq2=tq2, uq2=uq2, fr2=fr2:
                       h.tensor_sub(fr2[:], tq2[:], uq2[:]), inc="cV")
            sc.drain("gpsimd")
            gb = sc.op("gpsimd", lambda h, fr2=fr2, bn2=bn2: h.tensor_scalar(
                out=bn2[:], in0=fr2[:], scalar1=-128.5, scalar2=-1.0,
                op0=OP.is_lt, op1=OP.mult), waits=[("cV", vf)], inc="cG")
            sc.drain("vector")
            sc.op("vector", lambda h, tq2=tq2, uq2=uq2, fr2=fr2:
                  h.scalar_tensor_tensor(
                out=tq2[:], in0=fr2[:], scalar=-127.5, in1=uq2[:],
                op0=OP.is_ge, op1=OP.add))
            sc.drain("vector")
            q2done = sc.op("vector", lambda h, sl=sl, km=km, tq2=tq2, bn2=bn2:
                           h.scalar_tensor_tensor(
                out=hsl[km][:, sl], in0=bn2[:], scalar=-128.0, in1=tq2[:],
                op0=OP.add, op1=OP.add), waits=[("cG", gb)], inc="cV")
            v6_done[qg] = q2done
        q2_prev[0] = v6_done[NHC // 2 - 2]
        q2_prev[1] = v6_done[NHC // 2 - 1]
        piece_ct = {}
        for piece in range(NHC // 4):
            par = piece % 2
            dstw = w2c[:, par * 4 * E:(par + 1) * 4 * E] \
                .rearrange("p (b e) -> p b e", e=E)
            srcw = w2qT_d.rearrange("(b p) e -> p b e", p=128)[:, 4 * piece:4 * piece + 4, :]
            wld = [("cT", piece_ct.get(piece - 2, w2_prev[0]))]
            dhw = st_lw2.dma(lambda h, dstw=dstw, srcw=srcw: h.dma_start(
                out=dstw, in_=srcw), waits=wld)
            for pidx in range(4):
                hc = 4 * piece + pidx
                for sl2 in range(CH // 128):
                    for nch2 in range(2):
                        bky = pD[sl2 * 2 + nch2]
                        w = ([("cV", q2done)] if hc == 0 else []) + st_lw2.wait_for(dhw)
                        sc.op("tensor", lambda h, bky=bky, hc=hc, sl2=sl2, nch2=nch2,
                              km=km, par=par, pidx=pidx: h.matmul(
                            bky[:, 0:512],
                            hsl[km][:, hc * CH + 128 * sl2: hc * CH + 128 * (sl2 + 1)],
                            w2c[:, par * 4 * E + pidx * E + 512 * nch2:
                                par * 4 * E + pidx * E + 512 * (nch2 + 1)],
                            start=(hc == 0), stop=(hc == NHC - 1)),
                            waits=w,
                            inc="cT" if (pidx == 3 and sl2 == CH // 128 - 1
                                         and nch2 == 1) else None)
            piece_ct[piece] = sc.v("cT")
        ffn2_done[m] = sc.v("cT")
        w2_prev[0] = ffn2_done[m]
        for sl2 in range(CH // 128):
            for nch2 in range(2):
                bky = pD[sl2 * 2 + nch2]
                wv = [("cT", ffn2_done[m])] + st_sy.wait_for(len(y_dmas) - 1)
                sc.op("vector", lambda h, bky=bky, sl2=sl2, nch2=nch2: h.tensor_scalar(
                    out=y_sb[:, 512 * nch2: 512 * (nch2 + 1)],
                    in0=bky[:, 0:512], scalar1=rx2d[:, sl2:sl2 + 1], scalar2=None,
                    op0=OP.mult), waits=wv, inc="cV")
            st_sy.dma(lambda h, m=m, sl2=sl2: h.dma_start(
                out=y_d[CH * m + 128 * sl2: CH * m + 128 * (sl2 + 1), :],
                in_=y_sb[:]),
                waits=[("cV", sc.v("cV"))])
            y_dmas.append(st_sy.n)

    xq_dh[0] = load_xq_chunk(0)
    h_done[0] = ffn1(0)
    for m in range(NCH):
        if m + 1 < NCH:
            xq_dh[m + 1] = load_xq_chunk(m + 1)
            h_done[m + 1] = ffn1(m + 1)
        conv_quant_ffn2(m, last=(m == NCH - 1))
    sc.barrier()
    esD.close()
    esCD.close()

    esE = ExitStack()
    xa2 = sb(esE, "xa2", [128, 2 * E], F32)
    yb2 = sb(esE, "yb2", [128, 2 * E], F32)
    r2 = sb(esE, "r2", [128, E], F32)
    t2s = sb(esE, "t2s", [128, E], F32)
    ot = sb(esE, "ot", [128, 2 * E], F32)
    stE = sb(esE, "stE", [128, 12], F32)
    out_dmas = []
    for t in range(NT):
        par = t % 2
        xat = xa2[:, par * E:(par + 1) * E]
        ybt = yb2[:, par * E:(par + 1) * E]
        w = [("cV", sc.v("cV") if t >= 2 else 0)]
        i1 = st_le.dma(lambda h, xat=xat, t=t: h.dma_start(
            out=xat, in_=x2_d[128 * t:128 * (t + 1), :]), waits=w)
        i2 = st_le.dma(lambda h, ybt=ybt, t=t: h.dma_start(
            out=ybt, in_=y_d[128 * t:128 * (t + 1), :]))
        sc.drain("vector")
        sc.op("vector", lambda h, xat=xat, ybt=ybt: h.tensor_add(r2[:], xat, ybt),
              waits=st_le.wait_for(i1) + st_le.wait_for(i2), inc="cV")
        sc.drain("vector")
        sc.op("vector", lambda h: h.reduce_sum(stE[:, 0:1], r2[:], axis=AX))
        sc.drain("vector")
        sc.op("vector", lambda h: h.tensor_scalar(
            out=stE[:, 1:2], in0=stE[:, 0:1], scalar1=-1.0 / E, scalar2=None,
            op0=OP.mult), inc="cV")
        sqd = sc.op("scalar", lambda h: h.activation(
            out=t2s[:], in_=r2[:], func=AF.Square, bias=stE[:, 1:2],
            accum_out=stE[:, 2:3]), waits=[("cV", sc.v("cV"))], inc="cS")
        sc.op("vector", lambda h: h.tensor_scalar(
            out=stE[:, 3:4], in0=stE[:, 2:3], scalar1=1.0 / E, scalar2=1e-5,
            op0=OP.mult, op1=OP.add), waits=[("cS", sqd)], inc="cV")
        lnd = sc.op("scalar", lambda h: h.activation(
            out=stE[:, 4:5], in_=stE[:, 3:4], func=AF.Ln),
            waits=[("cV", sc.v("cV"))], inc="cS")
        rsd = sc.op("scalar", lambda h: h.activation(
            out=stE[:, 5:6], in_=stE[:, 4:5], func=AF.Exp, scale=cst[:, 1:2]),
            waits=[("cS", lnd)], inc="cS")
        otp = ot[:, par * E:(par + 1) * E]
        w2 = [("cS", rsd)] + st_se.wait_for(len(out_dmas) - 2)
        odone = sc.op("vector", lambda h, otp=otp: h.tensor_scalar(
            out=otp, in0=r2[:], scalar1=stE[:, 1:2], scalar2=stE[:, 5:6],
            op0=OP.add, op1=OP.mult), waits=w2, inc="cV")
        st_se.dma(lambda h, otp=otp, t=t: h.dma_start(
            out=out_d[128 * t:128 * (t + 1), :], in_=otp),
            waits=[("cV", odone)])
        out_dmas.append(st_se.n)
    sc.op("sync", None, waits=st_se.wait_all())
    esE.close()

    with nc.Block() as block:
        sc.emit(block, sems)
    root.close()
    return nc


def kernel(L0_attn_x, L0_attn_Wq, L0_attn_Wk, L0_attn_Wv, L0_attn_rel_table,
           L0_attn_gamma, L0_attn_beta, L0_ffn_w1, L0_ffn_conv_w, L0_ffn_conv_b,
           L0_ffn_w2):
    x = np.ascontiguousarray(np.asarray(L0_attn_x, dtype=np.float32))
    T = np.asarray(L0_attn_rel_table, dtype=np.float32)

    t_ext = np.concatenate([T, np.repeat(T[-1:, :], 2048, axis=0)], axis=0)
    dvals = np.arange(-15, 16)
    p_i = np.arange(128)
    a_i = np.arange(128)
    idx = (2047 + 128 * dvals[:, None, None] + p_i[None, :, None]
           - a_i[None, None, :])
    toe = np.ascontiguousarray(
        t_ext[idx].transpose(3, 0, 1, 2))

    cwv = np.asarray(L0_ffn_conv_w, dtype=np.float32).reshape(NHC, 128, 3)
    cw = np.ascontiguousarray(cwv.transpose(1, 0, 2).reshape(128, NHC * 3))
    cb = np.ascontiguousarray(
        np.asarray(L0_ffn_conv_b, dtype=np.float32).reshape(NHC, 128).T)

    base = dict(
        wq=np.ascontiguousarray(L0_attn_Wq, dtype=np.float32),
        wk=np.ascontiguousarray(L0_attn_Wk, dtype=np.float32),
        wv=np.ascontiguousarray(L0_attn_Wv, dtype=np.float32),
        toe=toe,
        gam=np.ascontiguousarray(np.asarray(L0_attn_gamma, np.float32)[None, :]),
        bet=np.ascontiguousarray(np.asarray(L0_attn_beta, np.float32)[None, :]),
        w1=np.ascontiguousarray(L0_ffn_w1, dtype=np.float32),
        cw=cw, cb=cb,
        w2=np.ascontiguousarray(L0_ffn_w2, dtype=np.float32),
    )
    in_maps = [dict(base, x=np.ascontiguousarray(x[b])) for b in range(8)]

    prog = build_program()
    res = run_bass_kernel_spmd(prog, in_maps, core_ids=list(range(8)))
    key = {0: "out", 1: "o_s", 2: "x2_s", 3: "y_s"}[DEBUG]
    return np.stack([np.asarray(r[key]) for r in res.results], axis=0)


# version 05
# speedup vs baseline: 1.3538x; 1.3538x over previous
import numpy as np
import ml_dtypes
from contextlib import ExitStack

import concourse.bass as bass
import concourse.mybir as mybir
from concourse.bass_utils import run_bass_kernel_spmd
from concourse.alu_op_type import AluOpType as OP

F32 = mybir.dt.float32
BF16 = mybir.dt.bfloat16
I32 = mybir.dt.int32
AF = mybir.ActivationFunctionType
AX = mybir.AxisListType.X
bfdt = ml_dtypes.bfloat16

S, E, H, DH, HC = 2048, 1024, 16, 64, 4096
NT = S // 128
NP = H // 2
NDT = E // 128
NHC = HC // 128
CH = 256
NCH = S // CH
NW = 18
DEBUG = 0

ENGINES = ("sync", "gpsimd", "vector", "scalar", "tensor")


class Sched:
    def __init__(self):
        self.ops = {e: [] for e in ENGINES}
        self.cnt = {}
        self.last_wait = {e: {} for e in ENGINES}

    def v(self, sem):
        return self.cnt.get(sem, 0)

    def op(self, eng, fn, waits=(), inc=None, inc_n=1, bare_inc=False):
        w = []
        for s, val in waits:
            if val > 0 and self.last_wait[eng].get(s, -1) < val:
                w.append((s, val))
                self.last_wait[eng][s] = val
        self.ops[eng].append((fn, w, inc, inc_n, bare_inc))
        if inc is not None:
            self.cnt[inc] = self.cnt.get(inc, 0) + inc_n
        return self.cnt.get(inc, 0) if inc else None

    def drain(self, eng):
        self.op(eng, lambda h: h.drain())

    def barrier(self):
        for st in getattr(self, "streams", []):
            for w in st.wait_all():
                self.op("sync", None, waits=(w,))
        for e in ENGINES:
            self.op(e, None, inc="bar", inc_n=1, bare_inc=True)
        tot = self.v("bar")
        for e in ENGINES:
            self.op(e, None, waits=(("bar", tot),))

    def emit(self, block, sems):
        def make(ename):
            def body(h):
                for fn, waits, inc, inc_n, bare in self.ops[ename]:
                    for s, val in waits:
                        h.wait_ge(sems[s], val)
                    if bare:
                        h.sem_inc(sems[inc], inc_n)
                        continue
                    if fn is None:
                        continue
                    instr = fn(h)
                    if inc is not None:
                        instr.then_inc(sems[inc], inc_n)
            return body
        block.sync(make("sync"))
        block.gpsimd(make("gpsimd"))
        block.vector(make("vector"))
        block.scalar(make("scalar"))
        block.tensor(make("tensor"))


class Stream:

    def __init__(self, sc, eng, sems, names):
        self.sc = sc
        self.eng = eng
        self.sems = names
        self.counts = [0] * len(names)
        self.n = 0

    def dma(self, fn, waits=()):
        i = self.n
        k = i % len(self.sems)
        self.n += 1
        self.counts[k] += 1
        self.sc.op(self.eng, fn, waits=waits, inc=self.sems[k], inc_n=16)
        return i

    def wait_for(self, i):
        if i < 0:
            return []
        k = i % len(self.sems)
        return [(self.sems[k], 16 * (i // len(self.sems) + 1))]

    def wait_last(self):
        return self.wait_for(self.n - 1)

    def wait_all(self):
        return [(s, 16 * c) for s, c in zip(self.sems, self.counts) if c > 0]


class Banks:

    def __init__(self, banks, free0=0):
        self.banks = banks
        self.free_at = [free0] * len(banks)
        self.i = 0

    def get(self):
        k = self.i % len(self.banks)
        self.i += 1
        return self.banks[k], self.free_at[k], k

    def set_free(self, k, cv):
        self.free_at[k] = cv


def build_program():
    nc = bass.Bass("TRN2", target_bir_lowering=False)

    x_d = nc.dram_tensor("x", [S, E], F32, kind="ExternalInput").ap()
    wq_d = nc.dram_tensor("wq", [E, E], F32, kind="ExternalInput").ap()
    wk_d = nc.dram_tensor("wk", [E, E], F32, kind="ExternalInput").ap()
    wv_d = nc.dram_tensor("wv", [E, E], F32, kind="ExternalInput").ap()
    toe_d = nc.dram_tensor("toe", [16, 31, 128, 128], F32, kind="ExternalInput").ap()
    gam_d = nc.dram_tensor("gam", [1, E], F32, kind="ExternalInput").ap()
    bet_d = nc.dram_tensor("bet", [1, E], F32, kind="ExternalInput").ap()
    w1_d = nc.dram_tensor("w1", [HC, E], F32, kind="ExternalInput").ap()
    cw_d = nc.dram_tensor("cw", [128, 3 * NHC], F32, kind="ExternalInput").ap()
    cb_d = nc.dram_tensor("cb", [128, NHC], F32, kind="ExternalInput").ap()
    w2_d = nc.dram_tensor("w2", [E, HC], F32, kind="ExternalInput").ap()
    out_d = nc.dram_tensor("out", [S, E], F32, kind="ExternalOutput").ap()

    qT_d = nc.dram_tensor("qT_s", [E, S], BF16, kind="Internal").ap()
    kT_d = nc.dram_tensor("kT_s", [E, S], BF16, kind="Internal").ap()
    o_d = nc.dram_tensor("o_s", [S, E], F32,
                         kind="ExternalOutput" if DEBUG == 1 else "Internal").ap()
    x2_d = nc.dram_tensor("x2_s", [S, E], F32,
                          kind="ExternalOutput" if DEBUG == 2 else "Internal").ap()
    y_d = nc.dram_tensor("y_s", [S, E], F32,
                         kind="ExternalOutput" if DEBUG == 3 else "Internal").ap()
    w2qT_d = nc.dram_tensor("w2qT_s", [HC, E], BF16, kind="Internal").ap()
    v_sd = nc.dram_tensor("v_sd", [S, E], BF16, kind="Internal").ap()
    rx_d = nc.dram_tensor("rx_d", [NT, 128], F32, kind="Internal").ap()
    mx_d = nc.dram_tensor("mx_d", [2, 128], F32, kind="Internal").ap()
    xqT_d = nc.dram_tensor("xqT_s", [E, S], BF16, kind="Internal").ap()

    sc = Sched()
    root = ExitStack()
    sems = {}
    stream_sems = ["gax", "toe", "cwl", "qk", "vl", "vs",
                   "sq0", "sq1", "so0", "so1", "wl0", "wl1", "sw",
                   "lc0", "lc1", "lc2", "lc3", "sx2", "sxq0", "sxq1", "srx",
                   "lq0", "lq1", "lw20", "lw21", "lr", "sm", "lm",
                   "sy0", "sy1", "le0", "le1", "le2", "le3", "se0", "se1"]
    for s in ["cT", "cV", "cS", "cG", "bar"] + stream_sems:
        sems[s] = root.enter_context(nc.semaphore(s))
    sc.streams = []

    def mkstream(eng, names):
        st = Stream(sc, eng, sems, names)
        sc.streams.append(st)
        return st

    st_gax = mkstream("gpsimd", ["gax"])
    st_toe = mkstream("gpsimd", ["toe"])
    st_cwl = mkstream("sync", ["cwl"])
    st_qk = mkstream("sync", ["qk"])
    st_vl = mkstream("sync", ["vl"])
    st_vs = mkstream("scalar", ["vs"])
    st_sq = mkstream("scalar", ["sq0", "sq1"])
    st_so = mkstream("scalar", ["so0", "so1"])
    st_wl = mkstream("sync", ["wl0", "wl1"])
    st_sw = mkstream("scalar", ["sw"])
    st_lc = mkstream("sync", ["lc0", "lc1", "lc2", "lc3"])
    st_sx2 = mkstream("scalar", ["sx2"])
    st_sxq = mkstream("scalar", ["sxq0", "sxq1"])
    st_srx = mkstream("scalar", ["srx"])
    st_lq = mkstream("sync", ["lq0", "lq1"])
    st_lw2 = mkstream("sync", ["lw20", "lw21"])
    st_lr = mkstream("sync", ["lr"])
    st_sm = mkstream("scalar", ["sm"])
    st_lm = mkstream("sync", ["lm"])
    st_sy = mkstream("scalar", ["sy0", "sy1"])
    st_le = mkstream("sync", ["le0", "le1", "le2", "le3"])
    st_se = mkstream("sync", ["se0", "se1"])

    def sb(stack, name, shape, dt=F32):
        return stack.enter_context(nc.sbuf_tensor(name, shape, dt))

    ident_b = sb(root, "ident_b", [128, 128], BF16)
    ident_f = sb(root, "ident_f", [128, 128], F32)
    onesc_f = sb(root, "onesc_f", [1, 128], F32)
    ones65 = sb(root, "ones65", [65, 128], F32)
    cst = sb(root, "cst", [128, 4], F32)
    rx1 = sb(root, "rx1", [128, NT], F32)
    cw_sb = sb(root, "cw_sb", [128, 3 * NHC], F32)
    cb_sb = sb(root, "cb_sb", [128, NHC], F32)
    w1sc = sb(root, "w1sc", [128, 2], F32)
    rxrow_s = sb(root, "rxrow_s", [16, 128], F32)
    w2sc = sb(root, "w2sc", [128, 2], F32)
    esAB = ExitStack()
    v_pad = sb(esAB, "v_pad", [128, NT * H * 65], BF16)
    v_km = sb(esAB, "v_km", [128, 16 * E], BF16)

    esA = ExitStack()
    x_bf = sb(esA, "x_bf", [128, NT * E], BF16)
    xT = sb(esA, "xT", [128, NDT * S], BF16)
    wq_sb = sb(esA, "wq_sb", [128, NDT * E], BF16)
    wk_sb = sb(esA, "wk_sb", [128, NDT * E], BF16)
    wv_sb = sb(esA, "wv_sb", [128, NDT * E], BF16)
    stgA = sb(esA, "stgA", [128, 2 * 512], BF16)
    pA = [esA.enter_context(nc.psum_tensor(f"pA{i}", [128, 512], F32)) for i in range(6)]
    pAb = [esA.enter_context(nc.psum_tensor(f"pAb{i}", [128, 1024], BF16)) for i in range(2)]

    sc.op("gpsimd", lambda h: h.memset(cst[:, 0:1], 0.03125))
    sc.op("gpsimd", lambda h: h.memset(cst[:, 1:2], -0.5))
    sc.op("gpsimd", lambda h: h.memset(onesc_f[:], 1.0))
    sc.op("gpsimd", lambda h: h.memset(ones65[:], 1.0))
    sc.op("gpsimd", lambda h: h.memset(ident_b[:], 0.0))
    sc.op("gpsimd", lambda h: h.memset(ident_f[:], 0.0))
    sc.drain("gpsimd")
    sc.op("gpsimd", lambda h: h.affine_select(out=ident_b[:], in_=ident_b[:],
          compare_op=OP.not_equal, fill=1.0, base=0, pattern=[[-1, 128]],
          channel_multiplier=1))
    sc.op("gpsimd", lambda h: h.affine_select(out=ident_f[:], in_=ident_f[:],
          compare_op=OP.not_equal, fill=1.0, base=0, pattern=[[-1, 128]],
          channel_multiplier=1), inc="cG")
    vp4 = v_pad[:].rearrange("p (t h e) -> p t h e", t=NT, h=H, e=65)
    sc.op("gpsimd", lambda h: h.memset(vp4[:, :, :, 64:65], 1.0), inc="cG")

    st_gax.dma(lambda h: h.dma_start(
        out=x_bf[:].rearrange("p (t d) -> p t d", d=E),
        in_=x_d.rearrange("(t p) d -> p t d", p=128)))
    for wd, wsb in ((wq_d, wq_sb), (wk_d, wk_sb), (wv_d, wv_sb)):
        st_gax.dma(lambda h, wd=wd, wsb=wsb: h.dma_start(
            out=wsb[:].rearrange("p (dt e) -> p dt e", e=E),
            in_=wd.rearrange("(dt p) e -> p dt e", p=128)))
    dG_A = st_gax.wait_all()

    st_cwl.dma(lambda h: h.dma_start(out=cw_sb[:], in_=cw_d))
    st_cwl.dma(lambda h: h.dma_start(out=cb_sb[:], in_=cb_d))

    bksA = Banks(pAb)
    for t in range(NT):
        for dt in range(NDT):
            bk, freev, ki = bksA.get()
            sc.op("tensor", lambda h, bk=bk, t=t, dt=dt: h.transpose(
                bk[:, 0:128], x_bf[:, t * E + 128 * dt: t * E + 128 * (dt + 1)],
                ident_b[:]), waits=dG_A + [("cG", 1), ("cV", freev)], inc="cT")
            ev = sc.op("vector", lambda h, bk=bk, t=t, dt=dt: h.tensor_copy(
                xT[:, dt * S + 128 * t: dt * S + 128 * (t + 1)], bk[:, 0:128]),
                waits=[("cT", sc.v("cT"))], inc="cV")
            bksA.set_free(ki, ev)
    cV_xT = sc.v("cV")

    bksQ = Banks(pA[0:3])
    nstg = 0
    stg_dmas = []
    for wsb, dst in ((wq_sb, qT_d), (wk_sb, kT_d)):
        for j in range(NDT):
            for nch in range(4):
                bk, freev, ki = bksQ.get()
                for dt in range(NDT):
                    waits = [("cV", cV_xT)] + ([("cV", freev)] if dt == 0 else [])
                    sc.op("tensor", lambda h, bk=bk, wsb=wsb, j=j, nch=nch, dt=dt:
                          h.matmul(bk[:, 0:512],
                                   wsb[:, dt * E + 128 * j: dt * E + 128 * (j + 1)],
                                   xT[:, dt * S + 512 * nch: dt * S + 512 * (nch + 1)],
                                   start=(dt == 0), stop=(dt == NDT - 1)),
                          waits=waits, inc="cT" if dt == NDT - 1 else None)
                stx = stgA[:, (nstg % 2) * 512:(nstg % 2) * 512 + 512]
                wv2 = [("cT", sc.v("cT"))] + st_sq.wait_for(nstg - 2)
                ev = sc.op("vector", lambda h, stx=stx, bk=bk: h.tensor_copy(
                    stx, bk[:, 0:512]), waits=wv2, inc="cV")
                bksQ.set_free(ki, ev)
                st_sq.dma(lambda h, stx=stx, dst=dst, j=j, nch=nch:
                          h.dma_start(out=dst[128 * j:128 * (j + 1),
                                              512 * nch:512 * (nch + 1)], in_=stx),
                          waits=[("cV", ev)])
                nstg += 1
    v_nat = x_bf
    bksV = Banks(pA[3:6], free0=sc.v("cV"))
    for t in range(NT):
        for nch in range(2):
            bk, freev, ki = bksV.get()
            for dt in range(NDT):
                waits = [("cV", cV_xT)] + ([("cV", freev)] if dt == 0 else [])
                sc.op("tensor", lambda h, bk=bk, t=t, nch=nch, dt=dt:
                      h.matmul(bk[:, 0:512],
                               xT[:, dt * S + 128 * t: dt * S + 128 * (t + 1)],
                               wv_sb[:, dt * E + 512 * nch: dt * E + 512 * (nch + 1)],
                               start=(dt == 0), stop=(dt == NDT - 1)),
                      waits=waits, inc="cT" if dt == NDT - 1 else None)
            ev = sc.op("vector", lambda h, t=t, nch=nch, bk=bk: h.tensor_copy(
                v_nat[:, t * E + 512 * nch: t * E + 512 * (nch + 1)], bk[:, 0:512]),
                waits=[("cT", sc.v("cT"))], inc="cV")
            bksV.set_free(ki, ev)
    cV_vp = sc.v("cV")
    st_vs.dma(lambda h: h.dma_start(
        out=v_sd.rearrange("(t p) e -> p t e", p=128),
        in_=v_nat[:].rearrange("p (t e) -> p t e", e=E)),
        waits=[("cV", cV_vp)])
    dvs = st_vs.wait_all()
    for t in range(NT):
        st_vl.dma(lambda h, t=t: h.dma_start(
            out=vp4[:, t, :, 0:64],
            in_=v_sd[128 * t:128 * (t + 1), :].rearrange(
                "p (hh e) -> p hh e", e=64)),
            waits=dvs)
    st_vl.dma(lambda h: h.dma_start(
        out=v_km[:].rearrange("p (k c) -> p k c", k=16),
        in_=v_sd.rearrange("(p k) c -> p k c", k=16)))
    sc.barrier()
    esA.close()

    esB = ExitStack()
    toe = sb(esB, "toe_sb", [128, 16 * NW * 128], BF16)
    qkp = sb(esB, "qkp", [128, 2 * S], BF16)
    p_bf = sb(esB, "p_bf", [128, 2 * 2048], BF16)
    pvA_sb = sb(esB, "pvA_sb", [65, 512], F32)
    pvB_sb = sb(esB, "pvB_sb", [65, 512], F32)
    bias_sb = sb(esB, "bias_sb", [128, 512], F32)
    rcp = sb(esB, "rcp", [128, 2], F32)
    tb_sb = sb(esB, "tb_sb", [128, 128], F32)
    o_stage = sb(esB, "o_stage", [128, 1024], F32)
    sAB = esB.enter_context(nc.psum_tensor("sAB", [128, 2048], F32))
    pvA = esB.enter_context(nc.psum_tensor("pvA", [128, 512], F32))
    pvB = esB.enter_context(nc.psum_tensor("pvB", [128, 512], F32))
    pb = [esB.enter_context(nc.psum_tensor(f"pb{i}", [128, 512], F32)) for i in range(2)]

    toev = toe[:].rearrange("p (k w a) -> p k w a", k=16, w=NW, a=128)

    def toe_load(dlt, wait_ct):
        slot = (dlt + 15) % NW
        w = [("cT", wait_ct)] if wait_ct > 0 else []
        st_toe.dma(lambda h, dlt=dlt, slot=slot: h.dma_start(
            out=toev[:, :, slot, :],
            in_=toe_d[:, dlt + 15].rearrange("k p a -> p k a")),
            waits=w)

    for dlt in range(-1, 16):
        toe_load(dlt, 0)
    dG_need = st_toe.wait_all()

    st_qk.dma(lambda h: h.dma_start(out=qkp[:, 0:S], in_=qT_d[0:128, :]))
    st_qk.dma(lambda h: h.dma_start(out=qkp[:, S:2 * S], in_=kT_d[0:128, :]))

    exp_done = 0
    o_dmas = []
    pv_free = 0
    tr_free = 0
    bias_bank_free = [0, 0]
    for j in range(NP):
        hA, hB = 2 * j, 2 * j + 1
        dH_pair = st_qk.wait_all()
        qperm = qkp[:, 0:S].rearrange("p (a c) -> p c a", a=128, c=16)
        for qc in range(4):
            c0 = 4 * qc
            slA0 = ((c0 - hA) + 15) % NW
            slB0 = ((c0 - hB) + 15) % NW
            fill = []
            for kap in range(16):
                w0 = []
                if kap == 0:
                    w0 = list(dG_need) + [("cV", bias_bank_free[0]),
                                          ("cV", bias_bank_free[1])]
                fill.append((lambda h, kap=kap, slA0=slA0, j=j: h.matmul(
                    pb[0][0:64, 0:128],
                    v_km[:, kap * E + 128 * j: kap * E + 128 * j + 64],
                    toe[:, kap * (NW * 128) + slA0 * 128:
                        kap * (NW * 128) + (slA0 + 1) * 128],
                    start=(kap == 0), stop=(kap == 15), tile_position=(0, 0)), w0))
                fill.append((lambda h, kap=kap, slB0=slB0, j=j: h.matmul(
                    pb[1][64:128, 0:128],
                    v_km[:, kap * E + 128 * j + 64: kap * E + 128 * (j + 1)],
                    toe[:, kap * (NW * 128) + slB0 * 128:
                        kap * (NW * 128) + (slB0 + 1) * 128],
                    start=(kap == 0), stop=(kap == 15), tile_position=(0, 64)), ()))
            fi = 0
            for g in range(8):
                u0, u1 = 2 * g, 2 * g + 1
                wqk = list(dH_pair)
                if exp_done:
                    wqk.append(("cS", exp_done))
                for ui, u in enumerate((u0, u1)):
                    off = ui * 1024
                    sc.op("tensor", lambda h, u=u, qc=qc, off=off, qperm=qperm:
                          h.matmul(sAB[:, off:off + 512],
                                   qkp[0:64, S + 128 * u: S + 128 * (u + 1)],
                                   qperm[0:64, 4 * qc:4 * qc + 4, :],
                                   start=True, stop=True),
                          waits=wqk if ui == 0 else ())
                    sc.op("tensor", lambda h, u=u, qc=qc, off=off, qperm=qperm:
                          h.matmul(sAB[:, off + 512:off + 1024],
                                   qkp[64:128, S + 128 * u: S + 128 * (u + 1)],
                                   qperm[64:128, 4 * qc:4 * qc + 4, :],
                                   start=True, stop=True),
                          inc="cT" if ui == 1 else None)
                qk_ct = sc.v("cT")
                if g > 0:
                    gp = (g - 1) % 2
                    for ui, u in enumerate((2 * g - 2, 2 * g - 1)):
                        wpv = []
                        if g == 1 and ui == 0 and pv_free:
                            wpv = [("cV", pv_free)]
                        sc.op("tensor", lambda h, u=u, hA=hA, gp=gp, ui=ui: h.matmul(
                            pvA[0:65, 0:512],
                            v_pad[:, u * (H * 65) + hA * 65: u * (H * 65) + hA * 65 + 65],
                            p_bf[:, gp * 2048 + ui * 1024: gp * 2048 + ui * 1024 + 512],
                            start=(u == 0), stop=False), waits=wpv)
                        sc.op("tensor", lambda h, u=u, hB=hB, gp=gp, ui=ui: h.matmul(
                            pvB[0:65, 0:512],
                            v_pad[:, u * (H * 65) + hB * 65: u * (H * 65) + hB * 65 + 65],
                            p_bf[:, gp * 2048 + ui * 1024 + 512:
                                 gp * 2048 + ui * 1024 + 1024],
                            start=(u == 0), stop=False))
                while fi < len(fill) and fi < 4 * (g + 1):
                    fn, w0 = fill[fi]
                    sc.op("tensor", fn, waits=w0,
                          inc="cT" if fi == len(fill) - 1 else None)
                    fi += 1
                exp_done = sc.op("scalar", lambda h, g=g: h.activation(
                    out=p_bf[:, (g % 2) * 2048:(g % 2) * 2048 + 2048],
                    in_=sAB[:], func=AF.Exp, scale=cst[:, 0:1]),
                    waits=[("cT", qk_ct), ("cG", 1)], inc="cS")
            while fi < len(fill):
                fn, w0 = fill[fi]
                sc.op("tensor", fn, waits=w0,
                      inc="cT" if fi == len(fill) - 1 else None)
                fi += 1
            bias_c0_ct = sc.v("cT")
            gp = 1
            for ui, u in enumerate((14, 15)):
                sc.op("tensor", lambda h, u=u, hA=hA, ui=ui: h.matmul(
                    pvA[0:65, 0:512],
                    v_pad[:, u * (H * 65) + hA * 65: u * (H * 65) + hA * 65 + 65],
                    p_bf[:, gp * 2048 + ui * 1024: gp * 2048 + ui * 1024 + 512],
                    start=False, stop=(u == 15)),
                    waits=[("cS", exp_done)])
                sc.op("tensor", lambda h, u=u, hB=hB, ui=ui: h.matmul(
                    pvB[0:65, 0:512],
                    v_pad[:, u * (H * 65) + hB * 65: u * (H * 65) + hB * 65 + 65],
                    p_bf[:, gp * 2048 + ui * 1024 + 512: gp * 2048 + ui * 1024 + 1024],
                    start=False, stop=(u == 15)),
                    inc="cT" if u == 15 else None)
            pv_done = sc.v("cT")
            sc.op("vector", lambda h: h.tensor_copy(pvA_sb[:], pvA[0:65, 0:512]),
                  waits=[("cT", pv_done)])
            pv_ev = sc.op("vector", lambda h: h.tensor_copy(
                pvB_sb[:], pvB[0:65, 0:512]), inc="cV")
            bias_ev = [0, 0, 0, 0]
            sc.op("vector", lambda h: h.tensor_copy(
                bias_sb[0:64, 0:128], pb[0][0:64, 0:128]),
                waits=[("cT", bias_c0_ct)])
            bias_ev[0] = sc.op("vector", lambda h: h.tensor_copy(
                bias_sb[64:128, 0:128], pb[1][64:128, 0:128]), inc="cV")
            for ci in range(1, 4):
                c = 4 * qc + ci
                slA = ((c - hA) + 15) % NW
                slB = ((c - hB) + 15) % NW
                for kap in range(16):
                    w0 = [("cV", bias_ev[ci - 1])] if kap == 0 else []
                    sc.op("tensor", lambda h, kap=kap, slA=slA, j=j: h.matmul(
                        pb[0][0:64, 0:128],
                        v_km[:, kap * E + 128 * j: kap * E + 128 * j + 64],
                        toe[:, kap * (NW * 128) + slA * 128:
                            kap * (NW * 128) + (slA + 1) * 128],
                        start=(kap == 0), stop=(kap == 15), tile_position=(0, 0)),
                        waits=w0)
                    sc.op("tensor", lambda h, kap=kap, slB=slB, j=j: h.matmul(
                        pb[1][64:128, 0:128],
                        v_km[:, kap * E + 128 * j + 64: kap * E + 128 * (j + 1)],
                        toe[:, kap * (NW * 128) + slB * 128:
                            kap * (NW * 128) + (slB + 1) * 128],
                        start=(kap == 0), stop=(kap == 15), tile_position=(0, 64)),
                        inc="cT" if kap == 15 else None)
                bct = sc.v("cT")
                sc.op("vector", lambda h, ci=ci: h.tensor_copy(
                    bias_sb[0:64, 128 * ci:128 * (ci + 1)], pb[0][0:64, 0:128]),
                    waits=[("cT", bct)])
                bias_ev[ci] = sc.op("vector", lambda h, ci=ci: h.tensor_copy(
                    bias_sb[64:128, 128 * ci:128 * (ci + 1)], pb[1][64:128, 0:128]),
                    inc="cV")
            bias_bank_free = [bias_ev[3], bias_ev[3]]
            half = (qc % 2) * 512
            last_stt = 0
            for ci in range(4):
                trbank = pb[ci % 2]
                wtr = [("cV", pv_ev)]
                if ci == 0:
                    wtr.append(("cV", bias_ev[3]))
                if ci >= 1:
                    wtr.append(("cV", last_stt))
                trA_ct = sc.op("tensor", lambda h, ci=ci: h.transpose(
                    pvA[:, 0:65], pvA_sb[0:65, 128 * ci:128 * (ci + 1)],
                    ident_f[0:65, 0:65]), waits=wtr, inc="cT")
                trB_ct = sc.op("tensor", lambda h, ci=ci: h.transpose(
                    pvB[:, 0:65], pvB_sb[0:65, 128 * ci:128 * (ci + 1)],
                    ident_f[0:65, 0:65]), inc="cT")
                trX_ct = sc.op("tensor", lambda h, ci=ci, trbank=trbank: h.transpose(
                    trbank[:, 0:128], bias_sb[:, 128 * ci:128 * (ci + 1)],
                    ident_f[:]), inc="cT")
                sc.op("vector", lambda h: h.reciprocal(rcp[:, 0:1], pvA[:, 64:65]),
                      waits=[("cT", trX_ct)])
                sc.drain("vector")
                sc.op("vector", lambda h: h.reciprocal(rcp[:, 1:2], pvB[:, 64:65]))
                sc.drain("vector")
                sc.op("vector", lambda h, trbank=trbank: h.tensor_copy(
                    tb_sb[:], trbank[:, 0:128]), inc="cV")
                sc.drain("vector")
                wst = st_so.wait_for(len(o_dmas) - 2)
                sc.op("vector", lambda h, ci=ci, half=half: h.scalar_tensor_tensor(
                    out=o_stage[:, half + 128 * ci: half + 128 * ci + 64],
                    in0=pvA[:, 0:64], scalar=rcp[:, 0:1], in1=tb_sb[:, 0:64],
                    op0=OP.mult, op1=OP.add), waits=wst)
                last_stt = sc.op("vector", lambda h, ci=ci, half=half:
                                 h.scalar_tensor_tensor(
                    out=o_stage[:, half + 128 * ci + 64: half + 128 * (ci + 1)],
                    in0=pvB[:, 0:64], scalar=rcp[:, 1:2], in1=tb_sb[:, 64:128],
                    op0=OP.mult, op1=OP.add), inc="cV")
            pv_free = last_stt
            tr_free = last_stt
            bias_bank_free = [last_stt, last_stt]
            dsto = o_d[:, 128 * j:128 * (j + 1)] \
                .rearrange("(a c) e -> a c e", a=128, c=16)[:, 4 * qc:4 * qc + 4, :]
            srco = o_stage[:, half:half + 512].rearrange("a (c e) -> a c e", c=4, e=128)
            st_so.dma(lambda h, dsto=dsto, srco=srco: h.dma_start(
                out=dsto, in_=srco), waits=[("cV", last_stt)])
            o_dmas.append(st_so.n)
        if j + 1 < NP:
            st_qk.dma(lambda h, j=j: h.dma_start(
                out=qkp[:, 0:S], in_=qT_d[128 * (j + 1):128 * (j + 2), :]),
                waits=[("cT", sc.v("cT"))])
            st_qk.dma(lambda h, j=j: h.dma_start(
                out=qkp[:, S:2 * S], in_=kT_d[128 * (j + 1):128 * (j + 2), :]))
            toe_load(-2 * (j + 1), sc.v("cT"))
            toe_load(-2 * (j + 1) - 1, sc.v("cT"))
            dG_need = st_toe.wait_all()
    sc.barrier()
    esB.close()
    esAB.close()

    esCD = ExitStack()
    w1qT = sb(esCD, "w1qT", [128, NDT * HC], BF16)
    esC = ExitStack()
    gb_sb = sb(esC, "gb_sb", [128, E], F32)
    bb_sb = sb(esC, "bb_sb", [128, E], F32)
    wt = sb(esC, "wt", [128, 2 * 2048], F32)
    mp_t = sb(esC, "mp_t", [128, 2048], F32)
    mm_t = sb(esC, "mm_t", [128, 2048], F32)
    wqn = sb(esC, "wqn", [128, 2048], BF16)
    stgW = sb(esC, "stgW", [128, HC], BF16)
    xa = sb(esC, "xa", [128, 2 * E], F32)
    ob = sb(esC, "ob", [128, 2 * E], F32)
    r1 = sb(esC, "r1", [128, E], F32)
    t1s = sb(esC, "t1s", [128, E], F32)
    frs = mm_t[:, 0:E]
    uq = esC.enter_context(nc.sbuf_tensor("uq", [128, E], I32))
    bns = mp_t[:, 0:E]
    xq_t = wqn[:, 0:E]
    stC = stgW
    stat = sb(esC, "stat", [128, 12], F32)
    wsum = sb(esC, "wsum", [128, NHC], F32)
    pC = [esC.enter_context(nc.psum_tensor(f"pC{i}", [128, 512], F32)) for i in range(4)]
    pCb = [esC.enter_context(nc.psum_tensor(f"pCb{i}", [128, 1024], BF16)) for i in range(4)]

    st_cwl.dma(lambda h: h.dma_start(out=gb_sb[0:1, :], in_=gam_d))
    st_cwl.dma(lambda h: h.dma_start(out=bb_sb[0:1, :], in_=bet_d))
    dH_gb = st_cwl.wait_all()
    for i, bc in enumerate((gb_sb, bb_sb)):
        for nch in range(2):
            ct = sc.op("tensor", lambda h, bc=bc, i=i, nch=nch: h.matmul(
                pC[2 * i + nch][:, 0:512], onesc_f[0:1, :],
                bc[0:1, 512 * nch:512 * (nch + 1)], start=True, stop=True),
                waits=dH_gb, inc="cT")
            sc.op("vector", lambda h, bc=bc, i=i, nch=nch: h.tensor_copy(
                bc[:, 512 * nch:512 * (nch + 1)], pC[2 * i + nch][:, 0:512]),
                waits=[("cT", ct)], inc="cV")
    gb_ready = sc.v("cV")

    def wquant(wi, wd, wscb):
        ntile = NHC if wi == 0 else NDT
        width = E if wi == 0 else HC
        npc = width // 2048 if width > 2048 else 1
        pw = width // npc
        cons = [sc.v("cV")] * 2
        ip = 0
        for tt in range(ntile):
            for pc in range(npc):
                par = ip % 2
                ip += 1
                buf = wt[:, par * 2048:par * 2048 + pw]
                i = st_wl.dma(lambda h, buf=buf, wd=wd, tt=tt, pc=pc, pw=pw:
                              h.dma_start(out=buf,
                                          in_=wd[128 * tt:128 * (tt + 1),
                                                 pw * pc:pw * (pc + 1)]),
                              waits=[("cV", cons[par])])
                cons[par] = sc.op("vector", lambda h, buf=buf, tt=tt, pc=pc, npc=npc:
                             h.reduce_sum(wsum[:, tt * npc + pc:tt * npc + pc + 1]
                                          if npc > 1 else wsum[:, tt:tt + 1],
                                          buf, axis=AX, apply_absolute_value=True),
                             waits=st_wl.wait_for(i), inc="cV")
        sc.drain("vector")
        sc.op("vector", lambda h, n=ntile * npc: h.reduce_sum(
            stat[:, 0:1], wsum[:, 0:n], axis=AX), inc="cV")
        ct = sc.op("tensor", lambda h: h.transpose(
            pC[0][0:1, 0:128], stat[:, 0:1], ident_f[0:128, 0:128]),
            waits=[("cV", sc.v("cV")), ("cV", gb_ready)], inc="cT")
        sc.op("vector", lambda h: h.reduce_sum(
            stat[0:1, 1:2], pC[0][0:1, 0:128], axis=AX), waits=[("cT", ct)])
        sc.drain("vector")
        sc.op("vector", lambda h: h.tensor_scalar(
            out=stat[0:1, 2:3], in0=stat[0:1, 1:2], scalar1=1.0 / (HC * E),
            scalar2=1e-5, op0=OP.mult, op1=OP.max))
        sc.drain("vector")
        cv = sc.op("vector", lambda h: h.reciprocal(stat[0:1, 3:4], stat[0:1, 2:3]),
                   inc="cV")
        ct = sc.op("tensor", lambda h: h.matmul(
            pC[1][:, 0:2], onesc_f[0:1, :], stat[0:1, 2:4], start=True, stop=True),
            waits=[("cV", cv)], inc="cT")
        scd = sc.op("vector", lambda h, wscb=wscb: h.tensor_copy(
            wscb[:], pC[1][:, 0:2]), waits=[("cT", ct)], inc="cV")
        bksW = Banks(pCb, free0=sc.v("cV"))
        stg_done = 0
        cons = [scd] * 2
        ip = 0
        wqn_free = 0
        for tt in range(ntile):
            for pc in range(npc):
                par = ip % 2
                ip += 1
                buf = wt[:, par * 2048:par * 2048 + pw]
                i = st_wl.dma(lambda h, buf=buf, wd=wd, tt=tt, pc=pc, pw=pw:
                              h.dma_start(out=buf,
                                          in_=wd[128 * tt:128 * (tt + 1),
                                                 pw * pc:pw * (pc + 1)]),
                              waits=[("cV", cons[par])])
                sc.op("vector", lambda h, buf=buf, pw=pw, wscb=wscb: h.tensor_scalar(
                    out=mp_t[:, 0:pw], in0=buf, scalar1=wscb[:, 1:2], scalar2=0.5,
                    op0=OP.mult, op1=OP.is_ge),
                    waits=st_wl.wait_for(i) + [("cT", wqn_free)])
                cons[par] = sc.op("vector", lambda h, buf=buf, pw=pw, wscb=wscb:
                             h.tensor_scalar(
                    out=mm_t[:, 0:pw], in0=buf, scalar1=wscb[:, 1:2], scalar2=-0.5,
                    op0=OP.mult, op1=OP.is_le), inc="cV")
                sc.drain("vector")
                qdone = sc.op("vector", lambda h, pw=pw: h.tensor_sub(
                    wqn[:, 0:pw], mp_t[:, 0:pw], mm_t[:, 0:pw]), inc="cV")
                nb = pw // 128
                for b in range(nb):
                    bg = pc * nb + b
                    bk, freev, ki = bksW.get()
                    ct = sc.op("tensor", lambda h, bk=bk, b=b: h.transpose(
                        bk[:, 0:128], wqn[:, 128 * b:128 * (b + 1)], ident_b[:]),
                        waits=[("cV", qdone), ("cV", freev)], inc="cT")
                    if wi == 0:
                        ev = sc.op("vector", lambda h, bk=bk, tt=tt, bg=bg:
                                   h.tensor_copy(
                            w1qT[:, bg * HC + 128 * tt: bg * HC + 128 * (tt + 1)],
                            bk[:, 0:128]), waits=[("cT", ct)], inc="cV")
                    else:
                        wv = [("cT", ct)] + st_sw.wait_for(st_sw.n - 1)
                        ev = sc.op("vector", lambda h, bk=bk, bg=bg: h.tensor_copy(
                            stgW[:, 128 * bg:128 * (bg + 1)], bk[:, 0:128]),
                            waits=wv, inc="cV")
                    bksW.set_free(ki, ev)
                wqn_free = sc.v("cT")
            if wi == 1:
                dstw = w2qT_d[:, 128 * tt:128 * (tt + 1)] \
                    .rearrange("(b p) e -> p b e", p=128)
                srcw = stgW[:].rearrange("p (b e) -> p b e", e=128)
                st_sw.dma(lambda h, dstw=dstw, srcw=srcw:
                          h.dma_start(out=dstw, in_=srcw),
                          waits=[("cV", sc.v("cV"))])

    wquant(0, w1_d, w1sc)
    wquant(1, w2_d, w2sc)
    wq_end_ct = sc.v("cT")
    wq_end_ds = st_sw.wait_all()

    bksX = Banks(pCb, free0=sc.v("cV"))
    xq_dmas = []
    for t in range(NT):
        par = t % 2
        xat = xa[:, par * E:(par + 1) * E]
        obt = ob[:, par * E:(par + 1) * E]
        w = [("cV", sc.v("cV") if t >= 2 else 0)]
        i1 = st_lc.dma(lambda h, xat=xat, t=t: h.dma_start(
            out=xat, in_=x_d[128 * t:128 * (t + 1), :]), waits=w)
        i2 = st_lc.dma(lambda h, obt=obt, t=t: h.dma_start(
            out=obt, in_=o_d[128 * t:128 * (t + 1), :]))
        sc.op("vector", lambda h, obt=obt: h.reduce_sum(
            stat[:, 0:1], obt, axis=AX),
            waits=st_lc.wait_for(i2) + [("cV", gb_ready)])
        sc.drain("vector")
        sc.op("vector", lambda h: h.tensor_scalar(
            out=stat[:, 1:2], in0=stat[:, 0:1], scalar1=-1.0 / E, scalar2=None,
            op0=OP.mult), inc="cV")
        mu_done = sc.v("cV")
        sq_done = sc.op("scalar", lambda h, obt=obt: h.activation(
            out=t1s[:], in_=obt, func=AF.Square, bias=stat[:, 1:2],
            accum_out=stat[:, 2:3]), waits=[("cV", mu_done)], inc="cS")
        sc.op("vector", lambda h: h.tensor_scalar(
            out=stat[:, 3:4], in0=stat[:, 2:3], scalar1=1.0 / E, scalar2=1e-3,
            op0=OP.mult, op1=OP.add), waits=[("cS", sq_done)], inc="cV")
        ln_done = sc.op("scalar", lambda h: h.activation(
            out=stat[:, 4:5], in_=stat[:, 3:4], func=AF.Ln),
            waits=[("cV", sc.v("cV"))], inc="cS")
        rstd_done = sc.op("scalar", lambda h: h.activation(
            out=stat[:, 5:6], in_=stat[:, 4:5], func=AF.Exp, scale=cst[:, 1:2]),
            waits=[("cS", ln_done)], inc="cS")
        sc.op("vector", lambda h, obt=obt: h.tensor_scalar(
            out=t1s[:], in0=obt, scalar1=stat[:, 1:2], scalar2=stat[:, 5:6],
            op0=OP.add, op1=OP.mult), waits=[("cS", rstd_done)])
        sc.drain("vector")
        sc.op("vector", lambda h: h.tensor_mul(t1s[:], t1s[:], gb_sb[:]))
        sc.drain("vector")
        sc.op("vector", lambda h: h.tensor_add(t1s[:], t1s[:], bb_sb[:]))
        sc.drain("vector")
        sc.op("vector", lambda h, xat=xat: h.tensor_add(r1[:], t1s[:], xat),
              waits=st_lc.wait_for(i1), inc="cV")
        sc.drain("vector")
        sc.op("vector", lambda h: h.reduce_sum(stat[:, 6:7], r1[:], axis=AX))
        sc.drain("vector")
        sc.op("vector", lambda h: h.tensor_scalar(
            out=stat[:, 7:8], in0=stat[:, 6:7], scalar1=-1.0 / E, scalar2=None,
            op0=OP.mult), inc="cV")
        sq2 = sc.op("scalar", lambda h: h.activation(
            out=t1s[:], in_=r1[:], func=AF.Square, bias=stat[:, 7:8],
            accum_out=stat[:, 8:9]), waits=[("cV", sc.v("cV"))], inc="cS")
        sc.op("vector", lambda h: h.tensor_scalar(
            out=stat[:, 9:10], in0=stat[:, 8:9], scalar1=1.0 / E, scalar2=1e-5,
            op0=OP.mult, op1=OP.add), waits=[("cS", sq2)], inc="cV")
        ln2 = sc.op("scalar", lambda h: h.activation(
            out=stat[:, 10:11], in_=stat[:, 9:10], func=AF.Ln),
            waits=[("cV", sc.v("cV"))], inc="cS")
        rstd2 = sc.op("scalar", lambda h: h.activation(
            out=stat[:, 11:12], in_=stat[:, 10:11], func=AF.Exp, scale=cst[:, 1:2]),
            waits=[("cS", ln2)], inc="cS")
        x2done = sc.op("vector", lambda h: h.tensor_scalar(
            out=t1s[:], in0=r1[:], scalar1=stat[:, 7:8], scalar2=stat[:, 11:12],
            op0=OP.add, op1=OP.mult), waits=[("cS", rstd2)], inc="cV")
        st_sx2.dma(lambda h, t=t: h.dma_start(
            out=x2_d[128 * t:128 * (t + 1), :], in_=t1s[:]),
            waits=[("cV", x2done)])
        sc.drain("vector")
        sc.op("vector", lambda h: h.tensor_reduce(
            out=stat[:, 0:1], in_=t1s[:], axis=AX, op=OP.max,
            apply_absolute_value=True))
        sc.drain("vector")
        sc.op("vector", lambda h: h.tensor_scalar(
            out=stat[:, 1:2], in0=stat[:, 0:1], scalar1=1e-5, scalar2=None,
            op0=OP.max))
        sc.drain("vector")
        sc.op("vector", lambda h, t=t: h.tensor_scalar(
            out=rx1[:, t:t + 1], in0=stat[:, 1:2], scalar1=1.0 / 127.0, scalar2=None,
            op0=OP.mult))
        sc.op("vector", lambda h: h.reciprocal(stat[:, 2:3], stat[:, 1:2]))
        sc.drain("vector")
        sc.op("vector", lambda h: h.tensor_scalar(
            out=stat[:, 3:4], in0=stat[:, 2:3], scalar1=127.0, scalar2=None,
            op0=OP.mult), inc="cV")
        sc.op("vector", lambda h: h.tensor_scalar(
            out=t1s[:], in0=t1s[:], scalar1=stat[:, 3:4], scalar2=128.5,
            op0=OP.mult, op1=OP.add),
            waits=[("cV", sc.v("cV"))] + st_sx2.wait_for(t))
        sc.drain("vector")
        sc.op("vector", lambda h: h.tensor_copy(uq[:], t1s[:]))
        sc.drain("vector")
        sc.op("vector", lambda h: h.tensor_sub(frs, t1s[:], uq[:]))
        sc.drain("vector")
        for thr in (2.0, 1.0, 0.0, -1.0):
            sc.op("vector", lambda h, thr=thr: h.scalar_tensor_tensor(
                out=t1s[:], in0=frs, scalar=thr,
                in1=uq[:] if thr == 2.0 else t1s[:],
                op0=OP.is_ge, op1=OP.add))
            sc.drain("vector")
        xqdone = sc.op("vector", lambda h: h.tensor_scalar(
            out=xq_t, in0=t1s[:], scalar1=1.0, scalar2=-130.0,
            op0=OP.mult, op1=OP.add), waits=[("cT", wq_end_ct)], inc="cV")
        for b in range(NDT):
            bk, freev, ki = bksX.get()
            ct = sc.op("tensor", lambda h, bk=bk, b=b: h.transpose(
                bk[:, 0:128], xq_t[:, 128 * b:128 * (b + 1)], ident_b[:]),
                waits=[("cV", xqdone), ("cV", freev)], inc="cT")
            wv = [("cT", ct)] + st_sxq.wait_for(len(xq_dmas) - 2)
            if len(xq_dmas) == 0 and b == 0:
                wv += wq_end_ds
            ev = sc.op("vector", lambda h, bk=bk, b=b, par=par: h.tensor_copy(
                stC[:, par * 1024 + 128 * b: par * 1024 + 128 * (b + 1)],
                bk[:, 0:128]), waits=wv, inc="cV")
            bksX.set_free(ki, ev)
        dstx = xqT_d[:, 128 * t:128 * (t + 1)].rearrange("(b p) e -> p b e", p=128)
        srcx = stC[:, par * 1024:(par + 1) * 1024].rearrange("p (b e) -> p b e", e=128)
        st_sxq.dma(lambda h, dstx=dstx, srcx=srcx: h.dma_start(
            out=dstx, in_=srcx), waits=[("cV", sc.v("cV"))])
        xq_dmas.append(st_sxq.n)
    ct = sc.op("tensor", lambda h: h.transpose(
        pC[2][0:16, 0:128], rx1[:], ident_f[:]),
        waits=[("cV", sc.v("cV"))], inc="cT")
    rxcv = sc.op("vector", lambda h: h.tensor_copy(rxrow_s[:], pC[2][0:16, 0:128]),
                 waits=[("cT", ct)], inc="cV")
    st_srx.dma(lambda h: h.dma_start(out=rx_d, in_=rxrow_s[:]),
               waits=[("cV", rxcv)])
    sc.barrier()
    esC.close()

    esD = ExitStack()
    GW = 2048
    hsl = [sb(esD, f"hsl{i}", [128, NHC * CH], BF16) for i in range(2)]
    gsl = sb(esD, "gsl", [128, NHC * CH], F32)
    xqc = sb(esD, "xqc", [128, 2 * NDT * CH], BF16)
    w2c = sb(esD, "w2c", [128, 2 * 4 * E], BF16)
    rxw = sb(esD, "rxw", [128, CH], F32)
    s2b = sb(esD, "s2b", [128, CH], F32)
    s2rep = sb(esD, "s2rep", [128, GW], F32)
    s2r2 = sb(esD, "s2r2", [2, 128], F32)
    rxr = sb(esD, "rxr", [1, CH], F32)
    rxr2 = sb(esD, "rxr2", [1, CH], F32)
    mxt = sb(esD, "mxt", [128, 8], F32)
    rx2d = sb(esD, "rx2d", [128, 2], F32)
    macc = sb(esD, "macc", [128, CH], F32)
    cacc = sb(esD, "cacc", [128, 2 * CH], F32)
    cacc2 = sb(esD, "cacc2", [128, 2 * CH], F32)
    cacc3 = sb(esD, "cacc3", [128, 2 * CH], F32)
    halo = [sb(esD, f"halo{i}", [128, NHC], BF16) for i in range(3)]
    zcol = sb(esD, "zcol", [128, NHC], BF16)
    tq2 = sb(esD, "tq2", [128, GW], F32)
    uq2 = esD.enter_context(nc.sbuf_tensor("uq2", [128, GW], I32))
    fr2 = sb(esD, "fr2", [128, GW], F32)
    y_sb = sb(esD, "y_sb", [128, E], F32)
    pD = [esD.enter_context(nc.psum_tensor(f"pD{i}", [128, 512], F32)) for i in range(8)]

    sc.op("gpsimd", lambda h: h.memset(zcol[:], 0.0), inc="cG")
    cG_z = sc.v("cG")

    xq_dh = {}
    h_done = {}
    rxr_free = [0]
    rxr2_free = [0]
    w2_prev = [0]
    ffn2_done = {}
    pd6_free = 0
    pd45_free = 0
    y_dmas = []
    gm_prev = [0, 0, 0, 0]

    def load_xq_chunk(k):
        dst = xqc[:, (k % 2) * NDT * CH:(k % 2 + 1) * NDT * CH] \
            .rearrange("p (b s) -> p b s", s=CH)
        src = xqT_d[:, CH * k:CH * (k + 1)].rearrange("(b p) s -> p b s", p=128)
        w = [("cT", h_done.get(k - 2, (0, 0))[1])] if k >= 2 else []
        return st_lq.dma(lambda h, dst=dst, src=src: h.dma_start(
            out=dst, in_=src), waits=w)

    def ffn1(k):
        nonlocal pd6_free
        dh = xq_dh[k]
        kk = k % 2
        i = st_lr.dma(lambda h, k=k: h.dma_start(
            out=rxr[0:1, :].rearrange("o (a b) -> o a b", a=2),
            in_=rx_d[2 * k:2 * k + 2, :]),
            waits=[("cT", rxr_free[0])] + st_srx.wait_all())
        ct = sc.op("tensor", lambda h: h.matmul(
            pD[6][:, 0:CH], onesc_f[0:1, :], rxr[0:1, :], start=True, stop=True),
            waits=st_lr.wait_for(i) + [("cV", pd6_free)], inc="cT")
        rxr_free[0] = ct
        rxwd = sc.op("vector", lambda h: h.tensor_scalar(
            out=rxw[:], in0=pD[6][:, 0:CH], scalar1=w1sc[:, 0:1], scalar2=None,
            op0=OP.mult), waits=[("cT", ct)], inc="cV")
        pd6_free = rxwd
        bks = Banks(pD[0:4], free0=sc.v("cV"))
        hguard = ffn2_done.get(k - 2, 0)
        for hc in range(NHC):
            bk, freev, ki = bks.get()
            for dt in range(NDT):
                w = st_lq.wait_for(dh) + ([("cV", freev)] if dt == 0 else [])
                sc.op("tensor", lambda h, bk=bk, hc=hc, dt=dt, kk=kk: h.matmul(
                    bk[:, 0:CH],
                    w1qT[:, dt * HC + 128 * hc: dt * HC + 128 * (hc + 1)],
                    xqc[:, kk * NDT * CH + dt * CH: kk * NDT * CH + (dt + 1) * CH],
                    start=(dt == 0), stop=(dt == NDT - 1)),
                    waits=w, inc="cT" if dt == NDT - 1 else None)
            ev = sc.op("vector", lambda h, bk=bk, hc=hc, kk=kk: h.tensor_mul(
                hsl[kk][:, hc * CH:(hc + 1) * CH], bk[:, 0:CH], rxw[:]),
                waits=[("cT", sc.v("cT")), ("cV", rxwd), ("cT", hguard)], inc="cV")
            bks.set_free(ki, ev)
        mm_ct = sc.v("cT")
        sc.drain("vector")
        snap = sc.op("vector", lambda h, kk=kk, k=k: h.tensor_copy(
            halo[k % 3][:], hsl[kk][:].rearrange("p (c s) -> p c s", s=CH)[:, :, CH - 1]),
            inc="cV")
        return (snap, mm_ct)

    def conv_quant_ffn2(m, last):
        nonlocal pd6_free, pd45_free
        km = m % 2
        hm = hsl[km]
        hl = halo[(m - 1) % 3] if m >= 1 else zcol
        h_cv = h_done[m][0]
        r_cv = h_done[m + 1][0] if not last else h_done[m][0]
        v1_at = [0, 0]
        sil_at = {}
        for hc in range(NHC):
            par = hc % 2
            A = cacc[:, par * CH:(par + 1) * CH]
            B = cacc2[:, par * CH:(par + 1) * CH]
            C = cacc3[:, par * CH:(par + 1) * CH]
            w = [("cV", v1_at[par]), ("cV", h_cv)]
            if m == 0 and hc == 0:
                w.append(("cG", cG_z))
            sc.op("scalar", lambda h, hc=hc, hm=hm, A=A: h.mul(
                A[:, 1:CH], hm[:, hc * CH: hc * CH + CH - 1],
                cw_sb[:, 3 * hc:3 * hc + 1]), waits=w)
            sA = sc.op("scalar", lambda h, hc=hc, hl=hl, A=A: h.mul(
                A[:, 0:1], hl[:, hc:hc + 1], cw_sb[:, 3 * hc:3 * hc + 1]),
                inc="cS")
            v1_at[par] = sc.op("vector", lambda h, hc=hc, hm=hm, A=A, B=B:
                               h.scalar_tensor_tensor(
                out=B, in0=hm[:, hc * CH:(hc + 1) * CH],
                scalar=cw_sb[:, 3 * hc + 1:3 * hc + 2], in1=A,
                op0=OP.mult, op1=OP.add), waits=[("cS", sA)], inc="cV")
            sc.drain("vector")
            wv2 = [("cS", sil_at.get(hc - 2, 0))]
            if last and hc == 0:
                wv2.append(("cG", cG_z))
            sc.op("vector", lambda h, hc=hc, hm=hm, B=B, C=C:
                  h.scalar_tensor_tensor(
                out=C[:, 0:CH - 1], in0=hm[:, hc * CH + 1:(hc + 1) * CH],
                scalar=cw_sb[:, 3 * hc + 2:3 * hc + 3], in1=B[:, 0:CH - 1],
                op0=OP.mult, op1=OP.add), waits=wv2)
            if last:
                rsl = zcol[:, hc:hc + 1]
            else:
                rsl = hsl[(m + 1) % 2][:, hc * CH: hc * CH + 1]
            vC = sc.op("vector", lambda h, hc=hc, rsl=rsl, B=B, C=C:
                       h.scalar_tensor_tensor(
                out=C[:, CH - 1:CH], in0=rsl,
                scalar=cw_sb[:, 3 * hc + 2:3 * hc + 3], in1=B[:, CH - 1:CH],
                op0=OP.mult, op1=OP.add), waits=[("cV", r_cv)], inc="cV")
            sc.drain("vector")
            wsil = [("cV", vC)]
            if m >= 1:
                wsil.append(("cV", gm_prev[hc // 8]))
            sil_at[hc] = sc.op("scalar", lambda h, hc=hc, C=C: h.activation(
                out=gsl[:, hc * CH:(hc + 1) * CH], in_=C, func=AF.Silu,
                bias=cb_sb[:, hc:hc + 1]), waits=wsil, inc="cS")
        sil_done = sil_at[NHC - 1]
        mdone = sc.op("vector", lambda h: h.tensor_reduce(
            out=macc[:], in_=gsl[:].rearrange("p (c s) -> p s c", c=NHC),
            axis=AX, op=OP.max, apply_absolute_value=True),
            waits=[("cS", sil_done)], inc="cV")
        for blk in range(CH // 128):
            ct = sc.op("tensor", lambda h, blk=blk: h.transpose(
                pD[4 + blk][:, 0:128], macc[:, 128 * blk:128 * (blk + 1)],
                ident_f[:]), waits=[("cV", mdone), ("cV", pd45_free)], inc="cT")
            sc.op("vector", lambda h, blk=blk: h.tensor_reduce(
                out=mxt[:, blk:blk + 1], in_=pD[4 + blk][:, 0:128], axis=AX,
                op=OP.max), waits=[("cT", ct)], inc="cV")
        pd45_free = sc.v("cV")
        sc.drain("vector")
        sc.op("vector", lambda h: h.tensor_scalar(
            out=mxt[:, 2:4], in0=mxt[:, 0:2], scalar1=1e-5, scalar2=None,
            op0=OP.max))
        sc.drain("vector")
        sc.op("vector", lambda h: h.reciprocal(mxt[:, 4:6], mxt[:, 2:4]))
        sc.drain("vector")
        sc.op("vector", lambda h: h.tensor_scalar(
            out=mxt[:, 6:8], in0=mxt[:, 4:6], scalar1=127.0, scalar2=None,
            op0=OP.mult))
        scq = sc.op("vector", lambda h: h.tensor_scalar(
            out=rx2d[:], in0=mxt[:, 2:4], scalar1=w2sc[:, 0:1], scalar2=1.0 / 127.0,
            op0=OP.mult, op1=OP.mult), inc="cV")
        ct = sc.op("tensor", lambda h: h.transpose(
            pD[7][0:2, 0:128], mxt[:, 6:8], ident_f[:]),
            waits=[("cV", scq), ("cV", pd6_free)], inc="cT")
        s2rd = sc.op("vector", lambda h: h.tensor_copy(s2r2[:], pD[7][0:2, 0:128]),
                     waits=[("cT", ct)], inc="cV")
        i = st_sm.dma(lambda h: h.dma_start(out=mx_d, in_=s2r2[:]),
                      waits=[("cV", s2rd)] + st_lm.wait_all())
        i2 = st_lm.dma(lambda h: h.dma_start(
            out=rxr2[0:1, :].rearrange("o (a b) -> o a b", a=2), in_=mx_d),
            waits=st_sm.wait_all() + [("cT", rxr2_free[0])])
        ct = sc.op("tensor", lambda h: h.matmul(
            pD[6][:, 0:CH], onesc_f[0:1, :], rxr2[0:1, :], start=True, stop=True),
            waits=st_lm.wait_for(i2) + [("cV", pd6_free)], inc="cT")
        rxr2_free[0] = ct
        s2bd0 = sc.op("vector", lambda h: h.tensor_copy(s2b[:], pD[6][:, 0:CH]),
                      waits=[("cT", ct)], inc="cV")
        pd6_free = s2bd0
        sc.drain("vector")
        sc.op("vector", lambda h: h.tensor_copy(s2rep[:, 0:CH], s2b[:]))
        sc.drain("vector")
        for r in range(3):
            w0 = CH * (1 << r)
            sc.op("vector", lambda h, w0=w0: h.tensor_copy(
                s2rep[:, w0:2 * w0], s2rep[:, 0:w0]),
                inc="cV" if w0 * 2 == GW else None)
            sc.drain("vector")
        s2bd = sc.v("cV")
        q2_at = {}
        for gq in range(NHC * CH // GW):
            sl = slice(gq * GW, (gq + 1) * GW)
            sc.op("vector", lambda h, sl=sl: h.tensor_tensor(
                out=tq2[:], in0=gsl[:, sl], in1=s2rep[:], op=OP.mult),
                waits=[("cV", s2bd), ("cS", sil_at[(gq + 1) * (GW // CH) - 1])])
            sc.drain("vector")
            gm_prev[gq] = sc.op("vector", lambda h: h.tensor_scalar(
                out=uq2[:], in0=tq2[:], scalar1=1.0, scalar2=128.5,
                op0=OP.mult, op1=OP.add), inc="cV")
            sc.drain("vector")
            sc.op("vector", lambda h: h.tensor_sub(fr2[:], tq2[:], uq2[:]))
            sc.drain("vector")
            for thr in (-129.5, -128.5, -127.5, -126.5):
                sc.op("vector", lambda h, thr=thr: h.scalar_tensor_tensor(
                    out=tq2[:], in0=fr2[:], scalar=thr,
                    in1=uq2[:] if thr == -129.5 else tq2[:],
                    op0=OP.is_ge, op1=OP.add))
                sc.drain("vector")
            q2_at[gq] = sc.op("vector", lambda h, sl=sl, km=km: h.tensor_scalar(
                out=hsl[km][:, sl], in0=tq2[:], scalar1=1.0, scalar2=-130.0,
                op0=OP.mult, op1=OP.add), inc="cV")
            sc.drain("vector")
        piece_ct = {}
        for piece in range(NHC // 4):
            par = piece % 2
            dstw = w2c[:, par * 4 * E:(par + 1) * 4 * E] \
                .rearrange("p (b e) -> p b e", e=E)
            srcw = w2qT_d.rearrange("(b p) e -> p b e", p=128)[:, 4 * piece:4 * piece + 4, :]
            wld = [("cT", piece_ct.get(piece - 2, w2_prev[0]))]
            dhw = st_lw2.dma(lambda h, dstw=dstw, srcw=srcw: h.dma_start(
                out=dstw, in_=srcw), waits=wld)
            for pidx in range(4):
                hc = 4 * piece + pidx
                for sl2 in range(CH // 128):
                    for nch2 in range(2):
                        bky = pD[sl2 * 2 + nch2]
                        w = [("cV", q2_at[hc // 8])] + st_lw2.wait_for(dhw)
                        sc.op("tensor", lambda h, bky=bky, hc=hc, sl2=sl2, nch2=nch2,
                              km=km, par=par, pidx=pidx: h.matmul(
                            bky[:, 0:512],
                            hsl[km][:, hc * CH + 128 * sl2: hc * CH + 128 * (sl2 + 1)],
                            w2c[:, par * 4 * E + pidx * E + 512 * nch2:
                                par * 4 * E + pidx * E + 512 * (nch2 + 1)],
                            start=(hc == 0), stop=(hc == NHC - 1)),
                            waits=w,
                            inc="cT" if (pidx == 3 and sl2 == CH // 128 - 1
                                         and nch2 == 1) else None)
            piece_ct[piece] = sc.v("cT")
        ffn2_done[m] = sc.v("cT")
        w2_prev[0] = ffn2_done[m]
        for sl2 in range(CH // 128):
            for nch2 in range(2):
                bky = pD[sl2 * 2 + nch2]
                wv = [("cT", ffn2_done[m])] + st_sy.wait_for(len(y_dmas) - 1)
                sc.op("vector", lambda h, bky=bky, sl2=sl2, nch2=nch2: h.tensor_scalar(
                    out=y_sb[:, 512 * nch2: 512 * (nch2 + 1)],
                    in0=bky[:, 0:512], scalar1=rx2d[:, sl2:sl2 + 1], scalar2=None,
                    op0=OP.mult), waits=wv, inc="cV")
            st_sy.dma(lambda h, m=m, sl2=sl2: h.dma_start(
                out=y_d[CH * m + 128 * sl2: CH * m + 128 * (sl2 + 1), :],
                in_=y_sb[:]),
                waits=[("cV", sc.v("cV"))])
            y_dmas.append(st_sy.n)

    xq_dh[0] = load_xq_chunk(0)
    h_done[0] = ffn1(0)
    for m in range(NCH):
        if m + 1 < NCH:
            xq_dh[m + 1] = load_xq_chunk(m + 1)
            h_done[m + 1] = ffn1(m + 1)
        conv_quant_ffn2(m, last=(m == NCH - 1))
    sc.barrier()
    esD.close()
    esCD.close()

    esE = ExitStack()
    xa2 = sb(esE, "xa2", [128, 2 * E], F32)
    yb2 = sb(esE, "yb2", [128, 2 * E], F32)
    r2 = sb(esE, "r2", [128, E], F32)
    t2s = sb(esE, "t2s", [128, E], F32)
    ot = sb(esE, "ot", [128, 2 * E], F32)
    stE = sb(esE, "stE", [128, 12], F32)
    out_dmas = []
    for t in range(NT):
        par = t % 2
        xat = xa2[:, par * E:(par + 1) * E]
        ybt = yb2[:, par * E:(par + 1) * E]
        w = [("cV", sc.v("cV") if t >= 2 else 0)]
        i1 = st_le.dma(lambda h, xat=xat, t=t: h.dma_start(
            out=xat, in_=x2_d[128 * t:128 * (t + 1), :]), waits=w)
        i2 = st_le.dma(lambda h, ybt=ybt, t=t: h.dma_start(
            out=ybt, in_=y_d[128 * t:128 * (t + 1), :]))
        sc.drain("vector")
        sc.op("vector", lambda h, xat=xat, ybt=ybt: h.tensor_add(r2[:], xat, ybt),
              waits=st_le.wait_for(i1) + st_le.wait_for(i2), inc="cV")
        sc.drain("vector")
        sc.op("vector", lambda h: h.reduce_sum(stE[:, 0:1], r2[:], axis=AX))
        sc.drain("vector")
        sc.op("vector", lambda h: h.tensor_scalar(
            out=stE[:, 1:2], in0=stE[:, 0:1], scalar1=-1.0 / E, scalar2=None,
            op0=OP.mult), inc="cV")
        sqd = sc.op("scalar", lambda h: h.activation(
            out=t2s[:], in_=r2[:], func=AF.Square, bias=stE[:, 1:2],
            accum_out=stE[:, 2:3]), waits=[("cV", sc.v("cV"))], inc="cS")
        sc.op("vector", lambda h: h.tensor_scalar(
            out=stE[:, 3:4], in0=stE[:, 2:3], scalar1=1.0 / E, scalar2=1e-5,
            op0=OP.mult, op1=OP.add), waits=[("cS", sqd)], inc="cV")
        lnd = sc.op("scalar", lambda h: h.activation(
            out=stE[:, 4:5], in_=stE[:, 3:4], func=AF.Ln),
            waits=[("cV", sc.v("cV"))], inc="cS")
        rsd = sc.op("scalar", lambda h: h.activation(
            out=stE[:, 5:6], in_=stE[:, 4:5], func=AF.Exp, scale=cst[:, 1:2]),
            waits=[("cS", lnd)], inc="cS")
        otp = ot[:, par * E:(par + 1) * E]
        w2 = [("cS", rsd)] + st_se.wait_for(len(out_dmas) - 2)
        odone = sc.op("vector", lambda h, otp=otp: h.tensor_scalar(
            out=otp, in0=r2[:], scalar1=stE[:, 1:2], scalar2=stE[:, 5:6],
            op0=OP.add, op1=OP.mult), waits=w2, inc="cV")
        st_se.dma(lambda h, otp=otp, t=t: h.dma_start(
            out=out_d[128 * t:128 * (t + 1), :], in_=otp),
            waits=[("cV", odone)])
        out_dmas.append(st_se.n)
    sc.op("sync", None, waits=st_se.wait_all())
    esE.close()

    with nc.Block() as block:
        sc.emit(block, sems)
    root.close()
    return nc


def kernel(L0_attn_x, L0_attn_Wq, L0_attn_Wk, L0_attn_Wv, L0_attn_rel_table,
           L0_attn_gamma, L0_attn_beta, L0_ffn_w1, L0_ffn_conv_w, L0_ffn_conv_b,
           L0_ffn_w2):
    x = np.ascontiguousarray(np.asarray(L0_attn_x, dtype=np.float32))
    T = np.asarray(L0_attn_rel_table, dtype=np.float32)

    t_ext = np.concatenate([T, np.repeat(T[-1:, :], 2048, axis=0)], axis=0)
    dvals = np.arange(-15, 16)
    p_i = np.arange(128)
    a_i = np.arange(128)
    idx = (2047 + 128 * dvals[:, None, None] + p_i[None, :, None]
           - a_i[None, None, :])
    toe = np.ascontiguousarray(
        t_ext[idx].transpose(3, 0, 1, 2))

    cwv = np.asarray(L0_ffn_conv_w, dtype=np.float32).reshape(NHC, 128, 3)
    cw = np.ascontiguousarray(cwv.transpose(1, 0, 2).reshape(128, NHC * 3))
    cb = np.ascontiguousarray(
        np.asarray(L0_ffn_conv_b, dtype=np.float32).reshape(NHC, 128).T)

    base = dict(
        wq=np.ascontiguousarray(L0_attn_Wq, dtype=np.float32),
        wk=np.ascontiguousarray(L0_attn_Wk, dtype=np.float32),
        wv=np.ascontiguousarray(L0_attn_Wv, dtype=np.float32),
        toe=toe,
        gam=np.ascontiguousarray(np.asarray(L0_attn_gamma, np.float32)[None, :]),
        bet=np.ascontiguousarray(np.asarray(L0_attn_beta, np.float32)[None, :]),
        w1=np.ascontiguousarray(L0_ffn_w1, dtype=np.float32),
        cw=cw, cb=cb,
        w2=np.ascontiguousarray(L0_ffn_w2, dtype=np.float32),
    )
    in_maps = [dict(base, x=np.ascontiguousarray(x[b])) for b in range(8)]

    prog = build_program()
    res = run_bass_kernel_spmd(prog, in_maps, core_ids=list(range(8)))
    key = {0: "out", 1: "o_s", 2: "x2_s", 3: "y_s"}[DEBUG]
    return np.stack([np.asarray(r[key]) for r in res.results], axis=0)


# version 06
# speedup vs baseline: 1.4335x; 1.0588x over previous
import numpy as np
import ml_dtypes
from contextlib import ExitStack

import concourse.bass as bass
import concourse.mybir as mybir
from concourse.bass_utils import run_bass_kernel_spmd
from concourse.alu_op_type import AluOpType as OP

F32 = mybir.dt.float32
BF16 = mybir.dt.bfloat16
I32 = mybir.dt.int32
AF = mybir.ActivationFunctionType
AX = mybir.AxisListType.X
bfdt = ml_dtypes.bfloat16

S, E, H, DH, HC = 2048, 1024, 16, 64, 4096
NT = S // 128
NP = H // 2
NDT = E // 128
NHC = HC // 128
CH = 256
NCH = S // CH
NW = 18
DEBUG = 0

ENGINES = ("sync", "gpsimd", "vector", "scalar", "tensor")


class Sched:
    def __init__(self):
        self.ops = {e: [] for e in ENGINES}
        self.cnt = {}
        self.last_wait = {e: {} for e in ENGINES}

    def v(self, sem):
        return self.cnt.get(sem, 0)

    def op(self, eng, fn, waits=(), inc=None, inc_n=1, bare_inc=False):
        w = []
        for s, val in waits:
            if val > 0 and self.last_wait[eng].get(s, -1) < val:
                w.append((s, val))
                self.last_wait[eng][s] = val
        self.ops[eng].append((fn, w, inc, inc_n, bare_inc))
        if inc is not None:
            self.cnt[inc] = self.cnt.get(inc, 0) + inc_n
        return self.cnt.get(inc, 0) if inc else None

    def drain(self, eng):
        self.op(eng, lambda h: h.drain())

    def barrier(self):
        for st in getattr(self, "streams", []):
            for w in st.wait_all():
                self.op("sync", None, waits=(w,))
        for e in ENGINES:
            self.op(e, None, inc="bar", inc_n=1, bare_inc=True)
        tot = self.v("bar")
        for e in ENGINES:
            self.op(e, None, waits=(("bar", tot),))

    def emit(self, block, sems):
        def make(ename):
            def body(h):
                for fn, waits, inc, inc_n, bare in self.ops[ename]:
                    for s, val in waits:
                        h.wait_ge(sems[s], val)
                    if bare:
                        h.sem_inc(sems[inc], inc_n)
                        continue
                    if fn is None:
                        continue
                    instr = fn(h)
                    if inc is not None:
                        instr.then_inc(sems[inc], inc_n)
            return body
        block.sync(make("sync"))
        block.gpsimd(make("gpsimd"))
        block.vector(make("vector"))
        block.scalar(make("scalar"))
        block.tensor(make("tensor"))


class Stream:

    def __init__(self, sc, eng, sems, names):
        self.sc = sc
        self.eng = eng
        self.sems = names
        self.counts = [0] * len(names)
        self.n = 0

    def dma(self, fn, waits=()):
        i = self.n
        k = i % len(self.sems)
        self.n += 1
        self.counts[k] += 1
        self.sc.op(self.eng, fn, waits=waits, inc=self.sems[k], inc_n=16)
        return i

    def wait_for(self, i):
        if i < 0:
            return []
        k = i % len(self.sems)
        return [(self.sems[k], 16 * (i // len(self.sems) + 1))]

    def wait_last(self):
        return self.wait_for(self.n - 1)

    def wait_all(self):
        return [(s, 16 * c) for s, c in zip(self.sems, self.counts) if c > 0]


class Banks:

    def __init__(self, banks, free0=0):
        self.banks = banks
        self.free_at = [free0] * len(banks)
        self.i = 0

    def get(self):
        k = self.i % len(self.banks)
        self.i += 1
        return self.banks[k], self.free_at[k], k

    def set_free(self, k, cv):
        self.free_at[k] = cv


def build_program():
    nc = bass.Bass("TRN2", target_bir_lowering=False)

    x_d = nc.dram_tensor("x", [S, E], F32, kind="ExternalInput").ap()
    wq_d = nc.dram_tensor("wq", [E, E], F32, kind="ExternalInput").ap()
    wk_d = nc.dram_tensor("wk", [E, E], F32, kind="ExternalInput").ap()
    wv_d = nc.dram_tensor("wv", [E, E], F32, kind="ExternalInput").ap()
    toe_d = nc.dram_tensor("toe", [16, 31, 128, 128], F32, kind="ExternalInput").ap()
    gam_d = nc.dram_tensor("gam", [1, E], F32, kind="ExternalInput").ap()
    bet_d = nc.dram_tensor("bet", [1, E], F32, kind="ExternalInput").ap()
    w1_d = nc.dram_tensor("w1", [HC, E], F32, kind="ExternalInput").ap()
    cw_d = nc.dram_tensor("cw", [128, 3 * NHC], F32, kind="ExternalInput").ap()
    cb_d = nc.dram_tensor("cb", [128, NHC], F32, kind="ExternalInput").ap()
    w2_d = nc.dram_tensor("w2", [E, HC], F32, kind="ExternalInput").ap()
    out_d = nc.dram_tensor("out", [S, E], F32, kind="ExternalOutput").ap()

    qT_d = nc.dram_tensor("qT_s", [E, S], BF16, kind="Internal").ap()
    kT_d = nc.dram_tensor("kT_s", [E, S], BF16, kind="Internal").ap()
    o_d = nc.dram_tensor("o_s", [S, E], F32,
                         kind="ExternalOutput" if DEBUG == 1 else "Internal").ap()
    x2_d = nc.dram_tensor("x2_s", [S, E], F32,
                          kind="ExternalOutput" if DEBUG == 2 else "Internal").ap()
    y_d = nc.dram_tensor("y_s", [S, E], F32,
                         kind="ExternalOutput" if DEBUG == 3 else "Internal").ap()
    w2qT_d = nc.dram_tensor("w2qT_s", [HC, E], BF16, kind="Internal").ap()
    v_sd = nc.dram_tensor("v_sd", [S, E], BF16, kind="Internal").ap()
    rx_d = nc.dram_tensor("rx_d", [NT, 128], F32, kind="Internal").ap()
    mx_d = nc.dram_tensor("mx_d", [2, 128], F32, kind="Internal").ap()
    xqT_d = nc.dram_tensor("xqT_s", [E, S], BF16, kind="Internal").ap()

    sc = Sched()
    root = ExitStack()
    sems = {}
    stream_sems = ["gax", "toe", "cwl", "qk", "vl", "vs",
                   "sq0", "sq1", "so0", "so1", "wl0", "wl1", "sw",
                   "lc0", "lc1", "lc2", "lc3", "sx2", "sxq0", "sxq1", "srx",
                   "lq0", "lq1", "lw20", "lw21", "lr", "sm", "lm",
                   "sy0", "sy1", "le0", "le1", "le2", "le3", "se0", "se1"]
    for s in ["cT", "cV", "cS", "cG", "bar"] + stream_sems:
        sems[s] = root.enter_context(nc.semaphore(s))
    sc.streams = []

    def mkstream(eng, names):
        st = Stream(sc, eng, sems, names)
        sc.streams.append(st)
        return st

    st_gax = mkstream("gpsimd", ["gax"])
    st_toe = mkstream("gpsimd", ["toe"])
    st_cwl = mkstream("sync", ["cwl"])
    st_qk = mkstream("sync", ["qk"])
    st_vl = mkstream("sync", ["vl"])
    st_vs = mkstream("scalar", ["vs"])
    st_sq = mkstream("scalar", ["sq0", "sq1"])
    st_so = mkstream("scalar", ["so0", "so1"])
    st_wl = mkstream("sync", ["wl0", "wl1"])
    st_sw = mkstream("scalar", ["sw"])
    st_lc = mkstream("sync", ["lc0", "lc1", "lc2", "lc3"])
    st_sx2 = mkstream("scalar", ["sx2"])
    st_sxq = mkstream("scalar", ["sxq0", "sxq1"])
    st_srx = mkstream("scalar", ["srx"])
    st_lq = mkstream("sync", ["lq0", "lq1"])
    st_lw2 = mkstream("sync", ["lw20", "lw21"])
    st_lr = mkstream("sync", ["lr"])
    st_sm = mkstream("scalar", ["sm"])
    st_lm = mkstream("sync", ["lm"])
    st_sy = mkstream("scalar", ["sy0", "sy1"])
    st_le = mkstream("sync", ["le0", "le1", "le2", "le3"])
    st_se = mkstream("sync", ["se0", "se1"])

    def sb(stack, name, shape, dt=F32):
        return stack.enter_context(nc.sbuf_tensor(name, shape, dt))

    ident_b = sb(root, "ident_b", [128, 128], BF16)
    ident_f = sb(root, "ident_f", [128, 128], F32)
    onesc_f = sb(root, "onesc_f", [1, 128], F32)
    ones65 = sb(root, "ones65", [65, 128], F32)
    cst = sb(root, "cst", [128, 4], F32)
    rx1 = sb(root, "rx1", [128, NT], F32)
    cw_sb = sb(root, "cw_sb", [128, 3 * NHC], F32)
    cb_sb = sb(root, "cb_sb", [128, NHC], F32)
    w1sc = sb(root, "w1sc", [128, 2], F32)
    rxrow_s = sb(root, "rxrow_s", [16, 128], F32)
    w2sc = sb(root, "w2sc", [128, 2], F32)
    esAB = ExitStack()
    v_pad = sb(esAB, "v_pad", [128, NT * H * 65], BF16)
    v_km = sb(esAB, "v_km", [128, 16 * E], BF16)

    esA = ExitStack()
    x_bf = sb(esA, "x_bf", [128, NT * E], BF16)
    xT = sb(esA, "xT", [128, NDT * S], BF16)
    wq_sb = sb(esA, "wq_sb", [128, NDT * E], BF16)
    wk_sb = sb(esA, "wk_sb", [128, NDT * E], BF16)
    wv_sb = sb(esA, "wv_sb", [128, NDT * E], BF16)
    stgA = sb(esA, "stgA", [128, 2 * 512], BF16)
    pA = [esA.enter_context(nc.psum_tensor(f"pA{i}", [128, 512], F32)) for i in range(6)]
    pAb = [esA.enter_context(nc.psum_tensor(f"pAb{i}", [128, 1024], BF16)) for i in range(2)]

    sc.op("gpsimd", lambda h: h.memset(cst[:, 0:1], 0.03125))
    sc.op("gpsimd", lambda h: h.memset(cst[:, 1:2], -0.5))
    sc.op("gpsimd", lambda h: h.memset(onesc_f[:], 1.0))
    sc.op("gpsimd", lambda h: h.memset(ones65[:], 1.0))
    sc.op("gpsimd", lambda h: h.memset(ident_b[:], 0.0))
    sc.op("gpsimd", lambda h: h.memset(ident_f[:], 0.0))
    sc.drain("gpsimd")
    sc.op("gpsimd", lambda h: h.affine_select(out=ident_b[:], in_=ident_b[:],
          compare_op=OP.not_equal, fill=1.0, base=0, pattern=[[-1, 128]],
          channel_multiplier=1))
    sc.op("gpsimd", lambda h: h.affine_select(out=ident_f[:], in_=ident_f[:],
          compare_op=OP.not_equal, fill=1.0, base=0, pattern=[[-1, 128]],
          channel_multiplier=1), inc="cG")
    vp4 = v_pad[:].rearrange("p (t h e) -> p t h e", t=NT, h=H, e=65)
    sc.op("gpsimd", lambda h: h.memset(vp4[:, :, :, 64:65], 1.0), inc="cG")

    st_gax.dma(lambda h: h.dma_start(
        out=x_bf[:].rearrange("p (t d) -> p t d", d=E),
        in_=x_d.rearrange("(t p) d -> p t d", p=128)))
    for wd, wsb in ((wq_d, wq_sb), (wk_d, wk_sb), (wv_d, wv_sb)):
        st_gax.dma(lambda h, wd=wd, wsb=wsb: h.dma_start(
            out=wsb[:].rearrange("p (dt e) -> p dt e", e=E),
            in_=wd.rearrange("(dt p) e -> p dt e", p=128)))
    dG_A = st_gax.wait_all()

    st_cwl.dma(lambda h: h.dma_start(out=cw_sb[:], in_=cw_d))
    st_cwl.dma(lambda h: h.dma_start(out=cb_sb[:], in_=cb_d))

    bksA = Banks(pAb)
    for t in range(NT):
        for dt in range(NDT):
            bk, freev, ki = bksA.get()
            sc.op("tensor", lambda h, bk=bk, t=t, dt=dt: h.transpose(
                bk[:, 0:128], x_bf[:, t * E + 128 * dt: t * E + 128 * (dt + 1)],
                ident_b[:]), waits=dG_A + [("cG", 1), ("cV", freev)], inc="cT")
            ev = sc.op("vector", lambda h, bk=bk, t=t, dt=dt: h.tensor_copy(
                xT[:, dt * S + 128 * t: dt * S + 128 * (t + 1)], bk[:, 0:128]),
                waits=[("cT", sc.v("cT"))], inc="cV")
            bksA.set_free(ki, ev)
    cV_xT = sc.v("cV")

    bksQ = Banks(pA[0:3])
    nstg = 0
    stg_dmas = []
    for wsb, dst in ((wq_sb, qT_d), (wk_sb, kT_d)):
        for j in range(NDT):
            for nch in range(4):
                bk, freev, ki = bksQ.get()
                for dt in range(NDT):
                    waits = [("cV", cV_xT)] + ([("cV", freev)] if dt == 0 else [])
                    sc.op("tensor", lambda h, bk=bk, wsb=wsb, j=j, nch=nch, dt=dt:
                          h.matmul(bk[:, 0:512],
                                   wsb[:, dt * E + 128 * j: dt * E + 128 * (j + 1)],
                                   xT[:, dt * S + 512 * nch: dt * S + 512 * (nch + 1)],
                                   start=(dt == 0), stop=(dt == NDT - 1)),
                          waits=waits, inc="cT" if dt == NDT - 1 else None)
                stx = stgA[:, (nstg % 2) * 512:(nstg % 2) * 512 + 512]
                wv2 = [("cT", sc.v("cT"))] + st_sq.wait_for(nstg - 2)
                ev = sc.op("vector", lambda h, stx=stx, bk=bk: h.tensor_copy(
                    stx, bk[:, 0:512]), waits=wv2, inc="cV")
                bksQ.set_free(ki, ev)
                st_sq.dma(lambda h, stx=stx, dst=dst, j=j, nch=nch:
                          h.dma_start(out=dst[128 * j:128 * (j + 1),
                                              512 * nch:512 * (nch + 1)], in_=stx),
                          waits=[("cV", ev)])
                nstg += 1
    v_nat = x_bf
    bksV = Banks(pA[3:6], free0=sc.v("cV"))
    for t in range(NT):
        for nch in range(2):
            bk, freev, ki = bksV.get()
            for dt in range(NDT):
                waits = [("cV", cV_xT)] + ([("cV", freev)] if dt == 0 else [])
                sc.op("tensor", lambda h, bk=bk, t=t, nch=nch, dt=dt:
                      h.matmul(bk[:, 0:512],
                               xT[:, dt * S + 128 * t: dt * S + 128 * (t + 1)],
                               wv_sb[:, dt * E + 512 * nch: dt * E + 512 * (nch + 1)],
                               start=(dt == 0), stop=(dt == NDT - 1)),
                      waits=waits, inc="cT" if dt == NDT - 1 else None)
            ev = sc.op("vector", lambda h, t=t, nch=nch, bk=bk: h.tensor_copy(
                v_nat[:, t * E + 512 * nch: t * E + 512 * (nch + 1)], bk[:, 0:512]),
                waits=[("cT", sc.v("cT"))], inc="cV")
            bksV.set_free(ki, ev)
    cV_vp = sc.v("cV")
    st_vs.dma(lambda h: h.dma_start(
        out=v_sd.rearrange("(t p) e -> p t e", p=128),
        in_=v_nat[:].rearrange("p (t e) -> p t e", e=E)),
        waits=[("cV", cV_vp)])
    dvs = st_vs.wait_all()
    for t in range(NT):
        st_vl.dma(lambda h, t=t: h.dma_start(
            out=vp4[:, t, :, 0:64],
            in_=v_sd[128 * t:128 * (t + 1), :].rearrange(
                "p (hh e) -> p hh e", e=64)),
            waits=dvs)
    st_vl.dma(lambda h: h.dma_start(
        out=v_km[:].rearrange("p (k c) -> p k c", k=16),
        in_=v_sd.rearrange("(p k) c -> p k c", k=16)))
    sc.barrier()
    esA.close()

    esB = ExitStack()
    toe = sb(esB, "toe_sb", [128, 16 * NW * 128], BF16)
    qkp = sb(esB, "qkp", [128, 2 * S], BF16)
    p_bf = sb(esB, "p_bf", [128, 2 * 2048], BF16)
    pvA_sb = sb(esB, "pvA_sb", [65, 512], F32)
    pvB_sb = sb(esB, "pvB_sb", [65, 512], F32)
    bias_sb = sb(esB, "bias_sb", [128, 512], F32)
    rcp = sb(esB, "rcp", [128, 2], F32)
    tb_sb = sb(esB, "tb_sb", [128, 128], F32)
    o_stage = sb(esB, "o_stage", [128, 1024], F32)
    sAB = esB.enter_context(nc.psum_tensor("sAB", [128, 2048], F32))
    pvA = esB.enter_context(nc.psum_tensor("pvA", [128, 512], F32))
    pvB = esB.enter_context(nc.psum_tensor("pvB", [128, 512], F32))
    pb = [esB.enter_context(nc.psum_tensor(f"pb{i}", [128, 512], F32)) for i in range(2)]

    toev = toe[:].rearrange("p (k w a) -> p k w a", k=16, w=NW, a=128)

    def toe_load(dlt, wait_ct):
        slot = (dlt + 15) % NW
        w = [("cT", wait_ct)] if wait_ct > 0 else []
        st_toe.dma(lambda h, dlt=dlt, slot=slot: h.dma_start(
            out=toev[:, :, slot, :],
            in_=toe_d[:, dlt + 15].rearrange("k p a -> p k a")),
            waits=w)

    for dlt in range(-1, 16):
        toe_load(dlt, 0)
    dG_need = st_toe.wait_all()

    st_qk.dma(lambda h: h.dma_start(out=qkp[:, 0:S], in_=qT_d[0:128, :]))
    st_qk.dma(lambda h: h.dma_start(out=qkp[:, S:2 * S], in_=kT_d[0:128, :]))

    exp_done = 0
    o_dmas = []
    pv_free = 0
    tr_free = 0
    bias_bank_free = [0, 0]
    for j in range(NP):
        hA, hB = 2 * j, 2 * j + 1
        dH_pair = st_qk.wait_all()
        qperm = qkp[:, 0:S].rearrange("p (a c) -> p c a", a=128, c=16)
        for qc in range(4):
            c0 = 4 * qc
            slA0 = ((c0 - hA) + 15) % NW
            slB0 = ((c0 - hB) + 15) % NW
            fill = []
            for kap in range(16):
                w0 = []
                if kap == 0:
                    w0 = list(dG_need) + [("cV", bias_bank_free[0]),
                                          ("cV", bias_bank_free[1])]
                fill.append((lambda h, kap=kap, slA0=slA0, j=j: h.matmul(
                    pb[0][0:64, 0:128],
                    v_km[:, kap * E + 128 * j: kap * E + 128 * j + 64],
                    toe[:, kap * (NW * 128) + slA0 * 128:
                        kap * (NW * 128) + (slA0 + 1) * 128],
                    start=(kap == 0), stop=(kap == 15), tile_position=(0, 0)), w0))
                fill.append((lambda h, kap=kap, slB0=slB0, j=j: h.matmul(
                    pb[1][64:128, 0:128],
                    v_km[:, kap * E + 128 * j + 64: kap * E + 128 * (j + 1)],
                    toe[:, kap * (NW * 128) + slB0 * 128:
                        kap * (NW * 128) + (slB0 + 1) * 128],
                    start=(kap == 0), stop=(kap == 15), tile_position=(0, 64)), ()))
            fi = 0
            for g in range(8):
                u0, u1 = 2 * g, 2 * g + 1
                wqk = list(dH_pair)
                if exp_done:
                    wqk.append(("cS", exp_done))
                for ui, u in enumerate((u0, u1)):
                    off = ui * 1024
                    sc.op("tensor", lambda h, u=u, qc=qc, off=off, qperm=qperm:
                          h.matmul(sAB[:, off:off + 512],
                                   qkp[0:64, S + 128 * u: S + 128 * (u + 1)],
                                   qperm[0:64, 4 * qc:4 * qc + 4, :],
                                   start=True, stop=True),
                          waits=wqk if ui == 0 else ())
                    sc.op("tensor", lambda h, u=u, qc=qc, off=off, qperm=qperm:
                          h.matmul(sAB[:, off + 512:off + 1024],
                                   qkp[64:128, S + 128 * u: S + 128 * (u + 1)],
                                   qperm[64:128, 4 * qc:4 * qc + 4, :],
                                   start=True, stop=True),
                          inc="cT" if ui == 1 else None)
                qk_ct = sc.v("cT")
                if g > 0:
                    gp = (g - 1) % 2
                    for ui, u in enumerate((2 * g - 2, 2 * g - 1)):
                        wpv = []
                        if g == 1 and ui == 0 and pv_free:
                            wpv = [("cV", pv_free)]
                        sc.op("tensor", lambda h, u=u, hA=hA, gp=gp, ui=ui: h.matmul(
                            pvA[0:65, 0:512],
                            v_pad[:, u * (H * 65) + hA * 65: u * (H * 65) + hA * 65 + 65],
                            p_bf[:, gp * 2048 + ui * 1024: gp * 2048 + ui * 1024 + 512],
                            start=(u == 0), stop=False), waits=wpv)
                        sc.op("tensor", lambda h, u=u, hB=hB, gp=gp, ui=ui: h.matmul(
                            pvB[0:65, 0:512],
                            v_pad[:, u * (H * 65) + hB * 65: u * (H * 65) + hB * 65 + 65],
                            p_bf[:, gp * 2048 + ui * 1024 + 512:
                                 gp * 2048 + ui * 1024 + 1024],
                            start=(u == 0), stop=False))
                while fi < len(fill) and fi < 4 * (g + 1):
                    fn, w0 = fill[fi]
                    sc.op("tensor", fn, waits=w0,
                          inc="cT" if fi == len(fill) - 1 else None)
                    fi += 1
                exp_done = sc.op("scalar", lambda h, g=g: h.activation(
                    out=p_bf[:, (g % 2) * 2048:(g % 2) * 2048 + 2048],
                    in_=sAB[:], func=AF.Exp, scale=cst[:, 0:1]),
                    waits=[("cT", qk_ct), ("cG", 1)], inc="cS")
            while fi < len(fill):
                fn, w0 = fill[fi]
                sc.op("tensor", fn, waits=w0,
                      inc="cT" if fi == len(fill) - 1 else None)
                fi += 1
            bias_c0_ct = sc.v("cT")
            gp = 1
            for ui, u in enumerate((14, 15)):
                sc.op("tensor", lambda h, u=u, hA=hA, ui=ui: h.matmul(
                    pvA[0:65, 0:512],
                    v_pad[:, u * (H * 65) + hA * 65: u * (H * 65) + hA * 65 + 65],
                    p_bf[:, gp * 2048 + ui * 1024: gp * 2048 + ui * 1024 + 512],
                    start=False, stop=(u == 15)),
                    waits=[("cS", exp_done)])
                sc.op("tensor", lambda h, u=u, hB=hB, ui=ui: h.matmul(
                    pvB[0:65, 0:512],
                    v_pad[:, u * (H * 65) + hB * 65: u * (H * 65) + hB * 65 + 65],
                    p_bf[:, gp * 2048 + ui * 1024 + 512: gp * 2048 + ui * 1024 + 1024],
                    start=False, stop=(u == 15)),
                    inc="cT" if u == 15 else None)
            pv_done = sc.v("cT")
            sc.op("vector", lambda h: h.tensor_copy(pvA_sb[:], pvA[0:65, 0:512]),
                  waits=[("cT", pv_done)])
            pv_ev = sc.op("vector", lambda h: h.tensor_copy(
                pvB_sb[:], pvB[0:65, 0:512]), inc="cV")
            bias_ev = [0, 0, 0, 0]
            sc.op("vector", lambda h: h.tensor_copy(
                bias_sb[0:64, 0:128], pb[0][0:64, 0:128]),
                waits=[("cT", bias_c0_ct)])
            bias_ev[0] = sc.op("vector", lambda h: h.tensor_copy(
                bias_sb[64:128, 0:128], pb[1][64:128, 0:128]), inc="cV")
            for ci in range(1, 4):
                c = 4 * qc + ci
                slA = ((c - hA) + 15) % NW
                slB = ((c - hB) + 15) % NW
                for kap in range(16):
                    w0 = [("cV", bias_ev[ci - 1])] if kap == 0 else []
                    sc.op("tensor", lambda h, kap=kap, slA=slA, j=j: h.matmul(
                        pb[0][0:64, 0:128],
                        v_km[:, kap * E + 128 * j: kap * E + 128 * j + 64],
                        toe[:, kap * (NW * 128) + slA * 128:
                            kap * (NW * 128) + (slA + 1) * 128],
                        start=(kap == 0), stop=(kap == 15), tile_position=(0, 0)),
                        waits=w0)
                    sc.op("tensor", lambda h, kap=kap, slB=slB, j=j: h.matmul(
                        pb[1][64:128, 0:128],
                        v_km[:, kap * E + 128 * j + 64: kap * E + 128 * (j + 1)],
                        toe[:, kap * (NW * 128) + slB * 128:
                            kap * (NW * 128) + (slB + 1) * 128],
                        start=(kap == 0), stop=(kap == 15), tile_position=(0, 64)),
                        inc="cT" if kap == 15 else None)
                bct = sc.v("cT")
                sc.op("vector", lambda h, ci=ci: h.tensor_copy(
                    bias_sb[0:64, 128 * ci:128 * (ci + 1)], pb[0][0:64, 0:128]),
                    waits=[("cT", bct)])
                bias_ev[ci] = sc.op("vector", lambda h, ci=ci: h.tensor_copy(
                    bias_sb[64:128, 128 * ci:128 * (ci + 1)], pb[1][64:128, 0:128]),
                    inc="cV")
            bias_bank_free = [bias_ev[3], bias_ev[3]]
            half = (qc % 2) * 512
            last_stt = 0
            for ci in range(4):
                trbank = pb[ci % 2]
                wtr = [("cV", pv_ev)]
                if ci == 0:
                    wtr.append(("cV", bias_ev[3]))
                if ci >= 1:
                    wtr.append(("cV", last_stt))
                trA_ct = sc.op("tensor", lambda h, ci=ci: h.transpose(
                    pvA[:, 0:65], pvA_sb[0:65, 128 * ci:128 * (ci + 1)],
                    ident_f[0:65, 0:65]), waits=wtr, inc="cT")
                trB_ct = sc.op("tensor", lambda h, ci=ci: h.transpose(
                    pvB[:, 0:65], pvB_sb[0:65, 128 * ci:128 * (ci + 1)],
                    ident_f[0:65, 0:65]), inc="cT")
                trX_ct = sc.op("tensor", lambda h, ci=ci, trbank=trbank: h.transpose(
                    trbank[:, 0:128], bias_sb[:, 128 * ci:128 * (ci + 1)],
                    ident_f[:]), inc="cT")
                sc.op("vector", lambda h: h.reciprocal(rcp[:, 0:1], pvA[:, 64:65]),
                      waits=[("cT", trX_ct)])
                sc.drain("vector")
                sc.op("vector", lambda h: h.reciprocal(rcp[:, 1:2], pvB[:, 64:65]))
                sc.drain("vector")
                sc.op("vector", lambda h, trbank=trbank: h.tensor_copy(
                    tb_sb[:], trbank[:, 0:128]), inc="cV")
                sc.drain("vector")
                wst = st_so.wait_for(len(o_dmas) - 2)
                sc.op("vector", lambda h, ci=ci, half=half: h.scalar_tensor_tensor(
                    out=o_stage[:, half + 128 * ci: half + 128 * ci + 64],
                    in0=pvA[:, 0:64], scalar=rcp[:, 0:1], in1=tb_sb[:, 0:64],
                    op0=OP.mult, op1=OP.add), waits=wst)
                last_stt = sc.op("vector", lambda h, ci=ci, half=half:
                                 h.scalar_tensor_tensor(
                    out=o_stage[:, half + 128 * ci + 64: half + 128 * (ci + 1)],
                    in0=pvB[:, 0:64], scalar=rcp[:, 1:2], in1=tb_sb[:, 64:128],
                    op0=OP.mult, op1=OP.add), inc="cV")
            pv_free = last_stt
            tr_free = last_stt
            bias_bank_free = [last_stt, last_stt]
            dsto = o_d[:, 128 * j:128 * (j + 1)] \
                .rearrange("(a c) e -> a c e", a=128, c=16)[:, 4 * qc:4 * qc + 4, :]
            srco = o_stage[:, half:half + 512].rearrange("a (c e) -> a c e", c=4, e=128)
            st_so.dma(lambda h, dsto=dsto, srco=srco: h.dma_start(
                out=dsto, in_=srco), waits=[("cV", last_stt)])
            o_dmas.append(st_so.n)
        if j + 1 < NP:
            st_qk.dma(lambda h, j=j: h.dma_start(
                out=qkp[:, 0:S], in_=qT_d[128 * (j + 1):128 * (j + 2), :]),
                waits=[("cT", sc.v("cT"))])
            st_qk.dma(lambda h, j=j: h.dma_start(
                out=qkp[:, S:2 * S], in_=kT_d[128 * (j + 1):128 * (j + 2), :]))
            toe_load(-2 * (j + 1), sc.v("cT"))
            toe_load(-2 * (j + 1) - 1, sc.v("cT"))
            dG_need = st_toe.wait_all()
    sc.barrier()
    esB.close()
    esAB.close()

    esCD = ExitStack()
    w1qT = sb(esCD, "w1qT", [128, NDT * HC], BF16)
    esC = ExitStack()
    gb_sb = sb(esC, "gb_sb", [128, E], F32)
    bb_sb = sb(esC, "bb_sb", [128, E], F32)
    wt = sb(esC, "wt", [128, 2 * 2048], F32)
    mp_t = sb(esC, "mp_t", [128, 2048], F32)
    mm_t = sb(esC, "mm_t", [128, 2048], F32)
    wqn = sb(esC, "wqn", [128, 2048], BF16)
    stgW = sb(esC, "stgW", [128, HC], BF16)
    xa = sb(esC, "xa", [128, 2 * E], F32)
    ob = sb(esC, "ob", [128, 2 * E], F32)
    r1 = sb(esC, "r1", [128, E], F32)
    t1s = sb(esC, "t1s", [128, E], F32)
    frs = mm_t[:, 0:E]
    uq = esC.enter_context(nc.sbuf_tensor("uq", [128, E], I32))
    bns = mp_t[:, 0:E]
    xq_t = wqn[:, 0:E]
    stC = stgW
    stat = sb(esC, "stat", [128, 12], F32)
    wsum = sb(esC, "wsum", [128, NHC], F32)
    pC = [esC.enter_context(nc.psum_tensor(f"pC{i}", [128, 512], F32)) for i in range(4)]
    pCb = [esC.enter_context(nc.psum_tensor(f"pCb{i}", [128, 1024], BF16)) for i in range(4)]

    st_cwl.dma(lambda h: h.dma_start(out=gb_sb[0:1, :], in_=gam_d))
    st_cwl.dma(lambda h: h.dma_start(out=bb_sb[0:1, :], in_=bet_d))
    dH_gb = st_cwl.wait_all()
    for i, bc in enumerate((gb_sb, bb_sb)):
        for nch in range(2):
            ct = sc.op("tensor", lambda h, bc=bc, i=i, nch=nch: h.matmul(
                pC[2 * i + nch][:, 0:512], onesc_f[0:1, :],
                bc[0:1, 512 * nch:512 * (nch + 1)], start=True, stop=True),
                waits=dH_gb, inc="cT")
            sc.op("vector", lambda h, bc=bc, i=i, nch=nch: h.tensor_copy(
                bc[:, 512 * nch:512 * (nch + 1)], pC[2 * i + nch][:, 0:512]),
                waits=[("cT", ct)], inc="cV")
    gb_ready = sc.v("cV")

    def wquant(wi, wd, wscb):
        ntile = NHC if wi == 0 else NDT
        width = E if wi == 0 else HC
        npc = width // 2048 if width > 2048 else 1
        pw = width // npc
        cons = [sc.v("cV")] * 2
        ip = 0
        for tt in range(ntile):
            for pc in range(npc):
                par = ip % 2
                ip += 1
                buf = wt[:, par * 2048:par * 2048 + pw]
                i = st_wl.dma(lambda h, buf=buf, wd=wd, tt=tt, pc=pc, pw=pw:
                              h.dma_start(out=buf,
                                          in_=wd[128 * tt:128 * (tt + 1),
                                                 pw * pc:pw * (pc + 1)]),
                              waits=[("cV", cons[par])])
                cons[par] = sc.op("vector", lambda h, buf=buf, tt=tt, pc=pc, npc=npc:
                             h.reduce_sum(wsum[:, tt * npc + pc:tt * npc + pc + 1]
                                          if npc > 1 else wsum[:, tt:tt + 1],
                                          buf, axis=AX, apply_absolute_value=True),
                             waits=st_wl.wait_for(i), inc="cV")
        sc.drain("vector")
        sc.op("vector", lambda h, n=ntile * npc: h.reduce_sum(
            stat[:, 0:1], wsum[:, 0:n], axis=AX), inc="cV")
        ct = sc.op("tensor", lambda h: h.transpose(
            pC[0][0:1, 0:128], stat[:, 0:1], ident_f[0:128, 0:128]),
            waits=[("cV", sc.v("cV")), ("cV", gb_ready)], inc="cT")
        sc.op("vector", lambda h: h.reduce_sum(
            stat[0:1, 1:2], pC[0][0:1, 0:128], axis=AX), waits=[("cT", ct)])
        sc.drain("vector")
        sc.op("vector", lambda h: h.tensor_scalar(
            out=stat[0:1, 2:3], in0=stat[0:1, 1:2], scalar1=1.0 / (HC * E),
            scalar2=1e-5, op0=OP.mult, op1=OP.max))
        sc.drain("vector")
        cv = sc.op("vector", lambda h: h.reciprocal(stat[0:1, 3:4], stat[0:1, 2:3]),
                   inc="cV")
        ct = sc.op("tensor", lambda h: h.matmul(
            pC[1][:, 0:2], onesc_f[0:1, :], stat[0:1, 2:4], start=True, stop=True),
            waits=[("cV", cv)], inc="cT")
        scd = sc.op("vector", lambda h, wscb=wscb: h.tensor_copy(
            wscb[:], pC[1][:, 0:2]), waits=[("cT", ct)], inc="cV")
        bksW = Banks(pCb, free0=sc.v("cV"))
        stg_done = 0
        cons = [scd] * 2
        ip = 0
        wqn_free = 0
        for tt in range(ntile):
            for pc in range(npc):
                par = ip % 2
                ip += 1
                buf = wt[:, par * 2048:par * 2048 + pw]
                i = st_wl.dma(lambda h, buf=buf, wd=wd, tt=tt, pc=pc, pw=pw:
                              h.dma_start(out=buf,
                                          in_=wd[128 * tt:128 * (tt + 1),
                                                 pw * pc:pw * (pc + 1)]),
                              waits=[("cV", cons[par])])
                sc.op("vector", lambda h, buf=buf, pw=pw, wscb=wscb: h.tensor_scalar(
                    out=mp_t[:, 0:pw], in0=buf, scalar1=wscb[:, 1:2], scalar2=0.5,
                    op0=OP.mult, op1=OP.is_ge),
                    waits=st_wl.wait_for(i) + [("cT", wqn_free)])
                cons[par] = sc.op("vector", lambda h, buf=buf, pw=pw, wscb=wscb:
                             h.tensor_scalar(
                    out=mm_t[:, 0:pw], in0=buf, scalar1=wscb[:, 1:2], scalar2=-0.5,
                    op0=OP.mult, op1=OP.is_le), inc="cV")
                sc.drain("vector")
                qdone = sc.op("vector", lambda h, pw=pw: h.tensor_sub(
                    wqn[:, 0:pw], mp_t[:, 0:pw], mm_t[:, 0:pw]), inc="cV")
                nb = pw // 128
                for b in range(nb):
                    bg = pc * nb + b
                    bk, freev, ki = bksW.get()
                    ct = sc.op("tensor", lambda h, bk=bk, b=b: h.transpose(
                        bk[:, 0:128], wqn[:, 128 * b:128 * (b + 1)], ident_b[:]),
                        waits=[("cV", qdone), ("cV", freev)], inc="cT")
                    if wi == 0:
                        ev = sc.op("vector", lambda h, bk=bk, tt=tt, bg=bg:
                                   h.tensor_copy(
                            w1qT[:, bg * HC + 128 * tt: bg * HC + 128 * (tt + 1)],
                            bk[:, 0:128]), waits=[("cT", ct)], inc="cV")
                    else:
                        wv = [("cT", ct)] + st_sw.wait_for(st_sw.n - 1)
                        ev = sc.op("vector", lambda h, bk=bk, bg=bg: h.tensor_copy(
                            stgW[:, 128 * bg:128 * (bg + 1)], bk[:, 0:128]),
                            waits=wv, inc="cV")
                    bksW.set_free(ki, ev)
                wqn_free = sc.v("cT")
            if wi == 1:
                dstw = w2qT_d[:, 128 * tt:128 * (tt + 1)] \
                    .rearrange("(b p) e -> p b e", p=128)
                srcw = stgW[:].rearrange("p (b e) -> p b e", e=128)
                st_sw.dma(lambda h, dstw=dstw, srcw=srcw:
                          h.dma_start(out=dstw, in_=srcw),
                          waits=[("cV", sc.v("cV"))])

    wquant(0, w1_d, w1sc)
    wquant(1, w2_d, w2sc)
    wq_end_ct = sc.v("cT")
    wq_end_ds = st_sw.wait_all()

    bksX = Banks(pCb, free0=sc.v("cV"))
    xq_dmas = []
    for t in range(NT):
        par = t % 2
        xat = xa[:, par * E:(par + 1) * E]
        obt = ob[:, par * E:(par + 1) * E]
        w = [("cV", sc.v("cV") if t >= 2 else 0)]
        i1 = st_lc.dma(lambda h, xat=xat, t=t: h.dma_start(
            out=xat, in_=x_d[128 * t:128 * (t + 1), :]), waits=w)
        i2 = st_lc.dma(lambda h, obt=obt, t=t: h.dma_start(
            out=obt, in_=o_d[128 * t:128 * (t + 1), :]))
        sc.op("vector", lambda h, obt=obt: h.reduce_sum(
            stat[:, 0:1], obt, axis=AX),
            waits=st_lc.wait_for(i2) + [("cV", gb_ready)])
        sc.drain("vector")
        sc.op("vector", lambda h: h.tensor_scalar(
            out=stat[:, 1:2], in0=stat[:, 0:1], scalar1=-1.0 / E, scalar2=None,
            op0=OP.mult), inc="cV")
        mu_done = sc.v("cV")
        sq_done = sc.op("scalar", lambda h, obt=obt: h.activation(
            out=t1s[:], in_=obt, func=AF.Square, bias=stat[:, 1:2],
            accum_out=stat[:, 2:3]), waits=[("cV", mu_done)], inc="cS")
        sc.op("vector", lambda h: h.tensor_scalar(
            out=stat[:, 3:4], in0=stat[:, 2:3], scalar1=1.0 / E, scalar2=1e-3,
            op0=OP.mult, op1=OP.add), waits=[("cS", sq_done)], inc="cV")
        ln_done = sc.op("scalar", lambda h: h.activation(
            out=stat[:, 4:5], in_=stat[:, 3:4], func=AF.Ln),
            waits=[("cV", sc.v("cV"))], inc="cS")
        rstd_done = sc.op("scalar", lambda h: h.activation(
            out=stat[:, 5:6], in_=stat[:, 4:5], func=AF.Exp, scale=cst[:, 1:2]),
            waits=[("cS", ln_done)], inc="cS")
        sc.op("vector", lambda h, obt=obt: h.tensor_scalar(
            out=t1s[:], in0=obt, scalar1=stat[:, 1:2], scalar2=stat[:, 5:6],
            op0=OP.add, op1=OP.mult), waits=[("cS", rstd_done)])
        sc.drain("vector")
        sc.op("vector", lambda h: h.tensor_mul(t1s[:], t1s[:], gb_sb[:]))
        sc.drain("vector")
        sc.op("vector", lambda h: h.tensor_add(t1s[:], t1s[:], bb_sb[:]))
        sc.drain("vector")
        sc.op("vector", lambda h, xat=xat: h.tensor_add(r1[:], t1s[:], xat),
              waits=st_lc.wait_for(i1), inc="cV")
        sc.drain("vector")
        sc.op("vector", lambda h: h.reduce_sum(stat[:, 6:7], r1[:], axis=AX))
        sc.drain("vector")
        sc.op("vector", lambda h: h.tensor_scalar(
            out=stat[:, 7:8], in0=stat[:, 6:7], scalar1=-1.0 / E, scalar2=None,
            op0=OP.mult), inc="cV")
        sq2 = sc.op("scalar", lambda h: h.activation(
            out=t1s[:], in_=r1[:], func=AF.Square, bias=stat[:, 7:8],
            accum_out=stat[:, 8:9]), waits=[("cV", sc.v("cV"))], inc="cS")
        sc.op("vector", lambda h: h.tensor_scalar(
            out=stat[:, 9:10], in0=stat[:, 8:9], scalar1=1.0 / E, scalar2=1e-5,
            op0=OP.mult, op1=OP.add), waits=[("cS", sq2)], inc="cV")
        ln2 = sc.op("scalar", lambda h: h.activation(
            out=stat[:, 10:11], in_=stat[:, 9:10], func=AF.Ln),
            waits=[("cV", sc.v("cV"))], inc="cS")
        rstd2 = sc.op("scalar", lambda h: h.activation(
            out=stat[:, 11:12], in_=stat[:, 10:11], func=AF.Exp, scale=cst[:, 1:2]),
            waits=[("cS", ln2)], inc="cS")
        x2done = sc.op("vector", lambda h: h.tensor_scalar(
            out=t1s[:], in0=r1[:], scalar1=stat[:, 7:8], scalar2=stat[:, 11:12],
            op0=OP.add, op1=OP.mult), waits=[("cS", rstd2)], inc="cV")
        st_sx2.dma(lambda h, t=t: h.dma_start(
            out=x2_d[128 * t:128 * (t + 1), :], in_=t1s[:]),
            waits=[("cV", x2done)])
        sc.drain("vector")
        sc.op("vector", lambda h: h.tensor_reduce(
            out=stat[:, 0:1], in_=t1s[:], axis=AX, op=OP.max,
            apply_absolute_value=True))
        sc.drain("vector")
        sc.op("vector", lambda h: h.tensor_scalar(
            out=stat[:, 1:2], in0=stat[:, 0:1], scalar1=1e-5, scalar2=None,
            op0=OP.max))
        sc.drain("vector")
        sc.op("vector", lambda h, t=t: h.tensor_scalar(
            out=rx1[:, t:t + 1], in0=stat[:, 1:2], scalar1=1.0 / 127.0, scalar2=None,
            op0=OP.mult))
        sc.op("vector", lambda h: h.reciprocal(stat[:, 2:3], stat[:, 1:2]))
        sc.drain("vector")
        sc.op("vector", lambda h: h.tensor_scalar(
            out=stat[:, 3:4], in0=stat[:, 2:3], scalar1=127.0, scalar2=None,
            op0=OP.mult), inc="cV")
        sc.op("vector", lambda h: h.tensor_scalar(
            out=t1s[:], in0=t1s[:], scalar1=stat[:, 3:4], scalar2=128.5,
            op0=OP.mult, op1=OP.add),
            waits=[("cV", sc.v("cV"))] + st_sx2.wait_for(t))
        sc.drain("vector")
        sc.op("vector", lambda h: h.tensor_copy(uq[:], t1s[:]))
        sc.drain("vector")
        sc.op("vector", lambda h: h.tensor_sub(frs, t1s[:], uq[:]))
        sc.drain("vector")
        for thr in (2.0, 1.0, 0.0, -1.0):
            sc.op("vector", lambda h, thr=thr: h.scalar_tensor_tensor(
                out=t1s[:], in0=frs, scalar=thr,
                in1=uq[:] if thr == 2.0 else t1s[:],
                op0=OP.is_ge, op1=OP.add))
            sc.drain("vector")
        xqdone = sc.op("vector", lambda h: h.tensor_scalar(
            out=xq_t, in0=t1s[:], scalar1=1.0, scalar2=-130.0,
            op0=OP.mult, op1=OP.add), waits=[("cT", wq_end_ct)], inc="cV")
        for b in range(NDT):
            bk, freev, ki = bksX.get()
            ct = sc.op("tensor", lambda h, bk=bk, b=b: h.transpose(
                bk[:, 0:128], xq_t[:, 128 * b:128 * (b + 1)], ident_b[:]),
                waits=[("cV", xqdone), ("cV", freev)], inc="cT")
            wv = [("cT", ct)] + st_sxq.wait_for(len(xq_dmas) - 2)
            if len(xq_dmas) == 0 and b == 0:
                wv += wq_end_ds
            ev = sc.op("vector", lambda h, bk=bk, b=b, par=par: h.tensor_copy(
                stC[:, par * 1024 + 128 * b: par * 1024 + 128 * (b + 1)],
                bk[:, 0:128]), waits=wv, inc="cV")
            bksX.set_free(ki, ev)
        dstx = xqT_d[:, 128 * t:128 * (t + 1)].rearrange("(b p) e -> p b e", p=128)
        srcx = stC[:, par * 1024:(par + 1) * 1024].rearrange("p (b e) -> p b e", e=128)
        st_sxq.dma(lambda h, dstx=dstx, srcx=srcx: h.dma_start(
            out=dstx, in_=srcx), waits=[("cV", sc.v("cV"))])
        xq_dmas.append(st_sxq.n)
    ct = sc.op("tensor", lambda h: h.transpose(
        pC[2][0:16, 0:128], rx1[:], ident_f[:]),
        waits=[("cV", sc.v("cV"))], inc="cT")
    rxcv = sc.op("vector", lambda h: h.tensor_copy(rxrow_s[:], pC[2][0:16, 0:128]),
                 waits=[("cT", ct)], inc="cV")
    st_srx.dma(lambda h: h.dma_start(out=rx_d, in_=rxrow_s[:]),
               waits=[("cV", rxcv)])
    sc.barrier()
    esC.close()

    esD = ExitStack()
    GW = 2048
    hsl = [sb(esD, f"hsl{i}", [128, NHC * CH], BF16) for i in range(2)]
    gsl = sb(esD, "gsl", [128, NHC * CH], F32)
    xqc = sb(esD, "xqc", [128, 2 * NDT * CH], BF16)
    w2c = sb(esD, "w2c", [128, 2 * 4 * E], BF16)
    rxw = sb(esD, "rxw", [128, CH], F32)
    s2b = sb(esD, "s2b", [128, CH], F32)
    s2rep = sb(esD, "s2rep", [128, GW], F32)
    s2r2 = sb(esD, "s2r2", [2, 128], F32)
    rxr = sb(esD, "rxr", [1, CH], F32)
    rxr2 = sb(esD, "rxr2", [1, CH], F32)
    mxt = sb(esD, "mxt", [128, 8], F32)
    rx2d = sb(esD, "rx2d", [128, 2], F32)
    macc = sb(esD, "macc", [128, CH], F32)
    cacc = sb(esD, "cacc", [128, 2 * CH], F32)
    cacc2 = sb(esD, "cacc2", [128, 2 * CH], F32)
    cacc3 = sb(esD, "cacc3", [128, 2 * CH], F32)
    halo = [sb(esD, f"halo{i}", [128, NHC], BF16) for i in range(3)]
    zcol = sb(esD, "zcol", [128, NHC], BF16)
    tq2 = sb(esD, "tq2", [128, GW], F32)
    uq2 = esD.enter_context(nc.sbuf_tensor("uq2", [128, GW], I32))
    fr2 = sb(esD, "fr2", [128, GW], F32)
    y_sb = sb(esD, "y_sb", [128, E], F32)
    pD = [esD.enter_context(nc.psum_tensor(f"pD{i}", [128, 512], F32)) for i in range(8)]

    sc.op("gpsimd", lambda h: h.memset(zcol[:], 0.0), inc="cG")
    cG_z = sc.v("cG")

    xq_dh = {}
    h_done = {}
    rxr_free = [0]
    rxr2_free = [0]
    w2_prev = [0]
    ffn2_done = {}
    pd6_free = 0
    pd45_free = 0
    y_dmas = []
    gm_prev = [0, 0, 0, 0]

    def load_xq_chunk(k):
        dst = xqc[:, (k % 2) * NDT * CH:(k % 2 + 1) * NDT * CH] \
            .rearrange("p (b s) -> p b s", s=CH)
        src = xqT_d[:, CH * k:CH * (k + 1)].rearrange("(b p) s -> p b s", p=128)
        w = [("cT", h_done.get(k - 2, (0, 0))[1])] if k >= 2 else []
        return st_lq.dma(lambda h, dst=dst, src=src: h.dma_start(
            out=dst, in_=src), waits=w)

    def ffn1(k):
        nonlocal pd6_free
        dh = xq_dh[k]
        kk = k % 2
        i = st_lr.dma(lambda h, k=k: h.dma_start(
            out=rxr[0:1, :].rearrange("o (a b) -> o a b", a=2),
            in_=rx_d[2 * k:2 * k + 2, :]),
            waits=[("cT", rxr_free[0])] + st_srx.wait_all())
        ct = sc.op("tensor", lambda h: h.matmul(
            pD[6][:, 0:CH], onesc_f[0:1, :], rxr[0:1, :], start=True, stop=True),
            waits=st_lr.wait_for(i) + [("cV", pd6_free)], inc="cT")
        rxr_free[0] = ct
        rxwd = sc.op("vector", lambda h: h.tensor_scalar(
            out=rxw[:], in0=pD[6][:, 0:CH], scalar1=w1sc[:, 0:1], scalar2=None,
            op0=OP.mult), waits=[("cT", ct)], inc="cV")
        pd6_free = rxwd
        bks = Banks(pD[0:4], free0=sc.v("cV"))
        hguard = ffn2_done.get(k - 2, 0)
        for hc in range(NHC):
            bk, freev, ki = bks.get()
            for dt in range(NDT):
                w = st_lq.wait_for(dh) + ([("cV", freev)] if dt == 0 else [])
                sc.op("tensor", lambda h, bk=bk, hc=hc, dt=dt, kk=kk: h.matmul(
                    bk[:, 0:CH],
                    w1qT[:, dt * HC + 128 * hc: dt * HC + 128 * (hc + 1)],
                    xqc[:, kk * NDT * CH + dt * CH: kk * NDT * CH + (dt + 1) * CH],
                    start=(dt == 0), stop=(dt == NDT - 1)),
                    waits=w, inc="cT" if dt == NDT - 1 else None)
            ev = sc.op("vector", lambda h, bk=bk, hc=hc, kk=kk: h.tensor_mul(
                hsl[kk][:, hc * CH:(hc + 1) * CH], bk[:, 0:CH], rxw[:]),
                waits=[("cT", sc.v("cT")), ("cV", rxwd), ("cT", hguard)], inc="cV")
            bks.set_free(ki, ev)
        mm_ct = sc.v("cT")
        sc.drain("vector")
        snap = sc.op("vector", lambda h, kk=kk, k=k: h.tensor_copy(
            halo[k % 3][:], hsl[kk][:].rearrange("p (c s) -> p c s", s=CH)[:, :, CH - 1]),
            inc="cV")
        return (snap, mm_ct)

    def conv_quant_ffn2(m, last):
        nonlocal pd6_free, pd45_free
        km = m % 2
        hm = hsl[km]
        hl = halo[(m - 1) % 3] if m >= 1 else zcol
        h_cv = h_done[m][0]
        r_cv = h_done[m + 1][0] if not last else h_done[m][0]
        v1_at = [0, 0]
        sil_at = {}
        vC_at = {}

        def emit_sil(hc):
            C = cacc3[:, (hc % 2) * CH:(hc % 2 + 1) * CH]
            wsil = [("cV", vC_at[hc])]
            if m >= 1:
                wsil.append(("cV", gm_prev[hc // 8]))
            sil_at[hc] = sc.op("scalar", lambda h, hc=hc, C=C: h.activation(
                out=gsl[:, hc * CH:(hc + 1) * CH], in_=C, func=AF.Silu,
                bias=cb_sb[:, hc:hc + 1]), waits=wsil, inc="cS")
        for hc in range(NHC):
            par = hc % 2
            A = cacc[:, par * CH:(par + 1) * CH]
            B = cacc2[:, par * CH:(par + 1) * CH]
            C = cacc3[:, par * CH:(par + 1) * CH]
            w = [("cV", v1_at[par]), ("cV", h_cv)]
            if m == 0 and hc == 0:
                w.append(("cG", cG_z))
            sc.op("scalar", lambda h, hc=hc, hm=hm, A=A: h.mul(
                A[:, 1:CH], hm[:, hc * CH: hc * CH + CH - 1],
                cw_sb[:, 3 * hc:3 * hc + 1]), waits=w)
            sA = sc.op("scalar", lambda h, hc=hc, hl=hl, A=A: h.mul(
                A[:, 0:1], hl[:, hc:hc + 1], cw_sb[:, 3 * hc:3 * hc + 1]),
                inc="cS")
            if hc >= 2:
                emit_sil(hc - 2)
            v1_at[par] = sc.op("vector", lambda h, hc=hc, hm=hm, A=A, B=B:
                               h.scalar_tensor_tensor(
                out=B, in0=hm[:, hc * CH:(hc + 1) * CH],
                scalar=cw_sb[:, 3 * hc + 1:3 * hc + 2], in1=A,
                op0=OP.mult, op1=OP.add), waits=[("cS", sA)], inc="cV")
            sc.drain("vector")
            wv2 = [("cS", sil_at.get(hc - 2, 0))]
            if last and hc == 0:
                wv2.append(("cG", cG_z))
            sc.op("vector", lambda h, hc=hc, hm=hm, B=B, C=C:
                  h.scalar_tensor_tensor(
                out=C[:, 0:CH - 1], in0=hm[:, hc * CH + 1:(hc + 1) * CH],
                scalar=cw_sb[:, 3 * hc + 2:3 * hc + 3], in1=B[:, 0:CH - 1],
                op0=OP.mult, op1=OP.add), waits=wv2)
            if last:
                rsl = zcol[:, hc:hc + 1]
            else:
                rsl = hsl[(m + 1) % 2][:, hc * CH: hc * CH + 1]
            vC = sc.op("vector", lambda h, hc=hc, rsl=rsl, B=B, C=C:
                       h.scalar_tensor_tensor(
                out=C[:, CH - 1:CH], in0=rsl,
                scalar=cw_sb[:, 3 * hc + 2:3 * hc + 3], in1=B[:, CH - 1:CH],
                op0=OP.mult, op1=OP.add), waits=[("cV", r_cv)], inc="cV")
            sc.drain("vector")
            vC_at[hc] = vC
        emit_sil(NHC - 2)
        emit_sil(NHC - 1)
        sil_done = sil_at[NHC - 1]
        mdone = sc.op("vector", lambda h: h.tensor_reduce(
            out=macc[:], in_=gsl[:].rearrange("p (c s) -> p s c", c=NHC),
            axis=AX, op=OP.max, apply_absolute_value=True),
            waits=[("cS", sil_done)], inc="cV")
        for blk in range(CH // 128):
            ct = sc.op("tensor", lambda h, blk=blk: h.transpose(
                pD[4 + blk][:, 0:128], macc[:, 128 * blk:128 * (blk + 1)],
                ident_f[:]), waits=[("cV", mdone), ("cV", pd45_free)], inc="cT")
            sc.op("vector", lambda h, blk=blk: h.tensor_reduce(
                out=mxt[:, blk:blk + 1], in_=pD[4 + blk][:, 0:128], axis=AX,
                op=OP.max), waits=[("cT", ct)], inc="cV")
        pd45_free = sc.v("cV")
        sc.drain("vector")
        sc.op("vector", lambda h: h.tensor_scalar(
            out=mxt[:, 2:4], in0=mxt[:, 0:2], scalar1=1e-5, scalar2=None,
            op0=OP.max))
        sc.drain("vector")
        sc.op("vector", lambda h: h.reciprocal(mxt[:, 4:6], mxt[:, 2:4]))
        sc.drain("vector")
        sc.op("vector", lambda h: h.tensor_scalar(
            out=mxt[:, 6:8], in0=mxt[:, 4:6], scalar1=127.0, scalar2=None,
            op0=OP.mult))
        scq = sc.op("vector", lambda h: h.tensor_scalar(
            out=rx2d[:], in0=mxt[:, 2:4], scalar1=w2sc[:, 0:1], scalar2=1.0 / 127.0,
            op0=OP.mult, op1=OP.mult), inc="cV")
        ct = sc.op("tensor", lambda h: h.transpose(
            pD[7][0:2, 0:128], mxt[:, 6:8], ident_f[:]),
            waits=[("cV", scq), ("cV", pd6_free)], inc="cT")
        s2rd = sc.op("vector", lambda h: h.tensor_copy(s2r2[:], pD[7][0:2, 0:128]),
                     waits=[("cT", ct)] + st_sm.wait_all(), inc="cV")
        i2 = st_sm.dma(lambda h: h.dma_start(
            out=rxr2[0:1, :].rearrange("o (a b) -> o a b", a=2), in_=s2r2[:]),
            waits=[("cV", s2rd), ("cT", rxr2_free[0])])
        ct = sc.op("tensor", lambda h: h.matmul(
            pD[6][:, 0:CH], onesc_f[0:1, :], rxr2[0:1, :], start=True, stop=True),
            waits=st_sm.wait_for(i2) + [("cV", pd6_free)], inc="cT")
        rxr2_free[0] = ct
        s2bd0 = sc.op("vector", lambda h: h.tensor_copy(s2b[:], pD[6][:, 0:CH]),
                      waits=[("cT", ct)], inc="cV")
        pd6_free = s2bd0
        sc.drain("vector")
        sc.op("vector", lambda h: h.tensor_copy(s2rep[:, 0:CH], s2b[:]))
        sc.drain("vector")
        for r in range(3):
            w0 = CH * (1 << r)
            sc.op("vector", lambda h, w0=w0: h.tensor_copy(
                s2rep[:, w0:2 * w0], s2rep[:, 0:w0]),
                inc="cV" if w0 * 2 == GW else None)
            sc.drain("vector")
        s2bd = sc.v("cV")
        q2_at = {}
        for gq in range(NHC * CH // GW):
            sl = slice(gq * GW, (gq + 1) * GW)
            sc.op("vector", lambda h, sl=sl: h.tensor_tensor(
                out=tq2[:], in0=gsl[:, sl], in1=s2rep[:], op=OP.mult),
                waits=[("cV", s2bd), ("cS", sil_at[(gq + 1) * (GW // CH) - 1])])
            sc.drain("vector")
            gm_prev[gq] = sc.op("vector", lambda h: h.tensor_scalar(
                out=uq2[:], in0=tq2[:], scalar1=1.0, scalar2=128.5,
                op0=OP.mult, op1=OP.add), inc="cV")
            sc.drain("vector")
            sc.op("vector", lambda h: h.tensor_sub(fr2[:], tq2[:], uq2[:]))
            sc.drain("vector")
            for thr in (-129.5, -128.5, -127.5, -126.5):
                sc.op("vector", lambda h, thr=thr: h.scalar_tensor_tensor(
                    out=tq2[:], in0=fr2[:], scalar=thr,
                    in1=uq2[:] if thr == -129.5 else tq2[:],
                    op0=OP.is_ge, op1=OP.add))
                sc.drain("vector")
            q2_at[gq] = sc.op("vector", lambda h, sl=sl, km=km: h.tensor_scalar(
                out=hsl[km][:, sl], in0=tq2[:], scalar1=1.0, scalar2=-130.0,
                op0=OP.mult, op1=OP.add), inc="cV")
            sc.drain("vector")
        piece_ct = {}
        for piece in range(NHC // 4):
            par = piece % 2
            dstw = w2c[:, par * 4 * E:(par + 1) * 4 * E] \
                .rearrange("p (b e) -> p b e", e=E)
            srcw = w2qT_d.rearrange("(b p) e -> p b e", p=128)[:, 4 * piece:4 * piece + 4, :]
            wld = [("cT", piece_ct.get(piece - 2, w2_prev[0]))]
            dhw = st_lw2.dma(lambda h, dstw=dstw, srcw=srcw: h.dma_start(
                out=dstw, in_=srcw), waits=wld)
            for pidx in range(4):
                hc = 4 * piece + pidx
                for sl2 in range(CH // 128):
                    for nch2 in range(2):
                        bky = pD[sl2 * 2 + nch2]
                        w = [("cV", q2_at[hc // 8])] + st_lw2.wait_for(dhw)
                        sc.op("tensor", lambda h, bky=bky, hc=hc, sl2=sl2, nch2=nch2,
                              km=km, par=par, pidx=pidx: h.matmul(
                            bky[:, 0:512],
                            hsl[km][:, hc * CH + 128 * sl2: hc * CH + 128 * (sl2 + 1)],
                            w2c[:, par * 4 * E + pidx * E + 512 * nch2:
                                par * 4 * E + pidx * E + 512 * (nch2 + 1)],
                            start=(hc == 0), stop=(hc == NHC - 1)),
                            waits=w,
                            inc="cT" if (pidx == 3 and sl2 == CH // 128 - 1
                                         and nch2 == 1) else None)
            piece_ct[piece] = sc.v("cT")
        ffn2_done[m] = sc.v("cT")
        w2_prev[0] = ffn2_done[m]
        for sl2 in range(CH // 128):
            for nch2 in range(2):
                bky = pD[sl2 * 2 + nch2]
                wv = [("cT", ffn2_done[m])] + st_sy.wait_for(len(y_dmas) - 1)
                sc.op("vector", lambda h, bky=bky, sl2=sl2, nch2=nch2: h.tensor_scalar(
                    out=y_sb[:, 512 * nch2: 512 * (nch2 + 1)],
                    in0=bky[:, 0:512], scalar1=rx2d[:, sl2:sl2 + 1], scalar2=None,
                    op0=OP.mult), waits=wv, inc="cV")
            st_sy.dma(lambda h, m=m, sl2=sl2: h.dma_start(
                out=y_d[CH * m + 128 * sl2: CH * m + 128 * (sl2 + 1), :],
                in_=y_sb[:]),
                waits=[("cV", sc.v("cV"))])
            y_dmas.append(st_sy.n)

    xq_dh[0] = load_xq_chunk(0)
    h_done[0] = ffn1(0)
    for m in range(NCH):
        if m + 1 < NCH:
            xq_dh[m + 1] = load_xq_chunk(m + 1)
            h_done[m + 1] = ffn1(m + 1)
        conv_quant_ffn2(m, last=(m == NCH - 1))
    sc.barrier()
    esD.close()
    esCD.close()

    esE = ExitStack()
    xa2 = sb(esE, "xa2", [128, 2 * E], F32)
    yb2 = sb(esE, "yb2", [128, 2 * E], F32)
    r2 = sb(esE, "r2", [128, E], F32)
    t2s = sb(esE, "t2s", [128, E], F32)
    ot = sb(esE, "ot", [128, 2 * E], F32)
    stE = sb(esE, "stE", [128, 12], F32)
    out_dmas = []
    for t in range(NT):
        par = t % 2
        xat = xa2[:, par * E:(par + 1) * E]
        ybt = yb2[:, par * E:(par + 1) * E]
        w = [("cV", sc.v("cV") if t >= 2 else 0)]
        i1 = st_le.dma(lambda h, xat=xat, t=t: h.dma_start(
            out=xat, in_=x2_d[128 * t:128 * (t + 1), :]), waits=w)
        i2 = st_le.dma(lambda h, ybt=ybt, t=t: h.dma_start(
            out=ybt, in_=y_d[128 * t:128 * (t + 1), :]))
        sc.drain("vector")
        sc.op("vector", lambda h, xat=xat, ybt=ybt: h.tensor_add(r2[:], xat, ybt),
              waits=st_le.wait_for(i1) + st_le.wait_for(i2), inc="cV")
        sc.drain("vector")
        sc.op("vector", lambda h: h.reduce_sum(stE[:, 0:1], r2[:], axis=AX))
        sc.drain("vector")
        sc.op("vector", lambda h: h.tensor_scalar(
            out=stE[:, 1:2], in0=stE[:, 0:1], scalar1=-1.0 / E, scalar2=None,
            op0=OP.mult), inc="cV")
        sqd = sc.op("scalar", lambda h: h.activation(
            out=t2s[:], in_=r2[:], func=AF.Square, bias=stE[:, 1:2],
            accum_out=stE[:, 2:3]), waits=[("cV", sc.v("cV"))], inc="cS")
        sc.op("vector", lambda h: h.tensor_scalar(
            out=stE[:, 3:4], in0=stE[:, 2:3], scalar1=1.0 / E, scalar2=1e-5,
            op0=OP.mult, op1=OP.add), waits=[("cS", sqd)], inc="cV")
        lnd = sc.op("scalar", lambda h: h.activation(
            out=stE[:, 4:5], in_=stE[:, 3:4], func=AF.Ln),
            waits=[("cV", sc.v("cV"))], inc="cS")
        rsd = sc.op("scalar", lambda h: h.activation(
            out=stE[:, 5:6], in_=stE[:, 4:5], func=AF.Exp, scale=cst[:, 1:2]),
            waits=[("cS", lnd)], inc="cS")
        otp = ot[:, par * E:(par + 1) * E]
        w2 = [("cS", rsd)] + st_se.wait_for(len(out_dmas) - 2)
        odone = sc.op("vector", lambda h, otp=otp: h.tensor_scalar(
            out=otp, in0=r2[:], scalar1=stE[:, 1:2], scalar2=stE[:, 5:6],
            op0=OP.add, op1=OP.mult), waits=w2, inc="cV")
        st_se.dma(lambda h, otp=otp, t=t: h.dma_start(
            out=out_d[128 * t:128 * (t + 1), :], in_=otp),
            waits=[("cV", odone)])
        out_dmas.append(st_se.n)
    sc.op("sync", None, waits=st_se.wait_all())
    esE.close()

    with nc.Block() as block:
        sc.emit(block, sems)
    root.close()
    return nc


def kernel(L0_attn_x, L0_attn_Wq, L0_attn_Wk, L0_attn_Wv, L0_attn_rel_table,
           L0_attn_gamma, L0_attn_beta, L0_ffn_w1, L0_ffn_conv_w, L0_ffn_conv_b,
           L0_ffn_w2):
    x = np.ascontiguousarray(np.asarray(L0_attn_x, dtype=np.float32))
    T = np.asarray(L0_attn_rel_table, dtype=np.float32)

    t_ext = np.concatenate([T, np.repeat(T[-1:, :], 2048, axis=0)], axis=0)
    dvals = np.arange(-15, 16)
    p_i = np.arange(128)
    a_i = np.arange(128)
    idx = (2047 + 128 * dvals[:, None, None] + p_i[None, :, None]
           - a_i[None, None, :])
    toe = np.ascontiguousarray(
        t_ext[idx].transpose(3, 0, 1, 2))

    cwv = np.asarray(L0_ffn_conv_w, dtype=np.float32).reshape(NHC, 128, 3)
    cw = np.ascontiguousarray(cwv.transpose(1, 0, 2).reshape(128, NHC * 3))
    cb = np.ascontiguousarray(
        np.asarray(L0_ffn_conv_b, dtype=np.float32).reshape(NHC, 128).T)

    base = dict(
        wq=np.ascontiguousarray(L0_attn_Wq, dtype=np.float32),
        wk=np.ascontiguousarray(L0_attn_Wk, dtype=np.float32),
        wv=np.ascontiguousarray(L0_attn_Wv, dtype=np.float32),
        toe=toe,
        gam=np.ascontiguousarray(np.asarray(L0_attn_gamma, np.float32)[None, :]),
        bet=np.ascontiguousarray(np.asarray(L0_attn_beta, np.float32)[None, :]),
        w1=np.ascontiguousarray(L0_ffn_w1, dtype=np.float32),
        cw=cw, cb=cb,
        w2=np.ascontiguousarray(L0_ffn_w2, dtype=np.float32),
    )
    in_maps = [dict(base, x=np.ascontiguousarray(x[b])) for b in range(8)]

    prog = build_program()
    res = run_bass_kernel_spmd(prog, in_maps, core_ids=list(range(8)))
    key = {0: "out", 1: "o_s", 2: "x2_s", 3: "y_s"}[DEBUG]
    return np.stack([np.asarray(r[key]) for r in res.results], axis=0)


# version 09
# speedup vs baseline: 1.4677x; 1.0239x over previous
import numpy as np
import ml_dtypes
from contextlib import ExitStack

import concourse.bass as bass
import concourse.mybir as mybir
from concourse.bass_utils import run_bass_kernel_spmd
from concourse.alu_op_type import AluOpType as OP

F32 = mybir.dt.float32
BF16 = mybir.dt.bfloat16
I32 = mybir.dt.int32
AF = mybir.ActivationFunctionType
AX = mybir.AxisListType.X
bfdt = ml_dtypes.bfloat16

S, E, H, DH, HC = 2048, 1024, 16, 64, 4096
NT = S // 128
NP = H // 2
NDT = E // 128
NHC = HC // 128
CH = 256
NCH = S // CH
NW = 18
DEBUG = 0

ENGINES = ("sync", "gpsimd", "vector", "scalar", "tensor")


class Sched:
    def __init__(self):
        self.ops = {e: [] for e in ENGINES}
        self.cnt = {}
        self.last_wait = {e: {} for e in ENGINES}

    def v(self, sem):
        return self.cnt.get(sem, 0)

    def op(self, eng, fn, waits=(), inc=None, inc_n=1, bare_inc=False):
        w = []
        for s, val in waits:
            if val > 0 and self.last_wait[eng].get(s, -1) < val:
                w.append((s, val))
                self.last_wait[eng][s] = val
        self.ops[eng].append((fn, w, inc, inc_n, bare_inc))
        if inc is not None:
            self.cnt[inc] = self.cnt.get(inc, 0) + inc_n
        return self.cnt.get(inc, 0) if inc else None

    def drain(self, eng):
        self.op(eng, lambda h: h.drain())

    def barrier(self):
        for st in getattr(self, "streams", []):
            for w in st.wait_all():
                self.op("sync", None, waits=(w,))
        for e in ENGINES:
            self.op(e, None, inc="bar", inc_n=1, bare_inc=True)
        tot = self.v("bar")
        for e in ENGINES:
            self.op(e, None, waits=(("bar", tot),))

    def emit(self, block, sems):
        def make(ename):
            def body(h):
                for fn, waits, inc, inc_n, bare in self.ops[ename]:
                    for s, val in waits:
                        h.wait_ge(sems[s], val)
                    if bare:
                        h.sem_inc(sems[inc], inc_n)
                        continue
                    if fn is None:
                        continue
                    instr = fn(h)
                    if inc is not None:
                        instr.then_inc(sems[inc], inc_n)
            return body
        block.sync(make("sync"))
        block.gpsimd(make("gpsimd"))
        block.vector(make("vector"))
        block.scalar(make("scalar"))
        block.tensor(make("tensor"))


class Stream:

    def __init__(self, sc, eng, sems, names):
        self.sc = sc
        self.eng = eng
        self.sems = names
        self.counts = [0] * len(names)
        self.n = 0

    def dma(self, fn, waits=()):
        i = self.n
        k = i % len(self.sems)
        self.n += 1
        self.counts[k] += 1
        self.sc.op(self.eng, fn, waits=waits, inc=self.sems[k], inc_n=16)
        return i

    def wait_for(self, i):
        if i < 0:
            return []
        k = i % len(self.sems)
        return [(self.sems[k], 16 * (i // len(self.sems) + 1))]

    def wait_last(self):
        return self.wait_for(self.n - 1)

    def wait_all(self):
        return [(s, 16 * c) for s, c in zip(self.sems, self.counts) if c > 0]


class Banks:

    def __init__(self, banks, free0=0):
        self.banks = banks
        self.free_at = [free0] * len(banks)
        self.i = 0

    def get(self):
        k = self.i % len(self.banks)
        self.i += 1
        return self.banks[k], self.free_at[k], k

    def set_free(self, k, cv):
        self.free_at[k] = cv


def build_program():
    nc = bass.Bass("TRN2", target_bir_lowering=False)

    x_d = nc.dram_tensor("x", [S, E], F32, kind="ExternalInput").ap()
    wq_d = nc.dram_tensor("wq", [E, E], F32, kind="ExternalInput").ap()
    wk_d = nc.dram_tensor("wk", [E, E], F32, kind="ExternalInput").ap()
    wv_d = nc.dram_tensor("wv", [E, E], F32, kind="ExternalInput").ap()
    toe_d = nc.dram_tensor("toe", [16, 31, 128, 128], F32, kind="ExternalInput").ap()
    gam_d = nc.dram_tensor("gam", [1, E], F32, kind="ExternalInput").ap()
    bet_d = nc.dram_tensor("bet", [1, E], F32, kind="ExternalInput").ap()
    w1_d = nc.dram_tensor("w1", [HC, E], F32, kind="ExternalInput").ap()
    cw_d = nc.dram_tensor("cw", [128, 3 * NHC], F32, kind="ExternalInput").ap()
    cb_d = nc.dram_tensor("cb", [128, NHC], F32, kind="ExternalInput").ap()
    w2_d = nc.dram_tensor("w2", [E, HC], F32, kind="ExternalInput").ap()
    out_d = nc.dram_tensor("out", [S, E], F32, kind="ExternalOutput").ap()

    qT_d = nc.dram_tensor("qT_s", [E, S], BF16, kind="Internal").ap()
    kT_d = nc.dram_tensor("kT_s", [E, S], BF16, kind="Internal").ap()
    o_d = nc.dram_tensor("o_s", [S, E], F32,
                         kind="ExternalOutput" if DEBUG == 1 else "Internal").ap()
    x2_d = nc.dram_tensor("x2_s", [S, E], F32,
                          kind="ExternalOutput" if DEBUG == 2 else "Internal").ap()
    y_d = nc.dram_tensor("y_s", [S, E], F32,
                         kind="ExternalOutput" if DEBUG == 3 else "Internal").ap()
    w2qT_d = nc.dram_tensor("w2qT_s", [HC, E], BF16, kind="Internal").ap()
    v_sd = nc.dram_tensor("v_sd", [S, E], BF16, kind="Internal").ap()
    rx_d = nc.dram_tensor("rx_d", [NT, 128], F32, kind="Internal").ap()
    mx_d = nc.dram_tensor("mx_d", [2, 128], F32, kind="Internal").ap()
    xqT_d = nc.dram_tensor("xqT_s", [E, S], BF16, kind="Internal").ap()

    sc = Sched()
    root = ExitStack()
    sems = {}
    stream_sems = ["gax", "toe", "cwl", "qk", "vl", "vs",
                   "sq0", "sq1", "so0", "so1", "wl0", "wl1", "sw",
                   "lc0", "lc1", "lc2", "lc3", "sx2", "sxq0", "sxq1", "srx",
                   "lq0", "lq1", "lw20", "lw21", "lr", "sm", "lm",
                   "sy0", "sy1", "le0", "le1", "le2", "le3", "se0", "se1"]
    for s in ["cT", "cV", "cS", "cG", "bar"] + stream_sems:
        sems[s] = root.enter_context(nc.semaphore(s))
    sc.streams = []

    def mkstream(eng, names):
        st = Stream(sc, eng, sems, names)
        sc.streams.append(st)
        return st

    st_gax = mkstream("gpsimd", ["gax"])
    st_toe = mkstream("gpsimd", ["toe"])
    st_cwl = mkstream("sync", ["cwl"])
    st_qk = mkstream("sync", ["qk"])
    st_vl = mkstream("sync", ["vl"])
    st_vs = mkstream("scalar", ["vs"])
    st_sq = mkstream("scalar", ["sq0", "sq1"])
    st_so = mkstream("scalar", ["so0", "so1"])
    st_wl = mkstream("sync", ["wl0", "wl1"])
    st_sw = mkstream("scalar", ["sw"])
    st_lc = mkstream("sync", ["lc0", "lc1", "lc2", "lc3"])
    st_sx2 = mkstream("scalar", ["sx2"])
    st_sxq = mkstream("scalar", ["sxq0", "sxq1"])
    st_srx = mkstream("scalar", ["srx"])
    st_lq = mkstream("sync", ["lq0", "lq1"])
    st_lw2 = mkstream("sync", ["lw20", "lw21"])
    st_lr = mkstream("sync", ["lr"])
    st_sm = mkstream("scalar", ["sm"])
    st_lm = mkstream("sync", ["lm"])
    st_sy = mkstream("scalar", ["sy0", "sy1"])
    st_le = mkstream("sync", ["le0", "le1", "le2", "le3"])
    st_se = mkstream("sync", ["se0", "se1"])

    def sb(stack, name, shape, dt=F32):
        return stack.enter_context(nc.sbuf_tensor(name, shape, dt))

    ident_b = sb(root, "ident_b", [128, 128], BF16)
    ident_f = sb(root, "ident_f", [128, 128], F32)
    onesc_f = sb(root, "onesc_f", [1, 128], F32)
    ones65 = sb(root, "ones65", [65, 128], F32)
    cst = sb(root, "cst", [128, 4], F32)
    rx1 = sb(root, "rx1", [128, NT], F32)
    cw_sb = sb(root, "cw_sb", [128, 3 * NHC], F32)
    cb_sb = sb(root, "cb_sb", [128, NHC], F32)
    w1sc = sb(root, "w1sc", [128, 2], F32)
    rxrow_s = sb(root, "rxrow_s", [16, 128], F32)
    w2sc = sb(root, "w2sc", [128, 2], F32)
    wsumB = sb(root, "wsumB", [128, 48], F32)
    esAB = ExitStack()
    v_pad = sb(esAB, "v_pad", [128, NT * H * 65], BF16)
    v_km = sb(esAB, "v_km", [128, 16 * E], BF16)

    esA = ExitStack()
    x_bf = sb(esA, "x_bf", [128, NT * E], BF16)
    xT = sb(esA, "xT", [128, NDT * S], BF16)
    wq_sb = sb(esA, "wq_sb", [128, NDT * E], BF16)
    wk_sb = sb(esA, "wk_sb", [128, NDT * E], BF16)
    wv_sb = sb(esA, "wv_sb", [128, NDT * E], BF16)
    stgA = sb(esA, "stgA", [128, 2 * 512], BF16)
    pA = [esA.enter_context(nc.psum_tensor(f"pA{i}", [128, 512], F32)) for i in range(6)]
    pAb = [esA.enter_context(nc.psum_tensor(f"pAb{i}", [128, 1024], BF16)) for i in range(2)]

    sc.op("gpsimd", lambda h: h.memset(cst[:, 0:1], 0.03125))
    sc.op("gpsimd", lambda h: h.memset(cst[:, 1:2], -0.5))
    sc.op("gpsimd", lambda h: h.memset(onesc_f[:], 1.0))
    sc.op("gpsimd", lambda h: h.memset(ones65[:], 1.0))
    sc.op("gpsimd", lambda h: h.memset(ident_b[:], 0.0))
    sc.op("gpsimd", lambda h: h.memset(ident_f[:], 0.0))
    sc.drain("gpsimd")
    sc.op("gpsimd", lambda h: h.affine_select(out=ident_b[:], in_=ident_b[:],
          compare_op=OP.not_equal, fill=1.0, base=0, pattern=[[-1, 128]],
          channel_multiplier=1))
    sc.op("gpsimd", lambda h: h.affine_select(out=ident_f[:], in_=ident_f[:],
          compare_op=OP.not_equal, fill=1.0, base=0, pattern=[[-1, 128]],
          channel_multiplier=1), inc="cG")
    vp4 = v_pad[:].rearrange("p (t h e) -> p t h e", t=NT, h=H, e=65)
    sc.op("gpsimd", lambda h: h.memset(vp4[:, :, :, 64:65], 1.0), inc="cG")

    st_gax.dma(lambda h: h.dma_start(
        out=x_bf[:].rearrange("p (t d) -> p t d", d=E),
        in_=x_d.rearrange("(t p) d -> p t d", p=128)))
    for wd, wsb in ((wq_d, wq_sb), (wk_d, wk_sb), (wv_d, wv_sb)):
        st_gax.dma(lambda h, wd=wd, wsb=wsb: h.dma_start(
            out=wsb[:].rearrange("p (dt e) -> p dt e", e=E),
            in_=wd.rearrange("(dt p) e -> p dt e", p=128)))
    dG_A = st_gax.wait_all()

    st_cwl.dma(lambda h: h.dma_start(out=cw_sb[:], in_=cw_d))
    st_cwl.dma(lambda h: h.dma_start(out=cb_sb[:], in_=cb_d))

    bksA = Banks(pAb)
    for t in range(NT):
        for dt in range(NDT):
            bk, freev, ki = bksA.get()
            sc.op("tensor", lambda h, bk=bk, t=t, dt=dt: h.transpose(
                bk[:, 0:128], x_bf[:, t * E + 128 * dt: t * E + 128 * (dt + 1)],
                ident_b[:]), waits=dG_A + [("cG", 1), ("cV", freev)], inc="cT")
            ev = sc.op("vector", lambda h, bk=bk, t=t, dt=dt: h.tensor_copy(
                xT[:, dt * S + 128 * t: dt * S + 128 * (t + 1)], bk[:, 0:128]),
                waits=[("cT", sc.v("cT"))], inc="cV")
            bksA.set_free(ki, ev)
    cV_xT = sc.v("cV")

    bksQ = Banks(pA[0:3])
    nstg = 0
    stg_dmas = []
    for wsb, dst in ((wq_sb, qT_d), (wk_sb, kT_d)):
        for j in range(NDT):
            for nch in range(4):
                bk, freev, ki = bksQ.get()
                for dt in range(NDT):
                    waits = [("cV", cV_xT)] + ([("cV", freev)] if dt == 0 else [])
                    sc.op("tensor", lambda h, bk=bk, wsb=wsb, j=j, nch=nch, dt=dt:
                          h.matmul(bk[:, 0:512],
                                   wsb[:, dt * E + 128 * j: dt * E + 128 * (j + 1)],
                                   xT[:, dt * S + 512 * nch: dt * S + 512 * (nch + 1)],
                                   start=(dt == 0), stop=(dt == NDT - 1)),
                          waits=waits, inc="cT" if dt == NDT - 1 else None)
                stx = stgA[:, (nstg % 2) * 512:(nstg % 2) * 512 + 512]
                wv2 = [("cT", sc.v("cT"))] + st_sq.wait_for(nstg - 2)
                ev = sc.op("vector", lambda h, stx=stx, bk=bk: h.tensor_copy(
                    stx, bk[:, 0:512]), waits=wv2, inc="cV")
                bksQ.set_free(ki, ev)
                st_sq.dma(lambda h, stx=stx, dst=dst, j=j, nch=nch:
                          h.dma_start(out=dst[128 * j:128 * (j + 1),
                                              512 * nch:512 * (nch + 1)], in_=stx),
                          waits=[("cV", ev)])
                nstg += 1
    v_nat = x_bf
    bksV = Banks(pA[3:6], free0=sc.v("cV"))
    for t in range(NT):
        for nch in range(2):
            bk, freev, ki = bksV.get()
            for dt in range(NDT):
                waits = [("cV", cV_xT)] + ([("cV", freev)] if dt == 0 else [])
                sc.op("tensor", lambda h, bk=bk, t=t, nch=nch, dt=dt:
                      h.matmul(bk[:, 0:512],
                               xT[:, dt * S + 128 * t: dt * S + 128 * (t + 1)],
                               wv_sb[:, dt * E + 512 * nch: dt * E + 512 * (nch + 1)],
                               start=(dt == 0), stop=(dt == NDT - 1)),
                      waits=waits, inc="cT" if dt == NDT - 1 else None)
            ev = sc.op("vector", lambda h, t=t, nch=nch, bk=bk: h.tensor_copy(
                v_nat[:, t * E + 512 * nch: t * E + 512 * (nch + 1)], bk[:, 0:512]),
                waits=[("cT", sc.v("cT"))], inc="cV")
            bksV.set_free(ki, ev)
    cV_vp = sc.v("cV")
    st_vs.dma(lambda h: h.dma_start(
        out=v_sd.rearrange("(t p) e -> p t e", p=128),
        in_=v_nat[:].rearrange("p (t e) -> p t e", e=E)),
        waits=[("cV", cV_vp)])
    dvs = st_vs.wait_all()
    for t in range(NT):
        st_vl.dma(lambda h, t=t: h.dma_start(
            out=vp4[:, t, :, 0:64],
            in_=v_sd[128 * t:128 * (t + 1), :].rearrange(
                "p (hh e) -> p hh e", e=64)),
            waits=dvs)
    st_vl.dma(lambda h: h.dma_start(
        out=v_km[:].rearrange("p (k c) -> p k c", k=16),
        in_=v_sd.rearrange("(p k) c -> p k c", k=16)))
    sc.barrier()
    esA.close()

    p1_cons = [0, 0]
    p1_ip = [0]
    p1_done = {}

    def p1_unit(wi, wd, tt, pc, pw, col, is_last):
        par = p1_ip[0] % 2
        p1_ip[0] += 1
        buf = wtB[:, par * 2048:par * 2048 + pw]
        i = st_wl.dma(lambda h, buf=buf, wd=wd, tt=tt, pc=pc, pw=pw:
                      h.dma_start(out=buf,
                                  in_=wd[128 * tt:128 * (tt + 1),
                                         pw * pc:pw * (pc + 1)]),
                      waits=[("cV", p1_cons[par])])
        p1_cons[par] = sc.op("vector", lambda h, buf=buf, col=col:
                             h.reduce_sum(wsumB[:, col:col + 1], buf, axis=AX,
                                          apply_absolute_value=True),
                             waits=st_wl.wait_for(i), inc="cV")
        if is_last:
            p1_done[wi] = p1_cons[par]

    p1_units = []
    for tt in range(NHC):
        p1_units.append((0, w1_d, tt, 0, E, tt, tt == NHC - 1))
    for tt in range(NDT):
        for pc in range(2):
            p1_units.append((1, w2_d, tt, pc, 2048, 32 + 2 * tt + pc,
                             tt == NDT - 1 and pc == 1))
    p1_next = [0]

    def p1_emit(n):
        while n > 0 and p1_next[0] < len(p1_units):
            p1_unit(*p1_units[p1_next[0]])
            p1_next[0] += 1
            n -= 1

    esB = ExitStack()
    wtB = sb(esB, "wtB", [128, 2 * 2048], F32)
    toe = sb(esB, "toe_sb", [128, 16 * NW * 128], BF16)
    qkp = sb(esB, "qkp", [128, 2 * S], BF16)
    p_bf = sb(esB, "p_bf", [128, 2 * 2048], BF16)
    pvA_sb = sb(esB, "pvA_sb", [65, 512], F32)
    pvB_sb = sb(esB, "pvB_sb", [65, 512], F32)
    bias_sb = sb(esB, "bias_sb", [128, 512], F32)
    rcp = sb(esB, "rcp", [128, 2], F32)
    tb_sb = sb(esB, "tb_sb", [128, 128], F32)
    o_stage = sb(esB, "o_stage", [128, 1024], F32)
    sAB = esB.enter_context(nc.psum_tensor("sAB", [128, 2048], F32))
    pvA = esB.enter_context(nc.psum_tensor("pvA", [128, 512], F32))
    pvB = esB.enter_context(nc.psum_tensor("pvB", [128, 512], F32))
    pb = [esB.enter_context(nc.psum_tensor(f"pb{i}", [128, 512], F32)) for i in range(2)]

    toev = toe[:].rearrange("p (k w a) -> p k w a", k=16, w=NW, a=128)

    def toe_load(dlt, wait_ct):
        slot = (dlt + 15) % NW
        w = [("cT", wait_ct)] if wait_ct > 0 else []
        st_toe.dma(lambda h, dlt=dlt, slot=slot: h.dma_start(
            out=toev[:, :, slot, :],
            in_=toe_d[:, dlt + 15].rearrange("k p a -> p k a")),
            waits=w)

    for dlt in range(-1, 16):
        toe_load(dlt, 0)
    dG_need = st_toe.wait_all()

    st_qk.dma(lambda h: h.dma_start(out=qkp[:, 0:S], in_=qT_d[0:128, :]))
    st_qk.dma(lambda h: h.dma_start(out=qkp[:, S:2 * S], in_=kT_d[0:128, :]))

    exp_done = 0
    o_dmas = []
    pv_free = 0
    tr_free = 0
    bias_bank_free = [0, 0]
    for j in range(NP):
        hA, hB = 2 * j, 2 * j + 1
        dH_pair = st_qk.wait_all()
        qperm = qkp[:, 0:S].rearrange("p (a c) -> p c a", a=128, c=16)
        for qc in range(4):
            c0 = 4 * qc
            slA0 = ((c0 - hA) + 15) % NW
            slB0 = ((c0 - hB) + 15) % NW
            fill = []
            for kap in range(16):
                w0 = []
                if kap == 0:
                    w0 = list(dG_need) + [("cV", bias_bank_free[0]),
                                          ("cV", bias_bank_free[1])]
                fill.append((lambda h, kap=kap, slA0=slA0, j=j: h.matmul(
                    pb[0][0:64, 0:128],
                    v_km[:, kap * E + 128 * j: kap * E + 128 * j + 64],
                    toe[:, kap * (NW * 128) + slA0 * 128:
                        kap * (NW * 128) + (slA0 + 1) * 128],
                    start=(kap == 0), stop=(kap == 15), tile_position=(0, 0)), w0))
                fill.append((lambda h, kap=kap, slB0=slB0, j=j: h.matmul(
                    pb[1][64:128, 0:128],
                    v_km[:, kap * E + 128 * j + 64: kap * E + 128 * (j + 1)],
                    toe[:, kap * (NW * 128) + slB0 * 128:
                        kap * (NW * 128) + (slB0 + 1) * 128],
                    start=(kap == 0), stop=(kap == 15), tile_position=(0, 64)), ()))
            fi = 0
            for g in range(8):
                u0, u1 = 2 * g, 2 * g + 1
                wqk = list(dH_pair)
                if exp_done:
                    wqk.append(("cS", exp_done))
                for ui, u in enumerate((u0, u1)):
                    off = ui * 1024
                    sc.op("tensor", lambda h, u=u, qc=qc, off=off, qperm=qperm:
                          h.matmul(sAB[:, off:off + 512],
                                   qkp[0:64, S + 128 * u: S + 128 * (u + 1)],
                                   qperm[0:64, 4 * qc:4 * qc + 4, :],
                                   start=True, stop=True),
                          waits=wqk if ui == 0 else ())
                    sc.op("tensor", lambda h, u=u, qc=qc, off=off, qperm=qperm:
                          h.matmul(sAB[:, off + 512:off + 1024],
                                   qkp[64:128, S + 128 * u: S + 128 * (u + 1)],
                                   qperm[64:128, 4 * qc:4 * qc + 4, :],
                                   start=True, stop=True),
                          inc="cT" if ui == 1 else None)
                qk_ct = sc.v("cT")
                if g > 0:
                    gp = (g - 1) % 2
                    for ui, u in enumerate((2 * g - 2, 2 * g - 1)):
                        wpv = []
                        if g == 1 and ui == 0 and pv_free:
                            wpv = [("cV", pv_free)]
                        sc.op("tensor", lambda h, u=u, hA=hA, gp=gp, ui=ui: h.matmul(
                            pvA[0:65, 0:512],
                            v_pad[:, u * (H * 65) + hA * 65: u * (H * 65) + hA * 65 + 65],
                            p_bf[:, gp * 2048 + ui * 1024: gp * 2048 + ui * 1024 + 512],
                            start=(u == 0), stop=False), waits=wpv)
                        sc.op("tensor", lambda h, u=u, hB=hB, gp=gp, ui=ui: h.matmul(
                            pvB[0:65, 0:512],
                            v_pad[:, u * (H * 65) + hB * 65: u * (H * 65) + hB * 65 + 65],
                            p_bf[:, gp * 2048 + ui * 1024 + 512:
                                 gp * 2048 + ui * 1024 + 1024],
                            start=(u == 0), stop=False))
                while fi < len(fill) and fi < 4 * (g + 1):
                    fn, w0 = fill[fi]
                    sc.op("tensor", fn, waits=w0,
                          inc="cT" if fi == len(fill) - 1 else None)
                    fi += 1
                exp_done = sc.op("scalar", lambda h, g=g: h.activation(
                    out=p_bf[:, (g % 2) * 2048:(g % 2) * 2048 + 2048],
                    in_=sAB[:], func=AF.Exp, scale=cst[:, 0:1]),
                    waits=[("cT", qk_ct), ("cG", 1)], inc="cS")
            while fi < len(fill):
                fn, w0 = fill[fi]
                sc.op("tensor", fn, waits=w0,
                      inc="cT" if fi == len(fill) - 1 else None)
                fi += 1
            bias_c0_ct = sc.v("cT")
            gp = 1
            for ui, u in enumerate((14, 15)):
                sc.op("tensor", lambda h, u=u, hA=hA, ui=ui: h.matmul(
                    pvA[0:65, 0:512],
                    v_pad[:, u * (H * 65) + hA * 65: u * (H * 65) + hA * 65 + 65],
                    p_bf[:, gp * 2048 + ui * 1024: gp * 2048 + ui * 1024 + 512],
                    start=False, stop=(u == 15)),
                    waits=[("cS", exp_done)])
                sc.op("tensor", lambda h, u=u, hB=hB, ui=ui: h.matmul(
                    pvB[0:65, 0:512],
                    v_pad[:, u * (H * 65) + hB * 65: u * (H * 65) + hB * 65 + 65],
                    p_bf[:, gp * 2048 + ui * 1024 + 512: gp * 2048 + ui * 1024 + 1024],
                    start=False, stop=(u == 15)),
                    inc="cT" if u == 15 else None)
            pv_done = sc.v("cT")
            sc.op("vector", lambda h: h.tensor_copy(pvA_sb[:], pvA[0:65, 0:512]),
                  waits=[("cT", pv_done)])
            pv_ev = sc.op("vector", lambda h: h.tensor_copy(
                pvB_sb[:], pvB[0:65, 0:512]), inc="cV")
            bias_ev = [0, 0, 0, 0]
            sc.op("vector", lambda h: h.tensor_copy(
                bias_sb[0:64, 0:128], pb[0][0:64, 0:128]),
                waits=[("cT", bias_c0_ct)])
            bias_ev[0] = sc.op("vector", lambda h: h.tensor_copy(
                bias_sb[64:128, 0:128], pb[1][64:128, 0:128]), inc="cV")
            for ci in range(1, 4):
                c = 4 * qc + ci
                slA = ((c - hA) + 15) % NW
                slB = ((c - hB) + 15) % NW
                for kap in range(16):
                    w0 = [("cV", bias_ev[ci - 1])] if kap == 0 else []
                    sc.op("tensor", lambda h, kap=kap, slA=slA, j=j: h.matmul(
                        pb[0][0:64, 0:128],
                        v_km[:, kap * E + 128 * j: kap * E + 128 * j + 64],
                        toe[:, kap * (NW * 128) + slA * 128:
                            kap * (NW * 128) + (slA + 1) * 128],
                        start=(kap == 0), stop=(kap == 15), tile_position=(0, 0)),
                        waits=w0)
                    sc.op("tensor", lambda h, kap=kap, slB=slB, j=j: h.matmul(
                        pb[1][64:128, 0:128],
                        v_km[:, kap * E + 128 * j + 64: kap * E + 128 * (j + 1)],
                        toe[:, kap * (NW * 128) + slB * 128:
                            kap * (NW * 128) + (slB + 1) * 128],
                        start=(kap == 0), stop=(kap == 15), tile_position=(0, 64)),
                        inc="cT" if kap == 15 else None)
                bct = sc.v("cT")
                sc.op("vector", lambda h, ci=ci: h.tensor_copy(
                    bias_sb[0:64, 128 * ci:128 * (ci + 1)], pb[0][0:64, 0:128]),
                    waits=[("cT", bct)])
                bias_ev[ci] = sc.op("vector", lambda h, ci=ci: h.tensor_copy(
                    bias_sb[64:128, 128 * ci:128 * (ci + 1)], pb[1][64:128, 0:128]),
                    inc="cV")
            bias_bank_free = [bias_ev[3], bias_ev[3]]
            half = (qc % 2) * 512
            last_stt = 0
            for ci in range(4):
                trbank = pb[ci % 2]
                wtr = [("cV", pv_ev)]
                if ci == 0:
                    wtr.append(("cV", bias_ev[3]))
                if ci >= 1:
                    wtr.append(("cV", last_stt))
                trA_ct = sc.op("tensor", lambda h, ci=ci: h.transpose(
                    pvA[:, 0:65], pvA_sb[0:65, 128 * ci:128 * (ci + 1)],
                    ident_f[0:65, 0:65]), waits=wtr, inc="cT")
                trB_ct = sc.op("tensor", lambda h, ci=ci: h.transpose(
                    pvB[:, 0:65], pvB_sb[0:65, 128 * ci:128 * (ci + 1)],
                    ident_f[0:65, 0:65]), inc="cT")
                trX_ct = sc.op("tensor", lambda h, ci=ci, trbank=trbank: h.transpose(
                    trbank[:, 0:128], bias_sb[:, 128 * ci:128 * (ci + 1)],
                    ident_f[:]), inc="cT")
                sc.op("vector", lambda h: h.reciprocal(rcp[:, 0:1], pvA[:, 64:65]),
                      waits=[("cT", trX_ct)])
                sc.drain("vector")
                sc.op("vector", lambda h: h.reciprocal(rcp[:, 1:2], pvB[:, 64:65]))
                sc.drain("vector")
                sc.op("vector", lambda h, trbank=trbank: h.tensor_copy(
                    tb_sb[:], trbank[:, 0:128]), inc="cV")
                sc.drain("vector")
                wst = st_so.wait_for(len(o_dmas) - 2)
                sc.op("vector", lambda h, ci=ci, half=half: h.scalar_tensor_tensor(
                    out=o_stage[:, half + 128 * ci: half + 128 * ci + 64],
                    in0=pvA[:, 0:64], scalar=rcp[:, 0:1], in1=tb_sb[:, 0:64],
                    op0=OP.mult, op1=OP.add), waits=wst)
                last_stt = sc.op("vector", lambda h, ci=ci, half=half:
                                 h.scalar_tensor_tensor(
                    out=o_stage[:, half + 128 * ci + 64: half + 128 * (ci + 1)],
                    in0=pvB[:, 0:64], scalar=rcp[:, 1:2], in1=tb_sb[:, 64:128],
                    op0=OP.mult, op1=OP.add), inc="cV")
            pv_free = last_stt
            tr_free = last_stt
            bias_bank_free = [last_stt, last_stt]
            dsto = o_d[:, 128 * j:128 * (j + 1)] \
                .rearrange("(a c) e -> a c e", a=128, c=16)[:, 4 * qc:4 * qc + 4, :]
            srco = o_stage[:, half:half + 512].rearrange("a (c e) -> a c e", c=4, e=128)
            st_so.dma(lambda h, dsto=dsto, srco=srco: h.dma_start(
                out=dsto, in_=srco), waits=[("cV", last_stt)])
            o_dmas.append(st_so.n)
            p1_emit(2)
        if j + 1 < NP:
            st_qk.dma(lambda h, j=j: h.dma_start(
                out=qkp[:, 0:S], in_=qT_d[128 * (j + 1):128 * (j + 2), :]),
                waits=[("cT", sc.v("cT"))])
            st_qk.dma(lambda h, j=j: h.dma_start(
                out=qkp[:, S:2 * S], in_=kT_d[128 * (j + 1):128 * (j + 2), :]))
            toe_load(-2 * (j + 1), sc.v("cT"))
            toe_load(-2 * (j + 1) - 1, sc.v("cT"))
            dG_need = st_toe.wait_all()
    p1_emit(len(p1_units))
    sc.barrier()
    esB.close()
    esAB.close()

    esCD = ExitStack()
    w1qT = sb(esCD, "w1qT", [128, NDT * HC], BF16)
    esC = ExitStack()
    gb_sb = sb(esC, "gb_sb", [128, E], F32)
    bb_sb = sb(esC, "bb_sb", [128, E], F32)
    wt = sb(esC, "wt", [128, 2 * 2048], F32)
    mp_t = sb(esC, "mp_t", [128, 2048], F32)
    mm_t = sb(esC, "mm_t", [128, 2048], F32)
    wqn = sb(esC, "wqn", [128, 2048], BF16)
    stgW = sb(esC, "stgW", [128, HC], BF16)
    xa = sb(esC, "xa", [128, 2 * E], F32)
    ob = sb(esC, "ob", [128, 2 * E], F32)
    r1 = sb(esC, "r1", [128, E], F32)
    t1s = sb(esC, "t1s", [128, E], F32)
    frs = mm_t[:, 0:E]
    uq = esC.enter_context(nc.sbuf_tensor("uq", [128, E], I32))
    bns = mp_t[:, 0:E]
    xq_t = wqn[:, 0:E]
    stC = stgW
    stat = sb(esC, "stat", [128, 12], F32)
    wsum = sb(esC, "wsum", [128, NHC], F32)
    pC = [esC.enter_context(nc.psum_tensor(f"pC{i}", [128, 512], F32)) for i in range(4)]
    pCb = [esC.enter_context(nc.psum_tensor(f"pCb{i}", [128, 1024], BF16)) for i in range(4)]

    st_cwl.dma(lambda h: h.dma_start(out=gb_sb[0:1, :], in_=gam_d))
    st_cwl.dma(lambda h: h.dma_start(out=bb_sb[0:1, :], in_=bet_d))
    dH_gb = st_cwl.wait_all()
    for i, bc in enumerate((gb_sb, bb_sb)):
        for nch in range(2):
            ct = sc.op("tensor", lambda h, bc=bc, i=i, nch=nch: h.matmul(
                pC[2 * i + nch][:, 0:512], onesc_f[0:1, :],
                bc[0:1, 512 * nch:512 * (nch + 1)], start=True, stop=True),
                waits=dH_gb, inc="cT")
            sc.op("vector", lambda h, bc=bc, i=i, nch=nch: h.tensor_copy(
                bc[:, 512 * nch:512 * (nch + 1)], pC[2 * i + nch][:, 0:512]),
                waits=[("cT", ct)], inc="cV")
    gb_ready = sc.v("cV")

    def wquant(wi, wd, wscb):
        ntile = NHC if wi == 0 else NDT
        width = E if wi == 0 else HC
        npc = width // 2048 if width > 2048 else 1
        pw = width // npc
        base, ncol = (0, 32) if wi == 0 else (32, 16)
        sc.drain("vector")
        sc.op("vector", lambda h, base=base, ncol=ncol: h.reduce_sum(
            stat[:, 0:1], wsumB[:, base:base + ncol], axis=AX),
            waits=[("cV", p1_done[wi])], inc="cV")
        ct = sc.op("tensor", lambda h: h.transpose(
            pC[0][0:1, 0:128], stat[:, 0:1], ident_f[0:128, 0:128]),
            waits=[("cV", sc.v("cV")), ("cV", gb_ready)], inc="cT")
        sc.op("vector", lambda h: h.reduce_sum(
            stat[0:1, 1:2], pC[0][0:1, 0:128], axis=AX), waits=[("cT", ct)])
        sc.drain("vector")
        sc.op("vector", lambda h: h.tensor_scalar(
            out=stat[0:1, 2:3], in0=stat[0:1, 1:2], scalar1=1.0 / (HC * E),
            scalar2=1e-5, op0=OP.mult, op1=OP.max))
        sc.drain("vector")
        cv = sc.op("vector", lambda h: h.reciprocal(stat[0:1, 3:4], stat[0:1, 2:3]),
                   inc="cV")
        ct = sc.op("tensor", lambda h: h.matmul(
            pC[1][:, 0:2], onesc_f[0:1, :], stat[0:1, 2:4], start=True, stop=True),
            waits=[("cV", cv)], inc="cT")
        scd = sc.op("vector", lambda h, wscb=wscb: h.tensor_copy(
            wscb[:], pC[1][:, 0:2]), waits=[("cT", ct)], inc="cV")
        bksW = Banks(pCb, free0=sc.v("cV"))
        stg_done = 0
        cons = [scd] * 2
        ip = 0
        wqn_free = 0
        for tt in range(ntile):
            for pc in range(npc):
                par = ip % 2
                ip += 1
                buf = wt[:, par * 2048:par * 2048 + pw]
                i = st_wl.dma(lambda h, buf=buf, wd=wd, tt=tt, pc=pc, pw=pw:
                              h.dma_start(out=buf,
                                          in_=wd[128 * tt:128 * (tt + 1),
                                                 pw * pc:pw * (pc + 1)]),
                              waits=[("cV", cons[par])])
                sc.op("vector", lambda h, buf=buf, pw=pw, wscb=wscb: h.tensor_scalar(
                    out=mp_t[:, 0:pw], in0=buf, scalar1=wscb[:, 1:2], scalar2=0.5,
                    op0=OP.mult, op1=OP.is_ge),
                    waits=st_wl.wait_for(i) + [("cT", wqn_free)])
                cons[par] = sc.op("vector", lambda h, buf=buf, pw=pw, wscb=wscb:
                             h.tensor_scalar(
                    out=mm_t[:, 0:pw], in0=buf, scalar1=wscb[:, 1:2], scalar2=-0.5,
                    op0=OP.mult, op1=OP.is_le), inc="cV")
                sc.drain("vector")
                qdone = sc.op("vector", lambda h, pw=pw: h.tensor_sub(
                    wqn[:, 0:pw], mp_t[:, 0:pw], mm_t[:, 0:pw]), inc="cV")
                nb = pw // 128
                for b in range(nb):
                    bg = pc * nb + b
                    bk, freev, ki = bksW.get()
                    ct = sc.op("tensor", lambda h, bk=bk, b=b: h.transpose(
                        bk[:, 0:128], wqn[:, 128 * b:128 * (b + 1)], ident_b[:]),
                        waits=[("cV", qdone), ("cV", freev)], inc="cT")
                    if wi == 0:
                        ev = sc.op("vector", lambda h, bk=bk, tt=tt, bg=bg:
                                   h.tensor_copy(
                            w1qT[:, bg * HC + 128 * tt: bg * HC + 128 * (tt + 1)],
                            bk[:, 0:128]), waits=[("cT", ct)], inc="cV")
                    else:
                        wv = [("cT", ct)] + st_sw.wait_for(st_sw.n - 1)
                        ev = sc.op("vector", lambda h, bk=bk, bg=bg: h.tensor_copy(
                            stgW[:, 128 * bg:128 * (bg + 1)], bk[:, 0:128]),
                            waits=wv, inc="cV")
                    bksW.set_free(ki, ev)
                wqn_free = sc.v("cT")
            if wi == 1:
                dstw = w2qT_d[:, 128 * tt:128 * (tt + 1)] \
                    .rearrange("(b p) e -> p b e", p=128)
                srcw = stgW[:].rearrange("p (b e) -> p b e", e=128)
                st_sw.dma(lambda h, dstw=dstw, srcw=srcw:
                          h.dma_start(out=dstw, in_=srcw),
                          waits=[("cV", sc.v("cV"))])

    wquant(0, w1_d, w1sc)
    wquant(1, w2_d, w2sc)
    wq_end_ct = sc.v("cT")
    wq_end_ds = st_sw.wait_all()

    bksX = Banks(pCb, free0=sc.v("cV"))
    xq_dmas = []
    for t in range(NT):
        par = t % 2
        xat = xa[:, par * E:(par + 1) * E]
        obt = ob[:, par * E:(par + 1) * E]
        w = [("cV", sc.v("cV") if t >= 2 else 0)]
        i1 = st_lc.dma(lambda h, xat=xat, t=t: h.dma_start(
            out=xat, in_=x_d[128 * t:128 * (t + 1), :]), waits=w)
        i2 = st_lc.dma(lambda h, obt=obt, t=t: h.dma_start(
            out=obt, in_=o_d[128 * t:128 * (t + 1), :]))
        sc.op("vector", lambda h, obt=obt: h.reduce_sum(
            stat[:, 0:1], obt, axis=AX),
            waits=st_lc.wait_for(i2) + [("cV", gb_ready)])
        sc.drain("vector")
        sc.op("vector", lambda h: h.tensor_scalar(
            out=stat[:, 1:2], in0=stat[:, 0:1], scalar1=-1.0 / E, scalar2=None,
            op0=OP.mult), inc="cV")
        mu_done = sc.v("cV")
        sq_done = sc.op("scalar", lambda h, obt=obt: h.activation(
            out=t1s[:], in_=obt, func=AF.Square, bias=stat[:, 1:2],
            accum_out=stat[:, 2:3]), waits=[("cV", mu_done)], inc="cS")
        sc.op("vector", lambda h: h.tensor_scalar(
            out=stat[:, 3:4], in0=stat[:, 2:3], scalar1=1.0 / E, scalar2=1e-3,
            op0=OP.mult, op1=OP.add), waits=[("cS", sq_done)], inc="cV")
        ln_done = sc.op("scalar", lambda h: h.activation(
            out=stat[:, 4:5], in_=stat[:, 3:4], func=AF.Ln),
            waits=[("cV", sc.v("cV"))], inc="cS")
        rstd_done = sc.op("scalar", lambda h: h.activation(
            out=stat[:, 5:6], in_=stat[:, 4:5], func=AF.Exp, scale=cst[:, 1:2]),
            waits=[("cS", ln_done)], inc="cS")
        sc.op("vector", lambda h, obt=obt: h.tensor_scalar(
            out=t1s[:], in0=obt, scalar1=stat[:, 1:2], scalar2=stat[:, 5:6],
            op0=OP.add, op1=OP.mult), waits=[("cS", rstd_done)])
        sc.drain("vector")
        sc.op("vector", lambda h: h.tensor_mul(t1s[:], t1s[:], gb_sb[:]))
        sc.drain("vector")
        sc.op("vector", lambda h: h.tensor_add(t1s[:], t1s[:], bb_sb[:]))
        sc.drain("vector")
        sc.op("vector", lambda h, xat=xat: h.tensor_add(r1[:], t1s[:], xat),
              waits=st_lc.wait_for(i1), inc="cV")
        sc.drain("vector")
        sc.op("vector", lambda h: h.reduce_sum(stat[:, 6:7], r1[:], axis=AX))
        sc.drain("vector")
        sc.op("vector", lambda h: h.tensor_scalar(
            out=stat[:, 7:8], in0=stat[:, 6:7], scalar1=-1.0 / E, scalar2=None,
            op0=OP.mult), inc="cV")
        sq2 = sc.op("scalar", lambda h: h.activation(
            out=t1s[:], in_=r1[:], func=AF.Square, bias=stat[:, 7:8],
            accum_out=stat[:, 8:9]), waits=[("cV", sc.v("cV"))], inc="cS")
        sc.op("vector", lambda h: h.tensor_scalar(
            out=stat[:, 9:10], in0=stat[:, 8:9], scalar1=1.0 / E, scalar2=1e-5,
            op0=OP.mult, op1=OP.add), waits=[("cS", sq2)], inc="cV")
        ln2 = sc.op("scalar", lambda h: h.activation(
            out=stat[:, 10:11], in_=stat[:, 9:10], func=AF.Ln),
            waits=[("cV", sc.v("cV"))], inc="cS")
        rstd2 = sc.op("scalar", lambda h: h.activation(
            out=stat[:, 11:12], in_=stat[:, 10:11], func=AF.Exp, scale=cst[:, 1:2]),
            waits=[("cS", ln2)], inc="cS")
        x2done = sc.op("vector", lambda h: h.tensor_scalar(
            out=t1s[:], in0=r1[:], scalar1=stat[:, 7:8], scalar2=stat[:, 11:12],
            op0=OP.add, op1=OP.mult), waits=[("cS", rstd2)], inc="cV")
        st_sx2.dma(lambda h, t=t: h.dma_start(
            out=x2_d[128 * t:128 * (t + 1), :], in_=t1s[:]),
            waits=[("cV", x2done)])
        sc.drain("vector")
        sc.op("vector", lambda h: h.tensor_reduce(
            out=stat[:, 0:1], in_=t1s[:], axis=AX, op=OP.max,
            apply_absolute_value=True))
        sc.drain("vector")
        sc.op("vector", lambda h: h.tensor_scalar(
            out=stat[:, 1:2], in0=stat[:, 0:1], scalar1=1e-5, scalar2=None,
            op0=OP.max))
        sc.drain("vector")
        sc.op("vector", lambda h, t=t: h.tensor_scalar(
            out=rx1[:, t:t + 1], in0=stat[:, 1:2], scalar1=1.0 / 127.0, scalar2=None,
            op0=OP.mult))
        sc.op("vector", lambda h: h.reciprocal(stat[:, 2:3], stat[:, 1:2]))
        sc.drain("vector")
        sc.op("vector", lambda h: h.tensor_scalar(
            out=stat[:, 3:4], in0=stat[:, 2:3], scalar1=127.0, scalar2=None,
            op0=OP.mult), inc="cV")
        sc.op("vector", lambda h: h.tensor_scalar(
            out=t1s[:], in0=t1s[:], scalar1=stat[:, 3:4], scalar2=128.5,
            op0=OP.mult, op1=OP.add),
            waits=[("cV", sc.v("cV"))] + st_sx2.wait_for(t))
        sc.drain("vector")
        sc.op("vector", lambda h: h.tensor_copy(uq[:], t1s[:]))
        sc.drain("vector")
        sc.op("vector", lambda h: h.tensor_sub(frs, t1s[:], uq[:]))
        sc.drain("vector")
        for thr in (2.0, 1.0, 0.0, -1.0):
            sc.op("vector", lambda h, thr=thr: h.scalar_tensor_tensor(
                out=t1s[:], in0=frs, scalar=thr,
                in1=uq[:] if thr == 2.0 else t1s[:],
                op0=OP.is_ge, op1=OP.add))
            sc.drain("vector")
        xqdone = sc.op("vector", lambda h: h.tensor_scalar(
            out=xq_t, in0=t1s[:], scalar1=1.0, scalar2=-130.0,
            op0=OP.mult, op1=OP.add), waits=[("cT", wq_end_ct)], inc="cV")
        for b in range(NDT):
            bk, freev, ki = bksX.get()
            ct = sc.op("tensor", lambda h, bk=bk, b=b: h.transpose(
                bk[:, 0:128], xq_t[:, 128 * b:128 * (b + 1)], ident_b[:]),
                waits=[("cV", xqdone), ("cV", freev)], inc="cT")
            wv = [("cT", ct)] + st_sxq.wait_for(len(xq_dmas) - 2)
            if len(xq_dmas) == 0 and b == 0:
                wv += wq_end_ds
            ev = sc.op("vector", lambda h, bk=bk, b=b, par=par: h.tensor_copy(
                stC[:, par * 1024 + 128 * b: par * 1024 + 128 * (b + 1)],
                bk[:, 0:128]), waits=wv, inc="cV")
            bksX.set_free(ki, ev)
        dstx = xqT_d[:, 128 * t:128 * (t + 1)].rearrange("(b p) e -> p b e", p=128)
        srcx = stC[:, par * 1024:(par + 1) * 1024].rearrange("p (b e) -> p b e", e=128)
        st_sxq.dma(lambda h, dstx=dstx, srcx=srcx: h.dma_start(
            out=dstx, in_=srcx), waits=[("cV", sc.v("cV"))])
        xq_dmas.append(st_sxq.n)
    ct = sc.op("tensor", lambda h: h.transpose(
        pC[2][0:16, 0:128], rx1[:], ident_f[:]),
        waits=[("cV", sc.v("cV"))], inc="cT")
    rxcv = sc.op("vector", lambda h: h.tensor_copy(rxrow_s[:], pC[2][0:16, 0:128]),
                 waits=[("cT", ct)], inc="cV")
    st_srx.dma(lambda h: h.dma_start(out=rx_d, in_=rxrow_s[:]),
               waits=[("cV", rxcv)])
    sc.barrier()
    esC.close()

    esD = ExitStack()
    GW = 2048
    hsl = [sb(esD, f"hsl{i}", [128, NHC * CH], BF16) for i in range(2)]
    gsl = sb(esD, "gsl", [128, NHC * CH], F32)
    xqc = sb(esD, "xqc", [128, 2 * NDT * CH], BF16)
    w2c = sb(esD, "w2c", [128, 2 * 4 * E], BF16)
    rxw = sb(esD, "rxw", [128, CH], F32)
    s2b = sb(esD, "s2b", [128, CH], F32)
    s2rep = sb(esD, "s2rep", [128, GW], F32)
    s2r2 = sb(esD, "s2r2", [2, 128], F32)
    rxr = sb(esD, "rxr", [1, CH], F32)
    rxr2 = sb(esD, "rxr2", [1, CH], F32)
    mxt = sb(esD, "mxt", [128, 8], F32)
    rx2d = sb(esD, "rx2d", [128, 2], F32)
    macc = sb(esD, "macc", [128, CH], F32)
    cacc = sb(esD, "cacc", [128, 2 * CH], F32)
    cacc2 = sb(esD, "cacc2", [128, 2 * CH], F32)
    cacc3 = sb(esD, "cacc3", [128, 2 * CH], F32)
    halo = [sb(esD, f"halo{i}", [128, NHC], BF16) for i in range(3)]
    zcol = sb(esD, "zcol", [128, NHC], BF16)
    tq2 = sb(esD, "tq2", [128, GW], F32)
    uq2 = esD.enter_context(nc.sbuf_tensor("uq2", [128, GW], I32))
    fr2 = sb(esD, "fr2", [128, GW], F32)
    y_sb = sb(esD, "y_sb", [128, E], F32)
    pD = [esD.enter_context(nc.psum_tensor(f"pD{i}", [128, 512], F32)) for i in range(8)]

    sc.op("gpsimd", lambda h: h.memset(zcol[:], 0.0), inc="cG")
    cG_z = sc.v("cG")

    xq_dh = {}
    h_done = {}
    rxr_free = [0]
    rxr2_free = [0]
    w2_prev = [0]
    ffn2_done = {}
    pd6_free = 0
    pd45_free = 0
    y_dmas = []
    gm_prev = [0, 0, 0, 0]

    def load_xq_chunk(k):
        dst = xqc[:, (k % 2) * NDT * CH:(k % 2 + 1) * NDT * CH] \
            .rearrange("p (b s) -> p b s", s=CH)
        src = xqT_d[:, CH * k:CH * (k + 1)].rearrange("(b p) s -> p b s", p=128)
        w = [("cT", h_done.get(k - 2, (0, 0))[1])] if k >= 2 else []
        return st_lq.dma(lambda h, dst=dst, src=src: h.dma_start(
            out=dst, in_=src), waits=w)

    def ffn1(k):
        nonlocal pd6_free
        dh = xq_dh[k]
        kk = k % 2
        i = st_lr.dma(lambda h, k=k: h.dma_start(
            out=rxr[0:1, :].rearrange("o (a b) -> o a b", a=2),
            in_=rx_d[2 * k:2 * k + 2, :]),
            waits=[("cT", rxr_free[0])] + st_srx.wait_all())
        ct = sc.op("tensor", lambda h: h.matmul(
            pD[6][:, 0:CH], onesc_f[0:1, :], rxr[0:1, :], start=True, stop=True),
            waits=st_lr.wait_for(i) + [("cV", pd6_free)], inc="cT")
        rxr_free[0] = ct
        rxwd = sc.op("vector", lambda h: h.tensor_scalar(
            out=rxw[:], in0=pD[6][:, 0:CH], scalar1=w1sc[:, 0:1], scalar2=None,
            op0=OP.mult), waits=[("cT", ct)], inc="cV")
        pd6_free = rxwd
        bks = Banks(pD[0:4], free0=sc.v("cV"))
        hguard = ffn2_done.get(k - 2, 0)
        for hc in range(NHC):
            bk, freev, ki = bks.get()
            for dt in range(NDT):
                w = st_lq.wait_for(dh) + ([("cV", freev)] if dt == 0 else [])
                sc.op("tensor", lambda h, bk=bk, hc=hc, dt=dt, kk=kk: h.matmul(
                    bk[:, 0:CH],
                    w1qT[:, dt * HC + 128 * hc: dt * HC + 128 * (hc + 1)],
                    xqc[:, kk * NDT * CH + dt * CH: kk * NDT * CH + (dt + 1) * CH],
                    start=(dt == 0), stop=(dt == NDT - 1)),
                    waits=w, inc="cT" if dt == NDT - 1 else None)
            ev = sc.op("vector", lambda h, bk=bk, hc=hc, kk=kk: h.tensor_mul(
                hsl[kk][:, hc * CH:(hc + 1) * CH], bk[:, 0:CH], rxw[:]),
                waits=[("cT", sc.v("cT")), ("cV", rxwd), ("cT", hguard)], inc="cV")
            bks.set_free(ki, ev)
        mm_ct = sc.v("cT")
        sc.drain("vector")
        snap = sc.op("vector", lambda h, kk=kk, k=k: h.tensor_copy(
            halo[k % 3][:], hsl[kk][:].rearrange("p (c s) -> p c s", s=CH)[:, :, CH - 1]),
            inc="cV")
        return (snap, mm_ct)

    def conv_quant_ffn2(m, last):
        nonlocal pd6_free, pd45_free
        km = m % 2
        hm = hsl[km]
        hl = halo[(m - 1) % 3] if m >= 1 else zcol
        h_cv = h_done[m][0]
        r_cv = h_done[m + 1][0] if not last else h_done[m][0]
        v1_at = [0, 0]
        sil_at = {}
        vC_at = {}

        def emit_sil(hc):
            C = cacc3[:, (hc % 2) * CH:(hc % 2 + 1) * CH]
            wsil = [("cV", vC_at[hc])]
            if m >= 1:
                wsil.append(("cV", gm_prev[hc // 8]))
            sil_at[hc] = sc.op("scalar", lambda h, hc=hc, C=C: h.activation(
                out=gsl[:, hc * CH:(hc + 1) * CH], in_=C, func=AF.Silu,
                bias=cb_sb[:, hc:hc + 1]), waits=wsil, inc="cS")
        for hc in range(NHC):
            par = hc % 2
            A = cacc[:, par * CH:(par + 1) * CH]
            B = cacc2[:, par * CH:(par + 1) * CH]
            C = cacc3[:, par * CH:(par + 1) * CH]
            w = [("cV", v1_at[par]), ("cV", h_cv)]
            if m == 0 and hc == 0:
                w.append(("cG", cG_z))
            sc.op("scalar", lambda h, hc=hc, hm=hm, A=A: h.mul(
                A[:, 1:CH], hm[:, hc * CH: hc * CH + CH - 1],
                cw_sb[:, 3 * hc:3 * hc + 1]), waits=w)
            sA = sc.op("scalar", lambda h, hc=hc, hl=hl, A=A: h.mul(
                A[:, 0:1], hl[:, hc:hc + 1], cw_sb[:, 3 * hc:3 * hc + 1]),
                inc="cS")
            if hc >= 2:
                emit_sil(hc - 2)
            v1_at[par] = sc.op("vector", lambda h, hc=hc, hm=hm, A=A, B=B:
                               h.scalar_tensor_tensor(
                out=B, in0=hm[:, hc * CH:(hc + 1) * CH],
                scalar=cw_sb[:, 3 * hc + 1:3 * hc + 2], in1=A,
                op0=OP.mult, op1=OP.add), waits=[("cS", sA)], inc="cV")
            sc.drain("vector")
            wv2 = [("cS", sil_at.get(hc - 2, 0))]
            if last and hc == 0:
                wv2.append(("cG", cG_z))
            sc.op("vector", lambda h, hc=hc, hm=hm, B=B, C=C:
                  h.scalar_tensor_tensor(
                out=C[:, 0:CH - 1], in0=hm[:, hc * CH + 1:(hc + 1) * CH],
                scalar=cw_sb[:, 3 * hc + 2:3 * hc + 3], in1=B[:, 0:CH - 1],
                op0=OP.mult, op1=OP.add), waits=wv2)
            if last:
                rsl = zcol[:, hc:hc + 1]
            else:
                rsl = hsl[(m + 1) % 2][:, hc * CH: hc * CH + 1]
            vC = sc.op("vector", lambda h, hc=hc, rsl=rsl, B=B, C=C:
                       h.scalar_tensor_tensor(
                out=C[:, CH - 1:CH], in0=rsl,
                scalar=cw_sb[:, 3 * hc + 2:3 * hc + 3], in1=B[:, CH - 1:CH],
                op0=OP.mult, op1=OP.add), waits=[("cV", r_cv)], inc="cV")
            sc.drain("vector")
            vC_at[hc] = vC
        emit_sil(NHC - 2)
        emit_sil(NHC - 1)
        sil_done = sil_at[NHC - 1]
        mdone = sc.op("vector", lambda h: h.tensor_reduce(
            out=macc[:], in_=gsl[:].rearrange("p (c s) -> p s c", c=NHC),
            axis=AX, op=OP.max, apply_absolute_value=True),
            waits=[("cS", sil_done)], inc="cV")
        for blk in range(CH // 128):
            ct = sc.op("tensor", lambda h, blk=blk: h.transpose(
                pD[4 + blk][:, 0:128], macc[:, 128 * blk:128 * (blk + 1)],
                ident_f[:]), waits=[("cV", mdone), ("cV", pd45_free)], inc="cT")
            sc.op("vector", lambda h, blk=blk: h.tensor_reduce(
                out=mxt[:, blk:blk + 1], in_=pD[4 + blk][:, 0:128], axis=AX,
                op=OP.max), waits=[("cT", ct)], inc="cV")
        pd45_free = sc.v("cV")
        sc.drain("vector")
        sc.op("vector", lambda h: h.tensor_scalar(
            out=mxt[:, 2:4], in0=mxt[:, 0:2], scalar1=1e-5, scalar2=None,
            op0=OP.max))
        sc.drain("vector")
        sc.op("vector", lambda h: h.reciprocal(mxt[:, 4:6], mxt[:, 2:4]))
        sc.drain("vector")
        sc.op("vector", lambda h: h.tensor_scalar(
            out=mxt[:, 6:8], in0=mxt[:, 4:6], scalar1=127.0, scalar2=None,
            op0=OP.mult))
        scq = sc.op("vector", lambda h: h.tensor_scalar(
            out=rx2d[:], in0=mxt[:, 2:4], scalar1=w2sc[:, 0:1], scalar2=1.0 / 127.0,
            op0=OP.mult, op1=OP.mult), inc="cV")
        ct = sc.op("tensor", lambda h: h.transpose(
            pD[7][0:2, 0:128], mxt[:, 6:8], ident_f[:]),
            waits=[("cV", scq), ("cV", pd6_free)], inc="cT")
        s2rd = sc.op("vector", lambda h: h.tensor_copy(s2r2[:], pD[7][0:2, 0:128]),
                     waits=[("cT", ct)] + st_sm.wait_all(), inc="cV")
        i2 = st_sm.dma(lambda h: h.dma_start(
            out=rxr2[0:1, :].rearrange("o (a b) -> o a b", a=2), in_=s2r2[:]),
            waits=[("cV", s2rd), ("cT", rxr2_free[0])])
        ct = sc.op("tensor", lambda h: h.matmul(
            pD[6][:, 0:CH], onesc_f[0:1, :], rxr2[0:1, :], start=True, stop=True),
            waits=st_sm.wait_for(i2) + [("cV", pd6_free)], inc="cT")
        rxr2_free[0] = ct
        s2bd0 = sc.op("vector", lambda h: h.tensor_copy(s2b[:], pD[6][:, 0:CH]),
                      waits=[("cT", ct)], inc="cV")
        pd6_free = s2bd0
        sc.drain("vector")
        sc.op("vector", lambda h: h.tensor_copy(s2rep[:, 0:CH], s2b[:]))
        sc.drain("vector")
        for r in range(3):
            w0 = CH * (1 << r)
            sc.op("vector", lambda h, w0=w0: h.tensor_copy(
                s2rep[:, w0:2 * w0], s2rep[:, 0:w0]),
                inc="cV" if w0 * 2 == GW else None)
            sc.drain("vector")
        s2bd = sc.v("cV")
        q2_at = {}
        for gq in range(NHC * CH // GW):
            sl = slice(gq * GW, (gq + 1) * GW)
            sc.op("vector", lambda h, sl=sl: h.tensor_tensor(
                out=tq2[:], in0=gsl[:, sl], in1=s2rep[:], op=OP.mult),
                waits=[("cV", s2bd), ("cS", sil_at[(gq + 1) * (GW // CH) - 1])])
            sc.drain("vector")
            gm_prev[gq] = sc.op("vector", lambda h: h.tensor_scalar(
                out=uq2[:], in0=tq2[:], scalar1=1.0, scalar2=128.5,
                op0=OP.mult, op1=OP.add), inc="cV")
            sc.drain("vector")
            sc.op("vector", lambda h: h.tensor_sub(fr2[:], tq2[:], uq2[:]))
            sc.drain("vector")
            for thr in (-129.5, -128.5, -127.5, -126.5):
                sc.op("vector", lambda h, thr=thr: h.scalar_tensor_tensor(
                    out=tq2[:], in0=fr2[:], scalar=thr,
                    in1=uq2[:] if thr == -129.5 else tq2[:],
                    op0=OP.is_ge, op1=OP.add))
                sc.drain("vector")
            q2_at[gq] = sc.op("vector", lambda h, sl=sl, km=km: h.tensor_scalar(
                out=hsl[km][:, sl], in0=tq2[:], scalar1=1.0, scalar2=-130.0,
                op0=OP.mult, op1=OP.add), inc="cV")
            sc.drain("vector")
        piece_ct = {}
        for piece in range(NHC // 4):
            par = piece % 2
            dstw = w2c[:, par * 4 * E:(par + 1) * 4 * E] \
                .rearrange("p (b e) -> p b e", e=E)
            srcw = w2qT_d.rearrange("(b p) e -> p b e", p=128)[:, 4 * piece:4 * piece + 4, :]
            wld = [("cT", piece_ct.get(piece - 2, w2_prev[0]))]
            dhw = st_lw2.dma(lambda h, dstw=dstw, srcw=srcw: h.dma_start(
                out=dstw, in_=srcw), waits=wld)
            for pidx in range(4):
                hc = 4 * piece + pidx
                for sl2 in range(CH // 128):
                    for nch2 in range(2):
                        bky = pD[sl2 * 2 + nch2]
                        w = [("cV", q2_at[hc // 8])] + st_lw2.wait_for(dhw)
                        sc.op("tensor", lambda h, bky=bky, hc=hc, sl2=sl2, nch2=nch2,
                              km=km, par=par, pidx=pidx: h.matmul(
                            bky[:, 0:512],
                            hsl[km][:, hc * CH + 128 * sl2: hc * CH + 128 * (sl2 + 1)],
                            w2c[:, par * 4 * E + pidx * E + 512 * nch2:
                                par * 4 * E + pidx * E + 512 * (nch2 + 1)],
                            start=(hc == 0), stop=(hc == NHC - 1)),
                            waits=w,
                            inc="cT" if (pidx == 3 and sl2 == CH // 128 - 1
                                         and nch2 == 1) else None)
            piece_ct[piece] = sc.v("cT")
        ffn2_done[m] = sc.v("cT")
        w2_prev[0] = ffn2_done[m]
        for sl2 in range(CH // 128):
            for nch2 in range(2):
                bky = pD[sl2 * 2 + nch2]
                wv = [("cT", ffn2_done[m])] + st_sy.wait_for(len(y_dmas) - 1)
                sc.op("vector", lambda h, bky=bky, sl2=sl2, nch2=nch2: h.tensor_scalar(
                    out=y_sb[:, 512 * nch2: 512 * (nch2 + 1)],
                    in0=bky[:, 0:512], scalar1=rx2d[:, sl2:sl2 + 1], scalar2=None,
                    op0=OP.mult), waits=wv, inc="cV")
            st_sy.dma(lambda h, m=m, sl2=sl2: h.dma_start(
                out=y_d[CH * m + 128 * sl2: CH * m + 128 * (sl2 + 1), :],
                in_=y_sb[:]),
                waits=[("cV", sc.v("cV"))])
            y_dmas.append(st_sy.n)

    xq_dh[0] = load_xq_chunk(0)
    h_done[0] = ffn1(0)
    for m in range(NCH):
        if m + 1 < NCH:
            xq_dh[m + 1] = load_xq_chunk(m + 1)
            h_done[m + 1] = ffn1(m + 1)
        conv_quant_ffn2(m, last=(m == NCH - 1))
    sc.barrier()
    esD.close()
    esCD.close()

    esE = ExitStack()
    xa2 = sb(esE, "xa2", [128, 2 * E], F32)
    yb2 = sb(esE, "yb2", [128, 2 * E], F32)
    r2 = sb(esE, "r2", [128, E], F32)
    t2s = sb(esE, "t2s", [128, E], F32)
    ot = sb(esE, "ot", [128, 2 * E], F32)
    stE = sb(esE, "stE", [128, 12], F32)
    out_dmas = []
    for t in range(NT):
        par = t % 2
        xat = xa2[:, par * E:(par + 1) * E]
        ybt = yb2[:, par * E:(par + 1) * E]
        w = [("cV", sc.v("cV") if t >= 2 else 0)]
        i1 = st_le.dma(lambda h, xat=xat, t=t: h.dma_start(
            out=xat, in_=x2_d[128 * t:128 * (t + 1), :]), waits=w)
        i2 = st_le.dma(lambda h, ybt=ybt, t=t: h.dma_start(
            out=ybt, in_=y_d[128 * t:128 * (t + 1), :]))
        sc.drain("vector")
        sc.op("vector", lambda h, xat=xat, ybt=ybt: h.tensor_add(r2[:], xat, ybt),
              waits=st_le.wait_for(i1) + st_le.wait_for(i2), inc="cV")
        sc.drain("vector")
        sc.op("vector", lambda h: h.reduce_sum(stE[:, 0:1], r2[:], axis=AX))
        sc.drain("vector")
        sc.op("vector", lambda h: h.tensor_scalar(
            out=stE[:, 1:2], in0=stE[:, 0:1], scalar1=-1.0 / E, scalar2=None,
            op0=OP.mult), inc="cV")
        sqd = sc.op("scalar", lambda h: h.activation(
            out=t2s[:], in_=r2[:], func=AF.Square, bias=stE[:, 1:2],
            accum_out=stE[:, 2:3]), waits=[("cV", sc.v("cV"))], inc="cS")
        sc.op("vector", lambda h: h.tensor_scalar(
            out=stE[:, 3:4], in0=stE[:, 2:3], scalar1=1.0 / E, scalar2=1e-5,
            op0=OP.mult, op1=OP.add), waits=[("cS", sqd)], inc="cV")
        lnd = sc.op("scalar", lambda h: h.activation(
            out=stE[:, 4:5], in_=stE[:, 3:4], func=AF.Ln),
            waits=[("cV", sc.v("cV"))], inc="cS")
        rsd = sc.op("scalar", lambda h: h.activation(
            out=stE[:, 5:6], in_=stE[:, 4:5], func=AF.Exp, scale=cst[:, 1:2]),
            waits=[("cS", lnd)], inc="cS")
        otp = ot[:, par * E:(par + 1) * E]
        w2 = [("cS", rsd)] + st_se.wait_for(len(out_dmas) - 2)
        odone = sc.op("vector", lambda h, otp=otp: h.tensor_scalar(
            out=otp, in0=r2[:], scalar1=stE[:, 1:2], scalar2=stE[:, 5:6],
            op0=OP.add, op1=OP.mult), waits=w2, inc="cV")
        st_se.dma(lambda h, otp=otp, t=t: h.dma_start(
            out=out_d[128 * t:128 * (t + 1), :], in_=otp),
            waits=[("cV", odone)])
        out_dmas.append(st_se.n)
    sc.op("sync", None, waits=st_se.wait_all())
    esE.close()

    with nc.Block() as block:
        sc.emit(block, sems)
    root.close()
    return nc


def kernel(L0_attn_x, L0_attn_Wq, L0_attn_Wk, L0_attn_Wv, L0_attn_rel_table,
           L0_attn_gamma, L0_attn_beta, L0_ffn_w1, L0_ffn_conv_w, L0_ffn_conv_b,
           L0_ffn_w2):
    x = np.ascontiguousarray(np.asarray(L0_attn_x, dtype=np.float32))
    T = np.asarray(L0_attn_rel_table, dtype=np.float32)

    t_ext = np.concatenate([T, np.repeat(T[-1:, :], 2048, axis=0)], axis=0)
    dvals = np.arange(-15, 16)
    p_i = np.arange(128)
    a_i = np.arange(128)
    idx = (2047 + 128 * dvals[:, None, None] + p_i[None, :, None]
           - a_i[None, None, :])
    toe = np.ascontiguousarray(
        t_ext[idx].transpose(3, 0, 1, 2))

    cwv = np.asarray(L0_ffn_conv_w, dtype=np.float32).reshape(NHC, 128, 3)
    cw = np.ascontiguousarray(cwv.transpose(1, 0, 2).reshape(128, NHC * 3))
    cb = np.ascontiguousarray(
        np.asarray(L0_ffn_conv_b, dtype=np.float32).reshape(NHC, 128).T)

    base = dict(
        wq=np.ascontiguousarray(L0_attn_Wq, dtype=np.float32),
        wk=np.ascontiguousarray(L0_attn_Wk, dtype=np.float32),
        wv=np.ascontiguousarray(L0_attn_Wv, dtype=np.float32),
        toe=toe,
        gam=np.ascontiguousarray(np.asarray(L0_attn_gamma, np.float32)[None, :]),
        bet=np.ascontiguousarray(np.asarray(L0_attn_beta, np.float32)[None, :]),
        w1=np.ascontiguousarray(L0_ffn_w1, dtype=np.float32),
        cw=cw, cb=cb,
        w2=np.ascontiguousarray(L0_ffn_w2, dtype=np.float32),
    )
    in_maps = [dict(base, x=np.ascontiguousarray(x[b])) for b in range(8)]

    prog = build_program()
    res = run_bass_kernel_spmd(prog, in_maps, core_ids=list(range(8)))
    key = {0: "out", 1: "o_s", 2: "x2_s", 3: "y_s"}[DEBUG]
    return np.stack([np.asarray(r[key]) for r in res.results], axis=0)


# version 10
# speedup vs baseline: 1.4726x; 1.0034x over previous
import numpy as np
import ml_dtypes
from contextlib import ExitStack

import concourse.bass as bass
import concourse.mybir as mybir
from concourse.bass_utils import run_bass_kernel_spmd
from concourse.alu_op_type import AluOpType as OP

F32 = mybir.dt.float32
BF16 = mybir.dt.bfloat16
I32 = mybir.dt.int32
AF = mybir.ActivationFunctionType
AX = mybir.AxisListType.X
bfdt = ml_dtypes.bfloat16

S, E, H, DH, HC = 2048, 1024, 16, 64, 4096
NT = S // 128
NP = H // 2
NDT = E // 128
NHC = HC // 128
CH = 256
NCH = S // CH
NW = 18
DEBUG = 0

ENGINES = ("sync", "gpsimd", "vector", "scalar", "tensor")


class Sched:
    def __init__(self):
        self.ops = {e: [] for e in ENGINES}
        self.cnt = {}
        self.last_wait = {e: {} for e in ENGINES}

    def v(self, sem):
        return self.cnt.get(sem, 0)

    def op(self, eng, fn, waits=(), inc=None, inc_n=1, bare_inc=False):
        w = []
        for s, val in waits:
            if val > 0 and self.last_wait[eng].get(s, -1) < val:
                w.append((s, val))
                self.last_wait[eng][s] = val
        self.ops[eng].append((fn, w, inc, inc_n, bare_inc))
        if inc is not None:
            self.cnt[inc] = self.cnt.get(inc, 0) + inc_n
        return self.cnt.get(inc, 0) if inc else None

    def drain(self, eng):
        self.op(eng, lambda h: h.drain())

    def barrier(self):
        for st in getattr(self, "streams", []):
            for w in st.wait_all():
                self.op("sync", None, waits=(w,))
        for e in ENGINES:
            self.op(e, None, inc="bar", inc_n=1, bare_inc=True)
        tot = self.v("bar")
        for e in ENGINES:
            self.op(e, None, waits=(("bar", tot),))

    def emit(self, block, sems):
        def make(ename):
            def body(h):
                for fn, waits, inc, inc_n, bare in self.ops[ename]:
                    for s, val in waits:
                        h.wait_ge(sems[s], val)
                    if bare:
                        h.sem_inc(sems[inc], inc_n)
                        continue
                    if fn is None:
                        continue
                    instr = fn(h)
                    if inc is not None:
                        instr.then_inc(sems[inc], inc_n)
            return body
        block.sync(make("sync"))
        block.gpsimd(make("gpsimd"))
        block.vector(make("vector"))
        block.scalar(make("scalar"))
        block.tensor(make("tensor"))


class Stream:

    def __init__(self, sc, eng, sems, names):
        self.sc = sc
        self.eng = eng
        self.sems = names
        self.counts = [0] * len(names)
        self.n = 0

    def dma(self, fn, waits=()):
        i = self.n
        k = i % len(self.sems)
        self.n += 1
        self.counts[k] += 1
        self.sc.op(self.eng, fn, waits=waits, inc=self.sems[k], inc_n=16)
        return i

    def wait_for(self, i):
        if i < 0:
            return []
        k = i % len(self.sems)
        return [(self.sems[k], 16 * (i // len(self.sems) + 1))]

    def wait_last(self):
        return self.wait_for(self.n - 1)

    def wait_all(self):
        return [(s, 16 * c) for s, c in zip(self.sems, self.counts) if c > 0]


class Banks:

    def __init__(self, banks, free0=0):
        self.banks = banks
        self.free_at = [free0] * len(banks)
        self.i = 0

    def get(self):
        k = self.i % len(self.banks)
        self.i += 1
        return self.banks[k], self.free_at[k], k

    def set_free(self, k, cv):
        self.free_at[k] = cv


def build_program():
    nc = bass.Bass("TRN2", target_bir_lowering=False)

    x_d = nc.dram_tensor("x", [S, E], F32, kind="ExternalInput").ap()
    wq_d = nc.dram_tensor("wq", [E, E], F32, kind="ExternalInput").ap()
    wk_d = nc.dram_tensor("wk", [E, E], F32, kind="ExternalInput").ap()
    wv_d = nc.dram_tensor("wv", [E, E], F32, kind="ExternalInput").ap()
    toe_d = nc.dram_tensor("toe", [16, 31, 128, 128], F32, kind="ExternalInput").ap()
    gam_d = nc.dram_tensor("gam", [1, E], F32, kind="ExternalInput").ap()
    bet_d = nc.dram_tensor("bet", [1, E], F32, kind="ExternalInput").ap()
    w1_d = nc.dram_tensor("w1", [HC, E], F32, kind="ExternalInput").ap()
    cw_d = nc.dram_tensor("cw", [128, 3 * NHC], F32, kind="ExternalInput").ap()
    cb_d = nc.dram_tensor("cb", [128, NHC], F32, kind="ExternalInput").ap()
    w2_d = nc.dram_tensor("w2", [E, HC], F32, kind="ExternalInput").ap()
    out_d = nc.dram_tensor("out", [S, E], F32, kind="ExternalOutput").ap()

    qT_d = nc.dram_tensor("qT_s", [E, S], BF16, kind="Internal").ap()
    kT_d = nc.dram_tensor("kT_s", [E, S], BF16, kind="Internal").ap()
    o_d = nc.dram_tensor("o_s", [S, E], F32,
                         kind="ExternalOutput" if DEBUG == 1 else "Internal").ap()
    x2_d = nc.dram_tensor("x2_s", [S, E], F32,
                          kind="ExternalOutput" if DEBUG == 2 else "Internal").ap()
    y_d = nc.dram_tensor("y_s", [S, E], F32,
                         kind="ExternalOutput" if DEBUG == 3 else "Internal").ap()
    w2qT_d = nc.dram_tensor("w2qT_s", [HC, E], BF16, kind="Internal").ap()
    v_sd = nc.dram_tensor("v_sd", [S, E], BF16, kind="Internal").ap()
    rx_d = nc.dram_tensor("rx_d", [NT, 128], F32, kind="Internal").ap()
    mx_d = nc.dram_tensor("mx_d", [2, 128], F32, kind="Internal").ap()
    xqT_d = nc.dram_tensor("xqT_s", [E, S], BF16, kind="Internal").ap()

    sc = Sched()
    root = ExitStack()
    sems = {}
    stream_sems = ["gax", "toe", "cwl", "qk", "vl", "vs",
                   "sq0", "sq1", "so0", "so1", "wl0", "wl1", "sw",
                   "lc0", "lc1", "lc2", "lc3", "sx2", "sxq0", "sxq1", "srx",
                   "lq0", "lq1", "lw20", "lw21", "lr", "sm", "lm",
                   "sy0", "sy1", "le0", "le1", "le2", "le3", "se0", "se1"]
    for s in ["cT", "cV", "cS", "cG", "bar"] + stream_sems:
        sems[s] = root.enter_context(nc.semaphore(s))
    sc.streams = []

    def mkstream(eng, names):
        st = Stream(sc, eng, sems, names)
        sc.streams.append(st)
        return st

    st_gax = mkstream("gpsimd", ["gax"])
    st_toe = mkstream("gpsimd", ["toe"])
    st_cwl = mkstream("sync", ["cwl"])
    st_qk = mkstream("sync", ["qk"])
    st_vl = mkstream("sync", ["vl"])
    st_vs = mkstream("scalar", ["vs"])
    st_sq = mkstream("scalar", ["sq0", "sq1"])
    st_so = mkstream("scalar", ["so0", "so1"])
    st_wl = mkstream("sync", ["wl0", "wl1"])
    st_sw = mkstream("scalar", ["sw"])
    st_lc = mkstream("sync", ["lc0", "lc1", "lc2", "lc3"])
    st_sx2 = mkstream("scalar", ["sx2"])
    st_sxq = mkstream("scalar", ["sxq0", "sxq1"])
    st_srx = mkstream("scalar", ["srx"])
    st_lq = mkstream("sync", ["lq0", "lq1"])
    st_lw2 = mkstream("sync", ["lw20", "lw21"])
    st_lr = mkstream("sync", ["lr"])
    st_sm = mkstream("scalar", ["sm"])
    st_lm = mkstream("sync", ["lm"])
    st_sy = mkstream("scalar", ["sy0", "sy1"])
    st_le = mkstream("sync", ["le0", "le1", "le2", "le3"])
    st_se = mkstream("sync", ["se0", "se1"])

    def sb(stack, name, shape, dt=F32):
        return stack.enter_context(nc.sbuf_tensor(name, shape, dt))

    ident_b = sb(root, "ident_b", [128, 128], BF16)
    ident_f = sb(root, "ident_f", [128, 128], F32)
    onesc_f = sb(root, "onesc_f", [1, 128], F32)
    ones65 = sb(root, "ones65", [65, 128], F32)
    cst = sb(root, "cst", [128, 4], F32)
    rx1 = sb(root, "rx1", [128, NT], F32)
    cw_sb = sb(root, "cw_sb", [128, 3 * NHC], F32)
    cb_sb = sb(root, "cb_sb", [128, NHC], F32)
    w1sc = sb(root, "w1sc", [128, 2], F32)
    rxrow_s = sb(root, "rxrow_s", [16, 128], F32)
    w2sc = sb(root, "w2sc", [128, 2], F32)
    wsumB = sb(root, "wsumB", [128, 48], F32)
    esAB = ExitStack()
    v_pad = sb(esAB, "v_pad", [128, NT * H * 65], BF16)
    v_km = sb(esAB, "v_km", [128, 16 * E], BF16)

    esA = ExitStack()
    x_bf = sb(esA, "x_bf", [128, NT * E], BF16)
    xT = sb(esA, "xT", [128, NDT * S], BF16)
    wq_sb = sb(esA, "wq_sb", [128, NDT * E], BF16)
    wk_sb = sb(esA, "wk_sb", [128, NDT * E], BF16)
    wv_sb = sb(esA, "wv_sb", [128, NDT * E], BF16)
    stgA = sb(esA, "stgA", [128, 2 * 512], BF16)
    pA = [esA.enter_context(nc.psum_tensor(f"pA{i}", [128, 512], F32)) for i in range(6)]
    pAb = [esA.enter_context(nc.psum_tensor(f"pAb{i}", [128, 1024], BF16)) for i in range(2)]

    sc.op("gpsimd", lambda h: h.memset(cst[:, 0:1], 0.03125))
    sc.op("gpsimd", lambda h: h.memset(cst[:, 1:2], -0.5))
    sc.op("gpsimd", lambda h: h.memset(onesc_f[:], 1.0))
    sc.op("gpsimd", lambda h: h.memset(ones65[:], 1.0))
    sc.op("gpsimd", lambda h: h.memset(ident_b[:], 0.0))
    sc.op("gpsimd", lambda h: h.memset(ident_f[:], 0.0))
    sc.drain("gpsimd")
    sc.op("gpsimd", lambda h: h.affine_select(out=ident_b[:], in_=ident_b[:],
          compare_op=OP.not_equal, fill=1.0, base=0, pattern=[[-1, 128]],
          channel_multiplier=1))
    sc.op("gpsimd", lambda h: h.affine_select(out=ident_f[:], in_=ident_f[:],
          compare_op=OP.not_equal, fill=1.0, base=0, pattern=[[-1, 128]],
          channel_multiplier=1), inc="cG")
    vp4 = v_pad[:].rearrange("p (t h e) -> p t h e", t=NT, h=H, e=65)
    sc.op("gpsimd", lambda h: h.memset(vp4[:, :, :, 64:65], 1.0), inc="cG")

    st_gax.dma(lambda h: h.dma_start(
        out=x_bf[:].rearrange("p (t d) -> p t d", d=E),
        in_=x_d.rearrange("(t p) d -> p t d", p=128)))
    for wd, wsb in ((wq_d, wq_sb), (wk_d, wk_sb), (wv_d, wv_sb)):
        st_gax.dma(lambda h, wd=wd, wsb=wsb: h.dma_start(
            out=wsb[:].rearrange("p (dt e) -> p dt e", e=E),
            in_=wd.rearrange("(dt p) e -> p dt e", p=128)))
    dG_A = st_gax.wait_all()

    st_cwl.dma(lambda h: h.dma_start(out=cw_sb[:], in_=cw_d))
    st_cwl.dma(lambda h: h.dma_start(out=cb_sb[:], in_=cb_d))

    bksA = Banks(pAb)
    for t in range(NT):
        for dt in range(NDT):
            bk, freev, ki = bksA.get()
            sc.op("tensor", lambda h, bk=bk, t=t, dt=dt: h.transpose(
                bk[:, 0:128], x_bf[:, t * E + 128 * dt: t * E + 128 * (dt + 1)],
                ident_b[:]), waits=dG_A + [("cG", 1), ("cV", freev)], inc="cT")
            ev = sc.op("vector", lambda h, bk=bk, t=t, dt=dt: h.tensor_copy(
                xT[:, dt * S + 128 * t: dt * S + 128 * (t + 1)], bk[:, 0:128]),
                waits=[("cT", sc.v("cT"))], inc="cV")
            bksA.set_free(ki, ev)
    cV_xT = sc.v("cV")

    bksQ = Banks(pA[0:3])
    nstg = 0
    stg_dmas = []
    for wsb, dst in ((wq_sb, qT_d), (wk_sb, kT_d)):
        for j in range(NDT):
            for nch in range(4):
                bk, freev, ki = bksQ.get()
                for dt in range(NDT):
                    waits = [("cV", cV_xT)] + ([("cV", freev)] if dt == 0 else [])
                    sc.op("tensor", lambda h, bk=bk, wsb=wsb, j=j, nch=nch, dt=dt:
                          h.matmul(bk[:, 0:512],
                                   wsb[:, dt * E + 128 * j: dt * E + 128 * (j + 1)],
                                   xT[:, dt * S + 512 * nch: dt * S + 512 * (nch + 1)],
                                   start=(dt == 0), stop=(dt == NDT - 1)),
                          waits=waits, inc="cT" if dt == NDT - 1 else None)
                stx = stgA[:, (nstg % 2) * 512:(nstg % 2) * 512 + 512]
                wv2 = [("cT", sc.v("cT"))] + st_sq.wait_for(nstg - 2)
                ev = sc.op("vector", lambda h, stx=stx, bk=bk: h.tensor_copy(
                    stx, bk[:, 0:512]), waits=wv2, inc="cV")
                bksQ.set_free(ki, ev)
                st_sq.dma(lambda h, stx=stx, dst=dst, j=j, nch=nch:
                          h.dma_start(out=dst[128 * j:128 * (j + 1),
                                              512 * nch:512 * (nch + 1)], in_=stx),
                          waits=[("cV", ev)])
                nstg += 1
    v_nat = x_bf
    bksV = Banks(pA[3:6], free0=sc.v("cV"))
    for t in range(NT):
        for nch in range(2):
            bk, freev, ki = bksV.get()
            for dt in range(NDT):
                waits = [("cV", cV_xT)] + ([("cV", freev)] if dt == 0 else [])
                sc.op("tensor", lambda h, bk=bk, t=t, nch=nch, dt=dt:
                      h.matmul(bk[:, 0:512],
                               xT[:, dt * S + 128 * t: dt * S + 128 * (t + 1)],
                               wv_sb[:, dt * E + 512 * nch: dt * E + 512 * (nch + 1)],
                               start=(dt == 0), stop=(dt == NDT - 1)),
                      waits=waits, inc="cT" if dt == NDT - 1 else None)
            ev = sc.op("vector", lambda h, t=t, nch=nch, bk=bk: h.tensor_copy(
                v_nat[:, t * E + 512 * nch: t * E + 512 * (nch + 1)], bk[:, 0:512]),
                waits=[("cT", sc.v("cT"))], inc="cV")
            bksV.set_free(ki, ev)
    cV_vp = sc.v("cV")
    st_vs.dma(lambda h: h.dma_start(
        out=v_sd.rearrange("(t p) e -> p t e", p=128),
        in_=v_nat[:].rearrange("p (t e) -> p t e", e=E)),
        waits=[("cV", cV_vp)])
    dvs = st_vs.wait_all()
    for t in range(NT):
        st_vl.dma(lambda h, t=t: h.dma_start(
            out=vp4[:, t, :, 0:64],
            in_=v_sd[128 * t:128 * (t + 1), :].rearrange(
                "p (hh e) -> p hh e", e=64)),
            waits=dvs)
    st_vl.dma(lambda h: h.dma_start(
        out=v_km[:].rearrange("p (k c) -> p k c", k=16),
        in_=v_sd.rearrange("(p k) c -> p k c", k=16)))
    sc.barrier()
    esA.close()

    p1_cons = [0, 0]
    p1_ip = [0]
    p1_done = {}

    def p1_unit(wi, wd, tt, pc, pw, col, is_last):
        par = p1_ip[0] % 2
        p1_ip[0] += 1
        buf = wtB[:, par * 2048:par * 2048 + pw]
        i = st_wl.dma(lambda h, buf=buf, wd=wd, tt=tt, pc=pc, pw=pw:
                      h.dma_start(out=buf,
                                  in_=wd[128 * tt:128 * (tt + 1),
                                         pw * pc:pw * (pc + 1)]),
                      waits=[("cV", p1_cons[par])])
        p1_cons[par] = sc.op("vector", lambda h, buf=buf, col=col:
                             h.reduce_sum(wsumB[:, col:col + 1], buf, axis=AX,
                                          apply_absolute_value=True),
                             waits=st_wl.wait_for(i), inc="cV")
        if is_last:
            p1_done[wi] = p1_cons[par]

    p1_units = []
    for tt in range(NHC):
        p1_units.append((0, w1_d, tt, 0, E, tt, tt == NHC - 1))
    for tt in range(NDT):
        for pc in range(2):
            p1_units.append((1, w2_d, tt, pc, 2048, 32 + 2 * tt + pc,
                             tt == NDT - 1 and pc == 1))
    p1_next = [0]

    def p1_emit(n):
        while n > 0 and p1_next[0] < len(p1_units):
            p1_unit(*p1_units[p1_next[0]])
            p1_next[0] += 1
            n -= 1

    esB = ExitStack()
    wtB = sb(esB, "wtB", [128, 2 * 2048], F32)
    toe = sb(esB, "toe_sb", [128, 16 * NW * 128], BF16)
    qkp = sb(esB, "qkp", [128, 2 * S], BF16)
    p_bf = sb(esB, "p_bf", [128, 2 * 2048], BF16)
    pvA_sb = sb(esB, "pvA_sb", [65, 512], F32)
    pvB_sb = sb(esB, "pvB_sb", [65, 512], F32)
    bias_sb = sb(esB, "bias_sb", [128, 512], F32)
    rcp = sb(esB, "rcp", [128, 2], F32)
    tb_sb = sb(esB, "tb_sb", [128, 128], F32)
    o_stage = sb(esB, "o_stage", [128, 1024], F32)
    sAB = esB.enter_context(nc.psum_tensor("sAB", [128, 2048], F32))
    pvA = esB.enter_context(nc.psum_tensor("pvA", [128, 512], F32))
    pvB = esB.enter_context(nc.psum_tensor("pvB", [128, 512], F32))
    pb = [esB.enter_context(nc.psum_tensor(f"pb{i}", [128, 512], F32)) for i in range(2)]

    toev = toe[:].rearrange("p (k w a) -> p k w a", k=16, w=NW, a=128)

    def toe_load(dlt, wait_ct):
        slot = (dlt + 15) % NW
        w = [("cT", wait_ct)] if wait_ct > 0 else []
        st_toe.dma(lambda h, dlt=dlt, slot=slot: h.dma_start(
            out=toev[:, :, slot, :],
            in_=toe_d[:, dlt + 15].rearrange("k p a -> p k a")),
            waits=w)

    for dlt in range(-1, 16):
        toe_load(dlt, 0)
    dG_need = st_toe.wait_all()

    st_qk.dma(lambda h: h.dma_start(out=qkp[:, 0:S], in_=qT_d[0:128, :]))
    st_qk.dma(lambda h: h.dma_start(out=qkp[:, S:2 * S], in_=kT_d[0:128, :]))

    exp_done = 0
    o_dmas = []
    pv_free = 0
    tr_free = 0
    bias_bank_free = [0, 0]
    for j in range(NP):
        hA, hB = 2 * j, 2 * j + 1
        dH_pair = st_qk.wait_all()
        qperm = qkp[:, 0:S].rearrange("p (a c) -> p c a", a=128, c=16)
        for qc in range(4):
            c0 = 4 * qc
            slA0 = ((c0 - hA) + 15) % NW
            slB0 = ((c0 - hB) + 15) % NW
            fill = []
            for kap in range(16):
                w0 = []
                if kap == 0:
                    w0 = list(dG_need) + [("cV", bias_bank_free[0]),
                                          ("cV", bias_bank_free[1])]
                fill.append((lambda h, kap=kap, slA0=slA0, j=j: h.matmul(
                    pb[0][0:64, 0:128],
                    v_km[:, kap * E + 128 * j: kap * E + 128 * j + 64],
                    toe[:, kap * (NW * 128) + slA0 * 128:
                        kap * (NW * 128) + (slA0 + 1) * 128],
                    start=(kap == 0), stop=(kap == 15), tile_position=(0, 0)), w0))
                fill.append((lambda h, kap=kap, slB0=slB0, j=j: h.matmul(
                    pb[1][64:128, 0:128],
                    v_km[:, kap * E + 128 * j + 64: kap * E + 128 * (j + 1)],
                    toe[:, kap * (NW * 128) + slB0 * 128:
                        kap * (NW * 128) + (slB0 + 1) * 128],
                    start=(kap == 0), stop=(kap == 15), tile_position=(0, 64)), ()))
            fi = 0
            for g in range(8):
                u0, u1 = 2 * g, 2 * g + 1
                wqk = list(dH_pair)
                if exp_done:
                    wqk.append(("cS", exp_done))
                for ui, u in enumerate((u0, u1)):
                    off = ui * 1024
                    sc.op("tensor", lambda h, u=u, qc=qc, off=off, qperm=qperm:
                          h.matmul(sAB[:, off:off + 512],
                                   qkp[0:64, S + 128 * u: S + 128 * (u + 1)],
                                   qperm[0:64, 4 * qc:4 * qc + 4, :],
                                   start=True, stop=True),
                          waits=wqk if ui == 0 else ())
                    sc.op("tensor", lambda h, u=u, qc=qc, off=off, qperm=qperm:
                          h.matmul(sAB[:, off + 512:off + 1024],
                                   qkp[64:128, S + 128 * u: S + 128 * (u + 1)],
                                   qperm[64:128, 4 * qc:4 * qc + 4, :],
                                   start=True, stop=True),
                          inc="cT" if ui == 1 else None)
                qk_ct = sc.v("cT")
                if g > 0:
                    gp = (g - 1) % 2
                    for ui, u in enumerate((2 * g - 2, 2 * g - 1)):
                        wpv = []
                        if g == 1 and ui == 0 and pv_free:
                            wpv = [("cV", pv_free)]
                        sc.op("tensor", lambda h, u=u, hA=hA, gp=gp, ui=ui: h.matmul(
                            pvA[0:65, 0:512],
                            v_pad[:, u * (H * 65) + hA * 65: u * (H * 65) + hA * 65 + 65],
                            p_bf[:, gp * 2048 + ui * 1024: gp * 2048 + ui * 1024 + 512],
                            start=(u == 0), stop=False), waits=wpv)
                        sc.op("tensor", lambda h, u=u, hB=hB, gp=gp, ui=ui: h.matmul(
                            pvB[0:65, 0:512],
                            v_pad[:, u * (H * 65) + hB * 65: u * (H * 65) + hB * 65 + 65],
                            p_bf[:, gp * 2048 + ui * 1024 + 512:
                                 gp * 2048 + ui * 1024 + 1024],
                            start=(u == 0), stop=False))
                while fi < len(fill) and fi < 4 * (g + 1):
                    fn, w0 = fill[fi]
                    sc.op("tensor", fn, waits=w0,
                          inc="cT" if fi == len(fill) - 1 else None)
                    fi += 1
                exp_done = sc.op("scalar", lambda h, g=g: h.activation(
                    out=p_bf[:, (g % 2) * 2048:(g % 2) * 2048 + 2048],
                    in_=sAB[:], func=AF.Exp, scale=cst[:, 0:1]),
                    waits=[("cT", qk_ct), ("cG", 1)], inc="cS")
            while fi < len(fill):
                fn, w0 = fill[fi]
                sc.op("tensor", fn, waits=w0,
                      inc="cT" if fi == len(fill) - 1 else None)
                fi += 1
            bias_c0_ct = sc.v("cT")
            gp = 1
            for ui, u in enumerate((14, 15)):
                sc.op("tensor", lambda h, u=u, hA=hA, ui=ui: h.matmul(
                    pvA[0:65, 0:512],
                    v_pad[:, u * (H * 65) + hA * 65: u * (H * 65) + hA * 65 + 65],
                    p_bf[:, gp * 2048 + ui * 1024: gp * 2048 + ui * 1024 + 512],
                    start=False, stop=(u == 15)),
                    waits=[("cS", exp_done)])
                sc.op("tensor", lambda h, u=u, hB=hB, ui=ui: h.matmul(
                    pvB[0:65, 0:512],
                    v_pad[:, u * (H * 65) + hB * 65: u * (H * 65) + hB * 65 + 65],
                    p_bf[:, gp * 2048 + ui * 1024 + 512: gp * 2048 + ui * 1024 + 1024],
                    start=False, stop=(u == 15)),
                    inc="cT" if u == 15 else None)
            pv_done = sc.v("cT")
            sc.op("vector", lambda h: h.tensor_copy(pvA_sb[:], pvA[0:65, 0:512]),
                  waits=[("cT", pv_done)])
            pv_ev = sc.op("vector", lambda h: h.tensor_copy(
                pvB_sb[:], pvB[0:65, 0:512]), inc="cV")
            bias_ev = [0, 0, 0, 0]
            sc.op("vector", lambda h: h.tensor_copy(
                bias_sb[0:64, 0:128], pb[0][0:64, 0:128]),
                waits=[("cT", bias_c0_ct)])
            bias_ev[0] = sc.op("vector", lambda h: h.tensor_copy(
                bias_sb[64:128, 0:128], pb[1][64:128, 0:128]), inc="cV")
            for ci in range(1, 4):
                c = 4 * qc + ci
                slA = ((c - hA) + 15) % NW
                slB = ((c - hB) + 15) % NW
                for kap in range(16):
                    w0 = [("cV", bias_ev[ci - 1])] if kap == 0 else []
                    sc.op("tensor", lambda h, kap=kap, slA=slA, j=j: h.matmul(
                        pb[0][0:64, 0:128],
                        v_km[:, kap * E + 128 * j: kap * E + 128 * j + 64],
                        toe[:, kap * (NW * 128) + slA * 128:
                            kap * (NW * 128) + (slA + 1) * 128],
                        start=(kap == 0), stop=(kap == 15), tile_position=(0, 0)),
                        waits=w0)
                    sc.op("tensor", lambda h, kap=kap, slB=slB, j=j: h.matmul(
                        pb[1][64:128, 0:128],
                        v_km[:, kap * E + 128 * j + 64: kap * E + 128 * (j + 1)],
                        toe[:, kap * (NW * 128) + slB * 128:
                            kap * (NW * 128) + (slB + 1) * 128],
                        start=(kap == 0), stop=(kap == 15), tile_position=(0, 64)),
                        inc="cT" if kap == 15 else None)
                bct = sc.v("cT")
                sc.op("vector", lambda h, ci=ci: h.tensor_copy(
                    bias_sb[0:64, 128 * ci:128 * (ci + 1)], pb[0][0:64, 0:128]),
                    waits=[("cT", bct)])
                bias_ev[ci] = sc.op("vector", lambda h, ci=ci: h.tensor_copy(
                    bias_sb[64:128, 128 * ci:128 * (ci + 1)], pb[1][64:128, 0:128]),
                    inc="cV")
            bias_bank_free = [bias_ev[3], bias_ev[3]]
            half = (qc % 2) * 512
            last_stt = 0
            for ci in range(4):
                trbank = pb[ci % 2]
                wtr = [("cV", pv_ev)]
                if ci == 0:
                    wtr.append(("cV", bias_ev[3]))
                if ci >= 1:
                    wtr.append(("cV", last_stt))
                trA_ct = sc.op("tensor", lambda h, ci=ci: h.transpose(
                    pvA[:, 0:65], pvA_sb[0:65, 128 * ci:128 * (ci + 1)],
                    ident_f[0:65, 0:65]), waits=wtr, inc="cT")
                trB_ct = sc.op("tensor", lambda h, ci=ci: h.transpose(
                    pvB[:, 0:65], pvB_sb[0:65, 128 * ci:128 * (ci + 1)],
                    ident_f[0:65, 0:65]), inc="cT")
                trX_ct = sc.op("tensor", lambda h, ci=ci, trbank=trbank: h.transpose(
                    trbank[:, 0:128], bias_sb[:, 128 * ci:128 * (ci + 1)],
                    ident_f[:]), inc="cT")
                sc.op("vector", lambda h: h.reciprocal(rcp[:, 0:1], pvA[:, 64:65]),
                      waits=[("cT", trX_ct)])
                sc.drain("vector")
                sc.op("vector", lambda h: h.reciprocal(rcp[:, 1:2], pvB[:, 64:65]))
                sc.drain("vector")
                sc.op("vector", lambda h, trbank=trbank: h.tensor_copy(
                    tb_sb[:], trbank[:, 0:128]), inc="cV")
                sc.drain("vector")
                wst = st_so.wait_for(len(o_dmas) - 2)
                sc.op("vector", lambda h, ci=ci, half=half: h.scalar_tensor_tensor(
                    out=o_stage[:, half + 128 * ci: half + 128 * ci + 64],
                    in0=pvA[:, 0:64], scalar=rcp[:, 0:1], in1=tb_sb[:, 0:64],
                    op0=OP.mult, op1=OP.add), waits=wst)
                last_stt = sc.op("vector", lambda h, ci=ci, half=half:
                                 h.scalar_tensor_tensor(
                    out=o_stage[:, half + 128 * ci + 64: half + 128 * (ci + 1)],
                    in0=pvB[:, 0:64], scalar=rcp[:, 1:2], in1=tb_sb[:, 64:128],
                    op0=OP.mult, op1=OP.add), inc="cV")
            pv_free = last_stt
            tr_free = last_stt
            bias_bank_free = [last_stt, last_stt]
            dsto = o_d[:, 128 * j:128 * (j + 1)] \
                .rearrange("(a c) e -> a c e", a=128, c=16)[:, 4 * qc:4 * qc + 4, :]
            srco = o_stage[:, half:half + 512].rearrange("a (c e) -> a c e", c=4, e=128)
            st_so.dma(lambda h, dsto=dsto, srco=srco: h.dma_start(
                out=dsto, in_=srco), waits=[("cV", last_stt)])
            o_dmas.append(st_so.n)
            p1_emit(2)
        if j + 1 < NP:
            st_qk.dma(lambda h, j=j: h.dma_start(
                out=qkp[:, 0:S], in_=qT_d[128 * (j + 1):128 * (j + 2), :]),
                waits=[("cT", sc.v("cT"))])
            st_qk.dma(lambda h, j=j: h.dma_start(
                out=qkp[:, S:2 * S], in_=kT_d[128 * (j + 1):128 * (j + 2), :]))
            toe_load(-2 * (j + 1), sc.v("cT"))
            toe_load(-2 * (j + 1) - 1, sc.v("cT"))
            dG_need = st_toe.wait_all()
    p1_emit(len(p1_units))
    sc.barrier()
    esB.close()
    esAB.close()

    esCD = ExitStack()
    w1qT = sb(esCD, "w1qT", [128, NDT * HC], BF16)
    esC = ExitStack()
    gb_sb = sb(esC, "gb_sb", [128, E], F32)
    bb_sb = sb(esC, "bb_sb", [128, E], F32)
    wt = sb(esC, "wt", [128, 2 * 2048], F32)
    mp_t = sb(esC, "mp_t", [128, 2048], F32)
    mm_t = sb(esC, "mm_t", [128, 2048], F32)
    wqn = sb(esC, "wqn", [128, 2048], BF16)
    stgW = sb(esC, "stgW", [128, HC], BF16)
    xa = sb(esC, "xa", [128, 2 * E], F32)
    ob = sb(esC, "ob", [128, 2 * E], F32)
    r1 = sb(esC, "r1", [128, E], F32)
    t1s = sb(esC, "t1s", [128, E], F32)
    frs = mm_t[:, 0:E]
    uq = esC.enter_context(nc.sbuf_tensor("uq", [128, E], I32))
    bns = mp_t[:, 0:E]
    xq_t = wqn[:, 0:E]
    stC = stgW
    stat = sb(esC, "stat", [128, 12], F32)
    wsum = sb(esC, "wsum", [128, NHC], F32)
    pC = [esC.enter_context(nc.psum_tensor(f"pC{i}", [128, 512], F32)) for i in range(4)]
    pCb = [esC.enter_context(nc.psum_tensor(f"pCb{i}", [128, 1024], BF16)) for i in range(4)]

    st_cwl.dma(lambda h: h.dma_start(out=gb_sb[0:1, :], in_=gam_d))
    st_cwl.dma(lambda h: h.dma_start(out=bb_sb[0:1, :], in_=bet_d))
    dH_gb = st_cwl.wait_all()
    for i, bc in enumerate((gb_sb, bb_sb)):
        for nch in range(2):
            ct = sc.op("tensor", lambda h, bc=bc, i=i, nch=nch: h.matmul(
                pC[2 * i + nch][:, 0:512], onesc_f[0:1, :],
                bc[0:1, 512 * nch:512 * (nch + 1)], start=True, stop=True),
                waits=dH_gb, inc="cT")
            sc.op("vector", lambda h, bc=bc, i=i, nch=nch: h.tensor_copy(
                bc[:, 512 * nch:512 * (nch + 1)], pC[2 * i + nch][:, 0:512]),
                waits=[("cT", ct)], inc="cV")
    gb_ready = sc.v("cV")

    def wquant(wi, wd, wscb):
        ntile = NHC if wi == 0 else NDT
        width = E if wi == 0 else HC
        npc = width // 2048 if width > 2048 else 1
        pw = width // npc
        base, ncol = (0, 32) if wi == 0 else (32, 16)
        sc.drain("vector")
        sc.op("vector", lambda h, base=base, ncol=ncol: h.reduce_sum(
            stat[:, 0:1], wsumB[:, base:base + ncol], axis=AX),
            waits=[("cV", p1_done[wi])], inc="cV")
        ct = sc.op("tensor", lambda h: h.transpose(
            pC[0][0:1, 0:128], stat[:, 0:1], ident_f[0:128, 0:128]),
            waits=[("cV", sc.v("cV")), ("cV", gb_ready)], inc="cT")
        sc.op("vector", lambda h: h.reduce_sum(
            stat[0:1, 1:2], pC[0][0:1, 0:128], axis=AX), waits=[("cT", ct)])
        sc.drain("vector")
        sc.op("vector", lambda h: h.tensor_scalar(
            out=stat[0:1, 2:3], in0=stat[0:1, 1:2], scalar1=1.0 / (HC * E),
            scalar2=1e-5, op0=OP.mult, op1=OP.max))
        sc.drain("vector")
        cv = sc.op("vector", lambda h: h.reciprocal(stat[0:1, 3:4], stat[0:1, 2:3]),
                   inc="cV")
        ct = sc.op("tensor", lambda h: h.matmul(
            pC[1][:, 0:2], onesc_f[0:1, :], stat[0:1, 2:4], start=True, stop=True),
            waits=[("cV", cv)], inc="cT")
        scd = sc.op("vector", lambda h, wscb=wscb: h.tensor_copy(
            wscb[:], pC[1][:, 0:2]), waits=[("cT", ct)], inc="cV")
        bksW = Banks(pCb, free0=sc.v("cV"))
        stg_done = 0
        cons = [scd] * 2
        ip = 0
        wqn_free = 0
        for tt in range(ntile):
            for pc in range(npc):
                par = ip % 2
                ip += 1
                buf = wt[:, par * 2048:par * 2048 + pw]
                i = st_wl.dma(lambda h, buf=buf, wd=wd, tt=tt, pc=pc, pw=pw:
                              h.dma_start(out=buf,
                                          in_=wd[128 * tt:128 * (tt + 1),
                                                 pw * pc:pw * (pc + 1)]),
                              waits=[("cV", cons[par])])
                sc.op("vector", lambda h, buf=buf, pw=pw, wscb=wscb: h.tensor_scalar(
                    out=mp_t[:, 0:pw], in0=buf, scalar1=wscb[:, 1:2], scalar2=0.5,
                    op0=OP.mult, op1=OP.is_ge),
                    waits=st_wl.wait_for(i) + [("cT", wqn_free)])
                cons[par] = sc.op("vector", lambda h, buf=buf, pw=pw, wscb=wscb:
                             h.tensor_scalar(
                    out=mm_t[:, 0:pw], in0=buf, scalar1=wscb[:, 1:2], scalar2=-0.5,
                    op0=OP.mult, op1=OP.is_le), inc="cV")
                sc.drain("vector")
                qdone = sc.op("vector", lambda h, pw=pw: h.tensor_sub(
                    wqn[:, 0:pw], mp_t[:, 0:pw], mm_t[:, 0:pw]), inc="cV")
                nb = pw // 128
                for b in range(nb):
                    bg = pc * nb + b
                    bk, freev, ki = bksW.get()
                    ct = sc.op("tensor", lambda h, bk=bk, b=b: h.transpose(
                        bk[:, 0:128], wqn[:, 128 * b:128 * (b + 1)], ident_b[:]),
                        waits=[("cV", qdone), ("cV", freev)], inc="cT")
                    if wi == 0:
                        ev = sc.op("vector", lambda h, bk=bk, tt=tt, bg=bg:
                                   h.tensor_copy(
                            w1qT[:, bg * HC + 128 * tt: bg * HC + 128 * (tt + 1)],
                            bk[:, 0:128]), waits=[("cT", ct)], inc="cV")
                    else:
                        wv = [("cT", ct)] + st_sw.wait_for(st_sw.n - 1)
                        ev = sc.op("vector", lambda h, bk=bk, bg=bg: h.tensor_copy(
                            stgW[:, 128 * bg:128 * (bg + 1)], bk[:, 0:128]),
                            waits=wv, inc="cV")
                    bksW.set_free(ki, ev)
                wqn_free = sc.v("cT")
            if wi == 1:
                dstw = w2qT_d[:, 128 * tt:128 * (tt + 1)] \
                    .rearrange("(b p) e -> p b e", p=128)
                srcw = stgW[:].rearrange("p (b e) -> p b e", e=128)
                st_sw.dma(lambda h, dstw=dstw, srcw=srcw:
                          h.dma_start(out=dstw, in_=srcw),
                          waits=[("cV", sc.v("cV"))])

    wquant(0, w1_d, w1sc)
    wquant(1, w2_d, w2sc)
    wq_end_ct = sc.v("cT")
    wq_end_ds = st_sw.wait_all()

    bksX = Banks(pCb, free0=sc.v("cV"))
    xq_dmas = []
    for t in range(NT):
        par = t % 2
        xat = xa[:, par * E:(par + 1) * E]
        obt = ob[:, par * E:(par + 1) * E]
        w = [("cV", sc.v("cV") if t >= 2 else 0)]
        i1 = st_lc.dma(lambda h, xat=xat, t=t: h.dma_start(
            out=xat, in_=x_d[128 * t:128 * (t + 1), :]), waits=w)
        i2 = st_lc.dma(lambda h, obt=obt, t=t: h.dma_start(
            out=obt, in_=o_d[128 * t:128 * (t + 1), :]))
        sc.op("vector", lambda h, obt=obt: h.reduce_sum(
            stat[:, 0:1], obt, axis=AX),
            waits=st_lc.wait_for(i2) + [("cV", gb_ready)])
        sc.drain("vector")
        sc.op("vector", lambda h: h.tensor_scalar(
            out=stat[:, 1:2], in0=stat[:, 0:1], scalar1=-1.0 / E, scalar2=None,
            op0=OP.mult), inc="cV")
        mu_done = sc.v("cV")
        sq_done = sc.op("scalar", lambda h, obt=obt: h.activation(
            out=t1s[:], in_=obt, func=AF.Square, bias=stat[:, 1:2],
            accum_out=stat[:, 2:3]), waits=[("cV", mu_done)], inc="cS")
        sc.op("vector", lambda h: h.tensor_scalar(
            out=stat[:, 3:4], in0=stat[:, 2:3], scalar1=1.0 / E, scalar2=1e-3,
            op0=OP.mult, op1=OP.add), waits=[("cS", sq_done)], inc="cV")
        ln_done = sc.op("scalar", lambda h: h.activation(
            out=stat[:, 4:5], in_=stat[:, 3:4], func=AF.Ln),
            waits=[("cV", sc.v("cV"))], inc="cS")
        rstd_done = sc.op("scalar", lambda h: h.activation(
            out=stat[:, 5:6], in_=stat[:, 4:5], func=AF.Exp, scale=cst[:, 1:2]),
            waits=[("cS", ln_done)], inc="cS")
        sc.op("vector", lambda h, obt=obt: h.tensor_scalar(
            out=t1s[:], in0=obt, scalar1=stat[:, 1:2], scalar2=stat[:, 5:6],
            op0=OP.add, op1=OP.mult), waits=[("cS", rstd_done)])
        sc.drain("vector")
        sc.op("vector", lambda h: h.tensor_mul(t1s[:], t1s[:], gb_sb[:]))
        sc.drain("vector")
        sc.op("vector", lambda h: h.tensor_add(t1s[:], t1s[:], bb_sb[:]))
        sc.drain("vector")
        sc.op("vector", lambda h, xat=xat: h.tensor_add(r1[:], t1s[:], xat),
              waits=st_lc.wait_for(i1), inc="cV")
        sc.drain("vector")
        sc.op("vector", lambda h: h.reduce_sum(stat[:, 6:7], r1[:], axis=AX))
        sc.drain("vector")
        sc.op("vector", lambda h: h.tensor_scalar(
            out=stat[:, 7:8], in0=stat[:, 6:7], scalar1=-1.0 / E, scalar2=None,
            op0=OP.mult), inc="cV")
        sq2 = sc.op("scalar", lambda h: h.activation(
            out=t1s[:], in_=r1[:], func=AF.Square, bias=stat[:, 7:8],
            accum_out=stat[:, 8:9]), waits=[("cV", sc.v("cV"))], inc="cS")
        sc.op("vector", lambda h: h.tensor_scalar(
            out=stat[:, 9:10], in0=stat[:, 8:9], scalar1=1.0 / E, scalar2=1e-5,
            op0=OP.mult, op1=OP.add), waits=[("cS", sq2)], inc="cV")
        ln2 = sc.op("scalar", lambda h: h.activation(
            out=stat[:, 10:11], in_=stat[:, 9:10], func=AF.Ln),
            waits=[("cV", sc.v("cV"))], inc="cS")
        rstd2 = sc.op("scalar", lambda h: h.activation(
            out=stat[:, 11:12], in_=stat[:, 10:11], func=AF.Exp, scale=cst[:, 1:2]),
            waits=[("cS", ln2)], inc="cS")
        x2done = sc.op("vector", lambda h: h.tensor_scalar(
            out=t1s[:], in0=r1[:], scalar1=stat[:, 7:8], scalar2=stat[:, 11:12],
            op0=OP.add, op1=OP.mult), waits=[("cS", rstd2)], inc="cV")
        st_sx2.dma(lambda h, t=t: h.dma_start(
            out=x2_d[128 * t:128 * (t + 1), :], in_=t1s[:]),
            waits=[("cV", x2done)])
        sc.drain("vector")
        sc.op("vector", lambda h: h.tensor_reduce(
            out=stat[:, 0:1], in_=t1s[:], axis=AX, op=OP.max,
            apply_absolute_value=True))
        sc.drain("vector")
        sc.op("vector", lambda h: h.tensor_scalar(
            out=stat[:, 1:2], in0=stat[:, 0:1], scalar1=1e-5, scalar2=None,
            op0=OP.max))
        sc.drain("vector")
        sc.op("vector", lambda h, t=t: h.tensor_scalar(
            out=rx1[:, t:t + 1], in0=stat[:, 1:2], scalar1=1.0 / 127.0, scalar2=None,
            op0=OP.mult))
        sc.op("vector", lambda h: h.reciprocal(stat[:, 2:3], stat[:, 1:2]))
        sc.drain("vector")
        sc.op("vector", lambda h: h.tensor_scalar(
            out=stat[:, 3:4], in0=stat[:, 2:3], scalar1=127.0, scalar2=None,
            op0=OP.mult), inc="cV")
        sc.op("vector", lambda h: h.tensor_scalar(
            out=t1s[:], in0=t1s[:], scalar1=stat[:, 3:4], scalar2=128.5,
            op0=OP.mult, op1=OP.add),
            waits=[("cV", sc.v("cV"))] + st_sx2.wait_for(t))
        sc.drain("vector")
        sc.op("vector", lambda h: h.tensor_copy(uq[:], t1s[:]))
        sc.drain("vector")
        sc.op("vector", lambda h: h.tensor_sub(frs, t1s[:], uq[:]))
        sc.drain("vector")
        for thr in (2.0, 1.0, 0.0, -1.0):
            sc.op("vector", lambda h, thr=thr: h.scalar_tensor_tensor(
                out=t1s[:], in0=frs, scalar=thr,
                in1=uq[:] if thr == 2.0 else t1s[:],
                op0=OP.is_ge, op1=OP.add))
            sc.drain("vector")
        xqdone = sc.op("vector", lambda h: h.tensor_scalar(
            out=xq_t, in0=t1s[:], scalar1=1.0, scalar2=-130.0,
            op0=OP.mult, op1=OP.add), waits=[("cT", wq_end_ct)], inc="cV")
        for b in range(NDT):
            bk, freev, ki = bksX.get()
            ct = sc.op("tensor", lambda h, bk=bk, b=b: h.transpose(
                bk[:, 0:128], xq_t[:, 128 * b:128 * (b + 1)], ident_b[:]),
                waits=[("cV", xqdone), ("cV", freev)], inc="cT")
            wv = [("cT", ct)] + st_sxq.wait_for(len(xq_dmas) - 2)
            if len(xq_dmas) == 0 and b == 0:
                wv += wq_end_ds
            ev = sc.op("vector", lambda h, bk=bk, b=b, par=par: h.tensor_copy(
                stC[:, par * 1024 + 128 * b: par * 1024 + 128 * (b + 1)],
                bk[:, 0:128]), waits=wv, inc="cV")
            bksX.set_free(ki, ev)
        dstx = xqT_d[:, 128 * t:128 * (t + 1)].rearrange("(b p) e -> p b e", p=128)
        srcx = stC[:, par * 1024:(par + 1) * 1024].rearrange("p (b e) -> p b e", e=128)
        st_sxq.dma(lambda h, dstx=dstx, srcx=srcx: h.dma_start(
            out=dstx, in_=srcx), waits=[("cV", sc.v("cV"))])
        xq_dmas.append(st_sxq.n)
    ct = sc.op("tensor", lambda h: h.transpose(
        pC[2][0:16, 0:128], rx1[:], ident_f[:]),
        waits=[("cV", sc.v("cV"))], inc="cT")
    rxcv = sc.op("vector", lambda h: h.tensor_copy(rxrow_s[:], pC[2][0:16, 0:128]),
                 waits=[("cT", ct)], inc="cV")
    st_srx.dma(lambda h: h.dma_start(out=rx_d, in_=rxrow_s[:]),
               waits=[("cV", rxcv)])
    sc.barrier()
    esC.close()

    esD = ExitStack()
    GW = 2048
    hsl = [sb(esD, f"hsl{i}", [128, NHC * CH], BF16) for i in range(2)]
    gsl = sb(esD, "gsl", [128, NHC * CH], F32)
    xqc = sb(esD, "xqc", [128, 2 * NDT * CH], BF16)
    w2c = sb(esD, "w2c", [128, 2 * 4 * E], BF16)
    rxw = sb(esD, "rxw", [128, CH], F32)
    s2b = sb(esD, "s2b", [128, CH], F32)
    s2rep = sb(esD, "s2rep", [128, GW], F32)
    s2r2 = sb(esD, "s2r2", [2, 128], F32)
    rxr = sb(esD, "rxr", [1, CH], F32)
    rxr2 = sb(esD, "rxr2", [1, CH], F32)
    mxt = sb(esD, "mxt", [128, 8], F32)
    rx2d = sb(esD, "rx2d", [128, 2], F32)
    macc = sb(esD, "macc", [128, CH], F32)
    cacc = sb(esD, "cacc", [128, 2 * CH], F32)
    cacc2 = sb(esD, "cacc2", [128, 2 * CH], F32)
    cacc3 = sb(esD, "cacc3", [128, 2 * CH], F32)
    halo = [sb(esD, f"halo{i}", [128, NHC], BF16) for i in range(3)]
    zcol = sb(esD, "zcol", [128, NHC], BF16)
    tq2 = sb(esD, "tq2", [128, GW], F32)
    uq2 = esD.enter_context(nc.sbuf_tensor("uq2", [128, GW], I32))
    fr2 = sb(esD, "fr2", [128, GW], F32)
    y_sb = sb(esD, "y_sb", [128, E], F32)
    pD = [esD.enter_context(nc.psum_tensor(f"pD{i}", [128, 512], F32)) for i in range(8)]

    sc.op("gpsimd", lambda h: h.memset(zcol[:], 0.0), inc="cG")
    cG_z = sc.v("cG")

    xq_dh = {}
    h_done = {}
    rxr_free = [0]
    rxr2_free = [0]
    w2_prev = [0]
    ffn2_done = {}
    pd6_free = 0
    pd45_free = 0
    y_dmas = []
    gm_prev = [0, 0, 0, 0]

    def load_xq_chunk(k):
        dst = xqc[:, (k % 2) * NDT * CH:(k % 2 + 1) * NDT * CH] \
            .rearrange("p (b s) -> p b s", s=CH)
        src = xqT_d[:, CH * k:CH * (k + 1)].rearrange("(b p) s -> p b s", p=128)
        w = [("cT", h_done.get(k - 2, (0, 0))[1])] if k >= 2 else []
        return st_lq.dma(lambda h, dst=dst, src=src: h.dma_start(
            out=dst, in_=src), waits=w)

    def ffn1(k):
        nonlocal pd6_free
        dh = xq_dh[k]
        kk = k % 2
        i = st_lr.dma(lambda h, k=k: h.dma_start(
            out=rxr[0:1, :].rearrange("o (a b) -> o a b", a=2),
            in_=rx_d[2 * k:2 * k + 2, :]),
            waits=[("cT", rxr_free[0])] + st_srx.wait_all())
        ct = sc.op("tensor", lambda h: h.matmul(
            pD[6][:, 0:CH], onesc_f[0:1, :], rxr[0:1, :], start=True, stop=True),
            waits=st_lr.wait_for(i) + [("cV", pd6_free)], inc="cT")
        rxr_free[0] = ct
        rxwd = sc.op("vector", lambda h: h.tensor_scalar(
            out=rxw[:], in0=pD[6][:, 0:CH], scalar1=w1sc[:, 0:1], scalar2=None,
            op0=OP.mult), waits=[("cT", ct)], inc="cV")
        pd6_free = rxwd
        bks = Banks(pD[0:4], free0=sc.v("cV"))
        hguard = ffn2_done.get(k - 2, 0)
        for hc in range(NHC):
            bk, freev, ki = bks.get()
            for dt in range(NDT):
                w = st_lq.wait_for(dh) + ([("cV", freev)] if dt == 0 else [])
                sc.op("tensor", lambda h, bk=bk, hc=hc, dt=dt, kk=kk: h.matmul(
                    bk[:, 0:CH],
                    w1qT[:, dt * HC + 128 * hc: dt * HC + 128 * (hc + 1)],
                    xqc[:, kk * NDT * CH + dt * CH: kk * NDT * CH + (dt + 1) * CH],
                    start=(dt == 0), stop=(dt == NDT - 1)),
                    waits=w, inc="cT" if dt == NDT - 1 else None)
            ev = sc.op("vector", lambda h, bk=bk, hc=hc, kk=kk: h.tensor_mul(
                hsl[kk][:, hc * CH:(hc + 1) * CH], bk[:, 0:CH], rxw[:]),
                waits=[("cT", sc.v("cT")), ("cV", rxwd), ("cT", hguard)], inc="cV")
            bks.set_free(ki, ev)
        mm_ct = sc.v("cT")
        sc.drain("vector")
        snap = sc.op("vector", lambda h, kk=kk, k=k: h.tensor_copy(
            halo[k % 3][:], hsl[kk][:].rearrange("p (c s) -> p c s", s=CH)[:, :, CH - 1]),
            inc="cV")
        return (snap, mm_ct)

    def conv_quant_ffn2(m, last):
        nonlocal pd6_free, pd45_free
        km = m % 2
        hm = hsl[km]
        hl = halo[(m - 1) % 3] if m >= 1 else zcol
        h_cv = h_done[m][0]
        r_cv = h_done[m + 1][0] if not last else h_done[m][0]
        v1_at = [0, 0]
        sil_at = {}
        vC_at = {}

        def emit_sil(hc):
            C = cacc3[:, (hc % 2) * CH:(hc % 2 + 1) * CH]
            wsil = [("cV", vC_at[hc])]
            if m >= 1:
                wsil.append(("cV", gm_prev[hc // 8]))
            sil_at[hc] = sc.op("scalar", lambda h, hc=hc, C=C: h.activation(
                out=gsl[:, hc * CH:(hc + 1) * CH], in_=C, func=AF.Silu,
                bias=cb_sb[:, hc:hc + 1]), waits=wsil, inc="cS")
        for hc in range(NHC):
            par = hc % 2
            A = cacc[:, par * CH:(par + 1) * CH]
            B = cacc2[:, par * CH:(par + 1) * CH]
            C = cacc3[:, par * CH:(par + 1) * CH]
            w = [("cV", v1_at[par]), ("cV", h_cv)]
            if m == 0 and hc == 0:
                w.append(("cG", cG_z))
            sc.op("scalar", lambda h, hc=hc, hm=hm, A=A: h.mul(
                A[:, 1:CH], hm[:, hc * CH: hc * CH + CH - 1],
                cw_sb[:, 3 * hc:3 * hc + 1]), waits=w)
            sA = sc.op("scalar", lambda h, hc=hc, hl=hl, A=A: h.mul(
                A[:, 0:1], hl[:, hc:hc + 1], cw_sb[:, 3 * hc:3 * hc + 1]),
                inc="cS")
            if hc >= 2:
                emit_sil(hc - 2)
            v1_at[par] = sc.op("vector", lambda h, hc=hc, hm=hm, A=A, B=B:
                               h.scalar_tensor_tensor(
                out=B, in0=hm[:, hc * CH:(hc + 1) * CH],
                scalar=cw_sb[:, 3 * hc + 1:3 * hc + 2], in1=A,
                op0=OP.mult, op1=OP.add), waits=[("cS", sA)], inc="cV")
            sc.drain("vector")
            wv2 = [("cS", sil_at.get(hc - 2, 0))]
            if last and hc == 0:
                wv2.append(("cG", cG_z))
            sc.op("vector", lambda h, hc=hc, hm=hm, B=B, C=C:
                  h.scalar_tensor_tensor(
                out=C[:, 0:CH - 1], in0=hm[:, hc * CH + 1:(hc + 1) * CH],
                scalar=cw_sb[:, 3 * hc + 2:3 * hc + 3], in1=B[:, 0:CH - 1],
                op0=OP.mult, op1=OP.add), waits=wv2)
            if last:
                rsl = zcol[:, hc:hc + 1]
            else:
                rsl = hsl[(m + 1) % 2][:, hc * CH: hc * CH + 1]
            vC = sc.op("vector", lambda h, hc=hc, rsl=rsl, B=B, C=C:
                       h.scalar_tensor_tensor(
                out=C[:, CH - 1:CH], in0=rsl,
                scalar=cw_sb[:, 3 * hc + 2:3 * hc + 3], in1=B[:, CH - 1:CH],
                op0=OP.mult, op1=OP.add), waits=[("cV", r_cv)], inc="cV")
            sc.drain("vector")
            vC_at[hc] = vC
        emit_sil(NHC - 2)
        emit_sil(NHC - 1)
        sil_done = sil_at[NHC - 1]
        mdone = sc.op("vector", lambda h: h.tensor_reduce(
            out=macc[:], in_=gsl[:].rearrange("p (c s) -> p s c", c=NHC),
            axis=AX, op=OP.max, apply_absolute_value=True),
            waits=[("cS", sil_done)], inc="cV")
        for blk in range(CH // 128):
            ct = sc.op("tensor", lambda h, blk=blk: h.transpose(
                pD[4 + blk][:, 0:128], macc[:, 128 * blk:128 * (blk + 1)],
                ident_f[:]), waits=[("cV", mdone), ("cV", pd45_free)], inc="cT")
            sc.op("vector", lambda h, blk=blk: h.tensor_reduce(
                out=mxt[:, blk:blk + 1], in_=pD[4 + blk][:, 0:128], axis=AX,
                op=OP.max), waits=[("cT", ct)], inc="cV")
        pd45_free = sc.v("cV")
        sc.drain("vector")
        sc.op("vector", lambda h: h.tensor_scalar(
            out=mxt[:, 2:4], in0=mxt[:, 0:2], scalar1=1e-5, scalar2=None,
            op0=OP.max))
        sc.drain("vector")
        sc.op("vector", lambda h: h.reciprocal(mxt[:, 4:6], mxt[:, 2:4]))
        sc.drain("vector")
        sc.op("vector", lambda h: h.tensor_scalar(
            out=mxt[:, 6:8], in0=mxt[:, 4:6], scalar1=127.0, scalar2=None,
            op0=OP.mult))
        scq = sc.op("vector", lambda h: h.tensor_scalar(
            out=rx2d[:], in0=mxt[:, 2:4], scalar1=w2sc[:, 0:1], scalar2=1.0 / 127.0,
            op0=OP.mult, op1=OP.mult), inc="cV")
        ct = sc.op("tensor", lambda h: h.transpose(
            pD[7][0:2, 0:128], mxt[:, 6:8], ident_f[:]),
            waits=[("cV", scq), ("cV", pd6_free)], inc="cT")
        s2rd = sc.op("vector", lambda h: h.tensor_copy(s2r2[:], pD[7][0:2, 0:128]),
                     waits=[("cT", ct)] + st_sm.wait_all(), inc="cV")
        i2 = st_sm.dma(lambda h: h.dma_start(
            out=rxr2[0:1, :].rearrange("o (a b) -> o a b", a=2), in_=s2r2[:]),
            waits=[("cV", s2rd), ("cT", rxr2_free[0])])
        ct = sc.op("tensor", lambda h: h.matmul(
            pD[6][:, 0:CH], onesc_f[0:1, :], rxr2[0:1, :], start=True, stop=True),
            waits=st_sm.wait_for(i2) + [("cV", pd6_free)], inc="cT")
        rxr2_free[0] = ct
        s2bd0 = sc.op("vector", lambda h: h.tensor_copy(s2b[:], pD[6][:, 0:CH]),
                      waits=[("cT", ct)], inc="cV")
        pd6_free = s2bd0
        sc.drain("vector")
        sc.op("vector", lambda h: h.tensor_copy(s2rep[:, 0:CH], s2b[:]))
        sc.drain("vector")
        for r in range(3):
            w0 = CH * (1 << r)
            sc.op("vector", lambda h, w0=w0: h.tensor_copy(
                s2rep[:, w0:2 * w0], s2rep[:, 0:w0]),
                inc="cV" if w0 * 2 == GW else None)
            sc.drain("vector")
        s2bd = sc.v("cV")
        q2_at = {}
        for gq in range(NHC * CH // GW):
            sl = slice(gq * GW, (gq + 1) * GW)
            sc.op("vector", lambda h, sl=sl: h.tensor_tensor(
                out=tq2[:], in0=gsl[:, sl], in1=s2rep[:], op=OP.mult),
                waits=[("cV", s2bd), ("cS", sil_at[(gq + 1) * (GW // CH) - 1])])
            sc.drain("vector")
            gm_prev[gq] = sc.op("vector", lambda h: h.tensor_scalar(
                out=uq2[:], in0=tq2[:], scalar1=1.0, scalar2=128.5,
                op0=OP.mult, op1=OP.add), inc="cV")
            sc.drain("vector")
            sc.op("vector", lambda h: h.tensor_sub(fr2[:], tq2[:], uq2[:]))
            sc.drain("vector")
            for thr in (-129.5, -128.5, -127.5, -126.5):
                sc.op("vector", lambda h, thr=thr: h.scalar_tensor_tensor(
                    out=tq2[:], in0=fr2[:], scalar=thr,
                    in1=uq2[:] if thr == -129.5 else tq2[:],
                    op0=OP.is_ge, op1=OP.add))
                sc.drain("vector")
            q2_at[gq] = sc.op("vector", lambda h, sl=sl, km=km: h.tensor_scalar(
                out=hsl[km][:, sl], in0=tq2[:], scalar1=1.0, scalar2=-130.0,
                op0=OP.mult, op1=OP.add), inc="cV")
            sc.drain("vector")
        piece_ct = {}
        for piece in range(NHC // 4):
            par = piece % 2
            dstw = w2c[:, par * 4 * E:(par + 1) * 4 * E] \
                .rearrange("p (b e) -> p b e", e=E)
            srcw = w2qT_d.rearrange("(b p) e -> p b e", p=128)[:, 4 * piece:4 * piece + 4, :]
            wld = [("cT", piece_ct.get(piece - 2, w2_prev[0]))]
            dhw = st_lw2.dma(lambda h, dstw=dstw, srcw=srcw: h.dma_start(
                out=dstw, in_=srcw), waits=wld)
            for pidx in range(4):
                hc = 4 * piece + pidx
                for sl2 in range(CH // 128):
                    for nch2 in range(2):
                        bky = pD[sl2 * 2 + nch2]
                        w = [("cV", q2_at[hc // 8])] + st_lw2.wait_for(dhw)
                        sc.op("tensor", lambda h, bky=bky, hc=hc, sl2=sl2, nch2=nch2,
                              km=km, par=par, pidx=pidx: h.matmul(
                            bky[:, 0:512],
                            hsl[km][:, hc * CH + 128 * sl2: hc * CH + 128 * (sl2 + 1)],
                            w2c[:, par * 4 * E + pidx * E + 512 * nch2:
                                par * 4 * E + pidx * E + 512 * (nch2 + 1)],
                            start=(hc == 0), stop=(hc == NHC - 1)),
                            waits=w,
                            inc="cT" if (pidx == 3 and sl2 == CH // 128 - 1
                                         and nch2 == 1) else None)
            piece_ct[piece] = sc.v("cT")
        ffn2_done[m] = sc.v("cT")
        w2_prev[0] = ffn2_done[m]
        for sl2 in range(CH // 128):
            for nch2 in range(2):
                bky = pD[sl2 * 2 + nch2]
                wv = [("cT", ffn2_done[m])] + st_sy.wait_for(len(y_dmas) - 1)
                sc.op("vector", lambda h, bky=bky, sl2=sl2, nch2=nch2: h.tensor_scalar(
                    out=y_sb[:, 512 * nch2: 512 * (nch2 + 1)],
                    in0=bky[:, 0:512], scalar1=rx2d[:, sl2:sl2 + 1], scalar2=None,
                    op0=OP.mult), waits=wv, inc="cV")
            st_sy.dma(lambda h, m=m, sl2=sl2: h.dma_start(
                out=y_d[CH * m + 128 * sl2: CH * m + 128 * (sl2 + 1), :],
                in_=y_sb[:]),
                waits=[("cV", sc.v("cV"))])
            y_dmas.append(st_sy.n)

    xq_dh[0] = load_xq_chunk(0)
    h_done[0] = ffn1(0)
    for m in range(NCH):
        if m + 1 < NCH:
            xq_dh[m + 1] = load_xq_chunk(m + 1)
            h_done[m + 1] = ffn1(m + 1)
        conv_quant_ffn2(m, last=(m == NCH - 1))
    sc.barrier()
    esD.close()
    esCD.close()

    esE = ExitStack()
    xa2 = sb(esE, "xa2", [128, 2 * E], F32)
    yb2 = sb(esE, "yb2", [128, 2 * E], F32)
    r2 = sb(esE, "r2", [128, 2 * E], F32)
    t2s = sb(esE, "t2s", [128, 2 * E], F32)
    ot = sb(esE, "ot", [128, 2 * E], F32)
    stE = sb(esE, "stE", [128, 24], F32)
    out_dmas = []
    odone_prev = 0
    for tp in range(NT // 2):
        tt2 = (2 * tp, 2 * tp + 1)
        dmai = {}
        for t in tt2:
            par = t % 2
            xat = xa2[:, par * E:(par + 1) * E]
            ybt = yb2[:, par * E:(par + 1) * E]
            w = [("cV", odone_prev)]
            i1 = st_le.dma(lambda h, xat=xat, t=t: h.dma_start(
                out=xat, in_=x2_d[128 * t:128 * (t + 1), :]), waits=w)
            i2 = st_le.dma(lambda h, ybt=ybt, t=t: h.dma_start(
                out=ybt, in_=y_d[128 * t:128 * (t + 1), :]))
            dmai[t] = (i1, i2)
        for t in tt2:
            par = t % 2
            sc.op("vector", lambda h, par=par: h.tensor_add(
                r2[:, par * E:(par + 1) * E], xa2[:, par * E:(par + 1) * E],
                yb2[:, par * E:(par + 1) * E]),
                waits=st_le.wait_for(dmai[t][0]) + st_le.wait_for(dmai[t][1]))
        sc.drain("vector")
        for t in tt2:
            par = t % 2
            sc.op("vector", lambda h, par=par: h.reduce_sum(
                stE[:, 12 * par:12 * par + 1], r2[:, par * E:(par + 1) * E],
                axis=AX))
        sc.drain("vector")
        mu_cv = 0
        for t in tt2:
            par = t % 2
            mu_cv = sc.op("vector", lambda h, par=par: h.tensor_scalar(
                out=stE[:, 12 * par + 1:12 * par + 2],
                in0=stE[:, 12 * par:12 * par + 1], scalar1=-1.0 / E,
                scalar2=None, op0=OP.mult), inc="cV")
        sc.drain("vector")
        sq_c = {}
        for t in tt2:
            par = t % 2
            sq_c[t] = sc.op("scalar", lambda h, par=par: h.activation(
                out=t2s[:, par * E:(par + 1) * E],
                in_=r2[:, par * E:(par + 1) * E], func=AF.Square,
                bias=stE[:, 12 * par + 1:12 * par + 2],
                accum_out=stE[:, 12 * par + 2:12 * par + 3]),
                waits=[("cV", mu_cv)], inc="cS")
        var_cv = 0
        for t in tt2:
            par = t % 2
            var_cv = sc.op("vector", lambda h, par=par: h.tensor_scalar(
                out=stE[:, 12 * par + 3:12 * par + 4],
                in0=stE[:, 12 * par + 2:12 * par + 3], scalar1=1.0 / E,
                scalar2=1e-5, op0=OP.mult, op1=OP.add),
                waits=[("cS", sq_c[t])], inc="cV")
        ln_c = {}
        for t in tt2:
            par = t % 2
            ln_c[t] = sc.op("scalar", lambda h, par=par: h.activation(
                out=stE[:, 12 * par + 4:12 * par + 5],
                in_=stE[:, 12 * par + 3:12 * par + 4], func=AF.Ln),
                waits=[("cV", var_cv)], inc="cS")
        rs_c = {}
        for t in tt2:
            par = t % 2
            rs_c[t] = sc.op("scalar", lambda h, par=par: h.activation(
                out=stE[:, 12 * par + 5:12 * par + 6],
                in_=stE[:, 12 * par + 4:12 * par + 5], func=AF.Exp,
                scale=cst[:, 1:2]), waits=[("cS", ln_c[t])], inc="cS")
        for t in tt2:
            par = t % 2
            otp = ot[:, par * E:(par + 1) * E]
            w2 = [("cS", rs_c[t])] + st_se.wait_for(len(out_dmas) - 2)
            odone = sc.op("vector", lambda h, otp=otp, par=par: h.tensor_scalar(
                out=otp, in0=r2[:, par * E:(par + 1) * E],
                scalar1=stE[:, 12 * par + 1:12 * par + 2],
                scalar2=stE[:, 12 * par + 5:12 * par + 6],
                op0=OP.add, op1=OP.mult), waits=w2, inc="cV")
            st_se.dma(lambda h, otp=otp, t=t: h.dma_start(
                out=out_d[128 * t:128 * (t + 1), :], in_=otp),
                waits=[("cV", odone)])
            out_dmas.append(st_se.n)
            odone_prev = odone
        sc.drain("vector")
    sc.op("sync", None, waits=st_se.wait_all())
    esE.close()

    with nc.Block() as block:
        sc.emit(block, sems)
    root.close()
    return nc


def kernel(L0_attn_x, L0_attn_Wq, L0_attn_Wk, L0_attn_Wv, L0_attn_rel_table,
           L0_attn_gamma, L0_attn_beta, L0_ffn_w1, L0_ffn_conv_w, L0_ffn_conv_b,
           L0_ffn_w2):
    x = np.ascontiguousarray(np.asarray(L0_attn_x, dtype=np.float32))
    T = np.asarray(L0_attn_rel_table, dtype=np.float32)

    t_ext = np.concatenate([T, np.repeat(T[-1:, :], 2048, axis=0)], axis=0)
    dvals = np.arange(-15, 16)
    p_i = np.arange(128)
    a_i = np.arange(128)
    idx = (2047 + 128 * dvals[:, None, None] + p_i[None, :, None]
           - a_i[None, None, :])
    toe = np.ascontiguousarray(
        t_ext[idx].transpose(3, 0, 1, 2))

    cwv = np.asarray(L0_ffn_conv_w, dtype=np.float32).reshape(NHC, 128, 3)
    cw = np.ascontiguousarray(cwv.transpose(1, 0, 2).reshape(128, NHC * 3))
    cb = np.ascontiguousarray(
        np.asarray(L0_ffn_conv_b, dtype=np.float32).reshape(NHC, 128).T)

    base = dict(
        wq=np.ascontiguousarray(L0_attn_Wq, dtype=np.float32),
        wk=np.ascontiguousarray(L0_attn_Wk, dtype=np.float32),
        wv=np.ascontiguousarray(L0_attn_Wv, dtype=np.float32),
        toe=toe,
        gam=np.ascontiguousarray(np.asarray(L0_attn_gamma, np.float32)[None, :]),
        bet=np.ascontiguousarray(np.asarray(L0_attn_beta, np.float32)[None, :]),
        w1=np.ascontiguousarray(L0_ffn_w1, dtype=np.float32),
        cw=cw, cb=cb,
        w2=np.ascontiguousarray(L0_ffn_w2, dtype=np.float32),
    )
    in_maps = [dict(base, x=np.ascontiguousarray(x[b])) for b in range(8)]

    prog = build_program()
    res = run_bass_kernel_spmd(prog, in_maps, core_ids=list(range(8)))
    key = {0: "out", 1: "o_s", 2: "x2_s", 3: "y_s"}[DEBUG]
    return np.stack([np.asarray(r[key]) for r in res.results], axis=0)


# version 13 (final)
# speedup vs baseline: 1.5455x; 1.0495x over previous
import numpy as np
import ml_dtypes
from contextlib import ExitStack

import concourse.bass as bass
import concourse.mybir as mybir
from concourse.bass_utils import run_bass_kernel_spmd
from concourse.alu_op_type import AluOpType as OP

F32 = mybir.dt.float32
BF16 = mybir.dt.bfloat16
I32 = mybir.dt.int32
AF = mybir.ActivationFunctionType
AX = mybir.AxisListType.X
bfdt = ml_dtypes.bfloat16

S, E, H, DH, HC = 2048, 1024, 16, 64, 4096
NT = S // 128
NP = H // 2
NDT = E // 128
NHC = HC // 128
CH = 256
NCH = S // CH
NW = 18
DEBUG = 0

ENGINES = ("sync", "gpsimd", "vector", "scalar", "tensor")


class Sched:
    def __init__(self):
        self.ops = {e: [] for e in ENGINES}
        self.cnt = {}
        self.last_wait = {e: {} for e in ENGINES}

    def v(self, sem):
        return self.cnt.get(sem, 0)

    def op(self, eng, fn, waits=(), inc=None, inc_n=1, bare_inc=False):
        w = []
        for s, val in waits:
            if val > 0 and self.last_wait[eng].get(s, -1) < val:
                w.append((s, val))
                self.last_wait[eng][s] = val
        self.ops[eng].append((fn, w, inc, inc_n, bare_inc))
        if inc is not None:
            self.cnt[inc] = self.cnt.get(inc, 0) + inc_n
        return self.cnt.get(inc, 0) if inc else None

    def drain(self, eng):
        self.op(eng, lambda h: h.drain())

    def barrier(self):
        for st in getattr(self, "streams", []):
            for w in st.wait_all():
                self.op("sync", None, waits=(w,))
        for e in ENGINES:
            self.op(e, None, inc="bar", inc_n=1, bare_inc=True)
        tot = self.v("bar")
        for e in ENGINES:
            self.op(e, None, waits=(("bar", tot),))

    def emit(self, block, sems):
        def make(ename):
            def body(h):
                for fn, waits, inc, inc_n, bare in self.ops[ename]:
                    for s, val in waits:
                        h.wait_ge(sems[s], val)
                    if bare:
                        h.sem_inc(sems[inc], inc_n)
                        continue
                    if fn is None:
                        continue
                    instr = fn(h)
                    if inc is not None:
                        instr.then_inc(sems[inc], inc_n)
            return body
        block.sync(make("sync"))
        block.gpsimd(make("gpsimd"))
        block.vector(make("vector"))
        block.scalar(make("scalar"))
        block.tensor(make("tensor"))


class Stream:

    def __init__(self, sc, eng, sems, names):
        self.sc = sc
        self.eng = eng
        self.sems = names
        self.counts = [0] * len(names)
        self.n = 0

    def dma(self, fn, waits=()):
        i = self.n
        k = i % len(self.sems)
        self.n += 1
        self.counts[k] += 1
        self.sc.op(self.eng, fn, waits=waits, inc=self.sems[k], inc_n=16)
        return i

    def wait_for(self, i):
        if i < 0:
            return []
        k = i % len(self.sems)
        return [(self.sems[k], 16 * (i // len(self.sems) + 1))]

    def wait_last(self):
        return self.wait_for(self.n - 1)

    def wait_all(self):
        return [(s, 16 * c) for s, c in zip(self.sems, self.counts) if c > 0]


class Banks:

    def __init__(self, banks, free0=0):
        self.banks = banks
        self.free_at = [free0] * len(banks)
        self.i = 0

    def get(self):
        k = self.i % len(self.banks)
        self.i += 1
        return self.banks[k], self.free_at[k], k

    def set_free(self, k, cv):
        self.free_at[k] = cv


def build_program():
    nc = bass.Bass("TRN2", target_bir_lowering=False)

    x_d = nc.dram_tensor("x", [S, E], F32, kind="ExternalInput").ap()
    wq_d = nc.dram_tensor("wq", [E, E], F32, kind="ExternalInput").ap()
    wk_d = nc.dram_tensor("wk", [E, E], F32, kind="ExternalInput").ap()
    wv_d = nc.dram_tensor("wv", [E, E], F32, kind="ExternalInput").ap()
    toe_d = nc.dram_tensor("toe", [16, 31, 128, 128], F32, kind="ExternalInput").ap()
    gam_d = nc.dram_tensor("gam", [1, E], F32, kind="ExternalInput").ap()
    bet_d = nc.dram_tensor("bet", [1, E], F32, kind="ExternalInput").ap()
    w1_d = nc.dram_tensor("w1", [HC, E], F32, kind="ExternalInput").ap()
    cw_d = nc.dram_tensor("cw", [128, 3 * NHC], F32, kind="ExternalInput").ap()
    cb_d = nc.dram_tensor("cb", [128, NHC], F32, kind="ExternalInput").ap()
    w2_d = nc.dram_tensor("w2", [E, HC], F32, kind="ExternalInput").ap()
    out_d = nc.dram_tensor("out", [S, E], F32, kind="ExternalOutput").ap()

    qT_d = nc.dram_tensor("qT_s", [E, S], BF16, kind="Internal").ap()
    kT_d = nc.dram_tensor("kT_s", [E, S], BF16, kind="Internal").ap()
    o_d = nc.dram_tensor("o_s", [S, E], F32,
                         kind="ExternalOutput" if DEBUG == 1 else "Internal").ap()
    x2_d = nc.dram_tensor("x2_s", [S, E], F32,
                          kind="ExternalOutput" if DEBUG == 2 else "Internal").ap()
    y_d = nc.dram_tensor("y_s", [S, E], F32,
                         kind="ExternalOutput" if DEBUG == 3 else "Internal").ap()
    w2qT_d = nc.dram_tensor("w2qT_s", [HC, E], BF16, kind="Internal").ap()
    v_sd = nc.dram_tensor("v_sd", [S, E], BF16, kind="Internal").ap()
    rx_d = nc.dram_tensor("rx_d", [NT, 128], F32, kind="Internal").ap()
    mx_d = nc.dram_tensor("mx_d", [2, 128], F32, kind="Internal").ap()
    xqT_d = nc.dram_tensor("xqT_s", [E, S], BF16, kind="Internal").ap()

    sc = Sched()
    root = ExitStack()
    sems = {}
    stream_sems = ["gax", "toe", "cwl", "qk", "vl", "vs",
                   "sq0", "sq1", "so0", "so1", "wl0", "wl1", "sw",
                   "lc0", "lc1", "lc2", "lc3", "sx2", "sxq0", "sxq1", "srx",
                   "lq0", "lq1", "lw20", "lw21", "lr", "sm", "lm",
                   "sy0", "sy1", "le0", "le1", "le2", "le3", "se0", "se1"]
    for s in ["cT", "cV", "cS", "cG", "bar"] + stream_sems:
        sems[s] = root.enter_context(nc.semaphore(s))
    sc.streams = []

    def mkstream(eng, names):
        st = Stream(sc, eng, sems, names)
        sc.streams.append(st)
        return st

    st_gax = mkstream("gpsimd", ["gax"])
    st_toe = mkstream("gpsimd", ["toe"])
    st_cwl = mkstream("sync", ["cwl"])
    st_qk = mkstream("sync", ["qk"])
    st_vl = mkstream("sync", ["vl"])
    st_vs = mkstream("scalar", ["vs"])
    st_sq = mkstream("scalar", ["sq0", "sq1"])
    st_so = mkstream("scalar", ["so0", "so1"])
    st_wl = mkstream("sync", ["wl0", "wl1"])
    st_sw = mkstream("scalar", ["sw"])
    st_lc = mkstream("sync", ["lc0", "lc1", "lc2", "lc3"])
    st_sx2 = mkstream("scalar", ["sx2"])
    st_sxq = mkstream("scalar", ["sxq0", "sxq1"])
    st_srx = mkstream("scalar", ["srx"])
    st_lq = mkstream("sync", ["lq0", "lq1"])
    st_lw2 = mkstream("sync", ["lw20", "lw21"])
    st_lr = mkstream("sync", ["lr"])
    st_sm = mkstream("scalar", ["sm"])
    st_lm = mkstream("sync", ["lm"])
    st_sy = mkstream("scalar", ["sy0", "sy1"])
    st_le = mkstream("sync", ["le0", "le1", "le2", "le3"])
    st_se = mkstream("sync", ["se0", "se1"])

    def sb(stack, name, shape, dt=F32):
        return stack.enter_context(nc.sbuf_tensor(name, shape, dt))

    ident_b = sb(root, "ident_b", [128, 128], BF16)
    ident_f = sb(root, "ident_f", [128, 128], F32)
    onesc_f = sb(root, "onesc_f", [1, 128], F32)
    ones65 = sb(root, "ones65", [65, 128], F32)
    cst = sb(root, "cst", [128, 4], F32)
    rx1 = sb(root, "rx1", [128, NT], F32)
    cw_sb = sb(root, "cw_sb", [128, 3 * NHC], F32)
    cb_sb = sb(root, "cb_sb", [128, NHC], F32)
    w1sc = sb(root, "w1sc", [128, 2], F32)
    rxrow_s = sb(root, "rxrow_s", [16, 128], F32)
    w2sc = sb(root, "w2sc", [128, 2], F32)
    wsumB = sb(root, "wsumB", [128, 48], F32)
    esAB = ExitStack()
    v_pad = sb(esAB, "v_pad", [128, NT * H * 65], BF16)
    v_km = sb(esAB, "v_km", [128, 16 * E], BF16)

    esA = ExitStack()
    x_bf = sb(esA, "x_bf", [128, NT * E], BF16)
    xT = sb(esA, "xT", [128, NDT * S], BF16)
    wq_sb = sb(esA, "wq_sb", [128, NDT * E], BF16)
    wk_sb = sb(esA, "wk_sb", [128, NDT * E], BF16)
    wv_sb = sb(esA, "wv_sb", [128, NDT * E], BF16)
    stgA = sb(esA, "stgA", [128, 2 * 512], BF16)
    pA = [esA.enter_context(nc.psum_tensor(f"pA{i}", [128, 512], F32)) for i in range(6)]
    pAb = [esA.enter_context(nc.psum_tensor(f"pAb{i}", [128, 1024], BF16)) for i in range(2)]

    sc.op("gpsimd", lambda h: h.memset(cst[:, 0:1], 0.03125))
    sc.op("gpsimd", lambda h: h.memset(cst[:, 1:2], -0.5))
    sc.op("gpsimd", lambda h: h.memset(onesc_f[:], 1.0))
    sc.op("gpsimd", lambda h: h.memset(ones65[:], 1.0))
    sc.op("gpsimd", lambda h: h.memset(ident_b[:], 0.0))
    sc.op("gpsimd", lambda h: h.memset(ident_f[:], 0.0))
    sc.drain("gpsimd")
    sc.op("gpsimd", lambda h: h.affine_select(out=ident_b[:], in_=ident_b[:],
          compare_op=OP.not_equal, fill=1.0, base=0, pattern=[[-1, 128]],
          channel_multiplier=1))
    sc.op("gpsimd", lambda h: h.affine_select(out=ident_f[:], in_=ident_f[:],
          compare_op=OP.not_equal, fill=1.0, base=0, pattern=[[-1, 128]],
          channel_multiplier=1), inc="cG")
    vp4 = v_pad[:].rearrange("p (t h e) -> p t h e", t=NT, h=H, e=65)
    sc.op("gpsimd", lambda h: h.memset(vp4[:, :, :, 64:65], 1.0), inc="cG")

    st_gax.dma(lambda h: h.dma_start(
        out=x_bf[:].rearrange("p (t d) -> p t d", d=E),
        in_=x_d.rearrange("(t p) d -> p t d", p=128)))
    for wd, wsb in ((wq_d, wq_sb), (wk_d, wk_sb), (wv_d, wv_sb)):
        st_gax.dma(lambda h, wd=wd, wsb=wsb: h.dma_start(
            out=wsb[:].rearrange("p (dt e) -> p dt e", e=E),
            in_=wd.rearrange("(dt p) e -> p dt e", p=128)))
    dG_A = st_gax.wait_all()

    st_cwl.dma(lambda h: h.dma_start(out=cw_sb[:], in_=cw_d))
    st_cwl.dma(lambda h: h.dma_start(out=cb_sb[:], in_=cb_d))

    bksA = Banks(pAb)
    for t in range(NT):
        for dt in range(NDT):
            bk, freev, ki = bksA.get()
            sc.op("tensor", lambda h, bk=bk, t=t, dt=dt: h.transpose(
                bk[:, 0:128], x_bf[:, t * E + 128 * dt: t * E + 128 * (dt + 1)],
                ident_b[:]), waits=dG_A + [("cG", 1), ("cV", freev)], inc="cT")
            ev = sc.op("vector", lambda h, bk=bk, t=t, dt=dt: h.tensor_copy(
                xT[:, dt * S + 128 * t: dt * S + 128 * (t + 1)], bk[:, 0:128]),
                waits=[("cT", sc.v("cT"))], inc="cV")
            bksA.set_free(ki, ev)
    cV_xT = sc.v("cV")

    bksQ = Banks(pA[0:3])
    nstg = 0
    stg_dmas = []
    for wsb, dst in ((wq_sb, qT_d), (wk_sb, kT_d)):
        for j in range(NDT):
            for nch in range(4):
                bk, freev, ki = bksQ.get()
                for dt in range(NDT):
                    waits = [("cV", cV_xT)] + ([("cV", freev)] if dt == 0 else [])
                    sc.op("tensor", lambda h, bk=bk, wsb=wsb, j=j, nch=nch, dt=dt:
                          h.matmul(bk[:, 0:512],
                                   wsb[:, dt * E + 128 * j: dt * E + 128 * (j + 1)],
                                   xT[:, dt * S + 512 * nch: dt * S + 512 * (nch + 1)],
                                   start=(dt == 0), stop=(dt == NDT - 1)),
                          waits=waits, inc="cT" if dt == NDT - 1 else None)
                stx = stgA[:, (nstg % 2) * 512:(nstg % 2) * 512 + 512]
                wv2 = [("cT", sc.v("cT"))] + st_sq.wait_for(nstg - 2)
                ev = sc.op("vector", lambda h, stx=stx, bk=bk: h.tensor_copy(
                    stx, bk[:, 0:512]), waits=wv2, inc="cV")
                bksQ.set_free(ki, ev)
                st_sq.dma(lambda h, stx=stx, dst=dst, j=j, nch=nch:
                          h.dma_start(out=dst[128 * j:128 * (j + 1),
                                              512 * nch:512 * (nch + 1)], in_=stx),
                          waits=[("cV", ev)])
                nstg += 1
    v_nat = x_bf
    bksV = Banks(pA[3:6], free0=sc.v("cV"))
    for t in range(NT):
        for nch in range(2):
            bk, freev, ki = bksV.get()
            for dt in range(NDT):
                waits = [("cV", cV_xT)] + ([("cV", freev)] if dt == 0 else [])
                sc.op("tensor", lambda h, bk=bk, t=t, nch=nch, dt=dt:
                      h.matmul(bk[:, 0:512],
                               xT[:, dt * S + 128 * t: dt * S + 128 * (t + 1)],
                               wv_sb[:, dt * E + 512 * nch: dt * E + 512 * (nch + 1)],
                               start=(dt == 0), stop=(dt == NDT - 1)),
                      waits=waits, inc="cT" if dt == NDT - 1 else None)
            ev = sc.op("vector", lambda h, t=t, nch=nch, bk=bk: h.tensor_copy(
                v_nat[:, t * E + 512 * nch: t * E + 512 * (nch + 1)], bk[:, 0:512]),
                waits=[("cT", sc.v("cT"))], inc="cV")
            bksV.set_free(ki, ev)
    cV_vp = sc.v("cV")
    st_vs.dma(lambda h: h.dma_start(
        out=v_sd.rearrange("(t p) e -> p t e", p=128),
        in_=v_nat[:].rearrange("p (t e) -> p t e", e=E)),
        waits=[("cV", cV_vp)])
    dvs = st_vs.wait_all()
    for t in range(NT):
        st_vl.dma(lambda h, t=t: h.dma_start(
            out=vp4[:, t, :, 0:64],
            in_=v_sd[128 * t:128 * (t + 1), :].rearrange(
                "p (hh e) -> p hh e", e=64)),
            waits=dvs)
    st_vl.dma(lambda h: h.dma_start(
        out=v_km[:].rearrange("p (k c) -> p k c", k=16),
        in_=v_sd.rearrange("(p k) c -> p k c", k=16)))
    sc.barrier()
    esA.close()

    p1_cons = [0, 0]
    p1_ip = [0]
    p1_done = {}

    def p1_unit(wi, wd, tt, pc, pw, col, is_last):
        par = p1_ip[0] % 2
        p1_ip[0] += 1
        buf = wtB[:, par * 2048:par * 2048 + pw]
        i = st_wl.dma(lambda h, buf=buf, wd=wd, tt=tt, pc=pc, pw=pw:
                      h.dma_start(out=buf,
                                  in_=wd[128 * tt:128 * (tt + 1),
                                         pw * pc:pw * (pc + 1)]),
                      waits=[("cV", p1_cons[par])])
        p1_cons[par] = sc.op("vector", lambda h, buf=buf, col=col:
                             h.reduce_sum(wsumB[:, col:col + 1], buf, axis=AX,
                                          apply_absolute_value=True),
                             waits=st_wl.wait_for(i), inc="cV")
        if is_last:
            p1_done[wi] = p1_cons[par]

    p1_units = []
    for tt in range(NHC):
        p1_units.append((0, w1_d, tt, 0, E, tt, tt == NHC - 1))
    for tt in range(NDT):
        for pc in range(2):
            p1_units.append((1, w2_d, tt, pc, 2048, 32 + 2 * tt + pc,
                             tt == NDT - 1 and pc == 1))
    p1_next = [0]

    def p1_emit(n):
        while n > 0 and p1_next[0] < len(p1_units):
            p1_unit(*p1_units[p1_next[0]])
            p1_next[0] += 1
            n -= 1

    esB = ExitStack()
    wtB = sb(esB, "wtB", [128, 2 * 2048], F32)
    toe = sb(esB, "toe_sb", [128, 16 * NW * 128], BF16)
    qkp = sb(esB, "qkp", [128, 2 * S], BF16)
    p_bf = sb(esB, "p_bf", [128, 2 * 2048], BF16)
    pvA_sb = sb(esB, "pvA_sb", [65, 512], F32)
    pvB_sb = sb(esB, "pvB_sb", [65, 512], F32)
    bias_sb = sb(esB, "bias_sb", [128, 512], F32)
    rcp = sb(esB, "rcp", [128, 2], F32)
    tb_sb = sb(esB, "tb_sb", [128, 128], F32)
    o_stage = sb(esB, "o_stage", [128, 1024], F32)
    sAB = esB.enter_context(nc.psum_tensor("sAB", [128, 2048], F32))
    pvA = esB.enter_context(nc.psum_tensor("pvA", [128, 512], F32))
    pvB = esB.enter_context(nc.psum_tensor("pvB", [128, 512], F32))
    pb = [esB.enter_context(nc.psum_tensor(f"pb{i}", [128, 512], F32)) for i in range(2)]

    toev = toe[:].rearrange("p (k w a) -> p k w a", k=16, w=NW, a=128)

    def toe_load(dlt, wait_ct):
        slot = (dlt + 15) % NW
        w = [("cT", wait_ct)] if wait_ct > 0 else []
        st_toe.dma(lambda h, dlt=dlt, slot=slot: h.dma_start(
            out=toev[:, :, slot, :],
            in_=toe_d[:, dlt + 15].rearrange("k p a -> p k a")),
            waits=w)

    for dlt in range(-1, 16):
        toe_load(dlt, 0)
    dG_need = st_toe.wait_all()

    st_qk.dma(lambda h: h.dma_start(out=qkp[:, 0:S], in_=qT_d[0:128, :]))
    st_qk.dma(lambda h: h.dma_start(out=qkp[:, S:2 * S], in_=kT_d[0:128, :]))

    exp_done = 0
    o_dmas = []
    pv_free = 0
    tr_free = 0
    bias_bank_free = [0, 0]
    for j in range(NP):
        hA, hB = 2 * j, 2 * j + 1
        dH_pair = st_qk.wait_all()
        qperm = qkp[:, 0:S].rearrange("p (a c) -> p c a", a=128, c=16)
        for qc in range(4):
            c0 = 4 * qc
            slA0 = ((c0 - hA) + 15) % NW
            slB0 = ((c0 - hB) + 15) % NW
            fill = []
            for kap in range(16):
                w0 = []
                if kap == 0:
                    w0 = list(dG_need) + [("cV", bias_bank_free[0]),
                                          ("cV", bias_bank_free[1])]
                fill.append((lambda h, kap=kap, slA0=slA0, j=j: h.matmul(
                    pb[0][:, 0:64],
                    toe[:, kap * (NW * 128) + slA0 * 128:
                        kap * (NW * 128) + (slA0 + 1) * 128],
                    v_km[:, kap * E + 128 * j: kap * E + 128 * j + 64],
                    start=(kap == 0), stop=(kap == 15)), w0))
                fill.append((lambda h, kap=kap, slB0=slB0, j=j: h.matmul(
                    pb[1][:, 64:128],
                    toe[:, kap * (NW * 128) + slB0 * 128:
                        kap * (NW * 128) + (slB0 + 1) * 128],
                    v_km[:, kap * E + 128 * j + 64: kap * E + 128 * (j + 1)],
                    start=(kap == 0), stop=(kap == 15)), ()))
            fi = 0
            for g in range(8):
                u0, u1 = 2 * g, 2 * g + 1
                wqk = list(dH_pair)
                if exp_done:
                    wqk.append(("cS", exp_done))
                for ui, u in enumerate((u0, u1)):
                    off = ui * 1024
                    sc.op("tensor", lambda h, u=u, qc=qc, off=off, qperm=qperm:
                          h.matmul(sAB[:, off:off + 512],
                                   qkp[0:64, S + 128 * u: S + 128 * (u + 1)],
                                   qperm[0:64, 4 * qc:4 * qc + 4, :],
                                   start=True, stop=True),
                          waits=wqk if ui == 0 else ())
                    sc.op("tensor", lambda h, u=u, qc=qc, off=off, qperm=qperm:
                          h.matmul(sAB[:, off + 512:off + 1024],
                                   qkp[64:128, S + 128 * u: S + 128 * (u + 1)],
                                   qperm[64:128, 4 * qc:4 * qc + 4, :],
                                   start=True, stop=True),
                          inc="cT" if ui == 1 else None)
                qk_ct = sc.v("cT")
                if g > 0:
                    gp = (g - 1) % 2
                    for ui, u in enumerate((2 * g - 2, 2 * g - 1)):
                        wpv = []
                        if g == 1 and ui == 0 and pv_free:
                            wpv = [("cV", pv_free)]
                        sc.op("tensor", lambda h, u=u, hA=hA, gp=gp, ui=ui: h.matmul(
                            pvA[0:65, 0:512],
                            v_pad[:, u * (H * 65) + hA * 65: u * (H * 65) + hA * 65 + 65],
                            p_bf[:, gp * 2048 + ui * 1024: gp * 2048 + ui * 1024 + 512],
                            start=(u == 0), stop=False), waits=wpv)
                        sc.op("tensor", lambda h, u=u, hB=hB, gp=gp, ui=ui: h.matmul(
                            pvB[0:65, 0:512],
                            v_pad[:, u * (H * 65) + hB * 65: u * (H * 65) + hB * 65 + 65],
                            p_bf[:, gp * 2048 + ui * 1024 + 512:
                                 gp * 2048 + ui * 1024 + 1024],
                            start=(u == 0), stop=False))
                while fi < len(fill) and fi < 4 * (g + 1):
                    fn, w0 = fill[fi]
                    sc.op("tensor", fn, waits=w0,
                          inc="cT" if fi == len(fill) - 1 else None)
                    fi += 1
                exp_done = sc.op("scalar", lambda h, g=g: h.activation(
                    out=p_bf[:, (g % 2) * 2048:(g % 2) * 2048 + 2048],
                    in_=sAB[:], func=AF.Exp, scale=cst[:, 0:1]),
                    waits=[("cT", qk_ct), ("cG", 1)], inc="cS")
            while fi < len(fill):
                fn, w0 = fill[fi]
                sc.op("tensor", fn, waits=w0,
                      inc="cT" if fi == len(fill) - 1 else None)
                fi += 1
            bias_c0_ct = sc.v("cT")
            gp = 1
            for ui, u in enumerate((14, 15)):
                sc.op("tensor", lambda h, u=u, hA=hA, ui=ui: h.matmul(
                    pvA[0:65, 0:512],
                    v_pad[:, u * (H * 65) + hA * 65: u * (H * 65) + hA * 65 + 65],
                    p_bf[:, gp * 2048 + ui * 1024: gp * 2048 + ui * 1024 + 512],
                    start=False, stop=(u == 15)),
                    waits=[("cS", exp_done)])
                sc.op("tensor", lambda h, u=u, hB=hB, ui=ui: h.matmul(
                    pvB[0:65, 0:512],
                    v_pad[:, u * (H * 65) + hB * 65: u * (H * 65) + hB * 65 + 65],
                    p_bf[:, gp * 2048 + ui * 1024 + 512: gp * 2048 + ui * 1024 + 1024],
                    start=False, stop=(u == 15)),
                    inc="cT" if u == 15 else None)
            pv_done = sc.v("cT")
            sc.op("vector", lambda h: h.tensor_copy(pvA_sb[:], pvA[0:65, 0:512]),
                  waits=[("cT", pv_done)])
            pv_ev = sc.op("vector", lambda h: h.tensor_copy(
                pvB_sb[:], pvB[0:65, 0:512]), inc="cV")
            bias_ev = [0, 0, 0, 0]
            sc.op("vector", lambda h: h.tensor_copy(
                bias_sb[:, 0:64], pb[0][:, 0:64]),
                waits=[("cT", bias_c0_ct)])
            bias_ev[0] = sc.op("vector", lambda h: h.tensor_copy(
                bias_sb[:, 64:128], pb[1][:, 64:128]), inc="cV")
            for ci in range(1, 4):
                c = 4 * qc + ci
                slA = ((c - hA) + 15) % NW
                slB = ((c - hB) + 15) % NW
                for kap in range(16):
                    w0 = [("cV", bias_ev[ci - 1])] if kap == 0 else []
                    sc.op("tensor", lambda h, kap=kap, slA=slA, j=j: h.matmul(
                        pb[0][:, 0:64],
                        toe[:, kap * (NW * 128) + slA * 128:
                            kap * (NW * 128) + (slA + 1) * 128],
                        v_km[:, kap * E + 128 * j: kap * E + 128 * j + 64],
                        start=(kap == 0), stop=(kap == 15)),
                        waits=w0)
                    sc.op("tensor", lambda h, kap=kap, slB=slB, j=j: h.matmul(
                        pb[1][:, 64:128],
                        toe[:, kap * (NW * 128) + slB * 128:
                            kap * (NW * 128) + (slB + 1) * 128],
                        v_km[:, kap * E + 128 * j + 64: kap * E + 128 * (j + 1)],
                        start=(kap == 0), stop=(kap == 15)),
                        inc="cT" if kap == 15 else None)
                bct = sc.v("cT")
                sc.op("vector", lambda h, ci=ci: h.tensor_copy(
                    bias_sb[:, 128 * ci:128 * ci + 64], pb[0][:, 0:64]),
                    waits=[("cT", bct)])
                bias_ev[ci] = sc.op("vector", lambda h, ci=ci: h.tensor_copy(
                    bias_sb[:, 128 * ci + 64:128 * (ci + 1)], pb[1][:, 64:128]),
                    inc="cV")
            bias_bank_free = [bias_ev[3], bias_ev[3]]
            half = (qc % 2) * 512
            last_stt = 0
            for ci in range(4):
                wtr = [("cV", pv_ev)]
                if ci >= 1:
                    wtr.append(("cV", last_stt))
                trA_ct = sc.op("tensor", lambda h, ci=ci: h.transpose(
                    pvA[:, 0:65], pvA_sb[0:65, 128 * ci:128 * (ci + 1)],
                    ident_f[0:65, 0:65]), waits=wtr, inc="cT")
                trB_ct = sc.op("tensor", lambda h, ci=ci: h.transpose(
                    pvB[:, 0:65], pvB_sb[0:65, 128 * ci:128 * (ci + 1)],
                    ident_f[0:65, 0:65]), inc="cT")
                sc.op("vector", lambda h: h.reciprocal(rcp[:, 0:1], pvA[:, 64:65]),
                      waits=[("cT", trB_ct)])
                sc.drain("vector")
                sc.op("vector", lambda h: h.reciprocal(rcp[:, 1:2], pvB[:, 64:65]),
                      inc="cV")
                sc.drain("vector")
                wst = st_so.wait_for(len(o_dmas) - 2)
                if ci == 0:
                    wst = wst + [("cV", bias_ev[3])]
                sc.op("vector", lambda h, ci=ci, half=half: h.scalar_tensor_tensor(
                    out=o_stage[:, half + 128 * ci: half + 128 * ci + 64],
                    in0=pvA[:, 0:64], scalar=rcp[:, 0:1],
                    in1=bias_sb[:, 128 * ci: 128 * ci + 64],
                    op0=OP.mult, op1=OP.add), waits=wst)
                last_stt = sc.op("vector", lambda h, ci=ci, half=half:
                                 h.scalar_tensor_tensor(
                    out=o_stage[:, half + 128 * ci + 64: half + 128 * (ci + 1)],
                    in0=pvB[:, 0:64], scalar=rcp[:, 1:2],
                    in1=bias_sb[:, 128 * ci + 64: 128 * (ci + 1)],
                    op0=OP.mult, op1=OP.add), inc="cV")
            pv_free = last_stt
            tr_free = last_stt
            bias_bank_free = [last_stt, last_stt]
            dsto = o_d[:, 128 * j:128 * (j + 1)] \
                .rearrange("(a c) e -> a c e", a=128, c=16)[:, 4 * qc:4 * qc + 4, :]
            srco = o_stage[:, half:half + 512].rearrange("a (c e) -> a c e", c=4, e=128)
            st_so.dma(lambda h, dsto=dsto, srco=srco: h.dma_start(
                out=dsto, in_=srco), waits=[("cV", last_stt)])
            o_dmas.append(st_so.n)
            p1_emit(2)
        if j + 1 < NP:
            st_qk.dma(lambda h, j=j: h.dma_start(
                out=qkp[:, 0:S], in_=qT_d[128 * (j + 1):128 * (j + 2), :]),
                waits=[("cT", sc.v("cT"))])
            st_qk.dma(lambda h, j=j: h.dma_start(
                out=qkp[:, S:2 * S], in_=kT_d[128 * (j + 1):128 * (j + 2), :]))
            toe_load(-2 * (j + 1), sc.v("cT"))
            toe_load(-2 * (j + 1) - 1, sc.v("cT"))
            dG_need = st_toe.wait_all()
    p1_emit(len(p1_units))
    sc.barrier()
    esB.close()
    esAB.close()

    esCD = ExitStack()
    w1qT = sb(esCD, "w1qT", [128, NDT * HC], BF16)
    esC = ExitStack()
    gb_sb = sb(esC, "gb_sb", [128, E], F32)
    bb_sb = sb(esC, "bb_sb", [128, E], F32)
    wt = sb(esC, "wt", [128, 2 * 2048], F32)
    mp_t = sb(esC, "mp_t", [128, 2048], F32)
    mm_t = sb(esC, "mm_t", [128, 2048], F32)
    wqn = sb(esC, "wqn", [128, 2048], BF16)
    stgW = sb(esC, "stgW", [128, HC], BF16)
    xa = sb(esC, "xa", [128, 2 * E], F32)
    ob = sb(esC, "ob", [128, 2 * E], F32)
    r1 = sb(esC, "r1", [128, E], F32)
    t1s = sb(esC, "t1s", [128, E], F32)
    frs = mm_t[:, 0:E]
    uq = esC.enter_context(nc.sbuf_tensor("uq", [128, E], I32))
    bns = mp_t[:, 0:E]
    xq_t = wqn[:, 0:E]
    stC = stgW
    stat = sb(esC, "stat", [128, 12], F32)
    wsum = sb(esC, "wsum", [128, NHC], F32)
    pC = [esC.enter_context(nc.psum_tensor(f"pC{i}", [128, 512], F32)) for i in range(4)]
    pCb = [esC.enter_context(nc.psum_tensor(f"pCb{i}", [128, 1024], BF16)) for i in range(4)]

    st_cwl.dma(lambda h: h.dma_start(out=gb_sb[0:1, :], in_=gam_d))
    st_cwl.dma(lambda h: h.dma_start(out=bb_sb[0:1, :], in_=bet_d))
    dH_gb = st_cwl.wait_all()
    for i, bc in enumerate((gb_sb, bb_sb)):
        for nch in range(2):
            ct = sc.op("tensor", lambda h, bc=bc, i=i, nch=nch: h.matmul(
                pC[2 * i + nch][:, 0:512], onesc_f[0:1, :],
                bc[0:1, 512 * nch:512 * (nch + 1)], start=True, stop=True),
                waits=dH_gb, inc="cT")
            sc.op("vector", lambda h, bc=bc, i=i, nch=nch: h.tensor_copy(
                bc[:, 512 * nch:512 * (nch + 1)], pC[2 * i + nch][:, 0:512]),
                waits=[("cT", ct)], inc="cV")
    gb_ready = sc.v("cV")

    def wquant(wi, wd, wscb):
        ntile = NHC if wi == 0 else NDT
        width = E if wi == 0 else HC
        npc = width // 2048 if width > 2048 else 1
        pw = width // npc
        base, ncol = (0, 32) if wi == 0 else (32, 16)
        sc.drain("vector")
        sc.op("vector", lambda h, base=base, ncol=ncol: h.reduce_sum(
            stat[:, 0:1], wsumB[:, base:base + ncol], axis=AX),
            waits=[("cV", p1_done[wi])], inc="cV")
        ct = sc.op("tensor", lambda h: h.transpose(
            pC[0][0:1, 0:128], stat[:, 0:1], ident_f[0:128, 0:128]),
            waits=[("cV", sc.v("cV")), ("cV", gb_ready)], inc="cT")
        sc.op("vector", lambda h: h.reduce_sum(
            stat[0:1, 1:2], pC[0][0:1, 0:128], axis=AX), waits=[("cT", ct)])
        sc.drain("vector")
        sc.op("vector", lambda h: h.tensor_scalar(
            out=stat[0:1, 2:3], in0=stat[0:1, 1:2], scalar1=1.0 / (HC * E),
            scalar2=1e-5, op0=OP.mult, op1=OP.max))
        sc.drain("vector")
        cv = sc.op("vector", lambda h: h.reciprocal(stat[0:1, 3:4], stat[0:1, 2:3]),
                   inc="cV")
        ct = sc.op("tensor", lambda h: h.matmul(
            pC[1][:, 0:2], onesc_f[0:1, :], stat[0:1, 2:4], start=True, stop=True),
            waits=[("cV", cv)], inc="cT")
        scd = sc.op("vector", lambda h, wscb=wscb: h.tensor_copy(
            wscb[:], pC[1][:, 0:2]), waits=[("cT", ct)], inc="cV")
        bksW = Banks(pCb, free0=sc.v("cV"))
        stg_done = 0
        cons = [scd] * 2
        ip = 0
        wqn_free = 0
        for tt in range(ntile):
            for pc in range(npc):
                par = ip % 2
                ip += 1
                buf = wt[:, par * 2048:par * 2048 + pw]
                i = st_wl.dma(lambda h, buf=buf, wd=wd, tt=tt, pc=pc, pw=pw:
                              h.dma_start(out=buf,
                                          in_=wd[128 * tt:128 * (tt + 1),
                                                 pw * pc:pw * (pc + 1)]),
                              waits=[("cV", cons[par])])
                sc.op("vector", lambda h, buf=buf, pw=pw, wscb=wscb: h.tensor_scalar(
                    out=mp_t[:, 0:pw], in0=buf, scalar1=wscb[:, 1:2], scalar2=0.5,
                    op0=OP.mult, op1=OP.is_ge),
                    waits=st_wl.wait_for(i) + [("cT", wqn_free)])
                cons[par] = sc.op("vector", lambda h, buf=buf, pw=pw, wscb=wscb:
                             h.tensor_scalar(
                    out=mm_t[:, 0:pw], in0=buf, scalar1=wscb[:, 1:2], scalar2=-0.5,
                    op0=OP.mult, op1=OP.is_le), inc="cV")
                sc.drain("vector")
                qdone = sc.op("vector", lambda h, pw=pw: h.tensor_sub(
                    wqn[:, 0:pw], mp_t[:, 0:pw], mm_t[:, 0:pw]), inc="cV")
                nb = pw // 128
                for b in range(nb):
                    bg = pc * nb + b
                    bk, freev, ki = bksW.get()
                    ct = sc.op("tensor", lambda h, bk=bk, b=b: h.transpose(
                        bk[:, 0:128], wqn[:, 128 * b:128 * (b + 1)], ident_b[:]),
                        waits=[("cV", qdone), ("cV", freev)], inc="cT")
                    if wi == 0:
                        ev = sc.op("vector", lambda h, bk=bk, tt=tt, bg=bg:
                                   h.tensor_copy(
                            w1qT[:, bg * HC + 128 * tt: bg * HC + 128 * (tt + 1)],
                            bk[:, 0:128]), waits=[("cT", ct)], inc="cV")
                    else:
                        wv = [("cT", ct)] + st_sw.wait_for(st_sw.n - 1)
                        ev = sc.op("vector", lambda h, bk=bk, bg=bg: h.tensor_copy(
                            stgW[:, 128 * bg:128 * (bg + 1)], bk[:, 0:128]),
                            waits=wv, inc="cV")
                    bksW.set_free(ki, ev)
                wqn_free = sc.v("cT")
            if wi == 1:
                dstw = w2qT_d[:, 128 * tt:128 * (tt + 1)] \
                    .rearrange("(b p) e -> p b e", p=128)
                srcw = stgW[:].rearrange("p (b e) -> p b e", e=128)
                st_sw.dma(lambda h, dstw=dstw, srcw=srcw:
                          h.dma_start(out=dstw, in_=srcw),
                          waits=[("cV", sc.v("cV"))])

    wquant(0, w1_d, w1sc)
    wquant(1, w2_d, w2sc)
    wq_end_ct = sc.v("cT")
    wq_end_ds = st_sw.wait_all()

    bksX = Banks(pCb, free0=sc.v("cV"))
    xq_dmas = []
    for t in range(NT):
        par = t % 2
        xat = xa[:, par * E:(par + 1) * E]
        obt = ob[:, par * E:(par + 1) * E]
        w = [("cV", sc.v("cV") if t >= 2 else 0)]
        i1 = st_lc.dma(lambda h, xat=xat, t=t: h.dma_start(
            out=xat, in_=x_d[128 * t:128 * (t + 1), :]), waits=w)
        i2 = st_lc.dma(lambda h, obt=obt, t=t: h.dma_start(
            out=obt, in_=o_d[128 * t:128 * (t + 1), :]))
        sc.op("vector", lambda h, obt=obt: h.reduce_sum(
            stat[:, 0:1], obt, axis=AX),
            waits=st_lc.wait_for(i2) + [("cV", gb_ready)])
        sc.drain("vector")
        sc.op("vector", lambda h: h.tensor_scalar(
            out=stat[:, 1:2], in0=stat[:, 0:1], scalar1=-1.0 / E, scalar2=None,
            op0=OP.mult), inc="cV")
        mu_done = sc.v("cV")
        sq_done = sc.op("scalar", lambda h, obt=obt: h.activation(
            out=t1s[:], in_=obt, func=AF.Square, bias=stat[:, 1:2],
            accum_out=stat[:, 2:3]), waits=[("cV", mu_done)], inc="cS")
        sc.op("vector", lambda h: h.tensor_scalar(
            out=stat[:, 3:4], in0=stat[:, 2:3], scalar1=1.0 / E, scalar2=1e-3,
            op0=OP.mult, op1=OP.add), waits=[("cS", sq_done)], inc="cV")
        ln_done = sc.op("scalar", lambda h: h.activation(
            out=stat[:, 4:5], in_=stat[:, 3:4], func=AF.Ln),
            waits=[("cV", sc.v("cV"))], inc="cS")
        rstd_done = sc.op("scalar", lambda h: h.activation(
            out=stat[:, 5:6], in_=stat[:, 4:5], func=AF.Exp, scale=cst[:, 1:2]),
            waits=[("cS", ln_done)], inc="cS")
        sc.op("vector", lambda h, obt=obt: h.tensor_scalar(
            out=t1s[:], in0=obt, scalar1=stat[:, 1:2], scalar2=stat[:, 5:6],
            op0=OP.add, op1=OP.mult), waits=[("cS", rstd_done)])
        sc.drain("vector")
        sc.op("vector", lambda h: h.tensor_mul(t1s[:], t1s[:], gb_sb[:]))
        sc.drain("vector")
        sc.op("vector", lambda h: h.tensor_add(t1s[:], t1s[:], bb_sb[:]))
        sc.drain("vector")
        sc.op("vector", lambda h, xat=xat: h.tensor_add(r1[:], t1s[:], xat),
              waits=st_lc.wait_for(i1), inc="cV")
        sc.drain("vector")
        sc.op("vector", lambda h: h.reduce_sum(stat[:, 6:7], r1[:], axis=AX))
        sc.drain("vector")
        sc.op("vector", lambda h: h.tensor_scalar(
            out=stat[:, 7:8], in0=stat[:, 6:7], scalar1=-1.0 / E, scalar2=None,
            op0=OP.mult), inc="cV")
        sq2 = sc.op("scalar", lambda h: h.activation(
            out=t1s[:], in_=r1[:], func=AF.Square, bias=stat[:, 7:8],
            accum_out=stat[:, 8:9]), waits=[("cV", sc.v("cV"))], inc="cS")
        sc.op("vector", lambda h: h.tensor_scalar(
            out=stat[:, 9:10], in0=stat[:, 8:9], scalar1=1.0 / E, scalar2=1e-5,
            op0=OP.mult, op1=OP.add), waits=[("cS", sq2)], inc="cV")
        ln2 = sc.op("scalar", lambda h: h.activation(
            out=stat[:, 10:11], in_=stat[:, 9:10], func=AF.Ln),
            waits=[("cV", sc.v("cV"))], inc="cS")
        rstd2 = sc.op("scalar", lambda h: h.activation(
            out=stat[:, 11:12], in_=stat[:, 10:11], func=AF.Exp, scale=cst[:, 1:2]),
            waits=[("cS", ln2)], inc="cS")
        x2done = sc.op("vector", lambda h: h.tensor_scalar(
            out=t1s[:], in0=r1[:], scalar1=stat[:, 7:8], scalar2=stat[:, 11:12],
            op0=OP.add, op1=OP.mult), waits=[("cS", rstd2)], inc="cV")
        st_sx2.dma(lambda h, t=t: h.dma_start(
            out=x2_d[128 * t:128 * (t + 1), :], in_=t1s[:]),
            waits=[("cV", x2done)])
        sc.drain("vector")
        sc.op("vector", lambda h: h.tensor_reduce(
            out=stat[:, 0:1], in_=t1s[:], axis=AX, op=OP.max,
            apply_absolute_value=True))
        sc.drain("vector")
        sc.op("vector", lambda h: h.tensor_scalar(
            out=stat[:, 1:2], in0=stat[:, 0:1], scalar1=1e-5, scalar2=None,
            op0=OP.max))
        sc.drain("vector")
        sc.op("vector", lambda h, t=t: h.tensor_scalar(
            out=rx1[:, t:t + 1], in0=stat[:, 1:2], scalar1=1.0 / 127.0, scalar2=None,
            op0=OP.mult))
        sc.op("vector", lambda h: h.reciprocal(stat[:, 2:3], stat[:, 1:2]))
        sc.drain("vector")
        sc.op("vector", lambda h: h.tensor_scalar(
            out=stat[:, 3:4], in0=stat[:, 2:3], scalar1=127.0, scalar2=None,
            op0=OP.mult), inc="cV")
        sc.op("vector", lambda h: h.tensor_scalar(
            out=t1s[:], in0=t1s[:], scalar1=stat[:, 3:4], scalar2=128.5,
            op0=OP.mult, op1=OP.add),
            waits=[("cV", sc.v("cV"))] + st_sx2.wait_for(t))
        sc.drain("vector")
        sc.op("vector", lambda h: h.tensor_copy(uq[:], t1s[:]))
        sc.drain("vector")
        sc.op("vector", lambda h: h.tensor_sub(frs, t1s[:], uq[:]))
        sc.drain("vector")
        for thr in (2.0, 1.0, 0.0, -1.0):
            sc.op("vector", lambda h, thr=thr: h.scalar_tensor_tensor(
                out=t1s[:], in0=frs, scalar=thr,
                in1=uq[:] if thr == 2.0 else t1s[:],
                op0=OP.is_ge, op1=OP.add))
            sc.drain("vector")
        xqdone = sc.op("vector", lambda h: h.tensor_scalar(
            out=xq_t, in0=t1s[:], scalar1=1.0, scalar2=-130.0,
            op0=OP.mult, op1=OP.add), waits=[("cT", wq_end_ct)], inc="cV")
        for b in range(NDT):
            bk, freev, ki = bksX.get()
            ct = sc.op("tensor", lambda h, bk=bk, b=b: h.transpose(
                bk[:, 0:128], xq_t[:, 128 * b:128 * (b + 1)], ident_b[:]),
                waits=[("cV", xqdone), ("cV", freev)], inc="cT")
            wv = [("cT", ct)] + st_sxq.wait_for(len(xq_dmas) - 2)
            if len(xq_dmas) == 0 and b == 0:
                wv += wq_end_ds
            ev = sc.op("vector", lambda h, bk=bk, b=b, par=par: h.tensor_copy(
                stC[:, par * 1024 + 128 * b: par * 1024 + 128 * (b + 1)],
                bk[:, 0:128]), waits=wv, inc="cV")
            bksX.set_free(ki, ev)
        dstx = xqT_d[:, 128 * t:128 * (t + 1)].rearrange("(b p) e -> p b e", p=128)
        srcx = stC[:, par * 1024:(par + 1) * 1024].rearrange("p (b e) -> p b e", e=128)
        st_sxq.dma(lambda h, dstx=dstx, srcx=srcx: h.dma_start(
            out=dstx, in_=srcx), waits=[("cV", sc.v("cV"))])
        xq_dmas.append(st_sxq.n)
    ct = sc.op("tensor", lambda h: h.transpose(
        pC[2][0:16, 0:128], rx1[:], ident_f[:]),
        waits=[("cV", sc.v("cV"))], inc="cT")
    rxcv = sc.op("vector", lambda h: h.tensor_copy(rxrow_s[:], pC[2][0:16, 0:128]),
                 waits=[("cT", ct)], inc="cV")
    st_srx.dma(lambda h: h.dma_start(out=rx_d, in_=rxrow_s[:]),
               waits=[("cV", rxcv)])
    sc.barrier()
    esC.close()

    esD = ExitStack()
    GW = 2048
    hsl = [sb(esD, f"hsl{i}", [128, NHC * CH], BF16) for i in range(2)]
    gsl = sb(esD, "gsl", [128, NHC * CH], F32)
    xqc = sb(esD, "xqc", [128, 2 * NDT * CH], BF16)
    w2c = sb(esD, "w2c", [128, 2 * 4 * E], BF16)
    rxw = sb(esD, "rxw", [128, CH], F32)
    s2b = sb(esD, "s2b", [128, CH], F32)
    s2rep = sb(esD, "s2rep", [128, GW], F32)
    s2r2 = sb(esD, "s2r2", [2, 128], F32)
    rxr = sb(esD, "rxr", [1, CH], F32)
    rxr2 = sb(esD, "rxr2", [1, CH], F32)
    mxt = sb(esD, "mxt", [128, 8], F32)
    rx2d = sb(esD, "rx2d", [128, 2], F32)
    macc = sb(esD, "macc", [128, CH], F32)
    cacc = sb(esD, "cacc", [128, 2 * CH], F32)
    cacc2 = sb(esD, "cacc2", [128, 2 * CH], F32)
    cacc3 = sb(esD, "cacc3", [128, 2 * CH], F32)
    halo = [sb(esD, f"halo{i}", [128, NHC], BF16) for i in range(3)]
    zcol = sb(esD, "zcol", [128, NHC], BF16)
    tq2 = sb(esD, "tq2", [128, GW], F32)
    uq2 = esD.enter_context(nc.sbuf_tensor("uq2", [128, GW], I32))
    fr2 = sb(esD, "fr2", [128, GW], F32)
    y_sb = sb(esD, "y_sb", [128, E], F32)
    pD = [esD.enter_context(nc.psum_tensor(f"pD{i}", [128, 512], F32)) for i in range(8)]

    sc.op("gpsimd", lambda h: h.memset(zcol[:], 0.0), inc="cG")
    cG_z = sc.v("cG")

    xq_dh = {}
    h_done = {}
    rxr_free = [0]
    rxr2_free = [0]
    w2_prev = [0]
    ffn2_done = {}
    pd6_free = 0
    pd45_free = 0
    y_dmas = []
    gm_prev = [0, 0, 0, 0]

    def load_xq_chunk(k):
        dst = xqc[:, (k % 2) * NDT * CH:(k % 2 + 1) * NDT * CH] \
            .rearrange("p (b s) -> p b s", s=CH)
        src = xqT_d[:, CH * k:CH * (k + 1)].rearrange("(b p) s -> p b s", p=128)
        w = [("cT", h_done.get(k - 2, (0, 0))[1])] if k >= 2 else []
        return st_lq.dma(lambda h, dst=dst, src=src: h.dma_start(
            out=dst, in_=src), waits=w)

    def ffn1(k):
        nonlocal pd6_free
        dh = xq_dh[k]
        kk = k % 2
        i = st_lr.dma(lambda h, k=k: h.dma_start(
            out=rxr[0:1, :].rearrange("o (a b) -> o a b", a=2),
            in_=rx_d[2 * k:2 * k + 2, :]),
            waits=[("cT", rxr_free[0])] + st_srx.wait_all())
        ct = sc.op("tensor", lambda h: h.matmul(
            pD[6][:, 0:CH], onesc_f[0:1, :], rxr[0:1, :], start=True, stop=True),
            waits=st_lr.wait_for(i) + [("cV", pd6_free)], inc="cT")
        rxr_free[0] = ct
        rxwd = sc.op("vector", lambda h: h.tensor_scalar(
            out=rxw[:], in0=pD[6][:, 0:CH], scalar1=w1sc[:, 0:1], scalar2=None,
            op0=OP.mult), waits=[("cT", ct)], inc="cV")
        pd6_free = rxwd
        bks = Banks(pD[0:4], free0=sc.v("cV"))
        hguard = ffn2_done.get(k - 2, 0)
        for hc in range(NHC):
            bk, freev, ki = bks.get()
            for dt in range(NDT):
                w = st_lq.wait_for(dh) + ([("cV", freev)] if dt == 0 else [])
                sc.op("tensor", lambda h, bk=bk, hc=hc, dt=dt, kk=kk: h.matmul(
                    bk[:, 0:CH],
                    w1qT[:, dt * HC + 128 * hc: dt * HC + 128 * (hc + 1)],
                    xqc[:, kk * NDT * CH + dt * CH: kk * NDT * CH + (dt + 1) * CH],
                    start=(dt == 0), stop=(dt == NDT - 1)),
                    waits=w, inc="cT" if dt == NDT - 1 else None)
            ev = sc.op("vector", lambda h, bk=bk, hc=hc, kk=kk: h.tensor_mul(
                hsl[kk][:, hc * CH:(hc + 1) * CH], bk[:, 0:CH], rxw[:]),
                waits=[("cT", sc.v("cT")), ("cV", rxwd), ("cT", hguard)], inc="cV")
            bks.set_free(ki, ev)
        mm_ct = sc.v("cT")
        sc.drain("vector")
        snap = sc.op("vector", lambda h, kk=kk, k=k: h.tensor_copy(
            halo[k % 3][:], hsl[kk][:].rearrange("p (c s) -> p c s", s=CH)[:, :, CH - 1]),
            inc="cV")
        return (snap, mm_ct)

    def conv_quant_ffn2(m, last):
        nonlocal pd6_free, pd45_free
        km = m % 2
        hm = hsl[km]
        hl = halo[(m - 1) % 3] if m >= 1 else zcol
        h_cv = h_done[m][0]
        r_cv = h_done[m + 1][0] if not last else h_done[m][0]
        v1_at = [0, 0]
        sil_at = {}
        vC_at = {}

        def emit_sil(hc):
            C = cacc3[:, (hc % 2) * CH:(hc % 2 + 1) * CH]
            wsil = [("cV", vC_at[hc])]
            if m >= 1:
                wsil.append(("cV", gm_prev[hc // 8]))
            sil_at[hc] = sc.op("scalar", lambda h, hc=hc, C=C: h.activation(
                out=gsl[:, hc * CH:(hc + 1) * CH], in_=C, func=AF.Silu,
                bias=cb_sb[:, hc:hc + 1]), waits=wsil, inc="cS")
        for hc in range(NHC):
            par = hc % 2
            A = cacc[:, par * CH:(par + 1) * CH]
            B = cacc2[:, par * CH:(par + 1) * CH]
            C = cacc3[:, par * CH:(par + 1) * CH]
            w = [("cV", v1_at[par]), ("cV", h_cv)]
            if m == 0 and hc == 0:
                w.append(("cG", cG_z))
            sc.op("scalar", lambda h, hc=hc, hm=hm, A=A: h.mul(
                A[:, 1:CH], hm[:, hc * CH: hc * CH + CH - 1],
                cw_sb[:, 3 * hc:3 * hc + 1]), waits=w)
            sA = sc.op("scalar", lambda h, hc=hc, hl=hl, A=A: h.mul(
                A[:, 0:1], hl[:, hc:hc + 1], cw_sb[:, 3 * hc:3 * hc + 1]),
                inc="cS")
            if hc >= 2:
                emit_sil(hc - 2)
            v1_at[par] = sc.op("vector", lambda h, hc=hc, hm=hm, A=A, B=B:
                               h.scalar_tensor_tensor(
                out=B, in0=hm[:, hc * CH:(hc + 1) * CH],
                scalar=cw_sb[:, 3 * hc + 1:3 * hc + 2], in1=A,
                op0=OP.mult, op1=OP.add), waits=[("cS", sA)], inc="cV")
            sc.drain("vector")
            wv2 = [("cS", sil_at.get(hc - 2, 0))]
            if last and hc == 0:
                wv2.append(("cG", cG_z))
            sc.op("vector", lambda h, hc=hc, hm=hm, B=B, C=C:
                  h.scalar_tensor_tensor(
                out=C[:, 0:CH - 1], in0=hm[:, hc * CH + 1:(hc + 1) * CH],
                scalar=cw_sb[:, 3 * hc + 2:3 * hc + 3], in1=B[:, 0:CH - 1],
                op0=OP.mult, op1=OP.add), waits=wv2)
            if last:
                rsl = zcol[:, hc:hc + 1]
            else:
                rsl = hsl[(m + 1) % 2][:, hc * CH: hc * CH + 1]
            vC = sc.op("vector", lambda h, hc=hc, rsl=rsl, B=B, C=C:
                       h.scalar_tensor_tensor(
                out=C[:, CH - 1:CH], in0=rsl,
                scalar=cw_sb[:, 3 * hc + 2:3 * hc + 3], in1=B[:, CH - 1:CH],
                op0=OP.mult, op1=OP.add), waits=[("cV", r_cv)], inc="cV")
            sc.drain("vector")
            vC_at[hc] = vC
        emit_sil(NHC - 2)
        emit_sil(NHC - 1)
        sil_done = sil_at[NHC - 1]
        mdone = sc.op("vector", lambda h: h.tensor_reduce(
            out=macc[:], in_=gsl[:].rearrange("p (c s) -> p s c", c=NHC),
            axis=AX, op=OP.max, apply_absolute_value=True),
            waits=[("cS", sil_done)], inc="cV")
        for blk in range(CH // 128):
            ct = sc.op("tensor", lambda h, blk=blk: h.transpose(
                pD[4 + blk][:, 0:128], macc[:, 128 * blk:128 * (blk + 1)],
                ident_f[:]), waits=[("cV", mdone), ("cV", pd45_free)], inc="cT")
            sc.op("vector", lambda h, blk=blk: h.tensor_reduce(
                out=mxt[:, blk:blk + 1], in_=pD[4 + blk][:, 0:128], axis=AX,
                op=OP.max), waits=[("cT", ct)], inc="cV")
        pd45_free = sc.v("cV")
        sc.drain("vector")
        sc.op("vector", lambda h: h.tensor_scalar(
            out=mxt[:, 2:4], in0=mxt[:, 0:2], scalar1=1e-5, scalar2=None,
            op0=OP.max))
        sc.drain("vector")
        sc.op("vector", lambda h: h.reciprocal(mxt[:, 4:6], mxt[:, 2:4]))
        sc.drain("vector")
        sc.op("vector", lambda h: h.tensor_scalar(
            out=mxt[:, 6:8], in0=mxt[:, 4:6], scalar1=127.0, scalar2=None,
            op0=OP.mult))
        scq = sc.op("vector", lambda h: h.tensor_scalar(
            out=rx2d[:], in0=mxt[:, 2:4], scalar1=w2sc[:, 0:1], scalar2=1.0 / 127.0,
            op0=OP.mult, op1=OP.mult), inc="cV")
        ct = sc.op("tensor", lambda h: h.transpose(
            pD[7][0:2, 0:128], mxt[:, 6:8], ident_f[:]),
            waits=[("cV", scq), ("cV", pd6_free)], inc="cT")
        s2rd = sc.op("vector", lambda h: h.tensor_copy(s2r2[:], pD[7][0:2, 0:128]),
                     waits=[("cT", ct)] + st_sm.wait_all(), inc="cV")
        i2 = st_sm.dma(lambda h: h.dma_start(
            out=rxr2[0:1, :].rearrange("o (a b) -> o a b", a=2), in_=s2r2[:]),
            waits=[("cV", s2rd), ("cT", rxr2_free[0])])
        ct = sc.op("tensor", lambda h: h.matmul(
            pD[6][:, 0:CH], onesc_f[0:1, :], rxr2[0:1, :], start=True, stop=True),
            waits=st_sm.wait_for(i2) + [("cV", pd6_free)], inc="cT")
        rxr2_free[0] = ct
        s2bd0 = sc.op("vector", lambda h: h.tensor_copy(s2b[:], pD[6][:, 0:CH]),
                      waits=[("cT", ct)], inc="cV")
        pd6_free = s2bd0
        sc.drain("vector")
        sc.op("vector", lambda h: h.tensor_copy(s2rep[:, 0:CH], s2b[:]))
        sc.drain("vector")
        for r in range(3):
            w0 = CH * (1 << r)
            sc.op("vector", lambda h, w0=w0: h.tensor_copy(
                s2rep[:, w0:2 * w0], s2rep[:, 0:w0]),
                inc="cV" if w0 * 2 == GW else None)
            sc.drain("vector")
        s2bd = sc.v("cV")
        q2_at = {}
        for gq in range(NHC * CH // GW):
            sl = slice(gq * GW, (gq + 1) * GW)
            sc.op("vector", lambda h, sl=sl: h.tensor_tensor(
                out=tq2[:], in0=gsl[:, sl], in1=s2rep[:], op=OP.mult),
                waits=[("cV", s2bd), ("cS", sil_at[(gq + 1) * (GW // CH) - 1])])
            sc.drain("vector")
            gm_prev[gq] = sc.op("vector", lambda h: h.tensor_scalar(
                out=uq2[:], in0=tq2[:], scalar1=1.0, scalar2=128.5,
                op0=OP.mult, op1=OP.add), inc="cV")
            sc.drain("vector")
            sc.op("vector", lambda h: h.tensor_sub(fr2[:], tq2[:], uq2[:]))
            sc.drain("vector")
            for thr in (-129.5, -128.5, -127.5, -126.5):
                sc.op("vector", lambda h, thr=thr: h.scalar_tensor_tensor(
                    out=tq2[:], in0=fr2[:], scalar=thr,
                    in1=uq2[:] if thr == -129.5 else tq2[:],
                    op0=OP.is_ge, op1=OP.add))
                sc.drain("vector")
            q2_at[gq] = sc.op("vector", lambda h, sl=sl, km=km: h.tensor_scalar(
                out=hsl[km][:, sl], in0=tq2[:], scalar1=1.0, scalar2=-130.0,
                op0=OP.mult, op1=OP.add), inc="cV")
            sc.drain("vector")
        piece_ct = {}
        for piece in range(NHC // 4):
            par = piece % 2
            dstw = w2c[:, par * 4 * E:(par + 1) * 4 * E] \
                .rearrange("p (b e) -> p b e", e=E)
            srcw = w2qT_d.rearrange("(b p) e -> p b e", p=128)[:, 4 * piece:4 * piece + 4, :]
            wld = [("cT", piece_ct.get(piece - 2, w2_prev[0]))]
            dhw = st_lw2.dma(lambda h, dstw=dstw, srcw=srcw: h.dma_start(
                out=dstw, in_=srcw), waits=wld)
            for pidx in range(4):
                hc = 4 * piece + pidx
                for sl2 in range(CH // 128):
                    for nch2 in range(2):
                        bky = pD[sl2 * 2 + nch2]
                        w = [("cV", q2_at[hc // 8])] + st_lw2.wait_for(dhw)
                        sc.op("tensor", lambda h, bky=bky, hc=hc, sl2=sl2, nch2=nch2,
                              km=km, par=par, pidx=pidx: h.matmul(
                            bky[:, 0:512],
                            hsl[km][:, hc * CH + 128 * sl2: hc * CH + 128 * (sl2 + 1)],
                            w2c[:, par * 4 * E + pidx * E + 512 * nch2:
                                par * 4 * E + pidx * E + 512 * (nch2 + 1)],
                            start=(hc == 0), stop=(hc == NHC - 1)),
                            waits=w,
                            inc="cT" if (pidx == 3 and sl2 == CH // 128 - 1
                                         and nch2 == 1) else None)
            piece_ct[piece] = sc.v("cT")
        ffn2_done[m] = sc.v("cT")
        w2_prev[0] = ffn2_done[m]
        for sl2 in range(CH // 128):
            for nch2 in range(2):
                bky = pD[sl2 * 2 + nch2]
                wv = [("cT", ffn2_done[m])] + st_sy.wait_for(len(y_dmas) - 1)
                sc.op("vector", lambda h, bky=bky, sl2=sl2, nch2=nch2: h.tensor_scalar(
                    out=y_sb[:, 512 * nch2: 512 * (nch2 + 1)],
                    in0=bky[:, 0:512], scalar1=rx2d[:, sl2:sl2 + 1], scalar2=None,
                    op0=OP.mult), waits=wv, inc="cV")
            st_sy.dma(lambda h, m=m, sl2=sl2: h.dma_start(
                out=y_d[CH * m + 128 * sl2: CH * m + 128 * (sl2 + 1), :],
                in_=y_sb[:]),
                waits=[("cV", sc.v("cV"))])
            y_dmas.append(st_sy.n)

    xq_dh[0] = load_xq_chunk(0)
    h_done[0] = ffn1(0)
    for m in range(NCH):
        if m + 1 < NCH:
            xq_dh[m + 1] = load_xq_chunk(m + 1)
            h_done[m + 1] = ffn1(m + 1)
        conv_quant_ffn2(m, last=(m == NCH - 1))
    sc.barrier()
    esD.close()
    esCD.close()

    esE = ExitStack()
    xa2 = sb(esE, "xa2", [128, 2 * E], F32)
    yb2 = sb(esE, "yb2", [128, 2 * E], F32)
    r2 = sb(esE, "r2", [128, 2 * E], F32)
    t2s = sb(esE, "t2s", [128, 2 * E], F32)
    ot = sb(esE, "ot", [128, 2 * E], F32)
    stE = sb(esE, "stE", [128, 24], F32)
    out_dmas = []
    odone_prev = 0
    for tp in range(NT // 2):
        tt2 = (2 * tp, 2 * tp + 1)
        dmai = {}
        for t in tt2:
            par = t % 2
            xat = xa2[:, par * E:(par + 1) * E]
            ybt = yb2[:, par * E:(par + 1) * E]
            w = [("cV", odone_prev)]
            i1 = st_le.dma(lambda h, xat=xat, t=t: h.dma_start(
                out=xat, in_=x2_d[128 * t:128 * (t + 1), :]), waits=w)
            i2 = st_le.dma(lambda h, ybt=ybt, t=t: h.dma_start(
                out=ybt, in_=y_d[128 * t:128 * (t + 1), :]))
            dmai[t] = (i1, i2)
        for t in tt2:
            par = t % 2
            sc.op("vector", lambda h, par=par: h.tensor_add(
                r2[:, par * E:(par + 1) * E], xa2[:, par * E:(par + 1) * E],
                yb2[:, par * E:(par + 1) * E]),
                waits=st_le.wait_for(dmai[t][0]) + st_le.wait_for(dmai[t][1]))
        sc.drain("vector")
        for t in tt2:
            par = t % 2
            sc.op("vector", lambda h, par=par: h.reduce_sum(
                stE[:, 12 * par:12 * par + 1], r2[:, par * E:(par + 1) * E],
                axis=AX))
        sc.drain("vector")
        mu_cv = 0
        for t in tt2:
            par = t % 2
            mu_cv = sc.op("vector", lambda h, par=par: h.tensor_scalar(
                out=stE[:, 12 * par + 1:12 * par + 2],
                in0=stE[:, 12 * par:12 * par + 1], scalar1=-1.0 / E,
                scalar2=None, op0=OP.mult), inc="cV")
        sc.drain("vector")
        sq_c = {}
        for t in tt2:
            par = t % 2
            sq_c[t] = sc.op("scalar", lambda h, par=par: h.activation(
                out=t2s[:, par * E:(par + 1) * E],
                in_=r2[:, par * E:(par + 1) * E], func=AF.Square,
                bias=stE[:, 12 * par + 1:12 * par + 2],
                accum_out=stE[:, 12 * par + 2:12 * par + 3]),
                waits=[("cV", mu_cv)], inc="cS")
        var_cv = 0
        for t in tt2:
            par = t % 2
            var_cv = sc.op("vector", lambda h, par=par: h.tensor_scalar(
                out=stE[:, 12 * par + 3:12 * par + 4],
                in0=stE[:, 12 * par + 2:12 * par + 3], scalar1=1.0 / E,
                scalar2=1e-5, op0=OP.mult, op1=OP.add),
                waits=[("cS", sq_c[t])], inc="cV")
        ln_c = {}
        for t in tt2:
            par = t % 2
            ln_c[t] = sc.op("scalar", lambda h, par=par: h.activation(
                out=stE[:, 12 * par + 4:12 * par + 5],
                in_=stE[:, 12 * par + 3:12 * par + 4], func=AF.Ln),
                waits=[("cV", var_cv)], inc="cS")
        rs_c = {}
        for t in tt2:
            par = t % 2
            rs_c[t] = sc.op("scalar", lambda h, par=par: h.activation(
                out=stE[:, 12 * par + 5:12 * par + 6],
                in_=stE[:, 12 * par + 4:12 * par + 5], func=AF.Exp,
                scale=cst[:, 1:2]), waits=[("cS", ln_c[t])], inc="cS")
        for t in tt2:
            par = t % 2
            otp = ot[:, par * E:(par + 1) * E]
            w2 = [("cS", rs_c[t])] + st_se.wait_for(len(out_dmas) - 2)
            odone = sc.op("vector", lambda h, otp=otp, par=par: h.tensor_scalar(
                out=otp, in0=r2[:, par * E:(par + 1) * E],
                scalar1=stE[:, 12 * par + 1:12 * par + 2],
                scalar2=stE[:, 12 * par + 5:12 * par + 6],
                op0=OP.add, op1=OP.mult), waits=w2, inc="cV")
            st_se.dma(lambda h, otp=otp, t=t: h.dma_start(
                out=out_d[128 * t:128 * (t + 1), :], in_=otp),
                waits=[("cV", odone)])
            out_dmas.append(st_se.n)
            odone_prev = odone
        sc.drain("vector")
    sc.op("sync", None, waits=st_se.wait_all())
    esE.close()

    with nc.Block() as block:
        sc.emit(block, sems)
    root.close()
    return nc


def kernel(L0_attn_x, L0_attn_Wq, L0_attn_Wk, L0_attn_Wv, L0_attn_rel_table,
           L0_attn_gamma, L0_attn_beta, L0_ffn_w1, L0_ffn_conv_w, L0_ffn_conv_b,
           L0_ffn_w2):
    x = np.ascontiguousarray(np.asarray(L0_attn_x, dtype=np.float32))
    T = np.asarray(L0_attn_rel_table, dtype=np.float32)

    t_ext = np.concatenate([T, np.repeat(T[-1:, :], 2048, axis=0)], axis=0)
    dvals = np.arange(-15, 16)
    p_i = np.arange(128)
    a_i = np.arange(128)
    idx = (2047 + 128 * dvals[:, None, None] + p_i[None, :, None]
           - a_i[None, None, :])
    toe = np.ascontiguousarray(
        t_ext[idx].transpose(3, 0, 1, 2))

    cwv = np.asarray(L0_ffn_conv_w, dtype=np.float32).reshape(NHC, 128, 3)
    cw = np.ascontiguousarray(cwv.transpose(1, 0, 2).reshape(128, NHC * 3))
    cb = np.ascontiguousarray(
        np.asarray(L0_ffn_conv_b, dtype=np.float32).reshape(NHC, 128).T)

    base = dict(
        wq=np.ascontiguousarray(L0_attn_Wq, dtype=np.float32),
        wk=np.ascontiguousarray(L0_attn_Wk, dtype=np.float32),
        wv=np.ascontiguousarray(L0_attn_Wv, dtype=np.float32),
        toe=toe,
        gam=np.ascontiguousarray(np.asarray(L0_attn_gamma, np.float32)[None, :]),
        bet=np.ascontiguousarray(np.asarray(L0_attn_beta, np.float32)[None, :]),
        w1=np.ascontiguousarray(L0_ffn_w1, dtype=np.float32),
        cw=cw, cb=cb,
        w2=np.ascontiguousarray(L0_ffn_w2, dtype=np.float32),
    )
    in_maps = [dict(base, x=np.ascontiguousarray(x[b])) for b in range(8)]

    prog = build_program()
    res = run_bass_kernel_spmd(prog, in_maps, core_ids=list(range(8)))
    key = {0: "out", 1: "o_s", 2: "x2_s", 3: "y_s"}[DEBUG]
    return np.stack([np.asarray(r[key]) for r in res.results], axis=0)
